# version 05
# baseline (speedup 1.0000x reference)
import math
import numpy as np
import ml_dtypes
from contextlib import ExitStack

import concourse.bass as bass
import concourse.mybir as mybir
from concourse.bass_utils import run_bass_kernel_spmd
from bass_rust import IndirectOffsetOnAxis

F32 = mybir.dt.float32
BF16 = mybir.dt.bfloat16
I32 = mybir.dt.int32
AF = mybir.ActivationFunctionType
ALU = mybir.AluOpType
AX = mybir.AxisListType

NCORE = 8
S = 16384
T = 2048
D = 1024
CT = 8
BF = ml_dtypes.bfloat16

L0A_W = 512 + T
L0A_NT = L0A_W // 128
L0A_XW = 2592

DEBUG_L0A = False


def host_consts():
    c = {}
    c["identb"] = np.eye(128, dtype=BF)
    c["ones128b"] = np.ones((128, 128), dtype=BF)
    c["onescol_b"] = np.ones((128, 1), dtype=BF)
    c["neghalf_f"] = np.full((128, 1), -0.5, dtype=np.float32)
    c["onesrow_f"] = np.ones((1, 128), dtype=np.float32)
    c["ones1f"] = np.ones((1, 1), dtype=np.float32)
    misc = np.zeros((128, 8), dtype=np.float32)
    misc[:, 0] = 1e-8
    misc[:, 1] = 1e-5
    misc[:, 2] = 1e-6
    misc[:, 3] = 1e-12
    misc[:, 4] = -2.0
    misc[:, 5] = 1.0
    misc[:, 6] = 1e-30
    misc[:, 7] = 0.5
    c["miscc"] = misc
    p = np.arange(128)[:, None]
    q = np.arange(128)[None, :]
    comb = np.zeros((128, 256), dtype=np.float32)
    comb[:, :128] = (q >= p - 31) & (q <= p)
    pb = p[:32]
    comb[:32, 128:] = ((q >= pb + 97) & (q <= pb + 127)) & (pb < 31)
    c["bandAB"] = comb.astype(BF)
    ind2 = np.zeros((128, 2), dtype=BF)
    ind2[0:64, 0] = 1
    ind2[64:128, 1] = 1
    c["ind2b"] = ind2
    sel8 = np.zeros((8, 4, 128), dtype=BF)
    for pt in range(4):
        sel8[2 * pt, pt, 0:64] = 1
        sel8[2 * pt + 1, pt, 64:128] = 1
    c["sel8b"] = sel8
    c["identf"] = np.eye(128, dtype=np.float32)
    c["ltrilf"] = (np.arange(128)[:, None] < np.arange(128)[None, :]).astype(np.float32)
    c["onesf8"] = np.ones((8, 128), dtype=np.float32)
    c["ident8f"] = np.eye(8, dtype=np.float32)
    c["onescolf"] = np.ones((128, 1), dtype=np.float32)
    oh8 = np.zeros((8, 8, 128), dtype=np.float32)
    for e in range(8):
        oh8[e, e, :] = 1.0
    c["oh8f"] = oh8
    return c


_CONSTS = host_consts()

_CHAINED = {
    "tensor_copy", "tensor_tensor", "tensor_add", "tensor_sub", "tensor_mul",
    "tensor_max", "tensor_scalar", "tensor_single_scalar", "tensor_scalar_add",
    "tensor_scalar_sub", "tensor_scalar_mul", "tensor_scalar_max",
    "tensor_scalar_min", "tensor_relu", "tensor_reduce", "reduce_max",
    "reduce_sum", "scalar_tensor_tensor", "tensor_tensor_scan", "memset",
    "memzero", "reciprocal", "select", "copy_predicated", "activation",
    "square", "sqrt", "mul", "add", "copy", "sign", "iota", "affine_select",
    "bn_stats", "bn_aggr", "stream_shuffle", "pool", "pool_avg",
    "pool_max", "max", "max_index",
}


class CEng:

    def __init__(self, kb, eng, sem_name):
        self._kb = kb
        self._eng = eng
        self._sem = sem_name

    def __getattr__(self, name):
        fn = getattr(self._eng, name)
        if name not in _CHAINED:
            return fn
        kb = self._kb
        sem = self._sem
        eng = self._eng

        def wrap(*a, **k):
            cnt = kb.n(sem)
            if cnt > 0:
                eng.wait_ge(kb.h(sem), cnt)
            inst = fn(*a, **k)
            kb.inc(inst, sem, 1)
            return inst
        return wrap


class KB:
    def __init__(self):
        self.nc = bass.Bass("TRN2", target_bir_lowering=False)
        self.ctx = ExitStack()
        self._sems = {}
        self.block = None
        self.stage = None
        self.cin_used = {}
        self._cins = {}
        self._names = {}

    def dram_in(self, name, shape, dtype):
        return self.nc.dram_tensor(name, list(shape), dtype, kind="ExternalInput").ap()

    def dram_out(self, name, shape, dtype):
        return self.nc.dram_tensor(name, list(shape), dtype, kind="ExternalOutput").ap()

    def dram_int(self, name, shape, dtype, shared=False):
        kw = {"addr_space": "Shared"} if shared else {}
        return self.nc.dram_tensor(name, list(shape), dtype, kind="Internal", **kw).ap()

    def push(self):
        self.stage = ExitStack()
        return self.stage

    def pop(self):
        self.stage.close()
        self.stage = None

    def sems_once(self, *names):
        for n in names:
            if n not in self._sems:
                self.sem(n)

    def cin(self, name):
        if name in self._cins:
            return self._cins[name]
        a = _CONSTS[name]
        self.cin_used[name] = a
        dt = BF16 if a.dtype == BF else (F32 if a.dtype == np.float32 else I32)
        ap = self.dram_in(name, a.shape, dt)
        self._cins[name] = ap
        return ap

    def _uniq(self, name):
        i = self._names.get(name, 0)
        self._names[name] = i + 1
        return name if i == 0 else f"{name}_{i}"

    def sbuf(self, name, shape, dtype):
        ctx = self.stage if self.stage is not None else self.ctx
        return ctx.enter_context(self.nc.sbuf_tensor(self._uniq(name), list(shape), dtype))

    def sbuf_cm(self, name, shape, dtype):
        return self.nc.sbuf_tensor(self._uniq(name), list(shape), dtype)

    def psum(self, name, shape, dtype=F32):
        ctx = self.stage if self.stage is not None else self.ctx
        return ctx.enter_context(self.nc.psum_tensor(self._uniq(name), list(shape), dtype))

    def sem(self, name):
        h = self.ctx.enter_context(self.nc.semaphore(name))
        self._sems[name] = [h, 0]

    def sems(self, *names):
        for n in names:
            self.sem(n)

    def inc(self, inst, name, k=1):
        s = self._sems[name]
        inst.then_inc(s[0], k)
        s[1] += k
        return inst

    def n(self, name):
        return self._sems[name][1]

    def h(self, name):
        return self._sems[name][0]

    def snap(self):
        return {k: v[1] for k, v in self._sems.items()}

    def start(self):
        self.block = self.ctx.enter_context(self.nc.Block())
        self.sems("V", "SC", "G")
        return self.block

    def phase(self, **engine_fns):
        snapshot = self.snap()
        for engname, fn in engine_fns.items():
            @getattr(self.block, engname)
            def _(eng, fn=fn, engname=engname):
                for k, c in snapshot.items():
                    if c > 0:
                        eng.wait_ge(self._sems[k][0], c)
                if engname == "vector":
                    fn(CEng(self, eng, "V"))
                elif engname == "scalar":
                    fn(CEng(self, eng, "SC"))
                elif engname == "gpsimd":
                    fn(CEng(self, eng, "G"))
                else:
                    fn(eng)

    def done(self):
        self.ctx.close()
        return self.nc


def alloc_common(kb):
    kb.t_ones = kb.sbuf("t_ones", [128, 128], BF16)
    kb.t_misc = kb.sbuf("t_misc", [128, 8], F32)
    kb.d_ones = kb.cin("ones128b")
    kb.d_misc = kb.cin("miscc")


def load_common(kb, g):
    kb.inc(g.dma_start(out=kb.t_ones[:], in_=kb.d_ones), "ld", 16)
    kb.inc(g.dma_start(out=kb.t_misc[:], in_=kb.d_misc), "ld", 16)


def alloc_ln(kb, scr4=None):
    kb.t_lnxb = kb.sbuf("lnxb", [128, CT, 512], BF16)
    kb.t_lnrow = scr4 if scr4 is not None else kb.sbuf("lnrow", [128, 4, 512], F32)


def emit_ln(kb, xt, xnb, x0, n0, w, pA, pB):
    done = 0
    while done < w:
        cw = min(512, w - done)
        xs = slice(x0 + done, x0 + done + cw)
        ns = slice(n0 + done, n0 + done + cw)
        done += cw

        def p_cast(sc, xs=xs, cw=cw):
            for k in range(CT):
                sc.activation(out=kb.t_lnxb[:, k, 0:cw], in_=xt[:, k, xs], func=AF.Copy)
        kb.phase(scalar=p_cast)

        def p_mmA(t, cw=cw):
            for k in range(CT):
                i = t.matmul(out=pA[:, 0:cw], lhsT=kb.t_ones[:], rhs=kb.t_lnxb[:, k, 0:cw],
                             start=(k == 0), stop=(k == CT - 1))
            kb.inc(i, "m")
        kb.phase(tensor=p_mmA)

        def p_sq(v, cw=cw):
            for k in range(CT):
                v.tensor_mul(out=kb.t_lnxb[:, k, 0:cw], in0=kb.t_lnxb[:, k, 0:cw],
                             in1=kb.t_lnxb[:, k, 0:cw])
        kb.phase(vector=p_sq)

        def p_mmB(t, cw=cw):
            for k in range(CT):
                i = t.matmul(out=pB[:, 0:cw], lhsT=kb.t_ones[:], rhs=kb.t_lnxb[:, k, 0:cw],
                             start=(k == 0), stop=(k == CT - 1))
            kb.inc(i, "m")
        kb.phase(tensor=p_mmB)

        def p_v1(v, cw=cw):
            v.tensor_scalar_mul(kb.t_lnrow[:, 0, 0:cw], pA[:, 0:cw], 1.0 / D)
            v.tensor_scalar_mul(kb.t_lnrow[:, 1, 0:cw], pB[:, 0:cw], 1.0 / D)
        kb.phase(vector=p_v1)

        def p_s1(sc, cw=cw):
            sc.activation(out=kb.t_lnrow[:, 2, 0:cw], in_=kb.t_lnrow[:, 0, 0:cw],
                          func=AF.Square)
        kb.phase(scalar=p_s1)

        def p_v2(v, cw=cw):
            v.tensor_sub(out=kb.t_lnrow[:, 1, 0:cw], in0=kb.t_lnrow[:, 1, 0:cw],
                         in1=kb.t_lnrow[:, 2, 0:cw])
        kb.phase(vector=p_v2)

        def p_s2(sc, cw=cw):
            sc.activation(out=kb.t_lnrow[:, 1, 0:cw], in_=kb.t_lnrow[:, 1, 0:cw],
                          func=AF.Sqrt, bias=kb.t_misc[:, 1:2])
        kb.phase(scalar=p_s2)

        def p_v3(v, xs=xs, ns=ns, cw=cw):
            v.reciprocal(out=kb.t_lnrow[:, 3, 0:cw], in_=kb.t_lnrow[:, 1, 0:cw])
            for k in range(CT):
                v.tensor_sub(out=kb.t_lnrow[:, 2, 0:cw], in0=xt[:, k, xs],
                             in1=kb.t_lnrow[:, 0, 0:cw])
                v.tensor_mul(out=xnb[:, k, ns], in0=kb.t_lnrow[:, 2, 0:cw],
                             in1=kb.t_lnrow[:, 3, 0:cw])
        kb.phase(vector=p_v3)


def build_l0a_sig():
    kb = KB()
    io = {"xh": kb.dram_in("xh", [D, T], F32),
          "encq": kb.dram_in("encq", [8, D, 128], F32),
          "qsums": kb.dram_out("qsums", [128, 16], F32), "x0": 0}
    alloc_common(kb)
    kb.sems("ld", "m", "st")
    kb.start()
    kb.push()
    emit_l0a_sig(kb, io)
    kb.pop()
    return kb


def emit_l0a_sig(kb, io):
    xh, encq, qsums, x0 = io["xh"], io["encq"], io["qsums"], io["x0"]
    xt = kb.sbuf("xt", [128, CT, T], F32)
    xnb = kb.sbuf("xnb", [128, CT, T], BF16)
    eq = kb.sbuf("eq", [128, 8, CT, 128], BF16)
    alloc_ln(kb)
    qtile = kb.sbuf("qtile", [128, 2, 512], F32)
    acc = kb.sbuf("acc", [128, 16], F32)
    red = kb.sbuf("red", [128, 2], F32)
    pQ = [kb.psum(f"pQ{i}", [128, 512]) for i in range(2)]
    pA = kb.psum("pA", [128, 512])
    pB = kb.psum("pB", [128, 512])

    def p_load_sy(sy):
        for k in range(CT):
            kb.inc(sy.dma_start(out=xt[:, k],
                                in_=xh[128 * k:128 * (k + 1), x0:x0 + T]), "ld", 16)

    def p_load_g(g):
        load_common(kb, g)
        for h in range(8):
            for k in range(CT):
                kb.inc(g.dma_start(out=eq[:, h, k],
                                   in_=encq[h, 128 * k:128 * (k + 1), :]), "ld", 16)
    kb.phase(sync=p_load_sy, gpsimd=p_load_g)

    emit_ln(kb, xt, xnb, 0, 0, T, pA, pB)

    NCH = T // 512
    NIT = NCH * 8
    b_m, b_sc, b_v = kb.n("m"), kb.n("SC"), kb.n("V")

    def p_q_mm(t):
        for idx in range(NIT):
            ch, h = divmod(idx, 8)
            if idx >= 2:
                t.wait_ge(kb.h("SC"), b_sc + idx - 1)
            for k in range(CT):
                i = t.matmul(out=pQ[idx % 2][:], lhsT=eq[:, h, k],
                             rhs=xnb[:, k, 512 * ch:512 * (ch + 1)],
                             start=(k == 0), stop=(k == CT - 1))
            kb.inc(i, "m")

    def p_q_relu(sc):
        for idx in range(NIT):
            sc._eng.wait_ge(kb.h("m"), b_m + idx + 1)
            if idx >= 2:
                sc._eng.wait_ge(kb.h("V"), b_v + 1 + 4 * (idx - 1))
            sc.activation(out=qtile[:, idx % 2], in_=pQ[idx % 2][:], func=AF.Relu)

    def p_q_sum(v):
        v.memset(acc[:], 0.0)
        for idx in range(NIT):
            h = idx % 8
            v._eng.wait_ge(kb.h("SC"), b_sc + idx + 1)
            v.reduce_sum(out=red[:, 0:1], in_=qtile[:, idx % 2], axis=AX.X)
            v.reduce_sum(out=red[:, 1:2], in_=qtile[:, idx % 2, ::8], axis=AX.X)
            v.tensor_add(out=acc[:, h:h + 1], in0=acc[:, h:h + 1], in1=red[:, 0:1])
            v.tensor_add(out=acc[:, 8 + h:9 + h], in0=acc[:, 8 + h:9 + h],
                         in1=red[:, 1:2])
    kb.phase(tensor=p_q_mm, scalar=p_q_relu, vector=p_q_sum)

    def p_store(g):
        kb.inc(g.dma_start(out=qsums, in_=acc[:]), "st", 16)
        g.wait_ge(kb.h("st"), kb.n("st"))
    kb.phase(gpsimd=p_store)


def build_l0a_main(nsplit=10):
    kb = KB()
    io = {"xh": kb.dram_in("xh", [D, L0A_XW], F32),
          "qsums_all": kb.dram_in("qsums_all", [NCORE, 128, 16], F32),
          "encq": kb.dram_in("encq", [8, D, 128], F32),
          "encv": kb.dram_in("encv", [8, D, 128], F32),
          "decoder": kb.dram_in("decoder", [D, D], F32),
          "hebbian": kb.dram_in("hebbian", [8, 128, 128], F32),
          "m_w1": kb.dram_in("m_w1", [D, 512], F32),
          "m_b1": kb.dram_in("m_b1", [1, 512], F32),
          "m_w2": kb.dram_in("m_w2", [512, D], F32),
          "m_b2": kb.dram_in("m_b2", [1, D], F32),
          "base_T": kb.dram_in("base_T", [128, 8], F32),
          "y1": kb.dram_out("y1", [D, L0A_W], F32)}
    alloc_common(kb)
    kb.sems("ld", "m", "st", "am", "tm", "qm", "im", "ic", "dn")
    kb.start()
    kb.push()
    emit_l0a_main(kb, io, nsplit=nsplit)
    kb.pop()
    return kb


def emit_l0a_main(kb, io, nsplit=10):
    xh, qsums_all = io["xh"], io["qsums_all"]
    encq, encv, dec, heb = io["encq"], io["encv"], io["decoder"], io["hebbian"]
    mw1, mb1, mw2, mb2 = io["m_w1"], io["m_b1"], io["m_w2"], io["m_b2"]
    baseT, y1 = io["base_T"], io["y1"]
    d_identb = kb.cin("identb")
    d_onescol_b = kb.cin("onescol_b")
    d_neghalf = kb.cin("neghalf_f")
    d_onesrow = kb.cin("onesrow_f")
    d_ones1f = kb.cin("ones1f")
    d_band = kb.cin("bandAB")
    if DEBUG_L0A:
        dbg_sig = kb.dram_out("dbg_sig", [128, 8], F32)
        dbg_mdiag = kb.dram_out("dbg_mdiag", [128, 24], F32)
        dbg_xnb = kb.dram_out("dbg_xnb", [128, CT, 256], F32)
        dbg_su = kb.dram_out("dbg_su", [128, 8, 256], F32)
        dbg_arow = kb.dram_out("dbg_arow", [1, 8, 416], F32)
        dbg_infl = kb.dram_out("dbg_infl", [128, 2, 256], F32)
        dbg_sinvq = kb.dram_out("dbg_sinvq", [128, 2, 8], F32)
        dbg_vtm = kb.dram_out("dbg_vtm", [128, 8, 128], F32)
        dbg_qtm = kb.dram_out("dbg_qtm", [128, 8, 128], F32)

    xt = kb.sbuf("xt", [128, CT, L0A_XW], F32)
    mlp_ctx = ExitStack()

    def mlp_sbuf(name, shape, dtype):
        return mlp_ctx.enter_context(kb.sbuf_cm(name, shape, dtype))

    t_idb = kb.sbuf("t_idb", [128, 128], BF16)
    t_oneb = kb.sbuf("t_oneb", [128, 1], BF16)
    t_nh = kb.sbuf("t_nh", [128, 1], F32)
    t_onr = kb.sbuf("t_onr", [1, 128], F32)
    t_one1f = kb.sbuf("t_one1f", [1, 1], F32)
    t_band = kb.sbuf("t_band", [128, 256], BF16)
    eq = kb.sbuf("eq", [128, 8, CT, 128], BF16)
    ev = kb.sbuf("ev", [128, 8, CT, 128], BF16)
    sdec = kb.sbuf("sdec", [128, CT, D], BF16)
    shebs = kb.sbuf("shebs", [128, 8, 128], F32)
    sbase = kb.sbuf("sbase", [128, 8], F32)
    modcol = kb.sbuf("modcol", [128, 8], F32)
    uscale = kb.sbuf("uscale", [128, 8], F32)
    recipm = kb.sbuf("recipm", [128, 8], F32)
    scr4 = kb.sbuf("scr4", [128, 4, 512], F32)
    alloc_ln(kb, scr4=scr4)
    srow1 = kb.sbuf("srow1", [1, 512], F32)
    srow2 = kb.sbuf("srow2", [1, D], F32)
    sig = kb.sbuf("sig", [128, 8], F32)
    sigb = kb.sbuf("sigb", [128, 8], BF16)
    h1row = kb.sbuf("h1row", [1, 512], F32)
    h1colb = kb.sbuf("h1colb", [128, 4], BF16)
    modrow = kb.sbuf("modrow", [1, D], F32)
    sheb = shebs
    smw1 = mlp_sbuf("smw1", [128, CT, 512], BF16)
    smw2 = mlp_sbuf("smw2", [128, 4, D], BF16)
    ssum = mlp_sbuf("ssum", [128, NCORE, 16], F32)

    pG = [kb.psum(f"pG{i}", [128, 256]) for i in range(2)]
    pCH = [kb.psum(f"pCH{i}", [128, 512]) for i in range(2)]
    pT = [kb.psum(f"pT{i}", [128, 128], BF16) for i in range(2)]
    pDec = [kb.psum(f"pDec{i}", [128, 512]) for i in range(2)]

    def p_load_sy(sy):
        for k in range(CT):
            kb.inc(sy.dma_start(out=xt[:, k], in_=xh[128 * k:128 * (k + 1), :]), "ld", 16)
        kb.inc(sy.dma_start(out=t_nh[:], in_=d_neghalf), "ld", 16)
        kb.inc(sy.dma_start(out=t_onr[:], in_=d_onesrow), "ld", 16)
        kb.inc(sy.dma_start(out=t_one1f[:], in_=d_ones1f), "ld", 16)
        for h in range(8):
            kb.inc(sy.dma_start(out=sheb[:, h], in_=heb[h]), "ld", 16)
        for r in range(NCORE):
            kb.inc(sy.dma_start(out=ssum[:, r], in_=qsums_all[r]), "ld", 16)
        kb.inc(sy.dma_start(out=srow1[:], in_=mb1), "ld", 16)
        kb.inc(sy.dma_start(out=srow2[:], in_=mb2), "ld", 16)
        kb.inc(sy.dma_start(out=sbase[:], in_=baseT), "ld", 16)

    def p_load_g(g):
        load_common(kb, g)
        kb.inc(g.dma_start(out=t_idb[:], in_=d_identb), "ld", 16)
        kb.inc(g.dma_start(out=t_oneb[:], in_=d_onescol_b), "ld", 16)
        kb.inc(g.dma_start(out=t_band[:], in_=d_band), "ld", 16)
        for h in range(8):
            for k in range(CT):
                kb.inc(g.dma_start(out=eq[:, h, k], in_=encq[h, 128 * k:128 * (k + 1), :]), "ld", 16)
                kb.inc(g.dma_start(out=ev[:, h, k], in_=encv[h, 128 * k:128 * (k + 1), :]), "ld", 16)
        for k in range(CT):
            kb.inc(g.dma_start(out=sdec[:, k], in_=dec[128 * k:128 * (k + 1), :]), "ld", 16)
            kb.inc(g.dma_start(out=smw1[:, k], in_=mw1[128 * k:128 * (k + 1), :]), "ld", 16)
        for j in range(4):
            kb.inc(g.dma_start(out=smw2[:, j], in_=mw2[128 * j:128 * (j + 1), :]), "ld", 16)
    kb.phase(sync=p_load_sy, gpsimd=p_load_g)

    def p_sig(v):
        v.tensor_add(out=sig[:], in0=ssum[:, 0, 0:8], in1=ssum[:, 1, 0:8])
        for r in range(2, NCORE):
            v.tensor_add(out=sig[:], in0=sig[:], in1=ssum[:, r, 0:8])
        v.tensor_add(out=modcol[:], in0=ssum[:, 0, 8:16], in1=ssum[:, 1, 8:16])
        for r in range(2, NCORE):
            v.tensor_add(out=modcol[:], in0=modcol[:], in1=ssum[:, r, 8:16])
        v.tensor_scalar_mul(sig[:], sig[:], 1.0 / S)
        v.tensor_scalar_mul(modcol[:], modcol[:], 0.5 / (S / 8.0))
        v.tensor_add(out=sig[:], in0=sig[:], in1=modcol[:])
        v.tensor_copy(out=sigb[:], in_=sig[:])
    kb.phase(vector=p_sig)

    def p_mlp1(t):
        for h in range(8):
            i = t.matmul(out=pDec[0][0:1, :], lhsT=sigb[:, h:h + 1], rhs=smw1[:, h],
                         start=(h == 0), stop=(h == 7))
        kb.inc(i, "m")
    kb.phase(tensor=p_mlp1)

    def p_mlp2(v):
        v.tensor_add(out=h1row[:], in0=pDec[0][0:1, :], in1=srow1[:])
    kb.phase(vector=p_mlp2)

    def p_mlp3(sc):
        sc.activation(out=h1row[:], in_=h1row[:], func=AF.Gelu)
    kb.phase(scalar=p_mlp3)

    def p_mlp4(t):
        for j in range(4):
            kb.inc(t.matmul(out=pCH[0][:, j:j + 1], lhsT=h1row[0:1, 128 * j:128 * (j + 1)],
                            rhs=t_one1f[:], start=True, stop=True), "m")
    kb.phase(tensor=p_mlp4)

    def p_mlp5(v):
        v.tensor_copy(out=h1colb[:], in_=pCH[0][:, 0:4])
    kb.phase(vector=p_mlp5)

    def p_mlp6(t):
        for hf in range(2):
            for j in range(4):
                i = t.matmul(out=pDec[hf][0:1, :], lhsT=h1colb[:, j:j + 1],
                             rhs=smw2[:, j, 512 * hf:512 * (hf + 1)],
                             start=(j == 0), stop=(j == 3))
            kb.inc(i, "m")
    kb.phase(tensor=p_mlp6)

    def p_mlp7(v):
        v.tensor_add(out=modrow[0:1, 0:512], in0=pDec[0][0:1, :], in1=srow2[0:1, 0:512])
        v.tensor_add(out=modrow[0:1, 512:1024], in0=pDec[1][0:1, :],
                     in1=srow2[0:1, 512:1024])
    kb.phase(vector=p_mlp7)

    def p_mlp8(t):
        for j in range(8):
            kb.inc(t.matmul(out=pCH[0][:, j:j + 1], lhsT=modrow[0:1, 128 * j:128 * (j + 1)],
                            rhs=t_one1f[:], start=True, stop=True), "m")
    kb.phase(tensor=p_mlp8)

    def p_md1(v):
        v.tensor_copy(out=modcol[:], in_=pCH[0][:, 0:8])
        v.tensor_scalar_mul(modcol[:], modcol[:], 0.1)
        v.tensor_add(out=modcol[:], in0=modcol[:], in1=sbase[:])
    kb.phase(vector=p_md1)

    def p_md2(sc):
        sc.activation(out=uscale[:], in_=modcol[:], func=AF.Abs)
        sc.activation(out=uscale[:], in_=uscale[:], func=AF.Exp, scale=-1.0)
        sc.activation(out=uscale[:], in_=uscale[:], func=AF.Ln, bias=kb.t_misc[:, 5:6])
    kb.phase(scalar=p_md2)

    def p_md3(v):
        v.tensor_scalar_max(recipm[:], modcol[:], 0.0)
        v.tensor_add(out=recipm[:], in0=recipm[:], in1=uscale[:])
        v.tensor_scalar_add(recipm[:], recipm[:], 1e-6)
    kb.phase(vector=p_md3)

    def p_md4(sc):
        sc.activation(out=uscale[:], in_=recipm[:], func=AF.Sqrt)
    kb.phase(scalar=p_md4)

    def p_md5(v):
        v.reciprocal(out=recipm[:], in_=recipm[:])
        v.reciprocal(out=modcol[:], in_=uscale[:])
    kb.phase(vector=p_md5)

    def p_hebs(v):
        for h in range(8):
            v.tensor_scalar_mul(shebs[:, h], sheb[:, h], modcol[:, h:h + 1])
    kb.phase(vector=p_hebs)

    mlp_ctx.close()
    NT_H = L0A_NT // nsplit
    WU = NT_H * 128 + 160
    su = kb.sbuf("su", [128, 8, WU], F32)
    arow = kb.sbuf("arow", [1, 8, WU], F32)
    xnb = kb.sbuf("xnb", [128, CT, WU], BF16)
    svtm = kb.sbuf("svtm", [128, NT_H + 1, 8, 128], BF16)
    sqtm = kb.sbuf("sqtm", [128, NT_H, 8, 128], BF16)
    sinvq = kb.sbuf("sinvq", [128, NT_H, 8], F32)
    u2s = scr4
    sinvd = kb.sbuf("sinvd", [128, 2], F32)
    sdq = scr4[:, 2:4, 0:256]
    scf = scr4[:, 0:2, 0:128]
    bfp = kb.t_lnxb
    sinfl = bfp[:, 0, :].rearrange("p (a b) -> p a b", a=2)
    schb = bfp[:, 1, 0:256].rearrange("p (a b) -> p a b", a=2)
    soh = bfp[:, 2, 0:256].rearrange("p (a b) -> p a b", a=2)
    sflat = bfp[:, 3:5, :].rearrange("p a b -> p (a b)").rearrange(
        "p (o h e) -> p o h e", o=1, h=8)

    for half in range(nsplit):
        m0 = half * NT_H
        u0 = m0 * 128

        wln = min(WU, L0A_XW - u0)
        emit_ln(kb, xt, xnb, u0, 0, wln, pDec[0], pDec[1])
        if wln < WU:
            def p_zf(v, wln=wln):
                for k in range(CT):
                    v.memset(xnb[:, k, wln:WU], 0.0)
            kb.phase(vector=p_zf)

        chunks = []
        off = 0
        while off < WU:
            chunks.append((off, min(512, WU - off)))
            off += 512
        b_am, b_sc, b_v = kb.n("am"), kb.n("SC"), kb.n("V")

        def p_u_t(t, chunks=chunks, b_am=b_am, b_sc=b_sc, b_v=b_v):
            for ci, (c0, w) in enumerate(chunks):
                for h in range(8):
                    idx = ci * 8 + h
                    if idx >= 2:
                        t.wait_ge(kb.h("SC"), b_sc + 2 * (idx - 2) + 1)
                    for k in range(CT):
                        i = t.matmul(out=pDec[idx % 2][:, 0:w], lhsT=eq[:, h, k],
                                     rhs=xnb[:, k, c0:c0 + w],
                                     start=(k == 0), stop=(k == CT - 1))
                    kb.inc(i, "am")
                    t.wait_ge(kb.h("SC"), b_sc + 2 * idx + 2)
                    if idx >= 2:
                        t.wait_ge(kb.h("V"), b_v + idx - 1)
                    kb.inc(t.matmul(out=pCH[idx % 2][0:1, 0:w], lhsT=t_nh[:],
                                    rhs=u2s[:, idx % 2, 0:w], start=True, stop=True), "am")

        def p_u_s(sc, chunks=chunks, b_am=b_am, b_v=b_v):
            for ci, (c0, w) in enumerate(chunks):
                for h in range(8):
                    idx = ci * 8 + h
                    sc._eng.wait_ge(kb.h("am"), b_am + 2 * idx + 1)
                    sc.activation(out=su[:, h, c0:c0 + w], in_=pDec[idx % 2][:, 0:w],
                                  func=AF.Relu, scale=uscale[:, h:h + 1])
                    if idx >= 2:
                        sc._eng.wait_ge(kb.h("V"), b_v + idx - 1)
                    sc.activation(out=u2s[:, idx % 2, 0:w], in_=su[:, h, c0:c0 + w],
                                  func=AF.Square)

        def p_u_v(v, chunks=chunks, b_am=b_am):
            for ci, (c0, w) in enumerate(chunks):
                for h in range(8):
                    idx = ci * 8 + h
                    v._eng.wait_ge(kb.h("am"), b_am + 2 * idx + 2)
                    v.tensor_copy(out=arow[0:1, h, c0:c0 + w],
                                  in_=pCH[idx % 2][0:1, 0:w])
        kb.phase(tensor=p_u_t, scalar=p_u_s, vector=p_u_v)

        b_tm, b_sc = kb.n("tm"), kb.n("SC")
        vt_jobs = [(kv, 128 * kv) for kv in range(NT_H + 1) if 128 * kv < WU - 30]

        def p_vt_t(t, vt_jobs=vt_jobs, b_sc=b_sc):
            for ji, (kv, uw) in enumerate(vt_jobs):
                for h in range(8):
                    idx = ji * 8 + h
                    if idx >= 2:
                        t.wait_ge(kb.h("SC"), b_sc + idx - 1)
                    w = min(128, WU - uw)
                    for k in range(CT):
                        i = t.matmul(out=pDec[idx % 2][:, 0:128], lhsT=xnb[:, k, uw:uw + w],
                                     rhs=ev[:, h, k], start=(k == 0), stop=(k == CT - 1))
                    kb.inc(i, "tm")

        def p_vt_s(sc, vt_jobs=vt_jobs, b_tm=b_tm):
            for ji, (kv, uw) in enumerate(vt_jobs):
                for h in range(8):
                    idx = ji * 8 + h
                    sc._eng.wait_ge(kb.h("tm"), b_tm + idx + 1)
                    w = min(128, WU - uw)
                    sc.activation(out=svtm[0:w, kv, h], in_=pDec[idx % 2][0:w, 0:128],
                                  func=AF.Relu)
        kb.phase(tensor=p_vt_t, scalar=p_vt_s)

        b_qm, b_sc, b_v = kb.n("qm"), kb.n("SC"), kb.n("V")

        def p_qt_t(t, b_qm=b_qm, b_sc=b_sc, b_v=b_v):
            for mi in range(NT_H):
                iw = 31 + 128 * mi
                for h in range(8):
                    idx = mi * 8 + h
                    if idx >= 2:
                        t.wait_ge(kb.h("SC"), b_sc + 2 * (idx - 2) + 1)
                    for k in range(CT):
                        i = t.matmul(out=pDec[idx % 2][:, 0:128], lhsT=xnb[:, k, iw:iw + 128],
                                     rhs=eq[:, h, k], start=(k == 0), stop=(k == CT - 1))
                    kb.inc(i, "qm")
                    t.wait_ge(kb.h("SC"), b_sc + 2 * idx + 2)
                    if idx >= 2:
                        t.wait_ge(kb.h("V"), b_v + idx - 1)
                    kb.inc(t.matmul(out=pG[idx % 2][0:128, 128:129],
                                    lhsT=u2s[:, idx % 2, 0:128],
                                    rhs=recipm[:, h:h + 1], start=True, stop=True), "qm")

        def p_qt_s(sc, b_qm=b_qm, b_v=b_v):
            for mi in range(NT_H):
                iw = 31 + 128 * mi
                for h in range(8):
                    idx = mi * 8 + h
                    sc._eng.wait_ge(kb.h("qm"), b_qm + 2 * idx + 1)
                    sc.activation(out=sqtm[:, mi, h], in_=pDec[idx % 2][:, 0:128],
                                  func=AF.Relu)
                    if idx >= 2:
                        sc._eng.wait_ge(kb.h("V"), b_v + idx - 1)
                    sc.activation(out=u2s[:, idx % 2, 0:128], in_=su[:, h, iw:iw + 128],
                                  func=AF.Square)

        def p_qt_v(v, b_qm=b_qm):
            for mi in range(NT_H):
                for h in range(8):
                    idx = mi * 8 + h
                    v._eng.wait_ge(kb.h("qm"), b_qm + 2 * idx + 2)
                    v.tensor_copy(out=sinvq[:, mi, h:h + 1],
                                  in_=pG[idx % 2][0:128, 128:129])
        kb.phase(tensor=p_qt_t, scalar=p_qt_s, vector=p_qt_v)

        def p_qf_s(sc):
            for mi in range(NT_H):
                sc.activation(out=sinvq[:, mi], in_=sinvq[:, mi], func=AF.Sqrt)
        kb.phase(scalar=p_qf_s)

        def p_qf_v(v):
            for mi in range(NT_H):
                v.tensor_scalar_max(sinvq[:, mi], sinvq[:, mi], 1e-12)
                v.reciprocal(out=sinvq[:, mi], in_=sinvq[:, mi])
        kb.phase(vector=p_qf_v)

        b_im, b_ic = kb.n("im"), kb.n("ic")
        b_sc, b_v = kb.n("SC"), kb.n("V")
        VPT = 8 * 7 + 8

        def vcnt(mi, h, j, b_v=b_v):
            return b_v + mi * VPT + h * 7 + j

        def scnt(mi, h, j, b_sc=b_sc):
            return b_sc + (mi * 8 + h) * 3 + j

        def p_at_t(t, b_im=b_im, b_ic=b_ic, b_v=b_v):
            for mi in range(NT_H):
                iw = 31 + 128 * mi
                jA = iw - 31
                jB = iw + 97
                for h in range(8):
                    idx = mi * 8 + h
                    pp = idx % 2
                    pmi, ph = divmod(idx - 2, 8)
                    if idx >= 2:
                        t.wait_ge(kb.h("SC"), scnt(pmi, ph, 2))
                    t.matmul(out=pG[pp][:, 0:128], lhsT=su[:, h, jA:jA + 128],
                             rhs=su[:, h, iw:iw + 128], start=True, stop=False)
                    t.matmul(out=pG[pp][:, 0:128], lhsT=arow[0:1, h, jA:jA + 128],
                             rhs=t_onr[0:1, 0:128], start=False, stop=False)
                    t.matmul(out=pG[pp][:, 0:128], lhsT=t_onr[0:1, 0:128],
                             rhs=arow[0:1, h, iw:iw + 128], start=False, stop=True)
                    t.matmul(out=pG[pp][0:31, 128:256], lhsT=su[:, h, jB:jB + 31],
                             rhs=su[:, h, iw:iw + 128], start=True, stop=False)
                    t.matmul(out=pG[pp][0:31, 128:256], lhsT=arow[0:1, h, jB:jB + 31],
                             rhs=t_onr[0:1, 0:128], start=False, stop=False)
                    kb.inc(t.matmul(out=pG[pp][0:31, 128:256], lhsT=t_onr[0:1, 0:31],
                                    rhs=arow[0:1, h, iw:iw + 128], start=False, stop=True),
                           "im")
                    t.wait_ge(kb.h("V"), vcnt(mi, h, 2))
                    t.matmul(out=pCH[pp][:, 256:257], lhsT=sinfl[:, pp, 0:128],
                             rhs=t_oneb[:], start=True, stop=False)
                    t.matmul(out=pCH[pp][:, 256:257], lhsT=sinfl[0:31, pp, 128:256],
                             rhs=t_oneb[0:31, :], start=False, stop=True)
                    t.matmul(out=pCH[pp][:, 0:128], lhsT=sinfl[:, pp, 0:128],
                             rhs=svtm[:, mi, h], start=True, stop=False)
                    t.matmul(out=pCH[pp][:, 0:128], lhsT=sinfl[0:31, pp, 128:256],
                             rhs=svtm[0:31, mi + 1, h], start=False, stop=True)
                    kb.inc(t.matmul(out=pCH[pp][:, 128:256], lhsT=su[:, h, iw:iw + 128],
                                    rhs=shebs[:, h], start=True, stop=True), "im")
                    t.wait_ge(kb.h("V"), vcnt(mi, h, 7))
                    if idx >= 2:
                        t.wait_ge(kb.h("SC"), scnt(pmi, ph, 3))
                    kb.inc(t.transpose(out=pT[pp][:], in_=soh[:, pp], identity=t_idb[:]),
                           "im")
                t.wait_ge(kb.h("SC"), scnt(mi, 7, 3))
                if mi >= 1:
                    t.wait_ge(kb.h("V"), b_v + mi * VPT)
                for ohalf in range(2):
                    for o in range(4):
                        oc = ohalf * 4 + o
                        for k in range(CT):
                            i = t.matmul(out=pDec[ohalf][:, 128 * o:128 * (o + 1)],
                                         lhsT=sdec[:, k, 128 * oc:128 * (oc + 1)],
                                         rhs=sflat[:, 0, k],
                                         start=(k == 0), stop=(k == CT - 1))
                kb.inc(i, "ic")

        def p_at_s(sc, b_im=b_im, b_ic=b_ic):
            for mi in range(NT_H):
                for h in range(8):
                    idx = mi * 8 + h
                    pp = idx % 2
                    sc._eng.wait_ge(kb.h("V"), vcnt(mi, h, 1))
                    sc.activation(out=sdq[:, pp], in_=sdq[:, pp], func=AF.Sqrt,
                                  scale=kb.t_misc[:, 4:5], bias=kb.t_misc[:, 0:1])
                    sc.activation(out=sinfl[:, pp], in_=sdq[:, pp], func=AF.Exp,
                                  scale=-1.0)
                    sc._eng.wait_ge(kb.h("im"), b_im + 3 * idx + 3)
                    if h == 0 and mi >= 1:
                        sc._eng.wait_ge(kb.h("ic"), b_ic + mi)
                    sc.activation(out=sflat[:, 0, h], in_=pT[pp][:], func=AF.Copy)

        def p_at_v(v, b_im=b_im, b_ic=b_ic, half=half, u0=u0):
            for mi in range(NT_H):
                for h in range(8):
                    idx = mi * 8 + h
                    pp = idx % 2
                    pmi, ph = divmod(idx - 2, 8)
                    v._eng.wait_ge(kb.h("im"), b_im + 3 * idx + 1)
                    if idx >= 2:
                        v._eng.wait_ge(kb.h("SC"), scnt(pmi, ph, 2))
                    v.tensor_scalar_min(sdq[:, pp], pG[pp][:], 0.0)
                    v._eng.wait_ge(kb.h("SC"), scnt(mi, h, 2))
                    v.tensor_mul(out=sinfl[:, pp], in0=sinfl[:, pp], in1=t_band[:])
                    v._eng.wait_ge(kb.h("im"), b_im + 3 * idx + 2)
                    v.tensor_scalar_add(sinvd[:, pp:pp + 1], pCH[pp][:, 256:257], 1e-8)
                    v.reciprocal(out=sinvd[:, pp:pp + 1], in_=sinvd[:, pp:pp + 1])
                    v.tensor_scalar_mul(scf[:, pp], pCH[pp][:, 0:128], sinvd[:, pp:pp + 1])
                    v.scalar_tensor_tensor(out=schb[:, pp], in0=pCH[pp][:, 128:256],
                                           scalar=sinvq[:, mi, h:h + 1], in1=scf[:, pp],
                                           op0=ALU.mult, op1=ALU.add)
                    v.tensor_mul(out=soh[:, pp], in0=sqtm[:, mi, h], in1=schb[:, pp])
                v._eng.wait_ge(kb.h("ic"), b_ic + mi + 1)
                xoff = u0 + 31 + 128 * mi
                for ohalf in range(2):
                    for o in range(4):
                        oc = ohalf * 4 + o
                        v.tensor_add(out=xt[:, oc, xoff:xoff + 128],
                                     in0=xt[:, oc, xoff:xoff + 128],
                                     in1=pDec[ohalf][:, 128 * o:128 * (o + 1)])
        kb.phase(tensor=p_at_t, scalar=p_at_s, vector=p_at_v)

    if DEBUG_L0A:
        def p_dbg(sy):
            kb.inc(sy.dma_start(out=dbg_sig, in_=sig[:]), "st", 16)
            kb.inc(sy.dma_start(out=dbg_mdiag[:, 0:8], in_=recipm[:]), "st", 16)
            kb.inc(sy.dma_start(out=dbg_mdiag[:, 8:16], in_=uscale[:]), "st", 16)
            kb.inc(sy.dma_start(out=dbg_mdiag[:, 16:24], in_=modcol[:]), "st", 16)
            sy.wait_ge(kb.h("st"), kb.n("st"))

        def p_dbg_g(g):
            for k in range(CT):
                kb.inc(g.dma_start(out=dbg_xnb[:, k], in_=xnb[:, k, 0:256]), "st", 16)
            for h in range(8):
                kb.inc(g.dma_start(out=dbg_su[:, h], in_=su[:, h, 0:256]), "st", 16)
                kb.inc(g.dma_start(out=dbg_vtm[:, h], in_=svtm[:, 0, h]), "st", 16)
                kb.inc(g.dma_start(out=dbg_qtm[:, h], in_=sqtm[:, 0, h]), "st", 16)
            kb.inc(g.dma_start(out=dbg_arow, in_=arow[0:1, :, :]), "st", 16)
            kb.inc(g.dma_start(out=dbg_infl, in_=sinfl[:]), "st", 16)
            kb.inc(g.dma_start(out=dbg_sinvq, in_=sinvq[:]), "st", 16)
            g.wait_ge(kb.h("st"), kb.n("st"))
        kb.phase(sync=p_dbg, gpsimd=p_dbg_g)

    def p_store(sy):
        for k in range(CT):
            kb.inc(sy.dma_start(out=y1[128 * k:128 * (k + 1), :],
                                in_=xt[:, k, 31:31 + L0A_W]), "st", 16)
        sy.wait_ge(kb.h("st"), kb.n("st"))
    kb.phase(sync=p_store)


F0A_XW = 2436
F0A_NKV = 19
F0A_NIT = 17


def build_f0_attn(dbg=False):
    kb = KB()
    io = {"xh": kb.dram_in("xh", [D, F0A_XW], F32),
          "wq": kb.dram_in("wq", [D, 1024], F32),
          "wk": kb.dram_in("wk", [D, 1024], F32),
          "wv": kb.dram_in("wv", [D, 1024], F32),
          "wo": kb.dram_in("wo", [1024, D], F32),
          "qs_col": kb.dram_in("qs_col", [128, 1], F32),
          "ks_col": kb.dram_in("ks_col", [128, 1], F32),
          "csq": kb.dram_in("csq", [128, 2176], BF16),
          "snq": kb.dram_in("snq", [128, 2176], BF16),
          "csk": kb.dram_in("csk", [128, 2432], BF16),
          "snk": kb.dram_in("snk", [128, 2432], BF16),
          "maskp": kb.dram_in("maskp", [4, 128, 384], BF16),
          "ya": kb.dram_out("ya", [D, T], F32)}
    kb.dram_in("qsp_col", [128, 1], F32)
    kb.dram_in("ksp_col", [128, 1], F32)
    alloc_common(kb)
    kb.sems("ld", "km", "qm", "sm", "wm", "st")
    kb.start()
    kb.push()
    emit_f0_attn(kb, io, dbg=dbg)
    kb.pop()
    return kb


def emit_f0_attn(kb, io, with_t0=False, dbg=False):
    xh = io["xh"]
    wq_d, wk_d, wv_d, wo_d = io["wq"], io["wk"], io["wv"], io["wo"]
    qs_d, ks_d = io["qs_col"], io["ks_col"]
    csq_d, snq_d, csk_d, snk_d = io["csq"], io["snq"], io["csk"], io["snk"]
    mask_d, ya = io["maskp"], io["ya"]
    M0 = 0 if with_t0 else 1
    d_identb = kb.cin("identb")
    d_ind2 = kb.cin("ind2b")
    d_sel8 = kb.cin("sel8b")
    if dbg:
        dbg_q = kb.dram_out("dbg_q", [128, 4, 128], F32)
        dbg_k = kb.dram_out("dbg_k", [128, 4, 128], F32)
        dbg_v = kb.dram_out("dbg_v", [128, 512], F32)
        dbg_p = kb.dram_out("dbg_p", [128, 2, 384], F32)
        dbg_av = kb.dram_out("dbg_av", [128, 4, 128], F32)

    xt = kb.sbuf("xt", [128, CT, F0A_XW], F32)
    t_idb = kb.sbuf("t_idb", [128, 128], BF16)
    t_ind2 = kb.sbuf("t_ind2", [128, 2], BF16)
    t_sel8 = kb.sbuf("t_sel8", [8, 4, 128], BF16)
    wkb = kb.sbuf("wkb", [128, CT, 512], BF16)
    wqb = wkb
    wvb = kb.sbuf("wvb", [128, CT, 512], BF16)
    wob = kb.sbuf("wob", [128, 4, D], BF16)
    qs_c = kb.sbuf("qs_c", [128, 4], F32)
    csf = kb.sbuf("csf", [128, 2432], BF16)
    snf = kb.sbuf("snf", [128, 2432], BF16)
    csqf = cskf = csf
    snqf = snkf = snf
    kfull = kb.sbuf("kfull", [128, 4, F0A_NKV, 128], BF16)
    vfull = kb.sbuf("vfull", [128, F0A_NKV, 512], BF16)
    maskp = kb.sbuf("maskp_s", [128, 4, 384], BF16)
    xsb = kb.sbuf("xsb", [128, 2, CT, 132], BF16)
    kq = kb.sbuf("kq", [128, 2, 4, 128], BF16)
    kq2 = kb.sbuf("kq2", [128, 2, 4, 128], BF16)
    rcol = kb.sbuf("rcol", [128, 2, 8], F32)
    rcolb = kb.sbuf("rcolb", [128, 2, 8], BF16)
    srowb = kb.sbuf("srowb", [8, 2, 128], BF16)
    kn1 = kb.sbuf("kn1", [128, 4, 128], BF16)
    rta = kb.sbuf("rta", [128, 128], F32)
    rtb = kb.sbuf("rtb", [128, 128], F32)
    qpb = kb.sbuf("qpb", [128, 2, 4, 128], BF16)
    pmb = kb.sbuf("pmb", [128, 2, 384], BF16)
    pmt = kb.sbuf("pmt", [128, 2, 3, 128], BF16)
    avs = kb.sbuf("avs", [128, 2, 4, 128], BF16)
    nrow = kb.sbuf("nrow", [128, 2, 2], F32)
    dlt = kb.sbuf("dlt", [128, CT, (F0A_NIT - M0) * 128], BF16)

    pQ = [kb.psum(f"pQ{i}", [128, 512]) for i in range(2)]
    pNs = kb.psum("pNs", [128, 512])
    pTT = [kb.psum(f"pTT{i}", [128, 384], BF16) for i in range(2)]
    pSA = [kb.psum(f"pSA{i}", [128, 512]) for i in range(2)]
    pWo = kb.psum("pWo", [128, 512])
    pBC = pWo

    def p_load_sy(sy):
        for k in range(CT):
            kb.inc(sy.dma_start(out=xt[:, k], in_=xh[128 * k:128 * (k + 1), :]), "ld", 16)
        kb.inc(sy.dma_start(out=qs_c[:, 0:1], in_=qs_d), "ld", 16)
        kb.inc(sy.dma_start(out=qs_c[:, 1:2], in_=ks_d), "ld", 16)

    def p_load_g(g):
        load_common(kb, g)
        kb.inc(g.dma_start(out=t_idb[:], in_=d_identb), "ld", 16)
        kb.inc(g.dma_start(out=t_ind2[:], in_=d_ind2), "ld", 16)
        kb.inc(g.dma_start(out=t_sel8[:], in_=d_sel8), "ld", 16)
        for m in range(4):
            kb.inc(g.dma_start(out=maskp[:, m], in_=mask_d[m]), "ld", 16)
    kb.phase(sync=p_load_sy, gpsimd=p_load_g)


    def _partner(qtr):
        return 32 * (qtr + 1) if qtr % 2 == 0 else 32 * (qtr - 1)

    for half in range(2):
        def p_wl(g, half=half):
            for k in range(CT):
                kb.inc(g.dma_start(out=wkb[:, k], in_=wk_d[128 * k:128 * (k + 1),
                                                          512 * half:512 * (half + 1)]), "ld", 16)
                kb.inc(g.dma_start(out=wvb[:, k], in_=wv_d[128 * k:128 * (k + 1),
                                                          512 * half:512 * (half + 1)]), "ld", 16)
            for c in range(4):
                r0 = 512 * half + 128 * c
                kb.inc(g.dma_start(out=wob[:, c], in_=wo_d[r0:r0 + 128, :]), "ld", 16)
            kb.inc(g.dma_start(out=cskf[:], in_=csk_d), "ld", 16)
            kb.inc(g.dma_start(out=snkf[:], in_=snk_d), "ld", 16)
        kb.phase(gpsimd=p_wl)

        def p_foldk(v):
            v.tensor_scalar_mul(cskf[:], cskf[:], qs_c[:, 1:2])
            v.tensor_scalar_mul(snkf[:], snkf[:], qs_c[:, 1:2])
        kb.phase(vector=p_foldk)

        b_km, b_sc, b_v = kb.n("km"), kb.n("SC"), kb.n("V")
        SCK = 18
        VCK = 33

        def p_kv_t(t, b_km=b_km, b_sc=b_sc, b_v=b_v):
            for kv in range(F0A_NKV):
                pp = kv % 2
                t.wait_ge(kb.h("SC"), b_sc + SCK * kv + 8)
                if kv >= 2:
                    t.wait_ge(kb.h("SC"), b_sc + SCK * (kv - 2) + 17)
                for pt in range(4):
                    for k in range(CT):
                        t.matmul(out=pQ[pp][:, 128 * pt:128 * (pt + 1)],
                                 lhsT=wkb[:, k, 128 * pt:128 * (pt + 1)],
                                 rhs=xsb[:, pp, k, 1:129] if k < 4 else xsb[:, pp, k, 0:128],
                                 start=(k == 0), stop=(k == CT - 1))
                for k in range(CT):
                    i = t.matmul(out=pSA[pp][:],
                                 lhsT=xsb[:, pp, k, 1:129] if k < 4 else xsb[:, pp, k, 0:128],
                                 rhs=wvb[:, k],
                                 start=(k == 0), stop=(k == CT - 1))
                kb.inc(i, "km")
                t.wait_ge(kb.h("SC"), b_sc + SCK * kv + 16)
                if kv >= 1:
                    t.wait_ge(kb.h("V"), b_v + VCK * (kv - 1) + 5)
                for pt in range(4):
                    i = t.matmul(out=pNs[:, 2 * pt:2 * pt + 2], lhsT=kq2[:, pp, pt],
                                 rhs=t_ind2[:], start=True, stop=True)
                kb.inc(i, "km")
                t.wait_ge(kb.h("V"), b_v + VCK * kv + 4)
                kb.inc(t.matmul(out=pNs[0:8, 128:256], lhsT=rcolb[:, pp], rhs=t_idb[:],
                                start=True, stop=True), "km")
                t.wait_ge(kb.h("V"), b_v + VCK * kv + 5)
                for pt in range(4):
                    i = t.matmul(out=pBC[:, 128 * pt:128 * (pt + 1)],
                                 lhsT=t_sel8[:, pt], rhs=srowb[:, pp],
                                 start=True, stop=True)
                kb.inc(i, "km")

        def p_kv_s(sc, b_km=b_km, b_v=b_v):
            for kv in range(F0A_NKV):
                pp = kv % 2
                x0 = 128 * kv
                if kv >= 2:
                    sc._eng.wait_ge(kb.h("km"), b_km + 4 * (kv - 2) + 1)
                for k in range(CT):
                    sc.activation(out=xsb[:, pp, k, 0:129], in_=xt[:, k, x0:x0 + 129],
                                  func=AF.Copy)
                sc._eng.wait_ge(kb.h("km"), b_km + 4 * kv + 1)
                for pt in range(4):
                    sc.activation(out=kq[:, pp, pt], in_=pQ[pp][:, 128 * pt:128 * (pt + 1)],
                                  func=AF.Copy)
                    sc.activation(out=kq2[:, pp, pt], in_=pQ[pp][:, 128 * pt:128 * (pt + 1)],
                                  func=AF.Square)
                sc.activation(out=vfull[:, kv], in_=pSA[pp][:], func=AF.Copy)
                sc._eng.wait_ge(kb.h("V"), b_v + VCK * kv + 1)
                sc.activation(out=rcol[:, pp], in_=rcol[:, pp], func=AF.Sqrt)

        def p_kv_v(v, b_km=b_km, b_sc=b_sc):
            for kv in range(F0A_NKV):
                pp = kv % 2
                j0 = 128 * kv
                v._eng.wait_ge(kb.h("km"), b_km + 4 * kv + 2)
                v.tensor_copy(out=rcol[:, pp], in_=pNs[:, 0:8])
                v._eng.wait_ge(kb.h("SC"), b_sc + SCK * kv + 18)
                v.tensor_scalar_max(rcol[:, pp], rcol[:, pp], 1e-12)
                v.reciprocal(out=rcol[:, pp], in_=rcol[:, pp])
                v.tensor_copy(out=rcolb[:, pp], in_=rcol[:, pp])
                v._eng.wait_ge(kb.h("km"), b_km + 4 * kv + 3)
                v.tensor_copy(out=srowb[:, pp], in_=pNs[0:8, 128:256])
                v._eng.wait_ge(kb.h("km"), b_km + 4 * kv + 4)
                for pt in range(4):
                    v.tensor_mul(out=kn1[:, pt], in0=kq[:, pp, pt],
                                 in1=pBC[:, 128 * pt:128 * (pt + 1)])
                    v.tensor_mul(out=rta[:], in0=kn1[:, pt], in1=cskf[:, j0:j0 + 128])
                    for qtr in range(4):
                        a = 32 * qtr
                        prt = _partner(qtr)
                        v.tensor_mul(out=rtb[a:a + 32, :], in0=kn1[prt:prt + 32, pt],
                                     in1=snkf[prt:prt + 32, j0:j0 + 128])
                    v.tensor_add(out=kfull[:, pt, kv], in0=rta[:], in1=rtb[:])
        kb.phase(tensor=p_kv_t, scalar=p_kv_s, vector=p_kv_v)

        def p_wq(g, half=half):
            for k in range(CT):
                kb.inc(g.dma_start(out=wqb[:, k], in_=wq_d[128 * k:128 * (k + 1),
                                                          512 * half:512 * (half + 1)]), "ld", 16)
            kb.inc(g.dma_start(out=csqf[:, 0:2176], in_=csq_d), "ld", 16)
            kb.inc(g.dma_start(out=snqf[:, 0:2176], in_=snq_d), "ld", 16)
        kb.phase(gpsimd=p_wq)

        def p_foldq(v):
            v.tensor_scalar_mul(csqf[:, 0:2176], csqf[:, 0:2176], qs_c[:, 0:1])
            v.tensor_scalar_mul(snqf[:, 0:2176], snqf[:, 0:2176], qs_c[:, 0:1])
        kb.phase(vector=p_foldq)

        b_qm, b_sm, b_wm = kb.n("qm"), kb.n("sm"), kb.n("wm")
        b_sc, b_v = kb.n("SC"), kb.n("V")
        SCM = 61
        VR = 8 if half == 0 else 16
        VCM = 73 + VR

        def sco(m, j):
            return b_sc + SCM * m + j

        def vco(m, j):
            return b_v + VCM * m + j

        def p_i_t(t, b_qm=b_qm, b_sm=b_sm, b_wm=b_wm):
            for jt, m in enumerate(range(M0, F0A_NIT)):
                pp = jt % 2
                i0 = 128 * m
                t.wait_ge(kb.h("SC"), sco(jt, 12))
                if jt >= 2:
                    t.wait_ge(kb.h("SC"), sco(jt - 2, 20))
                for pt in range(4):
                    for k in range(CT):
                        i = t.matmul(out=pQ[pp][:, 128 * pt:128 * (pt + 1)],
                                     lhsT=wqb[:, k, 128 * pt:128 * (pt + 1)],
                                     rhs=xsb[:, pp, k, 1:129] if k < 4 else xsb[:, pp, k, 0:128],
                                     start=(k == 0), stop=(k == CT - 1))
                kb.inc(i, "qm")
                t.wait_ge(kb.h("SC"), sco(jt, 20))
                if jt >= 1:
                    t.wait_ge(kb.h("V"), vco(jt - 1, 5))
                for pt in range(4):
                    i = t.matmul(out=pNs[:, 2 * pt:2 * pt + 2], lhsT=kq2[:, pp, pt],
                                 rhs=t_ind2[:], start=True, stop=True)
                kb.inc(i, "qm")
                t.wait_ge(kb.h("V"), vco(jt, 4))
                kb.inc(t.matmul(out=pNs[0:8, 128:256], lhsT=rcolb[:, pp], rhs=t_idb[:],
                                start=True, stop=True), "qm")
                t.wait_ge(kb.h("V"), vco(jt, 5))
                for pt in range(4):
                    i = t.matmul(out=pBC[:, 128 * pt:128 * (pt + 1)],
                                 lhsT=t_sel8[:, pt], rhs=srowb[:, pp],
                                 start=True, stop=True)
                kb.inc(i, "qm")
                for h in range(8):
                    sp = h % 2
                    pt = h // 2
                    es = 64 * (h % 2)
                    t.wait_ge(kb.h("V"), vco(jt, 33))
                    if h >= 2:
                        t.wait_ge(kb.h("SC"), sco(jt, 21 + 5 * (h - 1)))
                    elif jt >= 1:
                        t.wait_ge(kb.h("SC"), sco(jt - 1, 21 + 5 * (6 + h) + 5))
                    for jj in range(3):
                        i = t.matmul(out=pSA[sp][:, 128 * jj:128 * (jj + 1)],
                                     lhsT=qpb[es:es + 64, pp, pt, :],
                                     rhs=kfull[es:es + 64, pt, m + jj, :],
                                     start=True, stop=True)
                    kb.inc(i, "sm")
                    t.wait_ge(kb.h("V"), vco(jt, 33 + 5 * h + 5))
                    if h >= 2:
                        t.wait_ge(kb.h("SC"), sco(jt, 21 + 5 * (h - 2) + 4))
                    for jj in range(3):
                        i = t.transpose(out=pTT[sp][:, 128 * jj:128 * (jj + 1)],
                                        in_=pmb[:, sp, 128 * jj:128 * (jj + 1)],
                                        identity=t_idb[:])
                    kb.inc(i, "sm")
                    t.wait_ge(kb.h("SC"), sco(jt, 21 + 5 * h + 4))
                    for jj in range(3):
                        i = t.matmul(out=pSA[sp][0:64, 384:512],
                                     lhsT=vfull[:, m + jj, 64 * h:64 * h + 64],
                                     rhs=pmt[:, sp, jj], start=(jj == 0), stop=(jj == 2))
                    kb.inc(i, "sm")
                t.wait_ge(kb.h("SC"), sco(jt, 61))
                if jt >= 1:
                    t.wait_ge(kb.h("V"), vco(jt - 1, VCM))
                for grp in range(2):
                    for oc in range(4):
                        och = 4 * grp + oc
                        for c in range(4):
                            i = t.matmul(out=pWo[:, 128 * oc:128 * (oc + 1)],
                                         lhsT=wob[:, c, 128 * och:128 * (och + 1)],
                                         rhs=avs[:, pp, c, :],
                                         start=(c == 0), stop=(c == 3))
                    kb.inc(i, "wm")
                    if grp == 0:
                        t.wait_ge(kb.h("V"), vco(jt, 73 + VR // 2))

        def p_i_s(sc, b_qm=b_qm, b_sm=b_sm):
            for jt, m in enumerate(range(M0, F0A_NIT)):
                pp = jt % 2
                x0 = 256 + 128 * m
                if jt >= 2:
                    sc._eng.wait_ge(kb.h("qm"), b_qm + 4 * (jt - 2) + 1)
                for k in range(CT):
                    sc.activation(out=xsb[:, pp, k, 0:129], in_=xt[:, k, x0:x0 + 129],
                                  func=AF.Copy)
                for k in range(4, CT):
                    sc.activation(out=xsb[:, pp, k, 0:1],
                                  in_=(xt[:, k, x0:x0 + 1] if jt == 0
                                       else xsb[:, 1 - pp, k, 128:129]),
                                  func=AF.Copy)
                sc._eng.wait_ge(kb.h("qm"), b_qm + 4 * jt + 1)
                for pt in range(4):
                    sc.activation(out=kq[:, pp, pt], in_=pQ[pp][:, 128 * pt:128 * (pt + 1)],
                                  func=AF.Copy)
                    sc.activation(out=kq2[:, pp, pt], in_=pQ[pp][:, 128 * pt:128 * (pt + 1)],
                                  func=AF.Square)
                sc._eng.wait_ge(kb.h("V"), vco(jt, 1))
                sc.activation(out=rcol[:, pp], in_=rcol[:, pp], func=AF.Sqrt)
                for h in range(8):
                    sp = h % 2
                    sc._eng.wait_ge(kb.h("V"), vco(jt, 33 + 5 * h + 1))
                    sc.activation(out=pmb[:, sp], in_=pSA[sp][:, 0:384], func=AF.Exp,
                                  bias=nrow[:, sp, 0:1])
                    sc._eng.wait_ge(kb.h("sm"), b_sm + 3 * (8 * jt + h) + 2)
                    for jj in range(3):
                        sc.activation(out=pmt[:, sp, jj],
                                      in_=pTT[sp][:, 128 * jj:128 * (jj + 1)],
                                      func=AF.Copy)
                    sc._eng.wait_ge(kb.h("sm"), b_sm + 3 * (8 * jt + h) + 3)
                    if h == 0 and jt >= 2:
                        sc._eng.wait_ge(kb.h("wm"), b_wm + 2 * (jt - 1))
                    sc.activation(out=avs[64 * (h % 2):64 * (h % 2) + 64, pp, h // 2, :],
                                  in_=pSA[sp][0:64, 384:512], func=AF.Copy)

        def p_i_v(v, b_qm=b_qm, b_sm=b_sm, b_wm=b_wm):
            for jt, m in enumerate(range(M0, F0A_NIT)):
                pp = jt % 2
                i0 = 128 * m
                v._eng.wait_ge(kb.h("qm"), b_qm + 4 * jt + 2)
                v.tensor_copy(out=rcol[:, pp], in_=pNs[:, 0:8])
                v._eng.wait_ge(kb.h("SC"), sco(jt, 21))
                v.tensor_scalar_max(rcol[:, pp], rcol[:, pp], 1e-12)
                v.reciprocal(out=rcol[:, pp], in_=rcol[:, pp])
                v.tensor_copy(out=rcolb[:, pp], in_=rcol[:, pp])
                v._eng.wait_ge(kb.h("qm"), b_qm + 4 * jt + 3)
                v.tensor_copy(out=srowb[:, pp], in_=pNs[0:8, 128:256])
                v._eng.wait_ge(kb.h("qm"), b_qm + 4 * jt + 4)
                for pt in range(4):
                    v.tensor_mul(out=kn1[:, pt], in0=kq[:, pp, pt],
                                 in1=pBC[:, 128 * pt:128 * (pt + 1)])
                    v.tensor_mul(out=rta[:], in0=kn1[:, pt], in1=csqf[:, i0:i0 + 128])
                    for qtr in range(4):
                        a = 32 * qtr
                        prt = _partner(qtr)
                        v.tensor_mul(out=rtb[a:a + 32, :], in0=kn1[prt:prt + 32, pt],
                                     in1=snqf[prt:prt + 32, i0:i0 + 128])
                    v.tensor_add(out=qpb[:, pp, pt], in0=rta[:], in1=rtb[:])
                for h in range(8):
                    sp = h % 2
                    v._eng.wait_ge(kb.h("sm"), b_sm + 3 * (8 * jt + h) + 1)
                    v.reduce_max(out=nrow[:, sp, 0:1], in_=pSA[sp][:, 0:384],
                                 axis=AX.X, negate=True)
                    v._eng.wait_ge(kb.h("SC"), sco(jt, 21 + 5 * h + 1))
                    v.tensor_mul(out=pmb[:, sp], in0=pmb[:, sp], in1=maskp[:, min(m, 3)])
                    v.reduce_sum(out=nrow[:, sp, 1:2], in_=pmb[:, sp], axis=AX.X)
                    v.reciprocal(out=nrow[:, sp, 1:2], in_=nrow[:, sp, 1:2])
                    v.tensor_scalar_mul(pmb[:, sp], pmb[:, sp], nrow[:, sp, 1:2])
                xoff = 257 + 128 * m
                for grp in range(2):
                    v._eng.wait_ge(kb.h("wm"), b_wm + 2 * jt + grp + 1)
                    for oc in range(4):
                        och = 4 * grp + oc
                        if half == 0:
                            v.tensor_copy(out=dlt[:, och, 128 * (m - M0):128 * (m - M0 + 1)],
                                          in_=pWo[:, 128 * oc:128 * (oc + 1)])
                        else:
                            v.tensor_add(out=rta[:], in0=pWo[:, 128 * oc:128 * (oc + 1)],
                                         in1=dlt[:, och, 128 * (m - M0):128 * (m - M0 + 1)])
                            v.tensor_add(out=xt[:, och, xoff:xoff + 128],
                                         in0=xt[:, och, xoff:xoff + 128],
                                         in1=rta[:])
        kb.phase(tensor=p_i_t, scalar=p_i_s, vector=p_i_v)

    if dbg:
        def p_dbg(g):
            for pt in range(4):
                kb.inc(g.dma_start(out=dbg_q[:, pt], in_=qpb[:, 0, pt]), "st", 16)
                kb.inc(g.dma_start(out=dbg_k[:, pt], in_=kfull[:, pt, 18]), "st", 16)
                kb.inc(g.dma_start(out=dbg_av[:, pt], in_=avs[:, 0, pt]), "st", 16)
            kb.inc(g.dma_start(out=dbg_v, in_=vfull[:, 18]), "st", 16)
            kb.inc(g.dma_start(out=dbg_p, in_=pmb[:]), "st", 16)
            g.wait_ge(kb.h("st"), kb.n("st"))
        kb.phase(gpsimd=p_dbg)

    yx0 = 385 - 128 * (1 - M0)

    def p_store(sy):
        for k in range(CT):
            kb.inc(sy.dma_start(out=ya[128 * k:128 * (k + 1), :],
                                in_=xt[:, k, yx0:2433]), "ld", 16)
        sy.wait_ge(kb.h("ld"), kb.n("ld"))
    kb.phase(sync=p_store)


def build_f0_ffn():
    kb = KB()
    io = {"xf": kb.dram_in("xf", [D, 2052], F32),
          "w1p": kb.dram_in("w1p", [44, D, 128], F32),
          "w2po": kb.dram_in("w2po", [8, 2816, 128], F32),
          "g_col": kb.dram_in("g_col", [128, 8], F32),
          "b_col": kb.dram_in("b_col", [128, 8], F32),
          "b1a_col": kb.dram_in("b1a_col", [128, 22], F32),
          "b1g_col": kb.dram_in("b1g_col", [128, 22], F32),
          "b2_col": kb.dram_in("b2_col", [128, 8], F32),
          "yf": kb.dram_out("yf", [D, T], F32)}
    alloc_common(kb)
    kb.sems("ld", "m", "hm", "wm", "w1l", "w2l", "st")
    kb.start()
    kb.push()
    emit_f0_ffn(kb, io)
    kb.pop()
    return kb


def emit_f0_ffn(kb, io, hmask=None):
    xf, w1p_d, w2po_d = io["xf"], io["w1p"], io["w2po"]
    g_d, b_d, b1a_d, b1g_d, b2_d = (io["g_col"], io["b_col"], io["b1a_col"],
                                    io["b1g_col"], io["b2_col"])
    yf = io["yf"]
    xt = kb.sbuf("xt", [128, CT, 2052], F32)
    alloc_ln(kb)
    gcol = kb.sbuf("gcol", [128, 8], F32)
    bcol = kb.sbuf("bcol", [128, 8], F32)
    b1a = kb.sbuf("b1a", [128, 22], F32)
    b1g = kb.sbuf("b1g", [128, 22], F32)
    b2c = kb.sbuf("b2c", [128, 8], F32)
    xnb = kb.sbuf("xnb", [128, CT, 512], BF16)
    w1b = kb.sbuf("w1b", [128, 2, 2, CT, 128], BF16)
    w2b = kb.sbuf("w2b", [128, 2, 22, 128], BF16)
    hbuf = kb.sbuf("hbuf", [128, 22, 512], BF16)
    gel = kb.sbuf("gel", [128, 2, 512], BF16)
    yst = kb.sbuf("yst", [128, CT, 512], F32)
    pA = kb.psum("pA", [128, 512])
    pB = kb.psum("pB", [128, 512])
    pHa = [kb.psum(f"pHa{i}", [128, 512]) for i in range(2)]
    pHg = [kb.psum(f"pHg{i}", [128, 512]) for i in range(2)]
    pW = [kb.psum(f"pW{i}", [128, 512]) for i in range(2)]

    def p_load_sy(sy):
        for k in range(CT):
            kb.inc(sy.dma_start(out=xt[:, k], in_=xf[128 * k:128 * (k + 1), :]), "ld", 16)
        kb.inc(sy.dma_start(out=gcol[:], in_=g_d), "ld", 16)
        kb.inc(sy.dma_start(out=bcol[:], in_=b_d), "ld", 16)
        kb.inc(sy.dma_start(out=b1a[:], in_=b1a_d), "ld", 16)
        kb.inc(sy.dma_start(out=b1g[:], in_=b1g_d), "ld", 16)
        kb.inc(sy.dma_start(out=b2c[:], in_=b2_d), "ld", 16)

    def p_load_g(g):
        load_common(kb, g)
    kb.phase(sync=p_load_sy, gpsimd=p_load_g)

    if hmask is not None:
        def p_hm(v):
            for k in range(CT):
                v.tensor_scalar_mul(xt[:, k, 0:1], xt[:, k, 0:1], hmask[:, 0:1])
        kb.phase(vector=p_hm)

    NCH = T // 512
    for ch in range(NCH):
        xs0 = 1 + 512 * ch
        def p_cast(sc, xs0=xs0):
            for k in range(CT):
                off = xs0 if k < 4 else xs0 - 1
                sc.activation(out=kb.t_lnxb[:, k], in_=xt[:, k, off:off + 512],
                              func=AF.Copy)
        kb.phase(scalar=p_cast)

        def p_mmA(t):
            for k in range(CT):
                i = t.matmul(out=pA[:], lhsT=kb.t_ones[:], rhs=kb.t_lnxb[:, k],
                             start=(k == 0), stop=(k == CT - 1))
            kb.inc(i, "m")
        kb.phase(tensor=p_mmA)

        def p_sq(v):
            for k in range(CT):
                v.tensor_mul(out=kb.t_lnxb[:, k], in0=kb.t_lnxb[:, k],
                             in1=kb.t_lnxb[:, k])
        kb.phase(vector=p_sq)

        def p_mmB(t):
            for k in range(CT):
                i = t.matmul(out=pB[:], lhsT=kb.t_ones[:], rhs=kb.t_lnxb[:, k],
                             start=(k == 0), stop=(k == CT - 1))
            kb.inc(i, "m")
        kb.phase(tensor=p_mmB)

        def p_v1(v):
            v.tensor_scalar_mul(kb.t_lnrow[:, 0], pA[:], 1.0 / D)
            v.tensor_scalar_mul(kb.t_lnrow[:, 1], pB[:], 1.0 / D)
        kb.phase(vector=p_v1)

        def p_s1(sc):
            sc.activation(out=kb.t_lnrow[:, 2], in_=kb.t_lnrow[:, 0], func=AF.Square)
        kb.phase(scalar=p_s1)

        def p_v2(v):
            v.tensor_sub(out=kb.t_lnrow[:, 1], in0=kb.t_lnrow[:, 1],
                         in1=kb.t_lnrow[:, 2])
        kb.phase(vector=p_v2)

        def p_s2(sc):
            sc.activation(out=kb.t_lnrow[:, 1], in_=kb.t_lnrow[:, 1],
                          func=AF.Sqrt, bias=kb.t_misc[:, 1:2])
        kb.phase(scalar=p_s2)

        def p_v3b(v, xs0=xs0):
            v.reciprocal(out=kb.t_lnrow[:, 3], in_=kb.t_lnrow[:, 1])
            for k in range(CT):
                off = xs0 if k < 4 else xs0 - 1
                v.tensor_sub(out=kb.t_lnrow[:, 2], in0=xt[:, k, off:off + 512],
                             in1=kb.t_lnrow[:, 0])
                v.tensor_mul(out=yst[:, k], in0=kb.t_lnrow[:, 2], in1=kb.t_lnrow[:, 3])
        kb.phase(vector=p_v3b)

        def p_aff(v):
            for k in range(CT):
                v.tensor_scalar(out=xnb[:, k], in0=yst[:, k],
                                scalar1=gcol[:, k:k + 1], scalar2=bcol[:, k:k + 1],
                                op0=ALU.mult, op1=ALU.add)
        kb.phase(vector=p_aff)

        b_hm, b_sc, b_v, b_w1 = kb.n("hm"), kb.n("SC"), kb.n("V"), kb.n("w1l")

        def p_w1_g(g, ch=ch, b_hm=b_hm):
            for p in range(22):
                pp = p % 2
                if p >= 2:
                    g._eng.wait_ge(kb.h("hm"), b_hm + (p - 2) + 1)
                kb.inc(g.dma_start(out=w1b[:, pp, 0],
                                   in_=w1p_d[2 * p].rearrange("(k p) f -> p k f", p=128)),
                       "w1l", 16)
                kb.inc(g.dma_start(out=w1b[:, pp, 1],
                                   in_=w1p_d[2 * p + 1].rearrange("(k p) f -> p k f", p=128)),
                       "w1l", 16)

        def p_w1_t(t, b_w1=b_w1, b_sc=b_sc, b_v=b_v):
            for p in range(22):
                pp = p % 2
                t.wait_ge(kb.h("w1l"), b_w1 + 32 * (p + 1))
                if p >= 2:
                    t.wait_ge(kb.h("SC"), b_sc + (p - 2) + 1)
                    t.wait_ge(kb.h("V"), b_v + (p - 2) + 1)
                for k in range(CT):
                    t.matmul(out=pHa[pp][:], lhsT=w1b[:, pp, 0, k], rhs=xnb[:, k],
                             start=(k == 0), stop=(k == CT - 1))
                for k in range(CT):
                    i = t.matmul(out=pHg[pp][:], lhsT=w1b[:, pp, 1, k], rhs=xnb[:, k],
                                 start=(k == 0), stop=(k == CT - 1))
                kb.inc(i, "hm")

        def p_w1_s(sc, b_hm=b_hm):
            for p in range(22):
                sc._eng.wait_ge(kb.h("hm"), b_hm + p + 1)
                sc.activation(out=gel[:, p % 2], in_=pHg[p % 2][:], func=AF.Gelu,
                              bias=b1g[:, p:p + 1])

        def p_w1_v(v, b_sc=b_sc):
            for p in range(22):
                v._eng.wait_ge(kb.h("SC"), b_sc + p + 1)
                v.scalar_tensor_tensor(out=hbuf[:, p], in0=pHa[p % 2][:],
                                       scalar=b1a[:, p:p + 1], in1=gel[:, p % 2],
                                       op0=ALU.add, op1=ALU.mult)
        kb.phase(tensor=p_w1_t, scalar=p_w1_s, vector=p_w1_v, gpsimd=p_w1_g)

        b_wm, b_v, b_w2 = kb.n("wm"), kb.n("V"), kb.n("w2l")

        def p_w2_g(g, b_wm=b_wm):
            for oc in range(CT):
                pp = oc % 2
                if oc >= 2:
                    g._eng.wait_ge(kb.h("wm"), b_wm + (oc - 2) + 1)
                kb.inc(g.dma_start(out=w2b[:, pp],
                                   in_=w2po_d[oc].rearrange("(c p) f -> p c f", p=128)),
                       "w2l", 16)

        def p_w2_t(t, b_w2=b_w2, b_v=b_v):
            for oc in range(CT):
                pp = oc % 2
                t.wait_ge(kb.h("w2l"), b_w2 + 16 * (oc + 1))
                if oc >= 2:
                    t.wait_ge(kb.h("V"), b_v + (oc - 2) + 1)
                for c in range(22):
                    i = t.matmul(out=pW[pp][:], lhsT=w2b[:, pp, c], rhs=hbuf[:, c],
                                 start=(c == 0), stop=(c == 21))
                kb.inc(i, "wm")

        def p_w2_v(v, ch=ch, b_wm=b_wm):
            xs0 = 1 + 512 * ch
            for oc in range(CT):
                pp = oc % 2
                v._eng.wait_ge(kb.h("wm"), b_wm + oc + 1)
                v.scalar_tensor_tensor(out=yst[:, oc], in0=pW[pp][:],
                                       scalar=b2c[:, oc:oc + 1],
                                       in1=xt[:, oc, xs0:xs0 + 512],
                                       op0=ALU.add, op1=ALU.add)
        kb.phase(tensor=p_w2_t, vector=p_w2_v, gpsimd=p_w2_g)

        def p_yout(sy, ch=ch):
            for k in range(CT):
                kb.inc(sy.dma_start(out=yf[128 * k:128 * (k + 1), 512 * ch:512 * (ch + 1)],
                                    in_=yst[:, k]), "st", 16)
        kb.phase(sync=p_yout)

    def p_fin(sy):
        sy.wait_ge(kb.h("st"), kb.n("st"))
    kb.phase(sync=p_fin)


L1C_XW = 2052


def emit_rms(kb, xt, outb, x0, n0, w, gc, pA):
    done = 0
    while done < w:
        cw = min(512, w - done)
        xs = slice(x0 + done, x0 + done + cw)
        ns = slice(n0 + done, n0 + done + cw)
        done += cw

        def p_cast(sc, xs=xs, cw=cw):
            for k in range(CT):
                sc.activation(out=kb.t_lnxb[:, k, 0:cw], in_=xt[:, k, xs], func=AF.Copy)
        kb.phase(scalar=p_cast)

        def p_sq(v, cw=cw):
            for k in range(CT):
                v.tensor_mul(out=kb.t_lnxb[:, k, 0:cw], in0=kb.t_lnxb[:, k, 0:cw],
                             in1=kb.t_lnxb[:, k, 0:cw])
        kb.phase(vector=p_sq)

        def p_mm(t, cw=cw):
            for k in range(CT):
                i = t.matmul(out=pA[:, 0:cw], lhsT=kb.t_ones[:], rhs=kb.t_lnxb[:, k, 0:cw],
                             start=(k == 0), stop=(k == CT - 1))
            kb.inc(i, "m")
        kb.phase(tensor=p_mm)

        def p_r(sc, cw=cw):
            sc.activation(out=kb.t_lnrow[:, 0, 0:cw], in_=pA[:, 0:cw], func=AF.Sqrt)
        kb.phase(scalar=p_r)

        def p_rv(v, cw=cw):
            v.tensor_scalar_max(kb.t_lnrow[:, 0, 0:cw], kb.t_lnrow[:, 0, 0:cw], 1e-12)
            v.reciprocal(out=kb.t_lnrow[:, 1, 0:cw], in_=kb.t_lnrow[:, 0, 0:cw])
        kb.phase(vector=p_rv)

        def p_out(v, xs=xs, ns=ns, cw=cw):
            for k in range(CT):
                v.scalar_tensor_tensor(out=outb[:, k, ns], in0=xt[:, k, xs],
                                       scalar=gc[:, k:k + 1], in1=kb.t_lnrow[:, 1, 0:cw],
                                       op0=ALU.mult, op1=ALU.mult)
        kb.phase(vector=p_out)


def build_l1a_pre():
    kb = KB()
    io = {"xh": kb.dram_in("xh", [D, L1C_XW], F32),
          "dwc": kb.dram_in("dwc", [128, 32], F32),
          "dwb": kb.dram_in("dwb", [128, 8], F32),
          "pwbias": kb.dram_in("pwbias", [128, 8], F32),
          "gcv": kb.dram_in("gcv", [128, 8], F32),
          "ggr": kb.dram_in("ggr", [128, 8], F32),
          "pwT": kb.dram_in("pwT", [D, D], F32),
          "whg": kb.dram_in("whg", [4, 256, 768], F32),
          "xa": kb.dram_out("xa", [D, T], F32),
          "hl": kb.dram_out("hl", [128, 12, T], F32),
          "al": kb.dram_out("al", [128, 12, T], F32),
          "summ": kb.dram_out("summ", [128, 2, 12], F32)}
    alloc_common(kb)
    kb.sems("ld", "m", "st")
    kb.start()
    kb.push()
    emit_l1a_pre(kb, io)
    kb.pop()
    return kb


def emit_l1a_pre(kb, io, hmask=None):
    xh = io["xh"]
    dwc_d, dwb_d, pwbias_d = io["dwc"], io["dwb"], io["pwbias"]
    gcv_d, ggr_d, pwT_d, whg_d = io["gcv"], io["ggr"], io["pwT"], io["whg"]
    xa_d, hl_d, al_d, sm_d = io["xa"], io["hl"], io["al"], io["summ"]
    xt = kb.sbuf("xt", [128, CT, L1C_XW], F32)
    alloc_ln(kb)
    cols = kb.sbuf("cols", [128, 64], F32)
    summ = kb.sbuf("summ_s", [128, 2, 12], F32)
    zocol = kb.sbuf("zocol", [128, 2], F32)

    ov = ExitStack()

    def osb(name, shape, dtype):
        return ov.enter_context(kb.sbuf_cm(name, shape, dtype))

    xn = osb("xn", [128, CT, L1C_XW], BF16)
    pwb = osb("pwb", [128, CT, CT, 128], BF16)
    acc = osb("acc", [128, CT, 512], F32)
    ycb = osb("ycb", [128, CT, 512], BF16)

    pA = kb.psum("pA", [128, 512])
    pP = [kb.psum(f"pP{i}", [128, 512]) for i in range(2)]

    def p_load_sy(sy):
        for k in range(CT):
            kb.inc(sy.dma_start(out=xt[:, k], in_=xh[128 * k:128 * (k + 1), :]), "ld", 16)
        kb.inc(sy.dma_start(out=cols[:, 0:32], in_=dwc_d), "ld", 16)
        kb.inc(sy.dma_start(out=cols[:, 32:40], in_=dwb_d), "ld", 16)
        kb.inc(sy.dma_start(out=cols[:, 40:48], in_=pwbias_d), "ld", 16)
        kb.inc(sy.dma_start(out=cols[:, 48:56], in_=gcv_d), "ld", 16)
        kb.inc(sy.dma_start(out=cols[:, 56:64], in_=ggr_d), "ld", 16)

    def p_load_g(g):
        load_common(kb, g)
        for dd in range(CT):
            for ee in range(CT):
                kb.inc(g.dma_start(out=pwb[:, dd, ee],
                                   in_=pwT_d[128 * dd:128 * (dd + 1),
                                             128 * ee:128 * (ee + 1)]), "ld", 16)
    kb.phase(sync=p_load_sy, gpsimd=p_load_g)

    def p_gc(v):
        v.tensor_scalar(out=cols[:, 48:64], in0=cols[:, 48:64], scalar1=1.0,
                        scalar2=32.0, op0=ALU.add, op1=ALU.mult)
        v.memset(zocol[:, 0:1], 0.0)
        v.memset(zocol[:, 1:2], 1.0)
        if hmask is not None:
            for k in range(CT):
                v.tensor_scalar_mul(xt[:, k, 0:3], xt[:, k, 0:3], hmask[:, 0:1])
    kb.phase(vector=p_gc)

    emit_rms(kb, xt, xn, 0, 0, L1C_XW, cols[:, 48:56], pA)

    for ch in range(4):
        t0 = 512 * ch

        def p_conv(v, t0=t0):
            for c in range(CT):
                v.tensor_scalar_mul(acc[:, c], xn[:, c, t0:t0 + 512], cols[:, 4 * c:4 * c + 1])
                for kk in range(1, 4):
                    v.scalar_tensor_tensor(out=acc[:, c], in0=xn[:, c, t0 + kk:t0 + kk + 512],
                                           scalar=cols[:, 4 * c + kk:4 * c + kk + 1],
                                           in1=acc[:, c], op0=ALU.mult, op1=ALU.add)
                v.tensor_scalar_add(acc[:, c], acc[:, c], cols[:, 32 + c:33 + c])
        kb.phase(vector=p_conv)

        def p_cvt(sc):
            for c in range(CT):
                sc.activation(out=ycb[:, c], in_=acc[:, c], func=AF.Copy)
        kb.phase(scalar=p_cvt)

        b_m, b_vv = kb.n("m"), kb.n("V")

        def p_pw(t, b_vv=b_vv):
            for ee in range(CT):
                if ee >= 2:
                    t.wait_ge(kb.h("V"), b_vv + (ee - 1))
                for dd in range(CT):
                    i = t.matmul(out=pP[ee % 2][:], lhsT=pwb[:, dd, ee], rhs=ycb[:, dd],
                                 start=(dd == 0), stop=(dd == CT - 1))
                kb.inc(i, "m")

        def p_pwe(v, b_m=b_m, t0=t0):
            for ee in range(CT):
                v._eng.wait_ge(kb.h("m"), b_m + ee + 1)
                v.scalar_tensor_tensor(out=xt[:, ee, 3 + t0:3 + t0 + 512],
                                       in0=pP[ee % 2][:], scalar=cols[:, 40 + ee:41 + ee],
                                       in1=xt[:, ee, 3 + t0:3 + t0 + 512],
                                       op0=ALU.add, op1=ALU.add)
        kb.phase(tensor=p_pw, vector=p_pwe)

    def p_stxa(sy):
        for k in range(CT):
            kb.inc(sy.dma_start(out=xa_d[128 * k:128 * (k + 1), :], in_=xt[:, k, 3:2051]),
                   "st", 16)
    kb.phase(sync=p_stxa)

    ov.close()
    ov2 = ExitStack()

    def osb2(name, shape, dtype):
        return ov2.enter_context(kb.sbuf_cm(name, shape, dtype))

    ngb = osb2("ngb", [128, CT, T], BF16)
    whgb = osb2("whgb", [128, 2, 2, 128], BF16)
    hgf = osb2("hgf", [128, 2, T], F32)
    af = osb2("af", [128, T], F32)
    bf2 = osb2("bf2", [128, T], F32)
    tmp = osb2("tmp", [128, T], F32)
    hsc = osb2("hsc", [128, T], F32)
    asc = osb2("asc", [128, T], F32)
    ones2k = osb2("ones2k", [128, T], F32)

    def p_ones(v):
        v.memset(ones2k[:], 1.0)
    kb.phase(vector=p_ones)

    emit_rms(kb, xt, ngb, 3, 0, T, cols[:, 56:64], pA)

    for h4 in range(4):
        for et in range(3):
            j = 3 * h4 + et

            def p_wl(g, h4=h4, et=et):
                for dd in range(2):
                    kb.inc(g.dma_start(out=whgb[:, 0, dd],
                                       in_=whg_d[h4, 128 * dd:128 * (dd + 1),
                                                 128 * et:128 * (et + 1)]), "ld", 16)
                    kb.inc(g.dma_start(out=whgb[:, 1, dd],
                                       in_=whg_d[h4, 128 * dd:128 * (dd + 1),
                                                 128 * (3 + et):128 * (4 + et)]), "ld", 16)
            kb.phase(gpsimd=p_wl)

            b_m, b_sc = kb.n("m"), kb.n("SC")

            def p_mm(t, h4=h4, b_sc=b_sc):
                for tt in range(4):
                    if tt >= 1:
                        t.wait_ge(kb.h("SC"), b_sc + 2 * tt)
                    for hh in range(2):
                        for dd in range(2):
                            i = t.matmul(out=pP[hh][:],
                                         lhsT=whgb[:, hh, dd],
                                         rhs=ngb[:, 2 * h4 + dd, 512 * tt:512 * (tt + 1)],
                                         start=(dd == 0), stop=(dd == 1))
                        kb.inc(i, "m")

            def p_ev(sc, b_m=b_m):
                for tt in range(4):
                    sc._eng.wait_ge(kb.h("m"), b_m + 2 * tt + 2)
                    for hh in range(2):
                        sc.activation(out=hgf[:, hh, 512 * tt:512 * (tt + 1)],
                                      in_=pP[hh][:], func=AF.Copy)
            kb.phase(tensor=p_mm, scalar=p_ev)

            def p_el_s(sc):
                sc.activation(out=af[:], in_=hgf[:, 1], func=AF.Sigmoid, scale=-1.0)
                sc.activation(out=bf2[:], in_=hgf[:, 1], func=AF.Sigmoid)
                sc.activation(out=tmp[:], in_=hgf[:, 0], func=AF.Sigmoid)
            kb.phase(scalar=p_el_s)

            def p_el_v(v):
                v.tensor_scalar_add(hgf[:, 0], hgf[:, 0], 0.5)
                v.tensor_max(out=hgf[:, 0], in0=hgf[:, 0], in1=tmp[:])
                v.tensor_mul(out=bf2[:], in0=bf2[:], in1=hgf[:, 0])
                v.tensor_tensor_scan(out=hsc[:], data0=af[:], data1=bf2[:],
                                     initial=zocol[:, 0:1], op0=ALU.mult, op1=ALU.add)
                v.tensor_tensor_scan(out=asc[:], data0=af[:], data1=ones2k[:],
                                     initial=zocol[:, 1:2], op0=ALU.mult, op1=ALU.mult)
                v.tensor_copy(out=summ[:, 0, j:j + 1], in_=asc[:, T - 1:T])
                v.tensor_copy(out=summ[:, 1, j:j + 1], in_=hsc[:, T - 1:T])
            kb.phase(vector=p_el_v)

            def p_st(sy, j=j):
                kb.inc(sy.dma_start(out=hl_d[:, j], in_=hsc[:]), "st", 16)
                kb.inc(sy.dma_start(out=al_d[:, j], in_=asc[:]), "st", 16)
            kb.phase(sync=p_st)

    def p_fin(sy):
        if "summP" in io:
            kb.inc(sy.dma_start(out=io["summP"], in_=summ[:, 0]), "st", 16)
            kb.inc(sy.dma_start(out=io["summE"], in_=summ[:, 1]), "st", 16)
        else:
            kb.inc(sy.dma_start(out=sm_d, in_=summ[:]), "st", 16)
        sy.wait_ge(kb.h("st"), kb.n("st"))
    kb.phase(sync=p_fin)
    ov2.close()


def build_l1a_post():
    kb = KB()
    io = {"xa": kb.dram_in("xa", [D, T], F32),
          "hl": kb.dram_in("hl", [128, 12, T], F32),
          "al": kb.dram_in("al", [128, 12, T], F32),
          "smP": kb.dram_in("smP", [128, 8, 12], F32),
          "smE": kb.dram_in("smE", [128, 8, 12], F32),
          "wout": kb.dram_in("wout", [12, 128, 256], F32),
          "gff": kb.dram_in("gff", [128, 8], F32),
          "wg_h": kb.dram_in("wg_h", [22, D, 128], F32),
          "wv_h": kb.dram_in("wv_h", [22, D, 128], F32),
          "wf_h": kb.dram_in("wf_h", [8, 2816, 128], F32),
          "yb": kb.dram_out("yb", [D, T], F32)}
    alloc_common(kb)
    kb.sems("ld", "m", "st", "hx")
    kb.start()
    kb.push()
    emit_l1a_post(kb, io)
    kb.pop()
    return kb


def emit_l1a_post(kb, io, summ_ag=None):
    xa_d, hl_d, al_d = io["xa"], io["hl"], io["al"]
    wout_d, gff_d = io["wout"], io["gff"]
    wg_d, wv_d, wf_d = io["wg_h"], io["wv_h"], io["wf_h"]
    yb_d = io["yb"]
    x2_d = kb.dram_int("x2bounce", [D, T], F32)

    xa = kb.sbuf("xa_s", [128, CT, T], F32)
    alloc_ln(kb)
    gffc = kb.sbuf("gffc", [128, 8], F32)
    Hc = kb.sbuf("Hc", [128, 12], F32)
    sm = kb.sbuf("sm", [128, 2, 8, 12], F32)
    wob = kb.sbuf("wob", [128, 12, 256], BF16)
    nfb = kb.sbuf("nfb", [128, CT, T], BF16)

    ov = ExitStack()

    def osb(name, shape, dtype):
        return ov.enter_context(kb.sbuf_cm(name, shape, dtype))

    mcols = kb.sbuf("mcols", [128, 16], F32)
    hlc = osb("hlc", [128, 12, 512], F32)
    alc = osb("alc", [128, 12, 512], F32)
    htb = osb("htb", [128, 12, 512], BF16)

    pA = kb.psum("pA", [128, 512])
    pP = [kb.psum(f"pP{i}", [128, 512]) for i in range(2)]

    def p_load_sy(sy):
        for k in range(CT):
            kb.inc(sy.dma_start(out=xa[:, k], in_=xa_d[128 * k:128 * (k + 1), :]), "ld", 16)
        if summ_ag is None:
            kb.inc(sy.dma_start(out=sm[:, 0], in_=io["smP"]), "ld", 16)
            kb.inc(sy.dma_start(out=sm[:, 1], in_=io["smE"]), "ld", 16)
        else:
            agP, agE, mP_d, mE_d = summ_ag
            for jj in range(8):
                kb.inc(sy.dma_start(out=sm[:, 0, jj], in_=agP[jj]), "ld", 16)
                kb.inc(sy.dma_start(out=sm[:, 1, jj], in_=agE[jj]), "ld", 16)
            kb.inc(sy.dma_start(out=mcols[:, 0:8], in_=mP_d), "ld", 16)
            kb.inc(sy.dma_start(out=mcols[:, 8:16], in_=mE_d), "ld", 16)
        kb.inc(sy.dma_start(out=gffc[:], in_=gff_d), "ld", 16)

    def p_load_g(g):
        load_common(kb, g)
        for j in range(12):
            kb.inc(g.dma_start(out=wob[:, j], in_=wout_d[j]), "ld", 16)
    kb.phase(sync=p_load_sy, gpsimd=p_load_g)

    if summ_ag is not None:
        def p_meff(v):
            for jj in range(8):
                v.tensor_scalar_add(sm[:, 0, jj], sm[:, 0, jj], -1.0)
                v.tensor_scalar_mul(sm[:, 0, jj], sm[:, 0, jj], mcols[:, jj:jj + 1])
                v.tensor_scalar_add(sm[:, 0, jj], sm[:, 0, jj], 1.0)
                v.tensor_scalar_mul(sm[:, 1, jj], sm[:, 1, jj], mcols[:, 8 + jj:9 + jj])
        kb.phase(vector=p_meff)

    def p_comb(v):
        v.tensor_scalar(out=gffc[:], in0=gffc[:], scalar1=1.0, scalar2=32.0,
                        op0=ALU.add, op1=ALU.mult)
        v.memset(Hc[:], 0.0)
        for jj in range(8):
            v.tensor_mul(out=Hc[:], in0=Hc[:], in1=sm[:, 0, jj])
            v.tensor_add(out=Hc[:], in0=Hc[:], in1=sm[:, 1, jj])
    kb.phase(vector=p_comb)

    for tt in range(4):
        ts = slice(512 * tt, 512 * (tt + 1))

        def p_lh(sy, ts=ts):
            for j in range(12):
                kb.inc(sy.dma_start(out=hlc[:, j], in_=hl_d[:, j, ts]), "ld", 16)
                kb.inc(sy.dma_start(out=alc[:, j], in_=al_d[:, j, ts]), "ld", 16)
        kb.phase(sync=p_lh)

        def p_ht(v):
            for j in range(12):
                v.scalar_tensor_tensor(out=htb[:, j], in0=alc[:, j], scalar=Hc[:, j:j + 1],
                                       in1=hlc[:, j], op0=ALU.mult, op1=ALU.add)
        kb.phase(vector=p_ht)

        b_m, b_vv = kb.n("m"), kb.n("V")

        def p_go(t, b_vv=b_vv):
            for och in range(CT):
                if och >= 2:
                    t.wait_ge(kb.h("V"), b_vv + (och - 1))
                h4, oo = och // 2, och % 2
                for et in range(3):
                    i = t.matmul(out=pP[och % 2][:],
                                 lhsT=wob[:, 3 * h4 + et, 128 * oo:128 * (oo + 1)],
                                 rhs=htb[:, 3 * h4 + et],
                                 start=(et == 0), stop=(et == 2))
                kb.inc(i, "m")

        def p_goe(v, b_m=b_m, ts=ts):
            for och in range(CT):
                v._eng.wait_ge(kb.h("m"), b_m + och + 1)
                v.tensor_add(out=xa[:, och, ts], in0=pP[och % 2][:], in1=xa[:, och, ts])
        kb.phase(tensor=p_go, vector=p_goe)

    def p_stx2(sy):
        for k in range(CT):
            kb.inc(sy.dma_start(out=x2_d[128 * k:128 * (k + 1), :], in_=xa[:, k]), "hx", 16)
        sy.wait_ge(kb.h("hx"), kb.n("hx"))
    kb.phase(sync=p_stx2)

    emit_rms(kb, xa, nfb, 0, 0, T, gffc[:, 0:8], pA)

    ov.close()
    ov2 = ExitStack()

    def osb2(name, shape, dtype):
        return ov2.enter_context(kb.sbuf_cm(name, shape, dtype))

    x2c = osb2("x2c", [128, CT, 512], F32)
    gv = osb2("gv", [128, 22, 512], BF16)
    sgf = osb2("sgf", [128, 512], F32)
    svf = osb2("svf", [128, 512], F32)
    wgv = osb2("wgv", [128, 2, 2, CT, 128], BF16)
    wfb = osb2("wfb", [128, 2, 22, 128], BF16)
    yb = osb2("yb_s", [128, CT, 512], F32)

    pGg = kb.psum("pGg", [128, 512])
    pGv = kb.psum("pGv", [128, 512])
    pF = [kb.psum(f"pF{i}", [128, 512]) for i in range(2)]

    for tt in range(4):
        ts = slice(512 * tt, 512 * (tt + 1))

        def p_lx2(sy, ts=ts):
            for k in range(CT):
                kb.inc(sy.dma_start(out=x2c[:, k], in_=x2_d[128 * k:128 * (k + 1), ts]),
                       "ld", 16)
        kb.phase(sync=p_lx2)

        b_ld, b_m, b_sc, b_vv = kb.n("ld"), kb.n("m"), kb.n("SC"), kb.n("V")

        def p_wstr(g, b_m=b_m):
            for ft in range(22):
                pp = ft % 2
                if ft >= 2:
                    g._eng.wait_ge(kb.h("m"), b_m + 2 * (ft - 1))
                for dd in range(CT):
                    kb.inc(g.dma_start(out=wgv[:, pp, 0, dd],
                                       in_=wg_d[ft, 128 * dd:128 * (dd + 1), :]), "ld", 16)
                    kb.inc(g.dma_start(out=wgv[:, pp, 1, dd],
                                       in_=wv_d[ft, 128 * dd:128 * (dd + 1), :]), "ld", 16)

        def p_gvt(t, b_ld=b_ld, b_sc=b_sc, ts=ts):
            for ft in range(22):
                pp = ft % 2
                t.wait_ge(kb.h("ld"), b_ld + 16 * 16 * (ft + 1))
                if ft >= 1:
                    t.wait_ge(kb.h("SC"), b_sc + 2 * ft)
                for dd in range(CT):
                    i = t.matmul(out=pGg[:], lhsT=wgv[:, pp, 0, dd], rhs=nfb[:, dd, ts],
                                 start=(dd == 0), stop=(dd == CT - 1))
                kb.inc(i, "m")
                for dd in range(CT):
                    i = t.matmul(out=pGv[:], lhsT=wgv[:, pp, 1, dd], rhs=nfb[:, dd, ts],
                                 start=(dd == 0), stop=(dd == CT - 1))
                kb.inc(i, "m")

        def p_gvs(sc, b_m=b_m, b_vv=b_vv):
            for ft in range(22):
                sc._eng.wait_ge(kb.h("m"), b_m + 2 * ft + 1)
                if ft >= 1:
                    sc._eng.wait_ge(kb.h("V"), b_vv + ft)
                sc.activation(out=sgf[:], in_=pGg[:], func=AF.Silu)
                sc._eng.wait_ge(kb.h("m"), b_m + 2 * ft + 2)
                sc.activation(out=svf[:], in_=pGv[:], func=AF.Copy)

        def p_gvv(v, b_sc=b_sc):
            for ft in range(22):
                v._eng.wait_ge(kb.h("SC"), b_sc + 2 * ft + 2)
                v.tensor_mul(out=gv[:, ft], in0=sgf[:], in1=svf[:])
        kb.phase(gpsimd=p_wstr, tensor=p_gvt, scalar=p_gvs, vector=p_gvv)

        b_ld2, b_m2, b_vv2 = kb.n("ld"), kb.n("m"), kb.n("V")

        def p_fstr(g, b_m2=b_m2):
            for ot in range(CT):
                pp = ot % 2
                if ot >= 2:
                    g._eng.wait_ge(kb.h("m"), b_m2 + (ot - 1))
                for ft in range(22):
                    kb.inc(g.dma_start(out=wfb[:, pp, ft],
                                       in_=wf_d[ot, 128 * ft:128 * (ft + 1), :]), "ld", 16)

        def p_ft(t, b_ld2=b_ld2, b_vv2=b_vv2):
            for ot in range(CT):
                pp = ot % 2
                t.wait_ge(kb.h("ld"), b_ld2 + 16 * 22 * (ot + 1))
                if ot >= 2:
                    t.wait_ge(kb.h("V"), b_vv2 + (ot - 1))
                for ft in range(22):
                    i = t.matmul(out=pF[pp][:], lhsT=wfb[:, pp, ft], rhs=gv[:, ft],
                                 start=(ft == 0), stop=(ft == 21))
                kb.inc(i, "m")

        def p_fe(v, b_m2=b_m2):
            for ot in range(CT):
                v._eng.wait_ge(kb.h("m"), b_m2 + ot + 1)
                v.tensor_add(out=yb[:, ot], in0=pF[ot % 2][:], in1=x2c[:, ot])
        kb.phase(gpsimd=p_fstr, tensor=p_ft, vector=p_fe)

        def p_sty(sy, ts=ts):
            for k in range(CT):
                kb.inc(sy.dma_start(out=yb_d[128 * k:128 * (k + 1), ts], in_=yb[:, k]),
                       "st", 16)
        kb.phase(sync=p_sty)

    def p_fin(sy):
        sy.wait_ge(kb.h("st"), kb.n("st"))
    kb.phase(sync=p_fin)
    ov2.close()


def build_addln(with_add):
    kb = KB()
    io = {"a": kb.dram_in("a", [D, T], F32),
          "yln": kb.dram_out("yln", [D, T], F32)}
    if with_add:
        io["b"] = kb.dram_in("b", [D, T], F32)
        io["ysum"] = kb.dram_out("ysum", [D, T], F32)
    alloc_common(kb)
    kb.sems("ld", "m", "st")
    kb.start()
    kb.push()
    emit_addln(kb, io, with_add)
    kb.pop()
    return kb


def emit_addln(kb, io, with_add, W=T):
    a_d, yln = io["a"], io["yln"]
    xt = kb.sbuf("xt", [128, CT, W], F32)
    xnb = kb.sbuf("xnb", [128, CT, W], F32)
    alloc_ln(kb)
    pA = kb.psum("pA", [128, 512])
    pB = kb.psum("pB", [128, 512])

    def p_load_sy(sy):
        for k in range(CT):
            kb.inc(sy.dma_start(out=xt[:, k], in_=a_d[128 * k:128 * (k + 1), :]), "ld", 16)

    def p_load_g(g):
        load_common(kb, g)
        if with_add:
            for k in range(CT):
                kb.inc(g.dma_start(out=xnb[:, k],
                                   in_=io["b"][128 * k:128 * (k + 1), :]), "ld", 16)
    kb.phase(sync=p_load_sy, gpsimd=p_load_g)

    if with_add:
        def p_add(v):
            for k in range(CT):
                v.tensor_add(out=xt[:, k], in0=xt[:, k], in1=xnb[:, k])
        kb.phase(vector=p_add)

        def p_st1(sy):
            for k in range(CT):
                kb.inc(sy.dma_start(out=io["ysum"][128 * k:128 * (k + 1), :], in_=xt[:, k]),
                       "st", 16)
        kb.phase(sync=p_st1)

    emit_ln(kb, xt, xnb, 0, 0, W, pA, pB)

    def p_st2(sy):
        for k in range(CT):
            kb.inc(sy.dma_start(out=yln[128 * k:128 * (k + 1), :], in_=xnb[:, k]),
                   "st", 16)
        sy.wait_ge(kb.h("st"), kb.n("st"))
    kb.phase(sync=p_st2)


def _rotary_tables(pos):
    DH, WIN = 64, 256
    half = np.arange(0, DH, 2, dtype=np.float64)
    inv_freq = 1.0 / 10000.0 ** (half / DH)
    fr = pos[:, None] * inv_freq[None, :]
    freqs = np.concatenate([fr, fr], -1)
    base = (half + 0.4 * DH) / (1.4 * DH)
    power = (pos - S // 2) / (WIN // 2)
    sc = base[None, :] ** power[:, None]
    scale = np.concatenate([sc, sc], -1)
    cos, sin = np.cos(freqs), np.sin(freqs)
    sign = np.where(np.arange(64) < 32, -1.0, 1.0)[None, :]
    perm64 = np.concatenate([np.arange(32, 64), np.arange(0, 32)])
    out = []
    for s_, f in ((scale * 8.0, None), (1.0 / scale, None)):
        cs = (cos * s_).T
        sn = (sin * s_ * sign).T
        snp = sn[perm64]
        out.append(np.concatenate([cs, cs], 0).astype(BF))
        out.append(np.concatenate([snp, snp], 0).astype(BF))
    return out


def _f0_masks(s):
    m = np.zeros((4, 128, 384), dtype=BF)
    for mi in range(4):
        ig = s - 128 + 128 * mi + np.arange(128)[:, None]
        jg = s - 384 + 128 * mi + np.arange(384)[None, :]
        ok = (ig - jg >= 0) & (ig - jg <= 256) & (jg >= 0)
        ok = ok | (ig < 0)
        m[mi] = ok.astype(BF)
    return m


def _prep_f0_layer(inputs, li):
    Wqkv = inputs["L0_ffn_Wqkv"][li]
    wq = np.ascontiguousarray(Wqkv[:, 0:1024])
    wk = np.ascontiguousarray(Wqkv[:, 1024:2048])
    wv = np.ascontiguousarray(Wqkv[:, 2048:3072])
    qs = inputs["L0_ffn_q_scale"][li].astype(np.float32)
    ks = inputs["L0_ffn_k_scale"][li].astype(np.float32)
    qsp = np.concatenate([qs[32:], qs[:32]])
    ksp = np.concatenate([ks[32:], ks[:32]])
    wo = inputs["L0_ffn_Wo"][li]
    W1 = inputs["L0_ffn_W1"][li]
    b1 = inputs["L0_ffn_b1"][li]
    W2 = inputs["L0_ffn_W2"][li]
    b2 = inputs["L0_ffn_b2"][li]
    a, g = W1[:, 0:2730], W1[:, 2730:5460]
    w1p = np.zeros((44, D, 128), np.float32)
    b1a = np.zeros((128, 22), np.float32)
    b1g = np.zeros((128, 22), np.float32)
    for p in range(22):
        c0, c1 = 128 * p, min(128 * (p + 1), 2730)
        w1p[2 * p][:, 0:c1 - c0] = a[:, c0:c1]
        w1p[2 * p + 1][:, 0:c1 - c0] = g[:, c0:c1]
        b1a[0:c1 - c0, p] = b1[c0:c1]
        b1g[0:c1 - c0, p] = b1[2730 + c0:2730 + c1]
    w2pad = np.zeros((2816, D), np.float32)
    w2pad[0:2730] = W2
    w2po = np.ascontiguousarray(
        w2pad.reshape(2816, 8, 128).transpose(1, 0, 2))
    return {
        "wq": wq, "wk": wk, "wv": wv, "wo": wo,
        "qs_col": np.tile(qs, 2).reshape(128, 1),
        "ks_col": np.tile(ks, 2).reshape(128, 1),
        "qsp_col": np.tile(qsp, 2).reshape(128, 1),
        "ksp_col": np.tile(ksp, 2).reshape(128, 1),
        "w1p": w1p, "w2po": w2po,
        "g_col": np.ascontiguousarray(
            inputs["L0_ffn_ln_g"][li].reshape(8, 128).T.astype(np.float32)),
        "b_col": np.ascontiguousarray(
            inputs["L0_ffn_ln_b"][li].reshape(8, 128).T.astype(np.float32)),
        "b1a_col": b1a, "b1g_col": b1g,
        "b2_col": np.ascontiguousarray(b2.reshape(8, 128).T.astype(np.float32)),
    }


def build_moe_probs():
    kb = KB()
    io = {"xh": kb.dram_in("xh", [D, T], F32),
          "gw": kb.dram_in("gw", [D, 8], F32),
          "pr": kb.dram_out("pr", [128, 16, 8], F32)}
    alloc_common(kb)
    kb.sems("ld", "m", "st")
    kb.start()
    kb.push()
    emit_moe_probs(kb, io)
    kb.pop()
    return kb


def emit_moe_probs(kb, io):
    xh, gw_d, pr_d = io["xh"], io["gw"], io["pr"]
    xt = kb.sbuf("xt", [128, CT, T], F32)
    gwb = kb.sbuf("gwb", [128, CT, 8], F32)
    prb = kb.sbuf("prb", [128, 16, 8], F32)
    nmx = kb.sbuf("nmx", [128, 2], F32)
    rs = kb.sbuf("rs", [128, 2], F32)
    pS = [kb.psum(f"pS{i}", [128, 8]) for i in range(2)]

    def p_load(sy):
        for k in range(CT):
            kb.inc(sy.dma_start(out=xt[:, k], in_=xh[128 * k:128 * (k + 1), :]), "ld", 16)
            kb.inc(sy.dma_start(out=gwb[:, k], in_=gw_d[128 * k:128 * (k + 1), :]), "ld", 16)

    def p_load_g(g):
        load_common(kb, g)
    kb.phase(sync=p_load, gpsimd=p_load_g)

    b_m, b_sc = kb.n("m"), kb.n("SC")

    def p_mm(t, b_sc=b_sc):
        for lt in range(16):
            if lt >= 2:
                t.wait_ge(kb.h("SC"), b_sc + (lt - 1))
            for dd in range(CT):
                i = t.matmul(out=pS[lt % 2][:], lhsT=xt[:, dd, 128 * lt:128 * (lt + 1)],
                             rhs=gwb[:, dd], start=(dd == 0), stop=(dd == CT - 1))
            kb.inc(i, "m")

    b_vv = kb.n("V")

    def p_sm_v(v, b_m=b_m, b_sc=b_sc):
        for lt in range(16):
            pp = lt % 2
            v._eng.wait_ge(kb.h("m"), b_m + lt + 1)
            if lt >= 2:
                v._eng.wait_ge(kb.h("SC"), b_sc + (lt - 1))
            v.reduce_max(out=nmx[:, pp:pp + 1], in_=pS[pp][:], axis=AX.X, negate=True)
            v._eng.wait_ge(kb.h("SC"), b_sc + lt + 1)
            v.reciprocal(out=rs[:, pp:pp + 1], in_=rs[:, pp:pp + 1])
            v.tensor_scalar_mul(prb[:, lt], prb[:, lt], rs[:, pp:pp + 1])

    def p_sm_s(sc, b_vv=b_vv):
        for lt in range(16):
            pp = lt % 2
            sc._eng.wait_ge(kb.h("V"), b_vv + 3 * lt + 1)
            sc.activation(out=prb[:, lt], in_=pS[pp][:], func=AF.Exp,
                          bias=nmx[:, pp:pp + 1], accum_out=rs[:, pp:pp + 1])
    kb.phase(tensor=p_mm, vector=p_sm_v, scalar=p_sm_s)

    def p_st(sy):
        kb.inc(sy.dma_start(out=pr_d, in_=prb[:]), "st", 16)
        sy.wait_ge(kb.h("st"), kb.n("st"))
    kb.phase(sync=p_st)


def build_moe_main(niter=45):
    kb = KB()
    io = {"prA": kb.dram_in("prA", [128, 8, 128], F32),
          "xh": kb.dram_in("xh", [D, T], F32),
          "y3": kb.dram_in("y3", [D, T], F32),
          "scol": kb.dram_in("scol", [128, 16], F32),
          "w1": kb.dram_in("w1", [8, D, 4096], F32),
          "w2": kb.dram_in("w2", [8, 4096, D], F32),
          "yo": kb.dram_out("yo", [D, T], F32),
          "gates_o": kb.dram_out("gates_o", [128, 8, 128], F32),
          "grow_o": kb.dram_out("grow_o", [8, T], F32)}
    alloc_common(kb)
    kb.sems("ld", "m", "st")
    kb.start()
    kb.push()
    emit_moe_main(kb, io)
    kb.pop()
    return kb


def emit_moe_main(kb, io, niter=45, prall=None):
    CAP = 4096.0
    xh, y3_d, sc_d = io["xh"], io["y3"], io["scol"]
    w1_d, w2_d, out_d = io["w1"], io["w2"], io["yo"]
    gates_d, grow_d = io.get("gates_o"), io.get("grow_o")

    d_identf = kb.cin("identf")
    d_ltril = kb.cin("ltrilf")
    d_onesf8 = kb.cin("onesf8")
    d_ident8 = kb.cin("ident8f")
    d_onescolf = kb.cin("onescolf")
    d_onesrowf = kb.cin("onesrow_f")
    d_oh8 = kb.cin("oh8f")

    xnb = kb.sbuf("xnb", [128, CT, T], BF16)
    if prall is None:
        prA = kb.sbuf("prA_s", [128, 8, 128], F32)

        def pre(e):
            return prA[:, e]
    else:
        prB = kb.sbuf("prB_s", [128, 128, 8], F32)

        def pre(e):
            return prB[:, :, e]
    gates = kb.sbuf("gates", [128, 8, 128], F32)
    mgt = kb.sbuf("mgt", [128, 8, 128], F32)
    msk = kb.sbuf("msk", [128, 8, 128], F32)
    eqb = kb.sbuf("eqb", [128, 128], F32)
    cnts = kb.sbuf("cnts", [128, 8], F32)
    candm = kb.sbuf("candm", [128, 8], F32)
    rows = kb.sbuf("rows", [1, 64], F32)
    zero1 = kb.sbuf("zero1", [1, 1], F32)
    bc = kb.sbuf("bc", [128, 24], F32)
    vcol = kb.sbuf("vcol", [8, 1], F32)
    d8 = kb.sbuf("d8", [8, 8], F32)
    tident = kb.sbuf("tident", [128, 128], F32)
    tltril = kb.sbuf("tltril", [128, 128], F32)
    tonesf8 = kb.sbuf("tonesf8", [8, 128], F32)
    tident8 = kb.sbuf("tident8", [8, 8], F32)
    tonescolf = kb.sbuf("tonescolf", [128, 1], F32)
    tonesrowf = kb.sbuf("tonesrowf", [1, 128], F32)
    toh8 = kb.sbuf("toh8", [8, 8, 128], F32)
    scol = kb.sbuf("scol_s", [128, 16], F32)
    rank = kb.sbuf("rank", [128, 128], F32)
    gts = kb.sbuf("gts", [128, 128], F32)
    ownb = kb.sbuf("ownb", [128, 16], F32)
    ows = kb.sbuf("ows", [16, 128], F32)
    prow = kb.sbuf("prow", [1, 256], F32)
    grow = kb.sbuf("grow", [8, T], F32)

    ov0 = ExitStack()
    xtf = ov0.enter_context(kb.sbuf_cm("xtf", [128, CT, T], F32))

    pH = [kb.psum(f"pH{i}", [128, 512]) for i in range(2)]
    pW = [kb.psum(f"pW{i}", [128, 512]) for i in range(2)]
    pB = kb.psum("pB", [128, 512])
    pT = kb.psum("pT", [1, 512])
    pX = kb.psum("pX", [128, 128])
    pO = kb.psum("pO", [128, 128])

    def p_load(sy):
        for k in range(CT):
            kb.inc(sy.dma_start(out=xtf[:, k], in_=xh[128 * k:128 * (k + 1), :]), "ld", 16)
        if prall is None:
            kb.inc(sy.dma_start(out=prA[:], in_=io["prA"]), "ld", 16)
        else:
            for c in range(NCORE):
                kb.inc(sy.dma_start(out=prB[:, 16 * c:16 * (c + 1), :], in_=prall[c]),
                       "ld", 16)
        kb.inc(sy.dma_start(out=scol[:], in_=sc_d), "ld", 16)

    def p_load_g(g):
        load_common(kb, g)
        kb.inc(g.dma_start(out=tident[:], in_=d_identf), "ld", 16)
        kb.inc(g.dma_start(out=tltril[:], in_=d_ltril), "ld", 16)
        kb.inc(g.dma_start(out=tonesf8[:], in_=d_onesf8), "ld", 16)
        kb.inc(g.dma_start(out=tident8[:], in_=d_ident8), "ld", 16)
        kb.inc(g.dma_start(out=tonescolf[:], in_=d_onescolf), "ld", 16)
        kb.inc(g.dma_start(out=tonesrowf[:], in_=d_onesrowf), "ld", 16)
        kb.inc(g.dma_start(out=toh8[:], in_=d_oh8), "ld", 16)
    kb.phase(sync=p_load, gpsimd=p_load_g)

    def p_cast(sc):
        for k in range(CT):
            sc.activation(out=xnb[:, k], in_=xtf[:, k], func=AF.Copy)
    kb.phase(scalar=p_cast)

    def p_init(v):
        v.memset(rows[0:1, 0:8], 0.5)
        v.memset(rows[0:1, 8:16], 0.0)
        v.memset(rows[0:1, 16:24], 1.0)
        v.memset(zero1[:], 0.0)
    kb.phase(vector=p_init)

    for it in range(niter):
        delta = 0.25 * (0.5 ** it)

        def p_bc(t):
            i = t.matmul(out=pB[:, 0:8], lhsT=tonesrowf[:], rhs=rows[0:1, 0:8],
                         start=True, stop=True)
            kb.inc(i, "m")
        kb.phase(tensor=p_bc)

        def p_msk(v):
            v.tensor_copy(out=bc[:, 0:8], in_=pB[:, 0:8])
            for e in range(8):
                v.tensor_scalar(out=msk[:, e], in0=pre(e), scalar1=bc[:, e:e + 1],
                                scalar2=0.0, op0=ALU.is_gt, op1=ALU.bypass)
                v.reduce_sum(out=cnts[:, e:e + 1], in_=msk[:, e], axis=AX.X)
        kb.phase(vector=p_msk)

        def p_tot(t):
            i = t.matmul(out=pT[0:1, 0:8], lhsT=tonescolf[:], rhs=cnts[:],
                         start=True, stop=True)
            kb.inc(i, "m")
        kb.phase(tensor=p_tot)

        def p_upd(v, delta=delta):
            v.tensor_scalar(out=rows[0:1, 24:32], in0=pT[0:1, 0:8],
                            scalar1=CAP - 0.5, scalar2=0.0, op0=ALU.is_gt, op1=ALU.bypass)
            v.tensor_mul(out=rows[0:1, 32:40], in0=rows[0:1, 24:32], in1=rows[0:1, 0:8])
            v.tensor_max(out=rows[0:1, 8:16], in0=rows[0:1, 8:16], in1=rows[0:1, 32:40])
            v.tensor_scalar(out=rows[0:1, 40:48], in0=rows[0:1, 24:32], scalar1=2.0,
                            scalar2=0.0, op0=ALU.mult, op1=ALU.bypass)
            v.tensor_add(out=rows[0:1, 40:48], in0=rows[0:1, 40:48], in1=rows[0:1, 0:8])
            v.tensor_tensor(out=rows[0:1, 16:24], in0=rows[0:1, 16:24],
                            in1=rows[0:1, 40:48], op=ALU.min)
            v.tensor_scalar(out=rows[0:1, 32:40], in0=rows[0:1, 24:32], scalar1=2.0,
                            scalar2=-1.0, op0=ALU.mult, op1=ALU.add)
            v.tensor_scalar(out=rows[0:1, 32:40], in0=rows[0:1, 32:40],
                            scalar1=float(delta), scalar2=0.0, op0=ALU.mult, op1=ALU.bypass)
            v.tensor_add(out=rows[0:1, 0:8], in0=rows[0:1, 0:8], in1=rows[0:1, 32:40])
        kb.phase(vector=p_upd)

    def p_bc2(t):
        i = t.matmul(out=pB[:, 0:16], lhsT=tonesrowf[:], rhs=rows[0:1, 8:24],
                     start=True, stop=True)
        kb.inc(i, "m")
    kb.phase(tensor=p_bc2)

    def p_cand(v):
        v.tensor_copy(out=bc[:, 0:16], in_=pB[:, 0:16])
        for e in range(8):
            v.tensor_scalar(out=msk[:, e], in0=pre(e), scalar1=bc[:, e:e + 1],
                            scalar2=0.0, op0=ALU.is_gt, op1=ALU.bypass)
            v.tensor_scalar(out=gates[:, e], in0=pre(e), scalar1=bc[:, 8 + e:9 + e],
                            scalar2=0.0, op0=ALU.is_gt, op1=ALU.bypass)
            v.tensor_sub(out=msk[:, e], in0=msk[:, e], in1=gates[:, e])
            v.tensor_mul(out=msk[:, e], in0=msk[:, e], in1=pre(e))
            v.reduce_max(out=candm[:, e:e + 1], in_=msk[:, e], axis=AX.X)
    kb.phase(vector=p_cand)

    def p_tr1(t):
        i = t.transpose(out=pX[0:8, :], in_=candm[:], identity=tident[:])
        kb.inc(i, "m")
    kb.phase(tensor=p_tr1)

    def p_vst(v):
        v.reduce_max(out=vcol[:], in_=pX[0:8, 0:128], axis=AX.X)
        v.tensor_scalar_mul(d8[:], tident8[:], vcol[0:8, 0:1])
    kb.phase(vector=p_vst)

    def p_bc3(t):
        i = t.matmul(out=pB[:, 0:8], lhsT=tonesf8[:], rhs=d8[:], start=True, stop=True)
        kb.inc(i, "m")
    kb.phase(tensor=p_bc3)

    def p_ngt(v):
        v.tensor_copy(out=bc[:, 8:16], in_=pB[:, 0:8])
        for e in range(8):
            v.tensor_scalar(out=mgt[:, e], in0=pre(e), scalar1=bc[:, 8 + e:9 + e],
                            scalar2=0.0, op0=ALU.is_gt, op1=ALU.bypass)
            v.reduce_sum(out=cnts[:, e:e + 1], in_=mgt[:, e], axis=AX.X)
    kb.phase(vector=p_ngt)

    def p_tot2(t):
        i = t.matmul(out=pT[0:1, 0:8], lhsT=tonescolf[:], rhs=cnts[:],
                     start=True, stop=True)
        kb.inc(i, "m")
    kb.phase(tensor=p_tot2)

    def p_r(v):
        v.tensor_scalar(out=rows[0:1, 48:56], in0=pT[0:1, 0:8], scalar1=-1.0,
                        scalar2=CAP, op0=ALU.mult, op1=ALU.add)
    kb.phase(vector=p_r)

    def p_bc4(t):
        i = t.matmul(out=pB[:, 0:8], lhsT=tonesrowf[:], rhs=rows[0:1, 48:56],
                     start=True, stop=True)
        kb.inc(i, "m")
    kb.phase(tensor=p_bc4)

    def p_rbc(v):
        v.tensor_copy(out=bc[:, 16:24], in_=pB[:, 0:8])
    kb.phase(vector=p_rbc)

    for e in range(8):
        def p_eq(v, e=e):
            v.tensor_scalar(out=eqb[:], in0=pre(e), scalar1=bc[:, 8 + e:9 + e],
                            scalar2=0.0, op0=ALU.is_equal, op1=ALU.bypass)
        kb.phase(vector=p_eq)

        def p_pfx(t):
            i = t.matmul(out=pT[0:1, 0:128], lhsT=tonescolf[:], rhs=eqb[:],
                         start=True, stop=True)
            kb.inc(i, "m")
            i = t.matmul(out=pX[:], lhsT=tltril[:], rhs=eqb[:], start=True, stop=True)
            kb.inc(i, "m")
        kb.phase(tensor=p_pfx)

        def p_scan(v):
            v.tensor_copy(out=prow[0:1, 0:128], in_=pT[0:1, 0:128])
            v.tensor_tensor_scan(out=prow[0:1, 128:256], data0=tonesrowf[0:1, :],
                                 data1=prow[0:1, 0:128], initial=zero1[:],
                                 op0=ALU.mult, op1=ALU.add)
            v.tensor_sub(out=prow[0:1, 0:128], in0=prow[0:1, 128:256],
                         in1=prow[0:1, 0:128])
        kb.phase(vector=p_scan)

        def p_pcb(t):
            i = t.matmul(out=pB[:, 0:128], lhsT=tonesrowf[:], rhs=prow[0:1, 0:128],
                         start=True, stop=True)
            kb.inc(i, "m")
        kb.phase(tensor=p_pcb)

        def p_gate(v, e=e):
            v.tensor_copy(out=rank[:], in_=pX[:])
            v.tensor_add(out=rank[:], in0=rank[:], in1=pB[:, 0:128])
            v.tensor_scalar(out=rank[:], in0=rank[:], scalar1=bc[:, 16 + e:17 + e],
                            scalar2=-1.0, op0=ALU.subtract, op1=ALU.mult)
            v.tensor_scalar(out=rank[:], in0=rank[:], scalar1=0.5, scalar2=0.0, op0=ALU.is_gt, op1=ALU.bypass)
            v.tensor_mul(out=rank[:], in0=rank[:], in1=eqb[:])
            v.tensor_add(out=rank[:], in0=rank[:], in1=mgt[:, e])
            v.tensor_mul(out=gates[:, e], in0=rank[:], in1=pre(e))
        kb.phase(vector=p_gate)

        def p_tr2(t, e=e):
            i = t.transpose(out=pX[:], in_=gates[:, e], identity=tident[:])
            kb.inc(i, "m")
        kb.phase(tensor=p_tr2)

        def p_cp1(v):
            v.tensor_copy(out=gts[:], in_=pX[:])
        kb.phase(vector=p_cp1)

        def p_own(t):
            i = t.matmul(out=pO[:, 0:16], lhsT=gts[:], rhs=scol[:], start=True, stop=True)
            kb.inc(i, "m")
        kb.phase(tensor=p_own)

        def p_cp2(v):
            v.tensor_copy(out=ownb[:], in_=pO[:, 0:16])
        kb.phase(vector=p_cp2)

        def p_tr3(t):
            i = t.transpose(out=pO[0:16, 0:128], in_=ownb[:], identity=tident[:])
            kb.inc(i, "m")
        kb.phase(tensor=p_tr3)

        def p_cp3(v):
            v.tensor_copy(out=ows[:], in_=pO[0:16, 0:128])
        kb.phase(vector=p_cp3)

        def p_gr(sy, e=e):
            for lt in range(16):
                kb.inc(sy.dma_start(out=grow[e:e + 1, 128 * lt:128 * (lt + 1)],
                                    in_=ows[lt:lt + 1, :]), "st", 16)
            sy.wait_ge(kb.h("st"), kb.n("st"))
        kb.phase(sync=p_gr)

    if gates_d is not None:
        def p_dbg(sy):
            kb.inc(sy.dma_start(out=gates_d, in_=gates[:]), "st", 16)
            kb.inc(sy.dma_start(out=grow_d, in_=grow[:]), "st", 16)
        kb.phase(sync=p_dbg)

    ov0.close()
    ov1 = ExitStack()

    def osb(name, shape, dtype):
        return ov1.enter_context(kb.sbuf_cm(name, shape, dtype))

    hb = osb("hb", [128, 32, 1024], BF16)
    w1b = osb("w1b", [128, 2, CT, 128], BF16)
    w2b = osb("w2b", [128, 2, 32, 128], BF16)
    gbc = osb("gbc", [128, 1024], F32)
    yacc = osb("yacc", [128, CT, 1024], F32)
    ytmp = osb("ytmp", [128, 512], F32)
    y3c = osb("y3c", [128, CT, 512], F32)

    for ch in range(2):
        def p_z(v):
            for k in range(CT):
                v.memset(yacc[:, k], 0.0)
        kb.phase(vector=p_z)

        for e in range(8):
            b_m = kb.n("m")
            b_vg = kb.n("V")

            def p_gb(t, e=e, ch=ch, b_vg=b_vg):
                for half in range(2):
                    if half == 1:
                        t.wait_ge(kb.h("V"), b_vg + 1)
                    i = t.matmul(out=pB[:],
                                 lhsT=toh8[:, e], rhs=grow[0:8, 1024 * ch + 512 * half:
                                                           1024 * ch + 512 * (half + 1)],
                                 start=True, stop=True)
                    kb.inc(i, "m")

            def p_gbe(v, b_m=b_m):
                for half in range(2):
                    v._eng.wait_ge(kb.h("m"), b_m + half + 1)
                    v.tensor_copy(out=gbc[:, 512 * half:512 * (half + 1)], in_=pB[:])
            kb.phase(tensor=p_gb, vector=p_gbe)

            b_ld, b_m2, b_sc = kb.n("ld"), kb.n("m"), kb.n("SC")

            def p_w1s(g, e=e, b_m2=b_m2):
                for ht in range(32):
                    if ht >= 2:
                        g._eng.wait_ge(kb.h("m"), b_m2 + 2 * (ht - 1))
                    for dd in range(CT):
                        kb.inc(g.dma_start(out=w1b[:, ht % 2, dd],
                                           in_=w1_d[e, 128 * dd:128 * (dd + 1),
                                                    128 * ht:128 * (ht + 1)]), "ld", 16)

            def p_hm(t, ch=ch, b_ld=b_ld, b_sc=b_sc):
                for ht in range(32):
                    t.wait_ge(kb.h("ld"), b_ld + 16 * 8 * (ht + 1))
                    for half in range(2):
                        u = 2 * ht + half
                        if u >= 2:
                            t.wait_ge(kb.h("SC"), b_sc + u - 1)
                        for dd in range(CT):
                            i = t.matmul(out=pH[u % 2][:], lhsT=w1b[:, ht % 2, dd],
                                         rhs=xnb[:, dd, 1024 * ch + 512 * half:
                                                 1024 * ch + 512 * (half + 1)],
                                         start=(dd == 0), stop=(dd == CT - 1))
                        kb.inc(i, "m")

            def p_he(sc, b_m2=b_m2):
                for ht in range(32):
                    for half in range(2):
                        u = 2 * ht + half
                        sc._eng.wait_ge(kb.h("m"), b_m2 + u + 1)
                        sc.activation(out=hb[:, ht, 512 * half:512 * (half + 1)],
                                      in_=pH[u % 2][:], func=AF.Gelu)
            kb.phase(gpsimd=p_w1s, tensor=p_hm, scalar=p_he)

            b_ld3, b_m3, b_vv3 = kb.n("ld"), kb.n("m"), kb.n("V")

            def p_w2s(g, e=e, b_m3=b_m3):
                for ot in range(CT):
                    if ot >= 2:
                        g._eng.wait_ge(kb.h("m"), b_m3 + 2 * (ot - 1))
                    for ht in range(32):
                        kb.inc(g.dma_start(out=w2b[:, ot % 2, ht],
                                           in_=w2_d[e, 128 * ht:128 * (ht + 1),
                                                    128 * ot:128 * (ot + 1)]), "ld", 16)

            def p_wm(t, b_ld3=b_ld3, b_vv3=b_vv3):
                for ot in range(CT):
                    t.wait_ge(kb.h("ld"), b_ld3 + 16 * 32 * (ot + 1))
                    for half in range(2):
                        u = 2 * ot + half
                        if u >= 2:
                            t.wait_ge(kb.h("V"), b_vv3 + 2 * (u - 1))
                        for ht in range(32):
                            i = t.matmul(out=pW[u % 2][:], lhsT=w2b[:, ot % 2, ht],
                                         rhs=hb[:, ht, 512 * half:512 * (half + 1)],
                                         start=(ht == 0), stop=(ht == 31))
                        kb.inc(i, "m")

            def p_wa(v, b_m3=b_m3):
                for ot in range(CT):
                    for half in range(2):
                        u = 2 * ot + half
                        v._eng.wait_ge(kb.h("m"), b_m3 + u + 1)
                        v.tensor_mul(out=ytmp[:], in0=pW[u % 2][:],
                                     in1=gbc[:, 512 * half:512 * (half + 1)])
                        v.tensor_add(out=yacc[:, ot, 512 * half:512 * (half + 1)],
                                     in0=yacc[:, ot, 512 * half:512 * (half + 1)],
                                     in1=ytmp[:])
            kb.phase(gpsimd=p_w2s, tensor=p_wm, vector=p_wa)

        for half in range(2):
            cs = slice(1024 * ch + 512 * half, 1024 * ch + 512 * (half + 1))

            def p_ly3(sy, cs=cs):
                for k in range(CT):
                    kb.inc(sy.dma_start(out=y3c[:, k], in_=y3_d[128 * k:128 * (k + 1), cs]),
                           "ld", 16)
            kb.phase(sync=p_ly3)

            def p_add(v, half=half):
                for k in range(CT):
                    v.tensor_add(out=yacc[:, k, 512 * half:512 * (half + 1)],
                                 in0=yacc[:, k, 512 * half:512 * (half + 1)],
                                 in1=y3c[:, k])
            kb.phase(vector=p_add)

            def p_sty(sy, cs=cs, half=half):
                for k in range(CT):
                    kb.inc(sy.dma_start(out=out_d[128 * k:128 * (k + 1), cs],
                                        in_=yacc[:, k, 512 * half:512 * (half + 1)]),
                           "st", 16)
            kb.phase(sync=p_sty)

    def p_fin(sy):
        sy.wait_ge(kb.h("st"), kb.n("st"))
    kb.phase(sync=p_fin)
    ov1.close()


def emit_collective(kb, kind, in_ap, out_ap, op=None):
    kb.sems_once("cc")

    def p_cc(g):
        i = g.collective_compute(kind, op if op is not None else ALU.bypass,
                                 replica_groups=[list(range(NCORE))],
                                 ins=[in_ap.opt()], outs=[out_ap.opt()])
        kb.inc(i, "cc", 1)
        g.wait_ge(kb.h("cc"), kb.n("cc"))
    kb.phase(gpsimd=p_cc)


def emit_d2d(kb, dst_ap, src_ap, rows, wdt, dtype=F32):
    kb.push()
    eb = kb.sbuf("ebuf", [128, wdt], dtype)

    def p_cp(sy):
        for k in range(rows // 128):
            kb.inc(sy.dma_start(out=eb[:], in_=src_ap[128 * k:128 * (k + 1), :]), "ld", 16)
            sy.wait_ge(kb.h("ld"), kb.n("ld"))
            kb.inc(sy.dma_start(out=dst_ap[128 * k:128 * (k + 1), :], in_=eb[:]), "st", 16)
            sy.wait_ge(kb.h("st"), kb.n("st"))
    kb.phase(sync=p_cp)
    kb.pop()


def emit_halo_fetch(kb, ag_flat, hidx_s, hm_s, dst_ap, wdt):
    kb.push()
    hb = kb.sbuf("halob", [128, CT, wdt], F32)

    def p_g(g):
        for k in range(CT):
            kb.inc(g.indirect_dma_start(
                out=hb[:, k], out_offset=None, in_=ag_flat,
                in_offset=IndirectOffsetOnAxis(ap=hidx_s[:, k:k + 1], axis=0)), "ld", 16)
    kb.phase(gpsimd=p_g)

    def p_m(v):
        for k in range(CT):
            v.tensor_scalar_mul(hb[:, k], hb[:, k], hm_s[:, 0:1])
    kb.phase(vector=p_m)

    def p_o(sy):
        for k in range(CT):
            kb.inc(sy.dma_start(out=dst_ap[128 * k:128 * (k + 1), :], in_=hb[:, k]),
                   "st", 16)
        sy.wait_ge(kb.h("st"), kb.n("st"))
    kb.phase(sync=p_o)
    kb.pop()


def build_fused():
    kb = KB()
    xh = kb.dram_in("xh", [D, L0A_XW], F32)
    encq = kb.dram_in("encq", [8, D, 128], F32)
    encv = kb.dram_in("encv", [8, D, 128], F32)
    dec = kb.dram_in("decoder", [D, D], F32)
    heb = kb.dram_in("hebbian", [8, 128, 128], F32)
    mw1 = kb.dram_in("m_w1", [D, 512], F32)
    mb1 = kb.dram_in("m_b1", [1, 512], F32)
    mw2 = kb.dram_in("m_w2", [512, D], F32)
    mb2 = kb.dram_in("m_b2", [1, D], F32)
    baseT = kb.dram_in("base_T", [128, 8], F32)
    wq6 = kb.dram_in("wq6", [6, D, 1024], F32)
    wk6 = kb.dram_in("wk6", [6, D, 1024], F32)
    wv6 = kb.dram_in("wv6", [6, D, 1024], F32)
    wo6 = kb.dram_in("wo6", [6, 1024, D], F32)
    qs6 = kb.dram_in("qs6", [6, 128, 1], F32)
    ks6 = kb.dram_in("ks6", [6, 128, 1], F32)
    csq = kb.dram_in("csq", [128, 2176], BF16)
    snq = kb.dram_in("snq", [128, 2176], BF16)
    csk = kb.dram_in("csk", [128, 2432], BF16)
    snk = kb.dram_in("snk", [128, 2432], BF16)
    maskp = kb.dram_in("maskp", [4, 128, 384], BF16)
    w1p6 = kb.dram_in("w1p6", [6, 44, D, 128], F32)
    w2po6 = kb.dram_in("w2po6", [6, 8, 2816, 128], F32)
    g6 = kb.dram_in("g6", [6, 128, 8], F32)
    b6 = kb.dram_in("b6", [6, 128, 8], F32)
    b1a6 = kb.dram_in("b1a6", [6, 128, 22], F32)
    b1g6 = kb.dram_in("b1g6", [6, 128, 22], F32)
    b2_6 = kb.dram_in("b2_6", [6, 128, 8], F32)
    dwc = kb.dram_in("dwc", [128, 32], F32)
    dwb = kb.dram_in("dwb", [128, 8], F32)
    pwbias = kb.dram_in("pwbias", [128, 8], F32)
    gcv = kb.dram_in("gcv", [128, 8], F32)
    ggr = kb.dram_in("ggr", [128, 8], F32)
    pwT = kb.dram_in("pwT", [D, D], F32)
    whg = kb.dram_in("whg", [4, 256, 768], F32)
    wout = kb.dram_in("wout", [12, 128, 256], F32)
    gff = kb.dram_in("gff", [128, 8], F32)
    wg_h = kb.dram_in("wg_h", [22, D, 128], F32)
    wv_h = kb.dram_in("wv_h", [22, D, 128], F32)
    wf_h = kb.dram_in("wf_h", [8, 2816, 128], F32)
    mP_d = kb.dram_in("mP", [128, 8], F32)
    mE_d = kb.dram_in("mE", [128, 8], F32)
    gw = kb.dram_in("gw", [D, 8], F32)
    sc_d = kb.dram_in("scol", [128, 16], F32)
    w1_d = kb.dram_in("w1", [8, D, 4096], F32)
    w2_d = kb.dram_in("w2", [8, 4096, D], F32)
    hidx_d = kb.dram_in("hidx", [128, CT], I32)
    hmask_d = kb.dram_in("hmask", [128, 1], F32)
    yo = kb.dram_out("yo", [D, T], F32)

    qs_in = kb.dram_int("qs_in", [128, 16], F32)
    qs_ag = kb.dram_int("qs_ag", [NCORE, 128, 16], F32, shared=True)
    y1_d = kb.dram_int("y1_d", [D, L0A_W], F32)
    xcur = [kb.dram_int(f"xcur{i}", [D, F0A_XW], F32) for i in range(2)]
    ya_d = kb.dram_int("ya_d", [D, 2180], F32)
    ag_in = kb.dram_int("ag_in", [D, 385], F32)
    ag_out = kb.dram_int("ag_out", [NCORE * D, 385], F32, shared=True)
    x6_d = kb.dram_int("x6_d", [D, T], F32)
    y2_d = kb.dram_int("y2_d", [D, T], F32)
    y2ln_d = kb.dram_int("y2ln_d", [D, L1C_XW], F32)
    e2_in = kb.dram_int("e2_in", [D, 3], F32)
    e2_out = kb.dram_int("e2_out", [NCORE * D, 3], F32, shared=True)
    xa1_d = kb.dram_int("xa1_d", [D, T], F32)
    hl_d = kb.dram_int("hl_d", [128, 12, T], F32)
    al_d = kb.dram_int("al_d", [128, 12, T], F32)
    sp_in = kb.dram_int("sp_in", [128, 12], F32)
    se_in = kb.dram_int("se_in", [128, 12], F32)
    sp_ag = kb.dram_int("sp_ag", [NCORE, 128, 12], F32, shared=True)
    se_ag = kb.dram_int("se_ag", [NCORE, 128, 12], F32, shared=True)
    yb_d = kb.dram_int("yb_d", [D, T], F32)
    y3_d = kb.dram_int("y3_d", [D, T], F32)
    y3ln_d = kb.dram_int("y3ln_d", [D, T], F32)
    pr_in = kb.dram_int("pr_in", [128, 16, 8], F32)
    pr_ag = kb.dram_int("pr_ag", [NCORE, 128, 16, 8], F32, shared=True)

    alloc_common(kb)
    hm_s = kb.sbuf("hm_s", [128, 1], F32)
    hidx_s = kb.sbuf("hidx_s", [128, CT], I32)
    kb.sems("ld", "m", "st", "am", "tm", "qm", "im", "ic", "dn",
            "km", "sm", "wm", "hm", "w1l", "w2l", "hx", "cc")
    kb.start()

    def p_pre(sy):
        kb.inc(sy.dma_start(out=hm_s[:], in_=hmask_d), "ld", 16)
        kb.inc(sy.dma_start(out=hidx_s[:], in_=hidx_d), "ld", 16)
    kb.phase(sync=p_pre)

    kb.push()
    emit_l0a_sig(kb, {"xh": xh, "encq": encq, "qsums": qs_in, "x0": 543})
    kb.pop()
    emit_collective(kb, "AllGather", qs_in, qs_ag)
    kb.push()
    emit_l0a_main(kb, {"xh": xh, "qsums_all": qs_ag, "encq": encq, "encv": encv,
                       "decoder": dec, "hebbian": heb, "m_w1": mw1, "m_b1": mb1,
                       "m_w2": mw2, "m_b2": mb2, "base_T": baseT, "y1": y1_d})
    kb.pop()
    kb.push()
    emit_addln(kb, {"a": y1_d[:, 127:2560], "yln": xcur[0][:, 0:2433]},
               with_add=False, W=2433)
    kb.pop()

    for li in range(6):
        src = xcur[li % 2]
        nxt = xcur[1 - li % 2]
        kb.push()
        emit_f0_attn(kb, {"xh": src, "wq": wq6[li], "wk": wk6[li], "wv": wv6[li],
                          "wo": wo6[li], "qs_col": qs6[li], "ks_col": ks6[li],
                          "csq": csq, "snq": snq, "csk": csk, "snk": snk,
                          "maskp": maskp, "ya": ya_d[:, 0:2176]}, with_t0=True)
        kb.pop()
        kb.push()
        emit_f0_ffn(kb, {"xf": ya_d[:, 127:2179], "w1p": w1p6[li], "w2po": w2po6[li],
                         "g_col": g6[li], "b_col": b6[li], "b1a_col": b1a6[li],
                         "b1g_col": b1g6[li], "b2_col": b2_6[li],
                         "yf": (nxt[:, 385:2433] if li < 5 else x6_d)},
                    hmask=hm_s)
        kb.pop()
        if li < 5:
            emit_d2d(kb, ag_in, nxt[:, 2048:2433], D, 385)
            emit_collective(kb, "AllGather", ag_in, ag_out)
            emit_halo_fetch(kb, ag_out, hidx_s, hm_s, nxt[:, 0:385], 385)

    kb.push()
    emit_addln(kb, {"a": y1_d[:, 512:2560], "b": x6_d, "ysum": y2_d,
                    "yln": y2ln_d[:, 3:2051]}, with_add=True)
    kb.pop()
    emit_d2d(kb, e2_in, y2ln_d[:, 2048:2051], D, 3)
    emit_collective(kb, "AllGather", e2_in, e2_out)
    emit_halo_fetch(kb, e2_out, hidx_s, hm_s, y2ln_d[:, 0:3], 3)

    kb.push()
    emit_l1a_pre(kb, {"xh": y2ln_d, "dwc": dwc, "dwb": dwb, "pwbias": pwbias,
                      "gcv": gcv, "ggr": ggr, "pwT": pwT, "whg": whg,
                      "xa": xa1_d, "hl": hl_d, "al": al_d, "summ": None,
                      "summP": sp_in, "summE": se_in})
    kb.pop()
    emit_collective(kb, "AllGather", sp_in, sp_ag)
    emit_collective(kb, "AllGather", se_in, se_ag)
    kb.push()
    emit_l1a_post(kb, {"xa": xa1_d, "hl": hl_d, "al": al_d, "wout": wout,
                       "gff": gff, "wg_h": wg_h, "wv_h": wv_h, "wf_h": wf_h,
                       "yb": yb_d},
                  summ_ag=(sp_ag, se_ag, mP_d, mE_d))
    kb.pop()

    kb.push()
    emit_addln(kb, {"a": y2_d, "b": yb_d, "ysum": y3_d, "yln": y3ln_d},
               with_add=True)
    kb.pop()

    kb.push()
    emit_moe_probs(kb, {"xh": y3ln_d, "gw": gw, "pr": pr_in})
    kb.pop()
    emit_collective(kb, "AllGather", pr_in, pr_ag)
    kb.push()
    emit_moe_main(kb, {"xh": y3ln_d, "y3": y3_d, "scol": sc_d, "w1": w1_d,
                       "w2": w2_d, "yo": yo, "gates_o": None, "grow_o": None},
                  prall=pr_ag)
    kb.pop()
    return kb


def _prep_l1a(inputs):
    dw = inputs["L1_attn_dw_w"].astype(np.float32)
    o = {}
    o["dwc"] = np.ascontiguousarray(
        dw.reshape(8, 128, 4).transpose(1, 0, 2).reshape(128, 32))
    o["dwb"] = np.ascontiguousarray(inputs["L1_attn_dw_b"].reshape(8, 128).T)
    o["pwbias"] = np.ascontiguousarray(inputs["L1_attn_pw_b"].reshape(8, 128).T)
    o["gcv"] = np.ascontiguousarray(inputs["L1_attn_conv_gamma"].reshape(8, 128).T)
    o["ggr"] = np.ascontiguousarray(inputs["L1_attn_gru_gamma"].reshape(8, 128).T)
    o["gff"] = np.ascontiguousarray(inputs["L1_attn_ff_gamma"].reshape(8, 128).T)
    o["pwT"] = np.ascontiguousarray(inputs["L1_attn_pw_w"].T)
    o["whg"] = np.ascontiguousarray(inputs["L1_attn_w_hg"])
    wout = inputs["L1_attn_w_out"]
    o["wout"] = np.ascontiguousarray(
        wout.reshape(4, 3, 128, 256).reshape(12, 128, 256))
    wg = np.pad(inputs["L1_attn_w_gate"], ((0, 0), (0, 86)))
    wv = np.pad(inputs["L1_attn_w_value"], ((0, 0), (0, 86)))
    o["wg_h"] = np.ascontiguousarray(wg.reshape(1024, 22, 128).transpose(1, 0, 2))
    o["wv_h"] = np.ascontiguousarray(wv.reshape(1024, 22, 128).transpose(1, 0, 2))
    wf = np.pad(inputs["L1_attn_w_ff_out"], ((0, 86), (0, 0)))
    o["wf_h"] = np.ascontiguousarray(wf.reshape(2816, 8, 128).transpose(1, 0, 2))
    return o


def _pack_summ(summ_all, c):
    smP = np.ones((128, 8, 12), dtype=np.float32)
    smE = np.zeros((128, 8, 12), dtype=np.float32)
    for jj in range(c):
        smP[:, jj] = summ_all[jj][:, 0, :]
        smE[:, jj] = summ_all[jj][:, 1, :]
    return smP, smE


def _run(kb, maps):
    for m in maps:
        m.update(kb.cin_used)
    return run_bass_kernel_spmd(kb.nc, maps, core_ids=list(range(NCORE)))


def _slices(full, w=T, lpad=0, rpad=0):
    fp = np.pad(full, ((0, 0), (lpad, rpad))) if (lpad or rpad) else full
    return [np.ascontiguousarray(fp[:, c * T:c * T + w]) for c in range(NCORE)]


def kernel_unfused(**inputs):
    x = np.asarray(inputs["L0_attn_x"])
    xT = np.ascontiguousarray(x[0].T.astype(np.float32))

    kbS = build_l0a_sig()
    maps = [{"xh": sl, "encq": inputs["L0_attn_enc_q"]}
            for sl in _slices(xT)]
    r = _run(kbS, maps)
    qsums_all = np.stack([r.results[c]["qsums"] for c in range(NCORE)])

    kbM = build_l0a_main()
    maps = [{"xh": sl, "qsums_all": qsums_all,
             "encq": inputs["L0_attn_enc_q"], "encv": inputs["L0_attn_enc_v"],
             "decoder": inputs["L0_attn_decoder"], "hebbian": inputs["L0_attn_hebbian"],
             "m_w1": inputs["L0_attn_m_w1"],
             "m_b1": np.ascontiguousarray(inputs["L0_attn_m_b1"].reshape(1, 512)),
             "m_w2": inputs["L0_attn_m_w2"],
             "m_b2": np.ascontiguousarray(inputs["L0_attn_m_b2"].reshape(1, 1024)),
             "base_T": np.ascontiguousarray(inputs["L0_attn_base_metric"].T)}
            for sl in _slices(xT, w=L0A_XW, lpad=543, rpad=1)]
    r = _run(kbM, maps)
    y1 = np.concatenate([r.results[c]["y1"][:, 512:] for c in range(NCORE)], axis=1)

    kbL = build_addln(False)
    r = _run(kbL, [{"a": sl} for sl in _slices(y1)])
    cur = np.concatenate([r.results[c]["yln"] for c in range(NCORE)], axis=1)

    kbFA = build_f0_attn()
    kbFF = build_f0_ffn()
    for li in range(6):
        wts = _prep_f0_layer(inputs, li)
        maps = []
        for c, sl in enumerate(_slices(cur, w=F0A_XW, lpad=385, rpad=3)):
            s = c * T
            m = {"xh": sl}
            for key in ["wq", "wk", "wv", "wo", "qs_col", "ks_col", "qsp_col", "ksp_col"]:
                m[key] = wts[key]
            cq, sq_, _, _ = _rotary_tables(np.arange(s - 128, s + T, dtype=np.float64))
            _, _, ck, sk_ = _rotary_tables(np.arange(s - 384, s + T, dtype=np.float64))
            m["csq"], m["snq"], m["csk"], m["snk"] = cq, sq_, ck, sk_
            m["maskp"] = _f0_masks(s)
            maps.append(m)
        r = _run(kbFA, maps)
        xa = np.concatenate([r.results[c]["ya"] for c in range(NCORE)], axis=1)

        maps = []
        for sl in _slices(xa, w=2052, lpad=1, rpad=3):
            m = {"xf": sl}
            for key in ["w1p", "w2po", "g_col", "b_col", "b1a_col", "b1g_col", "b2_col"]:
                m[key] = wts[key]
            maps.append(m)
        r = _run(kbFF, maps)
        cur = np.concatenate([r.results[c]["yf"] for c in range(NCORE)], axis=1)

    kbAL = build_addln(True)
    maps = [{"a": a, "b": b} for a, b in zip(_slices(y1), _slices(cur))]
    r = _run(kbAL, maps)
    y2 = np.concatenate([r.results[c]["ysum"] for c in range(NCORE)], axis=1)
    y2ln = np.concatenate([r.results[c]["yln"] for c in range(NCORE)], axis=1)

    wl1 = _prep_l1a(inputs)
    kbP = build_l1a_pre()
    maps = []
    for sl in _slices(y2ln, w=L1C_XW, lpad=3, rpad=1):
        m = {"xh": sl}
        for key in ["dwc", "dwb", "pwbias", "gcv", "ggr", "pwT", "whg"]:
            m[key] = wl1[key]
        maps.append(m)
    r = _run(kbP, maps)
    xas = [r.results[c]["xa"] for c in range(NCORE)]
    hls = [r.results[c]["hl"] for c in range(NCORE)]
    als = [r.results[c]["al"] for c in range(NCORE)]
    summ_all = [r.results[c]["summ"] for c in range(NCORE)]

    kbQ = build_l1a_post()
    maps = []
    for c in range(NCORE):
        smP, smE = _pack_summ(summ_all, c)
        m = {"xa": xas[c], "hl": hls[c], "al": als[c], "smP": smP, "smE": smE}
        for key in ["wout", "gff", "wg_h", "wv_h", "wf_h"]:
            m[key] = wl1[key]
        maps.append(m)
    r = _run(kbQ, maps)
    yb = np.concatenate([r.results[c]["yb"] for c in range(NCORE)], axis=1)

    maps = [{"a": a, "b": b} for a, b in zip(_slices(y2), _slices(yb))]
    r = _run(kbAL, maps)
    y3 = np.concatenate([r.results[c]["ysum"] for c in range(NCORE)], axis=1)
    y3ln = np.concatenate([r.results[c]["yln"] for c in range(NCORE)], axis=1)

    kb1 = build_moe_probs()
    maps = [{"xh": sl, "gw": inputs["L1_ffn_gate_w"]} for sl in _slices(y3ln)]
    r = _run(kb1, maps)
    prs = [r.results[c]["pr"] for c in range(NCORE)]
    prA = np.zeros((128, 8, 128), dtype=np.float32)
    for c in range(NCORE):
        for lt in range(16):
            prA[:, :, 16 * c + lt] = prs[c][:, lt, :]

    kb2 = build_moe_main()
    maps = []
    for c, (slh, sl3) in enumerate(zip(_slices(y3ln), _slices(y3))):
        scol = np.zeros((128, 16), dtype=np.float32)
        for lt in range(16):
            scol[16 * c + lt, lt] = 1.0
        maps.append({"prA": prA, "xh": slh, "y3": sl3, "scol": scol,
                     "w1": inputs["L1_ffn_w1"], "w2": inputs["L1_ffn_w2"]})
    r = _run(kb2, maps)
    out = np.concatenate([r.results[c]["yo"] for c in range(NCORE)], axis=1)
    return np.ascontiguousarray(out.T).reshape(1, S, D).astype(np.float32)


def kernel(**inputs):
    x = np.asarray(inputs["L0_attn_x"])
    xT = np.ascontiguousarray(x[0].T.astype(np.float32))
    xTp = np.pad(xT, ((0, 0), (543, 1)))
    f0w = [_prep_f0_layer(inputs, li) for li in range(6)]
    wl1 = _prep_l1a(inputs)
    kb = build_fused()

    shared = {"encq": inputs["L0_attn_enc_q"], "encv": inputs["L0_attn_enc_v"],
              "decoder": inputs["L0_attn_decoder"], "hebbian": inputs["L0_attn_hebbian"],
              "m_w1": inputs["L0_attn_m_w1"],
              "m_b1": np.ascontiguousarray(inputs["L0_attn_m_b1"].reshape(1, 512)),
              "m_w2": inputs["L0_attn_m_w2"],
              "m_b2": np.ascontiguousarray(inputs["L0_attn_m_b2"].reshape(1, 1024)),
              "base_T": np.ascontiguousarray(inputs["L0_attn_base_metric"].T),
              "gw": inputs["L1_ffn_gate_w"],
              "w1": inputs["L1_ffn_w1"], "w2": inputs["L1_ffn_w2"]}
    for k6, kk in [("wq6", "wq"), ("wk6", "wk"), ("wv6", "wv"), ("wo6", "wo"),
                   ("w1p6", "w1p"), ("w2po6", "w2po"), ("g6", "g_col"),
                   ("b6", "b_col"), ("b1a6", "b1a_col"), ("b1g6", "b1g_col"),
                   ("b2_6", "b2_col")]:
        shared[k6] = np.ascontiguousarray(np.stack([f0w[li][kk] for li in range(6)]))
    shared["qs6"] = np.ascontiguousarray(
        np.stack([f0w[li]["qs_col"] for li in range(6)]))
    shared["ks6"] = np.ascontiguousarray(
        np.stack([f0w[li]["ks_col"] for li in range(6)]))
    for key in ["dwc", "dwb", "pwbias", "gcv", "ggr", "pwT", "whg", "wout",
                "gff", "wg_h", "wv_h", "wf_h"]:
        shared[key] = wl1[key]

    maps = []
    for c in range(NCORE):
        s = c * T
        m = dict(shared)
        m["xh"] = np.ascontiguousarray(xTp[:, s:s + L0A_XW])
        cq, sq_, _, _ = _rotary_tables(np.arange(s - 128, s + T, dtype=np.float64))
        _, _, ck, sk_ = _rotary_tables(np.arange(s - 384, s + T, dtype=np.float64))
        m["csq"], m["snq"], m["csk"], m["snk"] = cq, sq_, ck, sk_
        m["maskp"] = _f0_masks(s)
        scol = np.zeros((128, 16), dtype=np.float32)
        for lt in range(16):
            scol[16 * c + lt, lt] = 1.0
        m["scol"] = scol
        hidx = np.zeros((128, CT), dtype=np.int32)
        if c > 0:
            for k in range(CT):
                hidx[:, k] = (c - 1) * 1024 + 128 * k + np.arange(128)
        m["hidx"] = hidx
        m["hmask"] = np.full((128, 1), 1.0 if c > 0 else 0.0, dtype=np.float32)
        mm = np.zeros((128, 8), dtype=np.float32)
        mm[:, :c] = 1.0
        m["mP"] = mm
        m["mE"] = mm.copy()
        m.update(kb.cin_used)
        maps.append(m)
    r = run_bass_kernel_spmd(kb.nc, maps, core_ids=list(range(NCORE)))
    out = np.concatenate([r.results[c]["yo"] for c in range(NCORE)], axis=1)
    return np.ascontiguousarray(out.T).reshape(1, S, D).astype(np.float32)


# version 06
# speedup vs baseline: 1.0480x; 1.0480x over previous
import math
import numpy as np
import ml_dtypes
from contextlib import ExitStack

import concourse.bass as bass
import concourse.mybir as mybir
from concourse.bass_utils import run_bass_kernel_spmd
from bass_rust import IndirectOffsetOnAxis

F32 = mybir.dt.float32
BF16 = mybir.dt.bfloat16
I32 = mybir.dt.int32
AF = mybir.ActivationFunctionType
ALU = mybir.AluOpType
AX = mybir.AxisListType

NCORE = 8
S = 16384
T = 2048
D = 1024
CT = 8
BF = ml_dtypes.bfloat16

L0A_W = 512 + T
L0A_NT = L0A_W // 128
L0A_XW = 2592

DEBUG_L0A = False


def host_consts():
    c = {}
    c["identb"] = np.eye(128, dtype=BF)
    c["ones128b"] = np.ones((128, 128), dtype=BF)
    c["onescol_b"] = np.ones((128, 1), dtype=BF)
    c["neghalf_f"] = np.full((128, 1), -0.5, dtype=np.float32)
    c["onesrow_f"] = np.ones((1, 128), dtype=np.float32)
    c["ones1f"] = np.ones((1, 1), dtype=np.float32)
    misc = np.zeros((128, 8), dtype=np.float32)
    misc[:, 0] = 1e-8
    misc[:, 1] = 1e-5
    misc[:, 2] = 1e-6
    misc[:, 3] = 1e-12
    misc[:, 4] = -2.0
    misc[:, 5] = 1.0
    misc[:, 6] = 1e-30
    misc[:, 7] = 0.5
    c["miscc"] = misc
    p = np.arange(128)[:, None]
    q = np.arange(128)[None, :]
    comb = np.zeros((128, 256), dtype=np.float32)
    comb[:, :128] = (q >= p - 31) & (q <= p)
    pb = p[:32]
    comb[:32, 128:] = ((q >= pb + 97) & (q <= pb + 127)) & (pb < 31)
    c["bandAB"] = comb.astype(BF)
    ind2 = np.zeros((128, 2), dtype=BF)
    ind2[0:64, 0] = 1
    ind2[64:128, 1] = 1
    c["ind2b"] = ind2
    sel8 = np.zeros((8, 4, 128), dtype=BF)
    for pt in range(4):
        sel8[2 * pt, pt, 0:64] = 1
        sel8[2 * pt + 1, pt, 64:128] = 1
    c["sel8b"] = sel8
    c["identf"] = np.eye(128, dtype=np.float32)
    c["ltrilf"] = (np.arange(128)[:, None] < np.arange(128)[None, :]).astype(np.float32)
    c["onesf8"] = np.ones((8, 128), dtype=np.float32)
    c["ident8f"] = np.eye(8, dtype=np.float32)
    c["onescolf"] = np.ones((128, 1), dtype=np.float32)
    oh8 = np.zeros((8, 8, 128), dtype=np.float32)
    for e in range(8):
        oh8[e, e, :] = 1.0
    c["oh8f"] = oh8
    return c


_CONSTS = host_consts()

_CHAINED = {
    "tensor_copy", "tensor_tensor", "tensor_add", "tensor_sub", "tensor_mul",
    "tensor_max", "tensor_scalar", "tensor_single_scalar", "tensor_scalar_add",
    "tensor_scalar_sub", "tensor_scalar_mul", "tensor_scalar_max",
    "tensor_scalar_min", "tensor_relu", "tensor_reduce", "reduce_max",
    "reduce_sum", "scalar_tensor_tensor", "tensor_tensor_scan", "memset",
    "memzero", "reciprocal", "select", "copy_predicated", "activation",
    "square", "sqrt", "mul", "add", "copy", "sign", "iota", "affine_select",
    "bn_stats", "bn_aggr", "stream_shuffle", "pool", "pool_avg",
    "pool_max", "max", "max_index",
}


class CEng:

    def __init__(self, kb, eng, sem_name):
        self._kb = kb
        self._eng = eng
        self._sem = sem_name

    def __getattr__(self, name):
        fn = getattr(self._eng, name)
        if name not in _CHAINED:
            return fn
        kb = self._kb
        sem = self._sem
        eng = self._eng

        def wrap(*a, **k):
            cnt = kb.n(sem)
            if cnt > 0:
                eng.wait_ge(kb.h(sem), cnt)
            inst = fn(*a, **k)
            kb.inc(inst, sem, 1)
            return inst
        return wrap


class KB:
    def __init__(self):
        self.nc = bass.Bass("TRN2", target_bir_lowering=False)
        self.ctx = ExitStack()
        self._sems = {}
        self.block = None
        self.stage = None
        self.cin_used = {}
        self._cins = {}
        self._names = {}

    def dram_in(self, name, shape, dtype):
        return self.nc.dram_tensor(name, list(shape), dtype, kind="ExternalInput").ap()

    def dram_out(self, name, shape, dtype):
        return self.nc.dram_tensor(name, list(shape), dtype, kind="ExternalOutput").ap()

    def dram_int(self, name, shape, dtype, shared=False):
        kw = {"addr_space": "Shared"} if shared else {}
        return self.nc.dram_tensor(name, list(shape), dtype, kind="Internal", **kw).ap()

    def push(self):
        self.stage = ExitStack()
        return self.stage

    def pop(self):
        self.stage.close()
        self.stage = None

    def sems_once(self, *names):
        for n in names:
            if n not in self._sems:
                self.sem(n)

    def cin(self, name):
        if name in self._cins:
            return self._cins[name]
        a = _CONSTS[name]
        self.cin_used[name] = a
        dt = BF16 if a.dtype == BF else (F32 if a.dtype == np.float32 else I32)
        ap = self.dram_in(name, a.shape, dt)
        self._cins[name] = ap
        return ap

    def _uniq(self, name):
        i = self._names.get(name, 0)
        self._names[name] = i + 1
        return name if i == 0 else f"{name}_{i}"

    def sbuf(self, name, shape, dtype):
        ctx = self.stage if self.stage is not None else self.ctx
        return ctx.enter_context(self.nc.sbuf_tensor(self._uniq(name), list(shape), dtype))

    def sbuf_cm(self, name, shape, dtype):
        return self.nc.sbuf_tensor(self._uniq(name), list(shape), dtype)

    def psum(self, name, shape, dtype=F32):
        ctx = self.stage if self.stage is not None else self.ctx
        return ctx.enter_context(self.nc.psum_tensor(self._uniq(name), list(shape), dtype))

    def sem(self, name):
        h = self.ctx.enter_context(self.nc.semaphore(name))
        self._sems[name] = [h, 0]

    def sems(self, *names):
        for n in names:
            self.sem(n)

    def inc(self, inst, name, k=1):
        s = self._sems[name]
        inst.then_inc(s[0], k)
        s[1] += k
        return inst

    def n(self, name):
        return self._sems[name][1]

    def h(self, name):
        return self._sems[name][0]

    def snap(self):
        return {k: v[1] for k, v in self._sems.items()}

    def start(self):
        self.block = self.ctx.enter_context(self.nc.Block())
        self.sems("V", "SC", "G")
        return self.block

    def phase(self, **engine_fns):
        snapshot = self.snap()
        for engname, fn in engine_fns.items():
            @getattr(self.block, engname)
            def _(eng, fn=fn, engname=engname):
                for k, c in snapshot.items():
                    if c > 0:
                        eng.wait_ge(self._sems[k][0], c)
                if engname == "vector":
                    fn(CEng(self, eng, "V"))
                elif engname == "scalar":
                    fn(CEng(self, eng, "SC"))
                elif engname == "gpsimd":
                    fn(CEng(self, eng, "G"))
                else:
                    fn(eng)

    def done(self):
        self.ctx.close()
        return self.nc


def alloc_common(kb):
    kb.t_ones = kb.sbuf("t_ones", [128, 128], BF16)
    kb.t_misc = kb.sbuf("t_misc", [128, 8], F32)
    kb.d_ones = kb.cin("ones128b")
    kb.d_misc = kb.cin("miscc")


def load_common(kb, g):
    kb.inc(g.dma_start(out=kb.t_ones[:], in_=kb.d_ones), "ld", 16)
    kb.inc(g.dma_start(out=kb.t_misc[:], in_=kb.d_misc), "ld", 16)


def alloc_ln(kb, scr4=None):
    kb.t_lnxb = kb.sbuf("lnxb", [128, CT, 512], BF16)
    kb.t_lnrow = scr4 if scr4 is not None else kb.sbuf("lnrow", [128, 4, 512], F32)


def emit_ln(kb, xt, xnb, x0, n0, w, pA, pB):
    done = 0
    while done < w:
        cw = min(512, w - done)
        xs = slice(x0 + done, x0 + done + cw)
        ns = slice(n0 + done, n0 + done + cw)
        done += cw

        def p_cast(sc, xs=xs, cw=cw):
            for k in range(CT):
                sc.activation(out=kb.t_lnxb[:, k, 0:cw], in_=xt[:, k, xs], func=AF.Copy)
        kb.phase(scalar=p_cast)

        def p_mmA(t, cw=cw):
            for k in range(CT):
                i = t.matmul(out=pA[:, 0:cw], lhsT=kb.t_ones[:], rhs=kb.t_lnxb[:, k, 0:cw],
                             start=(k == 0), stop=(k == CT - 1))
            kb.inc(i, "m")
        kb.phase(tensor=p_mmA)

        def p_sq(v, cw=cw):
            for k in range(CT):
                v.tensor_mul(out=kb.t_lnxb[:, k, 0:cw], in0=kb.t_lnxb[:, k, 0:cw],
                             in1=kb.t_lnxb[:, k, 0:cw])
        kb.phase(vector=p_sq)

        def p_mmB(t, cw=cw):
            for k in range(CT):
                i = t.matmul(out=pB[:, 0:cw], lhsT=kb.t_ones[:], rhs=kb.t_lnxb[:, k, 0:cw],
                             start=(k == 0), stop=(k == CT - 1))
            kb.inc(i, "m")
        kb.phase(tensor=p_mmB)

        def p_v1(v, cw=cw):
            v.tensor_scalar_mul(kb.t_lnrow[:, 0, 0:cw], pA[:, 0:cw], 1.0 / D)
            v.tensor_scalar_mul(kb.t_lnrow[:, 1, 0:cw], pB[:, 0:cw], 1.0 / D)
        kb.phase(vector=p_v1)

        def p_s1(sc, cw=cw):
            sc.activation(out=kb.t_lnrow[:, 2, 0:cw], in_=kb.t_lnrow[:, 0, 0:cw],
                          func=AF.Square)
        kb.phase(scalar=p_s1)

        def p_v2(v, cw=cw):
            v.tensor_sub(out=kb.t_lnrow[:, 1, 0:cw], in0=kb.t_lnrow[:, 1, 0:cw],
                         in1=kb.t_lnrow[:, 2, 0:cw])
        kb.phase(vector=p_v2)

        def p_s2(sc, cw=cw):
            sc.activation(out=kb.t_lnrow[:, 1, 0:cw], in_=kb.t_lnrow[:, 1, 0:cw],
                          func=AF.Sqrt, bias=kb.t_misc[:, 1:2])
        kb.phase(scalar=p_s2)

        def p_v3(v, xs=xs, ns=ns, cw=cw):
            v.reciprocal(out=kb.t_lnrow[:, 3, 0:cw], in_=kb.t_lnrow[:, 1, 0:cw])
            for k in range(CT):
                v.tensor_sub(out=kb.t_lnrow[:, 2, 0:cw], in0=xt[:, k, xs],
                             in1=kb.t_lnrow[:, 0, 0:cw])
                v.tensor_mul(out=xnb[:, k, ns], in0=kb.t_lnrow[:, 2, 0:cw],
                             in1=kb.t_lnrow[:, 3, 0:cw])
        kb.phase(vector=p_v3)


def build_l0a_sig():
    kb = KB()
    io = {"xh": kb.dram_in("xh", [D, T], F32),
          "encq": kb.dram_in("encq", [8, D, 128], F32),
          "qsums": kb.dram_out("qsums", [128, 16], F32), "x0": 0}
    alloc_common(kb)
    kb.sems("ld", "m", "st")
    kb.start()
    kb.push()
    emit_l0a_sig(kb, io)
    kb.pop()
    return kb


def emit_l0a_sig(kb, io):
    xh, encq, qsums, x0 = io["xh"], io["encq"], io["qsums"], io["x0"]
    xt = kb.sbuf("xt", [128, CT, T], F32)
    xnb = kb.sbuf("xnb", [128, CT, T], BF16)
    eq = kb.sbuf("eq", [128, 8, CT, 128], BF16)
    alloc_ln(kb)
    qtile = kb.sbuf("qtile", [128, 2, 512], F32)
    acc = kb.sbuf("acc", [128, 16], F32)
    red = kb.sbuf("red", [128, 2], F32)
    pQ = [kb.psum(f"pQ{i}", [128, 512]) for i in range(2)]
    pA = kb.psum("pA", [128, 512])
    pB = kb.psum("pB", [128, 512])

    def p_load_sy(sy):
        for k in range(CT):
            kb.inc(sy.dma_start(out=xt[:, k],
                                in_=xh[128 * k:128 * (k + 1), x0:x0 + T]), "ld", 16)

    def p_load_g(g):
        load_common(kb, g)
        for h in range(8):
            for k in range(CT):
                kb.inc(g.dma_start(out=eq[:, h, k],
                                   in_=encq[h, 128 * k:128 * (k + 1), :]), "ld", 16)
    kb.phase(sync=p_load_sy, gpsimd=p_load_g)

    emit_ln(kb, xt, xnb, 0, 0, T, pA, pB)

    NCH = T // 512
    NIT = NCH * 8
    b_m, b_sc, b_v = kb.n("m"), kb.n("SC"), kb.n("V")

    def p_q_mm(t):
        for idx in range(NIT):
            ch, h = divmod(idx, 8)
            if idx >= 2:
                t.wait_ge(kb.h("SC"), b_sc + idx - 1)
            for k in range(CT):
                i = t.matmul(out=pQ[idx % 2][:], lhsT=eq[:, h, k],
                             rhs=xnb[:, k, 512 * ch:512 * (ch + 1)],
                             start=(k == 0), stop=(k == CT - 1))
            kb.inc(i, "m")

    def p_q_relu(sc):
        for idx in range(NIT):
            sc._eng.wait_ge(kb.h("m"), b_m + idx + 1)
            if idx >= 2:
                sc._eng.wait_ge(kb.h("V"), b_v + 1 + 4 * (idx - 1))
            sc.activation(out=qtile[:, idx % 2], in_=pQ[idx % 2][:], func=AF.Relu)

    def p_q_sum(v):
        v.memset(acc[:], 0.0)
        for idx in range(NIT):
            h = idx % 8
            v._eng.wait_ge(kb.h("SC"), b_sc + idx + 1)
            v.reduce_sum(out=red[:, 0:1], in_=qtile[:, idx % 2], axis=AX.X)
            v.reduce_sum(out=red[:, 1:2], in_=qtile[:, idx % 2, ::8], axis=AX.X)
            v.tensor_add(out=acc[:, h:h + 1], in0=acc[:, h:h + 1], in1=red[:, 0:1])
            v.tensor_add(out=acc[:, 8 + h:9 + h], in0=acc[:, 8 + h:9 + h],
                         in1=red[:, 1:2])
    kb.phase(tensor=p_q_mm, scalar=p_q_relu, vector=p_q_sum)

    def p_store(g):
        kb.inc(g.dma_start(out=qsums, in_=acc[:]), "st", 16)
        g.wait_ge(kb.h("st"), kb.n("st"))
    kb.phase(gpsimd=p_store)


def build_l0a_main(nsplit=10):
    kb = KB()
    io = {"xh": kb.dram_in("xh", [D, L0A_XW], F32),
          "qsums_all": kb.dram_in("qsums_all", [NCORE, 128, 16], F32),
          "encq": kb.dram_in("encq", [8, D, 128], F32),
          "encv": kb.dram_in("encv", [8, D, 128], F32),
          "decoder": kb.dram_in("decoder", [D, D], F32),
          "hebbian": kb.dram_in("hebbian", [8, 128, 128], F32),
          "m_w1": kb.dram_in("m_w1", [D, 512], F32),
          "m_b1": kb.dram_in("m_b1", [1, 512], F32),
          "m_w2": kb.dram_in("m_w2", [512, D], F32),
          "m_b2": kb.dram_in("m_b2", [1, D], F32),
          "base_T": kb.dram_in("base_T", [128, 8], F32),
          "y1": kb.dram_out("y1", [D, L0A_W], F32)}
    alloc_common(kb)
    kb.sems("ld", "m", "st", "am", "tm", "qm", "im", "ic", "dn")
    kb.start()
    kb.push()
    emit_l0a_main(kb, io, nsplit=nsplit)
    kb.pop()
    return kb


def emit_l0a_main(kb, io, nsplit=10):
    xh, qsums_all = io["xh"], io["qsums_all"]
    encq, encv, dec, heb = io["encq"], io["encv"], io["decoder"], io["hebbian"]
    mw1, mb1, mw2, mb2 = io["m_w1"], io["m_b1"], io["m_w2"], io["m_b2"]
    baseT, y1 = io["base_T"], io["y1"]
    d_identb = kb.cin("identb")
    d_onescol_b = kb.cin("onescol_b")
    d_neghalf = kb.cin("neghalf_f")
    d_onesrow = kb.cin("onesrow_f")
    d_ones1f = kb.cin("ones1f")
    d_band = kb.cin("bandAB")
    if DEBUG_L0A:
        dbg_sig = kb.dram_out("dbg_sig", [128, 8], F32)
        dbg_mdiag = kb.dram_out("dbg_mdiag", [128, 24], F32)
        dbg_xnb = kb.dram_out("dbg_xnb", [128, CT, 256], F32)
        dbg_su = kb.dram_out("dbg_su", [128, 8, 256], F32)
        dbg_arow = kb.dram_out("dbg_arow", [1, 8, 416], F32)
        dbg_infl = kb.dram_out("dbg_infl", [128, 2, 256], F32)
        dbg_sinvq = kb.dram_out("dbg_sinvq", [128, 2, 8], F32)
        dbg_vtm = kb.dram_out("dbg_vtm", [128, 8, 128], F32)
        dbg_qtm = kb.dram_out("dbg_qtm", [128, 8, 128], F32)

    xt = kb.sbuf("xt", [128, CT, L0A_XW], F32)
    mlp_ctx = ExitStack()

    def mlp_sbuf(name, shape, dtype):
        return mlp_ctx.enter_context(kb.sbuf_cm(name, shape, dtype))

    t_idb = kb.sbuf("t_idb", [128, 128], BF16)
    t_oneb = kb.sbuf("t_oneb", [128, 1], BF16)
    t_nh = kb.sbuf("t_nh", [128, 1], F32)
    t_onr = kb.sbuf("t_onr", [1, 128], F32)
    t_one1f = kb.sbuf("t_one1f", [1, 1], F32)
    t_band = kb.sbuf("t_band", [128, 256], BF16)
    eq = kb.sbuf("eq", [128, 8, CT, 128], BF16)
    ev = kb.sbuf("ev", [128, 8, CT, 128], BF16)
    sdec = kb.sbuf("sdec", [128, CT, D], BF16)
    shebs = kb.sbuf("shebs", [128, 8, 128], F32)
    sbase = kb.sbuf("sbase", [128, 8], F32)
    modcol = kb.sbuf("modcol", [128, 8], F32)
    uscale = kb.sbuf("uscale", [128, 8], F32)
    recipm = kb.sbuf("recipm", [128, 8], F32)
    scr4 = kb.sbuf("scr4", [128, 4, 512], F32)
    alloc_ln(kb, scr4=scr4)
    srow1 = kb.sbuf("srow1", [1, 512], F32)
    srow2 = kb.sbuf("srow2", [1, D], F32)
    sig = kb.sbuf("sig", [128, 8], F32)
    sigb = kb.sbuf("sigb", [128, 8], BF16)
    h1row = kb.sbuf("h1row", [1, 512], F32)
    h1colb = kb.sbuf("h1colb", [128, 4], BF16)
    modrow = kb.sbuf("modrow", [1, D], F32)
    sheb = shebs
    smw1 = mlp_sbuf("smw1", [128, CT, 512], BF16)
    smw2 = mlp_sbuf("smw2", [128, 4, D], BF16)
    ssum = mlp_sbuf("ssum", [128, NCORE, 16], F32)

    pG = [kb.psum(f"pG{i}", [128, 256]) for i in range(2)]
    pCH = [kb.psum(f"pCH{i}", [128, 512]) for i in range(2)]
    pT = [kb.psum(f"pT{i}", [128, 128], BF16) for i in range(2)]
    pDec = [kb.psum(f"pDec{i}", [128, 512]) for i in range(2)]

    def p_load_sy(sy):
        for k in range(CT):
            kb.inc(sy.dma_start(out=xt[:, k], in_=xh[128 * k:128 * (k + 1), :]), "ld", 16)
        kb.inc(sy.dma_start(out=t_nh[:], in_=d_neghalf), "ld", 16)
        kb.inc(sy.dma_start(out=t_onr[:], in_=d_onesrow), "ld", 16)
        kb.inc(sy.dma_start(out=t_one1f[:], in_=d_ones1f), "ld", 16)
        for h in range(8):
            kb.inc(sy.dma_start(out=sheb[:, h], in_=heb[h]), "ld", 16)
        for r in range(NCORE):
            kb.inc(sy.dma_start(out=ssum[:, r], in_=qsums_all[r]), "ld", 16)
        kb.inc(sy.dma_start(out=srow1[:], in_=mb1), "ld", 16)
        kb.inc(sy.dma_start(out=srow2[:], in_=mb2), "ld", 16)
        kb.inc(sy.dma_start(out=sbase[:], in_=baseT), "ld", 16)

    def p_load_g(g):
        load_common(kb, g)
        kb.inc(g.dma_start(out=t_idb[:], in_=d_identb), "ld", 16)
        kb.inc(g.dma_start(out=t_oneb[:], in_=d_onescol_b), "ld", 16)
        kb.inc(g.dma_start(out=t_band[:], in_=d_band), "ld", 16)
        for h in range(8):
            for k in range(CT):
                kb.inc(g.dma_start(out=eq[:, h, k], in_=encq[h, 128 * k:128 * (k + 1), :]), "ld", 16)
                kb.inc(g.dma_start(out=ev[:, h, k], in_=encv[h, 128 * k:128 * (k + 1), :]), "ld", 16)
        for k in range(CT):
            kb.inc(g.dma_start(out=sdec[:, k], in_=dec[128 * k:128 * (k + 1), :]), "ld", 16)
            kb.inc(g.dma_start(out=smw1[:, k], in_=mw1[128 * k:128 * (k + 1), :]), "ld", 16)
        for j in range(4):
            kb.inc(g.dma_start(out=smw2[:, j], in_=mw2[128 * j:128 * (j + 1), :]), "ld", 16)
    kb.phase(sync=p_load_sy, gpsimd=p_load_g)

    def p_sig(v):
        v.tensor_add(out=sig[:], in0=ssum[:, 0, 0:8], in1=ssum[:, 1, 0:8])
        for r in range(2, NCORE):
            v.tensor_add(out=sig[:], in0=sig[:], in1=ssum[:, r, 0:8])
        v.tensor_add(out=modcol[:], in0=ssum[:, 0, 8:16], in1=ssum[:, 1, 8:16])
        for r in range(2, NCORE):
            v.tensor_add(out=modcol[:], in0=modcol[:], in1=ssum[:, r, 8:16])
        v.tensor_scalar_mul(sig[:], sig[:], 1.0 / S)
        v.tensor_scalar_mul(modcol[:], modcol[:], 0.5 / (S / 8.0))
        v.tensor_add(out=sig[:], in0=sig[:], in1=modcol[:])
        v.tensor_copy(out=sigb[:], in_=sig[:])
    kb.phase(vector=p_sig)

    def p_mlp1(t):
        for h in range(8):
            i = t.matmul(out=pDec[0][0:1, :], lhsT=sigb[:, h:h + 1], rhs=smw1[:, h],
                         start=(h == 0), stop=(h == 7))
        kb.inc(i, "m")
    kb.phase(tensor=p_mlp1)

    def p_mlp2(v):
        v.tensor_add(out=h1row[:], in0=pDec[0][0:1, :], in1=srow1[:])
    kb.phase(vector=p_mlp2)

    def p_mlp3(sc):
        sc.activation(out=h1row[:], in_=h1row[:], func=AF.Gelu)
    kb.phase(scalar=p_mlp3)

    def p_mlp4(t):
        for j in range(4):
            kb.inc(t.matmul(out=pCH[0][:, j:j + 1], lhsT=h1row[0:1, 128 * j:128 * (j + 1)],
                            rhs=t_one1f[:], start=True, stop=True), "m")
    kb.phase(tensor=p_mlp4)

    def p_mlp5(v):
        v.tensor_copy(out=h1colb[:], in_=pCH[0][:, 0:4])
    kb.phase(vector=p_mlp5)

    def p_mlp6(t):
        for hf in range(2):
            for j in range(4):
                i = t.matmul(out=pDec[hf][0:1, :], lhsT=h1colb[:, j:j + 1],
                             rhs=smw2[:, j, 512 * hf:512 * (hf + 1)],
                             start=(j == 0), stop=(j == 3))
            kb.inc(i, "m")
    kb.phase(tensor=p_mlp6)

    def p_mlp7(v):
        v.tensor_add(out=modrow[0:1, 0:512], in0=pDec[0][0:1, :], in1=srow2[0:1, 0:512])
        v.tensor_add(out=modrow[0:1, 512:1024], in0=pDec[1][0:1, :],
                     in1=srow2[0:1, 512:1024])
    kb.phase(vector=p_mlp7)

    def p_mlp8(t):
        for j in range(8):
            kb.inc(t.matmul(out=pCH[0][:, j:j + 1], lhsT=modrow[0:1, 128 * j:128 * (j + 1)],
                            rhs=t_one1f[:], start=True, stop=True), "m")
    kb.phase(tensor=p_mlp8)

    def p_md1(v):
        v.tensor_copy(out=modcol[:], in_=pCH[0][:, 0:8])
        v.tensor_scalar_mul(modcol[:], modcol[:], 0.1)
        v.tensor_add(out=modcol[:], in0=modcol[:], in1=sbase[:])
    kb.phase(vector=p_md1)

    def p_md2(sc):
        sc.activation(out=uscale[:], in_=modcol[:], func=AF.Abs)
        sc.activation(out=uscale[:], in_=uscale[:], func=AF.Exp, scale=-1.0)
        sc.activation(out=uscale[:], in_=uscale[:], func=AF.Ln, bias=kb.t_misc[:, 5:6])
    kb.phase(scalar=p_md2)

    def p_md3(v):
        v.tensor_scalar_max(recipm[:], modcol[:], 0.0)
        v.tensor_add(out=recipm[:], in0=recipm[:], in1=uscale[:])
        v.tensor_scalar_add(recipm[:], recipm[:], 1e-6)
    kb.phase(vector=p_md3)

    def p_md4(sc):
        sc.activation(out=uscale[:], in_=recipm[:], func=AF.Sqrt)
    kb.phase(scalar=p_md4)

    def p_md5(v):
        v.reciprocal(out=recipm[:], in_=recipm[:])
        v.reciprocal(out=modcol[:], in_=uscale[:])
    kb.phase(vector=p_md5)

    def p_hebs(v):
        for h in range(8):
            v.tensor_scalar_mul(shebs[:, h], sheb[:, h], modcol[:, h:h + 1])
    kb.phase(vector=p_hebs)

    mlp_ctx.close()
    NT_H = L0A_NT // nsplit
    WU = NT_H * 128 + 160
    su = kb.sbuf("su", [128, 8, WU], F32)
    arow = kb.sbuf("arow", [1, 8, WU], F32)
    xnb = kb.sbuf("xnb", [128, CT, WU], BF16)
    svtm = kb.sbuf("svtm", [128, NT_H + 1, 8, 128], BF16)
    sqtm = kb.sbuf("sqtm", [128, NT_H, 8, 128], BF16)
    sinvq = kb.sbuf("sinvq", [128, NT_H, 8], F32)
    u2s = scr4
    sinvd = kb.sbuf("sinvd", [128, 2], F32)
    sdq = scr4[:, 2:4, 0:256]
    scf = scr4[:, 0:2, 0:128]
    bfp = kb.t_lnxb
    sinfl = bfp[:, 0, :].rearrange("p (a b) -> p a b", a=2)
    schb = bfp[:, 1, 0:256].rearrange("p (a b) -> p a b", a=2)
    soh = bfp[:, 2, 0:256].rearrange("p (a b) -> p a b", a=2)
    sflat = bfp[:, 3:5, :].rearrange("p a b -> p (a b)").rearrange(
        "p (o h e) -> p o h e", o=1, h=8)

    for half in range(nsplit):
        m0 = half * NT_H
        u0 = m0 * 128

        wln = min(WU, L0A_XW - u0)
        emit_ln(kb, xt, xnb, u0, 0, wln, pDec[0], pDec[1])
        if wln < WU:
            def p_zf(v, wln=wln):
                for k in range(CT):
                    v.memset(xnb[:, k, wln:WU], 0.0)
            kb.phase(vector=p_zf)

        chunks = []
        off = 0
        while off < WU:
            chunks.append((off, min(512, WU - off)))
            off += 512
        b_am, b_sc, b_v = kb.n("am"), kb.n("SC"), kb.n("V")

        def p_u_t(t, chunks=chunks, b_am=b_am, b_sc=b_sc, b_v=b_v):
            for ci, (c0, w) in enumerate(chunks):
                for h in range(8):
                    idx = ci * 8 + h
                    if idx >= 2:
                        t.wait_ge(kb.h("SC"), b_sc + 2 * (idx - 2) + 1)
                    for k in range(CT):
                        i = t.matmul(out=pDec[idx % 2][:, 0:w], lhsT=eq[:, h, k],
                                     rhs=xnb[:, k, c0:c0 + w],
                                     start=(k == 0), stop=(k == CT - 1))
                    kb.inc(i, "am")
                    t.wait_ge(kb.h("SC"), b_sc + 2 * idx + 2)
                    if idx >= 2:
                        t.wait_ge(kb.h("V"), b_v + idx - 1)
                    kb.inc(t.matmul(out=pCH[idx % 2][0:1, 0:w], lhsT=t_nh[:],
                                    rhs=u2s[:, idx % 2, 0:w], start=True, stop=True), "am")

        def p_u_s(sc, chunks=chunks, b_am=b_am, b_v=b_v):
            for ci, (c0, w) in enumerate(chunks):
                for h in range(8):
                    idx = ci * 8 + h
                    sc._eng.wait_ge(kb.h("am"), b_am + 2 * idx + 1)
                    sc.activation(out=su[:, h, c0:c0 + w], in_=pDec[idx % 2][:, 0:w],
                                  func=AF.Relu, scale=uscale[:, h:h + 1])
                    if idx >= 2:
                        sc._eng.wait_ge(kb.h("V"), b_v + idx - 1)
                    sc.activation(out=u2s[:, idx % 2, 0:w], in_=su[:, h, c0:c0 + w],
                                  func=AF.Square)

        def p_u_v(v, chunks=chunks, b_am=b_am):
            for ci, (c0, w) in enumerate(chunks):
                for h in range(8):
                    idx = ci * 8 + h
                    v._eng.wait_ge(kb.h("am"), b_am + 2 * idx + 2)
                    v.tensor_copy(out=arow[0:1, h, c0:c0 + w],
                                  in_=pCH[idx % 2][0:1, 0:w])
        kb.phase(tensor=p_u_t, scalar=p_u_s, vector=p_u_v)

        b_tm, b_sc = kb.n("tm"), kb.n("SC")
        vt_jobs = [(kv, 128 * kv) for kv in range(NT_H + 1) if 128 * kv < WU - 30]

        def p_vt_t(t, vt_jobs=vt_jobs, b_sc=b_sc):
            for ji, (kv, uw) in enumerate(vt_jobs):
                for h in range(8):
                    idx = ji * 8 + h
                    if idx >= 2:
                        t.wait_ge(kb.h("SC"), b_sc + idx - 1)
                    w = min(128, WU - uw)
                    for k in range(CT):
                        i = t.matmul(out=pDec[idx % 2][:, 0:128], lhsT=xnb[:, k, uw:uw + w],
                                     rhs=ev[:, h, k], start=(k == 0), stop=(k == CT - 1))
                    kb.inc(i, "tm")

        def p_vt_s(sc, vt_jobs=vt_jobs, b_tm=b_tm):
            for ji, (kv, uw) in enumerate(vt_jobs):
                for h in range(8):
                    idx = ji * 8 + h
                    sc._eng.wait_ge(kb.h("tm"), b_tm + idx + 1)
                    w = min(128, WU - uw)
                    sc.activation(out=svtm[0:w, kv, h], in_=pDec[idx % 2][0:w, 0:128],
                                  func=AF.Relu)
        kb.phase(tensor=p_vt_t, scalar=p_vt_s)

        b_qm, b_sc, b_v = kb.n("qm"), kb.n("SC"), kb.n("V")

        def p_qt_t(t, b_qm=b_qm, b_sc=b_sc, b_v=b_v):
            for mi in range(NT_H):
                iw = 31 + 128 * mi
                for h in range(8):
                    idx = mi * 8 + h
                    if idx >= 2:
                        t.wait_ge(kb.h("SC"), b_sc + 2 * (idx - 2) + 1)
                    for k in range(CT):
                        i = t.matmul(out=pDec[idx % 2][:, 0:128], lhsT=xnb[:, k, iw:iw + 128],
                                     rhs=eq[:, h, k], start=(k == 0), stop=(k == CT - 1))
                    kb.inc(i, "qm")
                    t.wait_ge(kb.h("SC"), b_sc + 2 * idx + 2)
                    if idx >= 2:
                        t.wait_ge(kb.h("V"), b_v + idx - 1)
                    kb.inc(t.matmul(out=pG[idx % 2][0:128, 128:129],
                                    lhsT=u2s[:, idx % 2, 0:128],
                                    rhs=recipm[:, h:h + 1], start=True, stop=True), "qm")

        def p_qt_s(sc, b_qm=b_qm, b_v=b_v):
            for mi in range(NT_H):
                iw = 31 + 128 * mi
                for h in range(8):
                    idx = mi * 8 + h
                    sc._eng.wait_ge(kb.h("qm"), b_qm + 2 * idx + 1)
                    sc.activation(out=sqtm[:, mi, h], in_=pDec[idx % 2][:, 0:128],
                                  func=AF.Relu)
                    if idx >= 2:
                        sc._eng.wait_ge(kb.h("V"), b_v + idx - 1)
                    sc.activation(out=u2s[:, idx % 2, 0:128], in_=su[:, h, iw:iw + 128],
                                  func=AF.Square)

        def p_qt_v(v, b_qm=b_qm):
            for mi in range(NT_H):
                for h in range(8):
                    idx = mi * 8 + h
                    v._eng.wait_ge(kb.h("qm"), b_qm + 2 * idx + 2)
                    v.tensor_copy(out=sinvq[:, mi, h:h + 1],
                                  in_=pG[idx % 2][0:128, 128:129])
        kb.phase(tensor=p_qt_t, scalar=p_qt_s, vector=p_qt_v)

        def p_qf_s(sc):
            for mi in range(NT_H):
                sc.activation(out=sinvq[:, mi], in_=sinvq[:, mi], func=AF.Sqrt)
        kb.phase(scalar=p_qf_s)

        def p_qf_v(v):
            for mi in range(NT_H):
                v.tensor_scalar_max(sinvq[:, mi], sinvq[:, mi], 1e-12)
                v.reciprocal(out=sinvq[:, mi], in_=sinvq[:, mi])
        kb.phase(vector=p_qf_v)

        b_im, b_ic = kb.n("im"), kb.n("ic")
        b_sc, b_v = kb.n("SC"), kb.n("V")
        VPT = 8 * 7 + 8

        def vcnt(mi, h, j, b_v=b_v):
            return b_v + mi * VPT + h * 7 + j

        def scnt(mi, h, j, b_sc=b_sc):
            return b_sc + (mi * 8 + h) * 3 + j

        def p_at_t(t, b_im=b_im, b_ic=b_ic, b_v=b_v):
            for mi in range(NT_H):
                iw = 31 + 128 * mi
                jA = iw - 31
                jB = iw + 97
                for h in range(8):
                    idx = mi * 8 + h
                    pp = idx % 2
                    pmi, ph = divmod(idx - 2, 8)
                    if idx >= 2:
                        t.wait_ge(kb.h("SC"), scnt(pmi, ph, 2))
                    t.matmul(out=pG[pp][:, 0:128], lhsT=su[:, h, jA:jA + 128],
                             rhs=su[:, h, iw:iw + 128], start=True, stop=False)
                    t.matmul(out=pG[pp][:, 0:128], lhsT=arow[0:1, h, jA:jA + 128],
                             rhs=t_onr[0:1, 0:128], start=False, stop=False)
                    t.matmul(out=pG[pp][:, 0:128], lhsT=t_onr[0:1, 0:128],
                             rhs=arow[0:1, h, iw:iw + 128], start=False, stop=True)
                    t.matmul(out=pG[pp][0:31, 128:256], lhsT=su[:, h, jB:jB + 31],
                             rhs=su[:, h, iw:iw + 128], start=True, stop=False)
                    t.matmul(out=pG[pp][0:31, 128:256], lhsT=arow[0:1, h, jB:jB + 31],
                             rhs=t_onr[0:1, 0:128], start=False, stop=False)
                    kb.inc(t.matmul(out=pG[pp][0:31, 128:256], lhsT=t_onr[0:1, 0:31],
                                    rhs=arow[0:1, h, iw:iw + 128], start=False, stop=True),
                           "im")
                    t.wait_ge(kb.h("V"), vcnt(mi, h, 2))
                    t.matmul(out=pCH[pp][:, 256:257], lhsT=sinfl[:, pp, 0:128],
                             rhs=t_oneb[:], start=True, stop=False)
                    t.matmul(out=pCH[pp][:, 256:257], lhsT=sinfl[0:31, pp, 128:256],
                             rhs=t_oneb[0:31, :], start=False, stop=True)
                    t.matmul(out=pCH[pp][:, 0:128], lhsT=sinfl[:, pp, 0:128],
                             rhs=svtm[:, mi, h], start=True, stop=False)
                    t.matmul(out=pCH[pp][:, 0:128], lhsT=sinfl[0:31, pp, 128:256],
                             rhs=svtm[0:31, mi + 1, h], start=False, stop=True)
                    kb.inc(t.matmul(out=pCH[pp][:, 128:256], lhsT=su[:, h, iw:iw + 128],
                                    rhs=shebs[:, h], start=True, stop=True), "im")
                    t.wait_ge(kb.h("V"), vcnt(mi, h, 7))
                    if idx >= 2:
                        t.wait_ge(kb.h("SC"), scnt(pmi, ph, 3))
                    kb.inc(t.transpose(out=pT[pp][:], in_=soh[:, pp], identity=t_idb[:]),
                           "im")
                t.wait_ge(kb.h("SC"), scnt(mi, 7, 3))
                if mi >= 1:
                    t.wait_ge(kb.h("V"), b_v + mi * VPT)
                for ohalf in range(2):
                    for o in range(4):
                        oc = ohalf * 4 + o
                        for k in range(CT):
                            i = t.matmul(out=pDec[ohalf][:, 128 * o:128 * (o + 1)],
                                         lhsT=sdec[:, k, 128 * oc:128 * (oc + 1)],
                                         rhs=sflat[:, 0, k],
                                         start=(k == 0), stop=(k == CT - 1))
                kb.inc(i, "ic")

        def p_at_s(sc, b_im=b_im, b_ic=b_ic):
            for mi in range(NT_H):
                for h in range(8):
                    idx = mi * 8 + h
                    pp = idx % 2
                    sc._eng.wait_ge(kb.h("V"), vcnt(mi, h, 1))
                    sc.activation(out=sdq[:, pp], in_=sdq[:, pp], func=AF.Sqrt,
                                  scale=kb.t_misc[:, 4:5], bias=kb.t_misc[:, 0:1])
                    sc.activation(out=sinfl[:, pp], in_=sdq[:, pp], func=AF.Exp,
                                  scale=-1.0)
                    sc._eng.wait_ge(kb.h("im"), b_im + 3 * idx + 3)
                    if h == 0 and mi >= 1:
                        sc._eng.wait_ge(kb.h("ic"), b_ic + mi)
                    sc.activation(out=sflat[:, 0, h], in_=pT[pp][:], func=AF.Copy)

        def p_at_v(v, b_im=b_im, b_ic=b_ic, half=half, u0=u0):
            for mi in range(NT_H):
                for h in range(8):
                    idx = mi * 8 + h
                    pp = idx % 2
                    pmi, ph = divmod(idx - 2, 8)
                    v._eng.wait_ge(kb.h("im"), b_im + 3 * idx + 1)
                    if idx >= 2:
                        v._eng.wait_ge(kb.h("SC"), scnt(pmi, ph, 2))
                    v.tensor_scalar_min(sdq[:, pp], pG[pp][:], 0.0)
                    v._eng.wait_ge(kb.h("SC"), scnt(mi, h, 2))
                    v.tensor_mul(out=sinfl[:, pp], in0=sinfl[:, pp], in1=t_band[:])
                    v._eng.wait_ge(kb.h("im"), b_im + 3 * idx + 2)
                    v.tensor_scalar_add(sinvd[:, pp:pp + 1], pCH[pp][:, 256:257], 1e-8)
                    v.reciprocal(out=sinvd[:, pp:pp + 1], in_=sinvd[:, pp:pp + 1])
                    v.tensor_scalar_mul(scf[:, pp], pCH[pp][:, 0:128], sinvd[:, pp:pp + 1])
                    v.scalar_tensor_tensor(out=schb[:, pp], in0=pCH[pp][:, 128:256],
                                           scalar=sinvq[:, mi, h:h + 1], in1=scf[:, pp],
                                           op0=ALU.mult, op1=ALU.add)
                    v.tensor_mul(out=soh[:, pp], in0=sqtm[:, mi, h], in1=schb[:, pp])
                v._eng.wait_ge(kb.h("ic"), b_ic + mi + 1)
                xoff = u0 + 31 + 128 * mi
                for ohalf in range(2):
                    for o in range(4):
                        oc = ohalf * 4 + o
                        v.tensor_add(out=xt[:, oc, xoff:xoff + 128],
                                     in0=xt[:, oc, xoff:xoff + 128],
                                     in1=pDec[ohalf][:, 128 * o:128 * (o + 1)])
        kb.phase(tensor=p_at_t, scalar=p_at_s, vector=p_at_v)

    if DEBUG_L0A:
        def p_dbg(sy):
            kb.inc(sy.dma_start(out=dbg_sig, in_=sig[:]), "st", 16)
            kb.inc(sy.dma_start(out=dbg_mdiag[:, 0:8], in_=recipm[:]), "st", 16)
            kb.inc(sy.dma_start(out=dbg_mdiag[:, 8:16], in_=uscale[:]), "st", 16)
            kb.inc(sy.dma_start(out=dbg_mdiag[:, 16:24], in_=modcol[:]), "st", 16)
            sy.wait_ge(kb.h("st"), kb.n("st"))

        def p_dbg_g(g):
            for k in range(CT):
                kb.inc(g.dma_start(out=dbg_xnb[:, k], in_=xnb[:, k, 0:256]), "st", 16)
            for h in range(8):
                kb.inc(g.dma_start(out=dbg_su[:, h], in_=su[:, h, 0:256]), "st", 16)
                kb.inc(g.dma_start(out=dbg_vtm[:, h], in_=svtm[:, 0, h]), "st", 16)
                kb.inc(g.dma_start(out=dbg_qtm[:, h], in_=sqtm[:, 0, h]), "st", 16)
            kb.inc(g.dma_start(out=dbg_arow, in_=arow[0:1, :, :]), "st", 16)
            kb.inc(g.dma_start(out=dbg_infl, in_=sinfl[:]), "st", 16)
            kb.inc(g.dma_start(out=dbg_sinvq, in_=sinvq[:]), "st", 16)
            g.wait_ge(kb.h("st"), kb.n("st"))
        kb.phase(sync=p_dbg, gpsimd=p_dbg_g)

    def p_store(sy):
        for k in range(CT):
            kb.inc(sy.dma_start(out=y1[128 * k:128 * (k + 1), :],
                                in_=xt[:, k, 31:31 + L0A_W]), "st", 16)
        sy.wait_ge(kb.h("st"), kb.n("st"))
    kb.phase(sync=p_store)


F0A_XW = 2436
F0A_NKV = 19
F0A_NIT = 17


def build_f0_attn(dbg=False):
    kb = KB()
    io = {"xh": kb.dram_in("xh", [D, F0A_XW], F32),
          "wq": kb.dram_in("wq", [D, 1024], F32),
          "wk": kb.dram_in("wk", [D, 1024], F32),
          "wv": kb.dram_in("wv", [D, 1024], F32),
          "wo": kb.dram_in("wo", [1024, D], F32),
          "qs_col": kb.dram_in("qs_col", [128, 1], F32),
          "ks_col": kb.dram_in("ks_col", [128, 1], F32),
          "csq": kb.dram_in("csq", [128, 2176], BF16),
          "snq": kb.dram_in("snq", [128, 2176], BF16),
          "csk": kb.dram_in("csk", [128, 2432], BF16),
          "snk": kb.dram_in("snk", [128, 2432], BF16),
          "maskp": kb.dram_in("maskp", [4, 128, 384], BF16),
          "ya": kb.dram_out("ya", [D, T], F32)}
    kb.dram_in("qsp_col", [128, 1], F32)
    kb.dram_in("ksp_col", [128, 1], F32)
    alloc_common(kb)
    kb.sems("ld", "km", "qm", "sm", "wm", "st")
    kb.start()
    kb.push()
    emit_f0_attn(kb, io, dbg=dbg)
    kb.pop()
    return kb


def emit_f0_attn(kb, io, with_t0=False, dbg=False):
    xh = io["xh"]
    wq_d, wk_d, wv_d, wo_d = io["wq"], io["wk"], io["wv"], io["wo"]
    qs_d, ks_d = io["qs_col"], io["ks_col"]
    csq_d, snq_d, csk_d, snk_d = io["csq"], io["snq"], io["csk"], io["snk"]
    mask_d, ya = io["maskp"], io["ya"]
    M0 = 0 if with_t0 else 1
    d_identb = kb.cin("identb")
    d_ind2 = kb.cin("ind2b")
    d_sel8 = kb.cin("sel8b")
    if dbg:
        dbg_q = kb.dram_out("dbg_q", [128, 4, 128], F32)
        dbg_k = kb.dram_out("dbg_k", [128, 4, 128], F32)
        dbg_v = kb.dram_out("dbg_v", [128, 512], F32)
        dbg_p = kb.dram_out("dbg_p", [128, 2, 384], F32)
        dbg_av = kb.dram_out("dbg_av", [128, 4, 128], F32)

    xt = kb.sbuf("xt", [128, CT, F0A_XW], F32)
    t_idb = kb.sbuf("t_idb", [128, 128], BF16)
    t_ind2 = kb.sbuf("t_ind2", [128, 2], BF16)
    t_sel8 = kb.sbuf("t_sel8", [8, 4, 128], BF16)
    wkb = kb.sbuf("wkb", [128, CT, 512], BF16)
    wqb = wkb
    wvb = kb.sbuf("wvb", [128, CT, 512], BF16)
    wob = kb.sbuf("wob", [128, 4, D], BF16)
    qs_c = kb.sbuf("qs_c", [128, 4], F32)
    csf = kb.sbuf("csf", [128, 2432], BF16)
    snf = kb.sbuf("snf", [128, 2432], BF16)
    csqf = cskf = csf
    snqf = snkf = snf
    kfull = kb.sbuf("kfull", [128, 4, F0A_NKV, 128], BF16)
    vfull = kb.sbuf("vfull", [128, F0A_NKV, 512], BF16)
    maskp = kb.sbuf("maskp_s", [128, 4, 384], BF16)
    xsb = kb.sbuf("xsb", [128, 2, CT, 132], BF16)
    kq = kb.sbuf("kq", [128, 2, 4, 128], BF16)
    kq2 = kb.sbuf("kq2", [128, 2, 4, 128], BF16)
    rcol = kb.sbuf("rcol", [128, 2, 8], F32)
    rcolb = kb.sbuf("rcolb", [128, 2, 8], BF16)
    srowb = kb.sbuf("srowb", [8, 2, 128], BF16)
    kn1 = kb.sbuf("kn1", [128, 4, 128], BF16)
    rta = kb.sbuf("rta", [128, 128], F32)
    rtb = kb.sbuf("rtb", [128, 128], F32)
    qpb = kb.sbuf("qpb", [128, 2, 4, 128], BF16)
    pmb = kb.sbuf("pmb", [128, 2, 384], BF16)
    pmt = kb.sbuf("pmt", [128, 2, 3, 128], BF16)
    avs = kb.sbuf("avs", [128, 2, 4, 128], BF16)
    nrow = kb.sbuf("nrow", [128, 2, 2], F32)
    dlt = kb.sbuf("dlt", [128, CT, (F0A_NIT - M0) * 128], BF16)

    pQ = [kb.psum(f"pQ{i}", [128, 512]) for i in range(2)]
    pNs = kb.psum("pNs", [128, 512])
    pTT = [kb.psum(f"pTT{i}", [128, 384], BF16) for i in range(2)]
    pSA = [kb.psum(f"pSA{i}", [128, 512]) for i in range(2)]
    pWo = kb.psum("pWo", [128, 512])
    pBC = pWo

    def p_load_sy(sy):
        for k in range(CT):
            kb.inc(sy.dma_start(out=xt[:, k], in_=xh[128 * k:128 * (k + 1), :]), "ld", 16)
        kb.inc(sy.dma_start(out=qs_c[:, 0:1], in_=qs_d), "ld", 16)
        kb.inc(sy.dma_start(out=qs_c[:, 1:2], in_=ks_d), "ld", 16)

    def p_load_g(g):
        load_common(kb, g)
        kb.inc(g.dma_start(out=t_idb[:], in_=d_identb), "ld", 16)
        kb.inc(g.dma_start(out=t_ind2[:], in_=d_ind2), "ld", 16)
        kb.inc(g.dma_start(out=t_sel8[:], in_=d_sel8), "ld", 16)
        for m in range(4):
            kb.inc(g.dma_start(out=maskp[:, m], in_=mask_d[m]), "ld", 16)
    kb.phase(sync=p_load_sy, gpsimd=p_load_g)


    def _partner(qtr):
        return 32 * (qtr + 1) if qtr % 2 == 0 else 32 * (qtr - 1)

    for half in range(2):
        def p_wl(g, half=half):
            for k in range(CT):
                kb.inc(g.dma_start(out=wkb[:, k], in_=wk_d[128 * k:128 * (k + 1),
                                                          512 * half:512 * (half + 1)]), "ld", 16)
                kb.inc(g.dma_start(out=wvb[:, k], in_=wv_d[128 * k:128 * (k + 1),
                                                          512 * half:512 * (half + 1)]), "ld", 16)
            for c in range(4):
                r0 = 512 * half + 128 * c
                kb.inc(g.dma_start(out=wob[:, c], in_=wo_d[r0:r0 + 128, :]), "ld", 16)
            kb.inc(g.dma_start(out=cskf[:], in_=csk_d), "ld", 16)
            kb.inc(g.dma_start(out=snkf[:], in_=snk_d), "ld", 16)
        kb.phase(gpsimd=p_wl)

        def p_foldk(v):
            v.tensor_scalar_mul(cskf[:], cskf[:], qs_c[:, 1:2])
            v.tensor_scalar_mul(snkf[:], snkf[:], qs_c[:, 1:2])
        kb.phase(vector=p_foldk)

        b_km, b_sc, b_v = kb.n("km"), kb.n("SC"), kb.n("V")
        SCK = 18
        VCK = 33

        def p_kv_t(t, b_km=b_km, b_sc=b_sc, b_v=b_v):
            for kv in range(F0A_NKV):
                pp = kv % 2
                t.wait_ge(kb.h("SC"), b_sc + SCK * kv + 8)
                if kv >= 2:
                    t.wait_ge(kb.h("SC"), b_sc + SCK * (kv - 2) + 17)
                for pt in range(4):
                    for k in range(CT):
                        t.matmul(out=pQ[pp][:, 128 * pt:128 * (pt + 1)],
                                 lhsT=wkb[:, k, 128 * pt:128 * (pt + 1)],
                                 rhs=xsb[:, pp, k, 1:129] if k < 4 else xsb[:, pp, k, 0:128],
                                 start=(k == 0), stop=(k == CT - 1))
                for k in range(CT):
                    i = t.matmul(out=pSA[pp][:],
                                 lhsT=xsb[:, pp, k, 1:129] if k < 4 else xsb[:, pp, k, 0:128],
                                 rhs=wvb[:, k],
                                 start=(k == 0), stop=(k == CT - 1))
                kb.inc(i, "km")
                t.wait_ge(kb.h("SC"), b_sc + SCK * kv + 16)
                if kv >= 1:
                    t.wait_ge(kb.h("V"), b_v + VCK * (kv - 1) + 5)
                for pt in range(4):
                    i = t.matmul(out=pNs[:, 2 * pt:2 * pt + 2], lhsT=kq2[:, pp, pt],
                                 rhs=t_ind2[:], start=True, stop=True)
                kb.inc(i, "km")
                t.wait_ge(kb.h("V"), b_v + VCK * kv + 4)
                kb.inc(t.matmul(out=pNs[0:8, 128:256], lhsT=rcolb[:, pp], rhs=t_idb[:],
                                start=True, stop=True), "km")
                t.wait_ge(kb.h("V"), b_v + VCK * kv + 5)
                for pt in range(4):
                    i = t.matmul(out=pBC[:, 128 * pt:128 * (pt + 1)],
                                 lhsT=t_sel8[:, pt], rhs=srowb[:, pp],
                                 start=True, stop=True)
                kb.inc(i, "km")

        def p_kv_s(sc, b_km=b_km, b_v=b_v):
            for kv in range(F0A_NKV):
                pp = kv % 2
                x0 = 128 * kv
                if kv >= 2:
                    sc._eng.wait_ge(kb.h("km"), b_km + 4 * (kv - 2) + 1)
                for k in range(CT):
                    sc.activation(out=xsb[:, pp, k, 0:129], in_=xt[:, k, x0:x0 + 129],
                                  func=AF.Copy)
                sc._eng.wait_ge(kb.h("km"), b_km + 4 * kv + 1)
                for pt in range(4):
                    sc.activation(out=kq[:, pp, pt], in_=pQ[pp][:, 128 * pt:128 * (pt + 1)],
                                  func=AF.Copy)
                    sc.activation(out=kq2[:, pp, pt], in_=pQ[pp][:, 128 * pt:128 * (pt + 1)],
                                  func=AF.Square)
                sc.activation(out=vfull[:, kv], in_=pSA[pp][:], func=AF.Copy)
                sc._eng.wait_ge(kb.h("V"), b_v + VCK * kv + 1)
                sc.activation(out=rcol[:, pp], in_=rcol[:, pp], func=AF.Sqrt)

        def p_kv_v(v, b_km=b_km, b_sc=b_sc):
            for kv in range(F0A_NKV):
                pp = kv % 2
                j0 = 128 * kv
                v._eng.wait_ge(kb.h("km"), b_km + 4 * kv + 2)
                v.tensor_copy(out=rcol[:, pp], in_=pNs[:, 0:8])
                v._eng.wait_ge(kb.h("SC"), b_sc + SCK * kv + 18)
                v.tensor_scalar_max(rcol[:, pp], rcol[:, pp], 1e-12)
                v.reciprocal(out=rcol[:, pp], in_=rcol[:, pp])
                v.tensor_copy(out=rcolb[:, pp], in_=rcol[:, pp])
                v._eng.wait_ge(kb.h("km"), b_km + 4 * kv + 3)
                v.tensor_copy(out=srowb[:, pp], in_=pNs[0:8, 128:256])
                v._eng.wait_ge(kb.h("km"), b_km + 4 * kv + 4)
                for pt in range(4):
                    v.tensor_mul(out=kn1[:, pt], in0=kq[:, pp, pt],
                                 in1=pBC[:, 128 * pt:128 * (pt + 1)])
                    v.tensor_mul(out=rta[:], in0=kn1[:, pt], in1=cskf[:, j0:j0 + 128])
                    for qtr in range(4):
                        a = 32 * qtr
                        prt = _partner(qtr)
                        v.tensor_mul(out=rtb[a:a + 32, :], in0=kn1[prt:prt + 32, pt],
                                     in1=snkf[prt:prt + 32, j0:j0 + 128])
                    v.tensor_add(out=kfull[:, pt, kv], in0=rta[:], in1=rtb[:])
        kb.phase(tensor=p_kv_t, scalar=p_kv_s, vector=p_kv_v)

        def p_wq(g, half=half):
            for k in range(CT):
                kb.inc(g.dma_start(out=wqb[:, k], in_=wq_d[128 * k:128 * (k + 1),
                                                          512 * half:512 * (half + 1)]), "ld", 16)
            kb.inc(g.dma_start(out=csqf[:, 0:2176], in_=csq_d), "ld", 16)
            kb.inc(g.dma_start(out=snqf[:, 0:2176], in_=snq_d), "ld", 16)
        kb.phase(gpsimd=p_wq)

        def p_foldq(v):
            v.tensor_scalar_mul(csqf[:, 0:2176], csqf[:, 0:2176], qs_c[:, 0:1])
            v.tensor_scalar_mul(snqf[:, 0:2176], snqf[:, 0:2176], qs_c[:, 0:1])
        kb.phase(vector=p_foldq)

        b_qm, b_sm, b_wm = kb.n("qm"), kb.n("sm"), kb.n("wm")
        b_sc, b_v = kb.n("SC"), kb.n("V")
        SCM = 61
        VR = 8 if half == 0 else 16
        VCM = 73 + VR

        def sco(m, j):
            return b_sc + SCM * m + j

        def vco(m, j):
            return b_v + VCM * m + j

        def p_i_t(t, b_qm=b_qm, b_sm=b_sm, b_wm=b_wm):
            for jt, m in enumerate(range(M0, F0A_NIT)):
                pp = jt % 2
                i0 = 128 * m
                t.wait_ge(kb.h("SC"), sco(jt, 12))
                if jt >= 2:
                    t.wait_ge(kb.h("SC"), sco(jt - 2, 20))
                for pt in range(4):
                    for k in range(CT):
                        i = t.matmul(out=pQ[pp][:, 128 * pt:128 * (pt + 1)],
                                     lhsT=wqb[:, k, 128 * pt:128 * (pt + 1)],
                                     rhs=xsb[:, pp, k, 1:129] if k < 4 else xsb[:, pp, k, 0:128],
                                     start=(k == 0), stop=(k == CT - 1))
                kb.inc(i, "qm")
                t.wait_ge(kb.h("SC"), sco(jt, 20))
                if jt >= 1:
                    t.wait_ge(kb.h("V"), vco(jt - 1, 5))
                for pt in range(4):
                    i = t.matmul(out=pNs[:, 2 * pt:2 * pt + 2], lhsT=kq2[:, pp, pt],
                                 rhs=t_ind2[:], start=True, stop=True)
                kb.inc(i, "qm")
                t.wait_ge(kb.h("V"), vco(jt, 4))
                kb.inc(t.matmul(out=pNs[0:8, 128:256], lhsT=rcolb[:, pp], rhs=t_idb[:],
                                start=True, stop=True), "qm")
                t.wait_ge(kb.h("V"), vco(jt, 5))
                for pt in range(4):
                    i = t.matmul(out=pBC[:, 128 * pt:128 * (pt + 1)],
                                 lhsT=t_sel8[:, pt], rhs=srowb[:, pp],
                                 start=True, stop=True)
                kb.inc(i, "qm")
                for h in range(8):
                    sp = h % 2
                    pt = h // 2
                    es = 64 * (h % 2)
                    t.wait_ge(kb.h("V"), vco(jt, 33))
                    if h >= 2:
                        t.wait_ge(kb.h("SC"), sco(jt, 21 + 5 * (h - 1)))
                    elif jt >= 1:
                        t.wait_ge(kb.h("SC"), sco(jt - 1, 21 + 5 * (6 + h) + 5))
                    for jj in range(3):
                        i = t.matmul(out=pSA[sp][:, 128 * jj:128 * (jj + 1)],
                                     lhsT=qpb[es:es + 64, pp, pt, :],
                                     rhs=kfull[es:es + 64, pt, m + jj, :],
                                     start=True, stop=True)
                    kb.inc(i, "sm")
                    t.wait_ge(kb.h("V"), vco(jt, 33 + 5 * h + 5))
                    if h >= 2:
                        t.wait_ge(kb.h("SC"), sco(jt, 21 + 5 * (h - 2) + 4))
                    for jj in range(3):
                        i = t.transpose(out=pTT[sp][:, 128 * jj:128 * (jj + 1)],
                                        in_=pmb[:, sp, 128 * jj:128 * (jj + 1)],
                                        identity=t_idb[:])
                    kb.inc(i, "sm")
                    t.wait_ge(kb.h("SC"), sco(jt, 21 + 5 * h + 4))
                    for jj in range(3):
                        i = t.matmul(out=pSA[sp][0:64, 384:512],
                                     lhsT=vfull[:, m + jj, 64 * h:64 * h + 64],
                                     rhs=pmt[:, sp, jj], start=(jj == 0), stop=(jj == 2))
                    kb.inc(i, "sm")
                t.wait_ge(kb.h("SC"), sco(jt, 61))
                if jt >= 1:
                    t.wait_ge(kb.h("V"), vco(jt - 1, VCM))
                for grp in range(2):
                    for oc in range(4):
                        och = 4 * grp + oc
                        for c in range(4):
                            i = t.matmul(out=pWo[:, 128 * oc:128 * (oc + 1)],
                                         lhsT=wob[:, c, 128 * och:128 * (och + 1)],
                                         rhs=avs[:, pp, c, :],
                                         start=(c == 0), stop=(c == 3))
                    kb.inc(i, "wm")
                    if grp == 0:
                        t.wait_ge(kb.h("V"), vco(jt, 73 + VR // 2))

        def p_i_s(sc, b_qm=b_qm, b_sm=b_sm):
            for jt, m in enumerate(range(M0, F0A_NIT)):
                pp = jt % 2
                x0 = 256 + 128 * m
                if jt >= 2:
                    sc._eng.wait_ge(kb.h("qm"), b_qm + 4 * (jt - 2) + 1)
                for k in range(CT):
                    sc.activation(out=xsb[:, pp, k, 0:129], in_=xt[:, k, x0:x0 + 129],
                                  func=AF.Copy)
                for k in range(4, CT):
                    sc.activation(out=xsb[:, pp, k, 0:1],
                                  in_=(xt[:, k, x0:x0 + 1] if jt == 0
                                       else xsb[:, 1 - pp, k, 128:129]),
                                  func=AF.Copy)
                sc._eng.wait_ge(kb.h("qm"), b_qm + 4 * jt + 1)
                for pt in range(4):
                    sc.activation(out=kq[:, pp, pt], in_=pQ[pp][:, 128 * pt:128 * (pt + 1)],
                                  func=AF.Copy)
                    sc.activation(out=kq2[:, pp, pt], in_=pQ[pp][:, 128 * pt:128 * (pt + 1)],
                                  func=AF.Square)
                sc._eng.wait_ge(kb.h("V"), vco(jt, 1))
                sc.activation(out=rcol[:, pp], in_=rcol[:, pp], func=AF.Sqrt)
                for h in range(8):
                    sp = h % 2
                    sc._eng.wait_ge(kb.h("V"), vco(jt, 33 + 5 * h + 1))
                    sc.activation(out=pmb[:, sp], in_=pSA[sp][:, 0:384], func=AF.Exp,
                                  bias=nrow[:, sp, 0:1])
                    sc._eng.wait_ge(kb.h("sm"), b_sm + 3 * (8 * jt + h) + 2)
                    for jj in range(3):
                        sc.activation(out=pmt[:, sp, jj],
                                      in_=pTT[sp][:, 128 * jj:128 * (jj + 1)],
                                      func=AF.Copy)
                    sc._eng.wait_ge(kb.h("sm"), b_sm + 3 * (8 * jt + h) + 3)
                    if h == 0 and jt >= 2:
                        sc._eng.wait_ge(kb.h("wm"), b_wm + 2 * (jt - 1))
                    sc.activation(out=avs[64 * (h % 2):64 * (h % 2) + 64, pp, h // 2, :],
                                  in_=pSA[sp][0:64, 384:512], func=AF.Copy)

        def p_i_v(v, b_qm=b_qm, b_sm=b_sm, b_wm=b_wm):
            for jt, m in enumerate(range(M0, F0A_NIT)):
                pp = jt % 2
                i0 = 128 * m
                v._eng.wait_ge(kb.h("qm"), b_qm + 4 * jt + 2)
                v.tensor_copy(out=rcol[:, pp], in_=pNs[:, 0:8])
                v._eng.wait_ge(kb.h("SC"), sco(jt, 21))
                v.tensor_scalar_max(rcol[:, pp], rcol[:, pp], 1e-12)
                v.reciprocal(out=rcol[:, pp], in_=rcol[:, pp])
                v.tensor_copy(out=rcolb[:, pp], in_=rcol[:, pp])
                v._eng.wait_ge(kb.h("qm"), b_qm + 4 * jt + 3)
                v.tensor_copy(out=srowb[:, pp], in_=pNs[0:8, 128:256])
                v._eng.wait_ge(kb.h("qm"), b_qm + 4 * jt + 4)
                for pt in range(4):
                    v.tensor_mul(out=kn1[:, pt], in0=kq[:, pp, pt],
                                 in1=pBC[:, 128 * pt:128 * (pt + 1)])
                    v.tensor_mul(out=rta[:], in0=kn1[:, pt], in1=csqf[:, i0:i0 + 128])
                    for qtr in range(4):
                        a = 32 * qtr
                        prt = _partner(qtr)
                        v.tensor_mul(out=rtb[a:a + 32, :], in0=kn1[prt:prt + 32, pt],
                                     in1=snqf[prt:prt + 32, i0:i0 + 128])
                    v.tensor_add(out=qpb[:, pp, pt], in0=rta[:], in1=rtb[:])
                for h in range(8):
                    sp = h % 2
                    v._eng.wait_ge(kb.h("sm"), b_sm + 3 * (8 * jt + h) + 1)
                    v.reduce_max(out=nrow[:, sp, 0:1], in_=pSA[sp][:, 0:384],
                                 axis=AX.X, negate=True)
                    v._eng.wait_ge(kb.h("SC"), sco(jt, 21 + 5 * h + 1))
                    v.tensor_mul(out=pmb[:, sp], in0=pmb[:, sp], in1=maskp[:, min(m, 3)])
                    v.reduce_sum(out=nrow[:, sp, 1:2], in_=pmb[:, sp], axis=AX.X)
                    v.reciprocal(out=nrow[:, sp, 1:2], in_=nrow[:, sp, 1:2])
                    v.tensor_scalar_mul(pmb[:, sp], pmb[:, sp], nrow[:, sp, 1:2])
                xoff = 257 + 128 * m
                for grp in range(2):
                    v._eng.wait_ge(kb.h("wm"), b_wm + 2 * jt + grp + 1)
                    for oc in range(4):
                        och = 4 * grp + oc
                        if half == 0:
                            v.tensor_copy(out=dlt[:, och, 128 * (m - M0):128 * (m - M0 + 1)],
                                          in_=pWo[:, 128 * oc:128 * (oc + 1)])
                        else:
                            v.tensor_add(out=rta[:], in0=pWo[:, 128 * oc:128 * (oc + 1)],
                                         in1=dlt[:, och, 128 * (m - M0):128 * (m - M0 + 1)])
                            v.tensor_add(out=xt[:, och, xoff:xoff + 128],
                                         in0=xt[:, och, xoff:xoff + 128],
                                         in1=rta[:])
        kb.phase(tensor=p_i_t, scalar=p_i_s, vector=p_i_v)

    if dbg:
        def p_dbg(g):
            for pt in range(4):
                kb.inc(g.dma_start(out=dbg_q[:, pt], in_=qpb[:, 0, pt]), "st", 16)
                kb.inc(g.dma_start(out=dbg_k[:, pt], in_=kfull[:, pt, 18]), "st", 16)
                kb.inc(g.dma_start(out=dbg_av[:, pt], in_=avs[:, 0, pt]), "st", 16)
            kb.inc(g.dma_start(out=dbg_v, in_=vfull[:, 18]), "st", 16)
            kb.inc(g.dma_start(out=dbg_p, in_=pmb[:]), "st", 16)
            g.wait_ge(kb.h("st"), kb.n("st"))
        kb.phase(gpsimd=p_dbg)

    yx0 = 385 - 128 * (1 - M0)

    def p_store(sy):
        for k in range(CT):
            kb.inc(sy.dma_start(out=ya[128 * k:128 * (k + 1), :],
                                in_=xt[:, k, yx0:2433]), "ld", 16)
        sy.wait_ge(kb.h("ld"), kb.n("ld"))
    kb.phase(sync=p_store)


def build_f0_ffn():
    kb = KB()
    io = {"xf": kb.dram_in("xf", [D, 2052], F32),
          "w1p": kb.dram_in("w1p", [44, D, 128], F32),
          "w2po": kb.dram_in("w2po", [8, 2816, 128], F32),
          "g_col": kb.dram_in("g_col", [128, 8], F32),
          "b_col": kb.dram_in("b_col", [128, 8], F32),
          "b1a_col": kb.dram_in("b1a_col", [128, 22], F32),
          "b1g_col": kb.dram_in("b1g_col", [128, 22], F32),
          "b2_col": kb.dram_in("b2_col", [128, 8], F32),
          "yf": kb.dram_out("yf", [D, T], F32)}
    alloc_common(kb)
    kb.sems("ld", "m", "hm", "wm", "w1l", "w2l", "st")
    kb.start()
    kb.push()
    emit_f0_ffn(kb, io)
    kb.pop()
    return kb


def emit_f0_ffn(kb, io, hmask=None):
    xf, w1p_d, w2po_d = io["xf"], io["w1p"], io["w2po"]
    g_d, b_d, b1a_d, b1g_d, b2_d = (io["g_col"], io["b_col"], io["b1a_col"],
                                    io["b1g_col"], io["b2_col"])
    yf = io["yf"]
    gcol = kb.sbuf("gcol", [128, 8], F32)
    bcol = kb.sbuf("bcol", [128, 8], F32)
    b1a = kb.sbuf("b1a", [128, 22], F32)
    b1g = kb.sbuf("b1g", [128, 22], F32)
    b2c = kb.sbuf("b2c", [128, 8], F32)
    xnb = kb.sbuf("xnb", [128, CT, T], BF16)

    pA = kb.psum("pA", [128, 512])
    pB = kb.psum("pB", [128, 512])
    pHa = [kb.psum(f"pHa{i}", [128, 512]) for i in range(2)]
    pHg = [kb.psum(f"pHg{i}", [128, 512]) for i in range(2)]
    pW = [kb.psum(f"pW{i}", [128, 512]) for i in range(2)]

    ov_a = ExitStack()
    xt = ov_a.enter_context(kb.sbuf_cm("xt", [128, CT, 2052], F32))
    kb.t_lnxb = ov_a.enter_context(kb.sbuf_cm("lnxb", [128, CT, 512], BF16))
    kb.t_lnrow = ov_a.enter_context(kb.sbuf_cm("lnrow", [128, 4, 512], F32))
    sc512 = ov_a.enter_context(kb.sbuf_cm("sc512", [128, 512], F32))

    def p_load_sy(sy):
        for k in range(CT):
            kb.inc(sy.dma_start(out=xt[:, k], in_=xf[128 * k:128 * (k + 1), :]), "ld", 16)
        kb.inc(sy.dma_start(out=gcol[:], in_=g_d), "ld", 16)
        kb.inc(sy.dma_start(out=bcol[:], in_=b_d), "ld", 16)
        kb.inc(sy.dma_start(out=b1a[:], in_=b1a_d), "ld", 16)
        kb.inc(sy.dma_start(out=b1g[:], in_=b1g_d), "ld", 16)
        kb.inc(sy.dma_start(out=b2c[:], in_=b2_d), "ld", 16)

    def p_load_g(g):
        load_common(kb, g)
    kb.phase(sync=p_load_sy, gpsimd=p_load_g)

    if hmask is not None:
        def p_hm(v):
            for k in range(CT):
                v.tensor_scalar_mul(xt[:, k, 0:1], xt[:, k, 0:1], hmask[:, 0:1])
        kb.phase(vector=p_hm)

    NCH = T // 512
    for ch in range(NCH):
        xs0 = 1 + 512 * ch

        def p_cast(sc, xs0=xs0):
            for k in range(CT):
                off = xs0 if k < 4 else xs0 - 1
                sc.activation(out=kb.t_lnxb[:, k], in_=xt[:, k, off:off + 512],
                              func=AF.Copy)
        kb.phase(scalar=p_cast)

        def p_mmA(t):
            for k in range(CT):
                i = t.matmul(out=pA[:], lhsT=kb.t_ones[:], rhs=kb.t_lnxb[:, k],
                             start=(k == 0), stop=(k == CT - 1))
            kb.inc(i, "m")
        kb.phase(tensor=p_mmA)

        def p_sq(v):
            for k in range(CT):
                v.tensor_mul(out=kb.t_lnxb[:, k], in0=kb.t_lnxb[:, k],
                             in1=kb.t_lnxb[:, k])
        kb.phase(vector=p_sq)

        def p_mmB(t):
            for k in range(CT):
                i = t.matmul(out=pB[:], lhsT=kb.t_ones[:], rhs=kb.t_lnxb[:, k],
                             start=(k == 0), stop=(k == CT - 1))
            kb.inc(i, "m")
        kb.phase(tensor=p_mmB)

        def p_v1(v):
            v.tensor_scalar_mul(kb.t_lnrow[:, 0], pA[:], 1.0 / D)
            v.tensor_scalar_mul(kb.t_lnrow[:, 1], pB[:], 1.0 / D)
        kb.phase(vector=p_v1)

        def p_s1(sc):
            sc.activation(out=kb.t_lnrow[:, 2], in_=kb.t_lnrow[:, 0], func=AF.Square)
        kb.phase(scalar=p_s1)

        def p_v2(v):
            v.tensor_sub(out=kb.t_lnrow[:, 1], in0=kb.t_lnrow[:, 1],
                         in1=kb.t_lnrow[:, 2])
        kb.phase(vector=p_v2)

        def p_s2(sc):
            sc.activation(out=kb.t_lnrow[:, 1], in_=kb.t_lnrow[:, 1],
                          func=AF.Sqrt, bias=kb.t_misc[:, 1:2])
        kb.phase(scalar=p_s2)

        def p_v3b(v, xs0=xs0, ch=ch):
            v.reciprocal(out=kb.t_lnrow[:, 3], in_=kb.t_lnrow[:, 1])
            for k in range(CT):
                off = xs0 if k < 4 else xs0 - 1
                v.tensor_sub(out=kb.t_lnrow[:, 2], in0=xt[:, k, off:off + 512],
                             in1=kb.t_lnrow[:, 0])
                v.tensor_mul(out=sc512[:], in0=kb.t_lnrow[:, 2], in1=kb.t_lnrow[:, 3])
                v.tensor_scalar(out=xnb[:, k, 512 * ch:512 * (ch + 1)], in0=sc512[:],
                                scalar1=gcol[:, k:k + 1], scalar2=bcol[:, k:k + 1],
                                op0=ALU.mult, op1=ALU.add)
        kb.phase(vector=p_v3b)
    ov_a.close()

    hbuf = kb.sbuf("hbuf", [128, 22, T], BF16)
    w1b = kb.sbuf("w1b", [128, 2, 2, CT, 128], BF16)
    w2b = kb.sbuf("w2b", [128, CT, 22, 128], BF16)
    gel = kb.sbuf("gel", [128, 2, 512], BF16)
    xres = kb.sbuf("xres", [128, CT, 512], F32)
    yst = kb.sbuf("yst", [128, 2, 512], F32)

    b_hm, b_sc, b_v, b_w1 = kb.n("hm"), kb.n("SC"), kb.n("V"), kb.n("w1l")

    def p_w1_g(g, b_hm=b_hm):
        for p in range(22):
            pp = p % 2
            if p >= 2:
                g._eng.wait_ge(kb.h("hm"), b_hm + 4 * (p - 1))
            kb.inc(g.dma_start(out=w1b[:, pp, 0],
                               in_=w1p_d[2 * p].rearrange("(k p) f -> p k f", p=128)),
                   "w1l", 16)
            kb.inc(g.dma_start(out=w1b[:, pp, 1],
                               in_=w1p_d[2 * p + 1].rearrange("(k p) f -> p k f", p=128)),
                   "w1l", 16)
        for oc in range(CT):
            kb.inc(g.dma_start(out=w2b[:, oc],
                               in_=w2po_d[oc].rearrange("(c p) f -> p c f", p=128)),
                   "w2l", 16)

    def p_w1_t(t, b_w1=b_w1, b_sc=b_sc, b_v=b_v):
        for p in range(22):
            pp = p % 2
            t.wait_ge(kb.h("w1l"), b_w1 + 32 * (p + 1))
            for ch in range(4):
                u = 4 * p + ch
                if u >= 2:
                    t.wait_ge(kb.h("SC"), b_sc + u - 1)
                    t.wait_ge(kb.h("V"), b_v + u - 1)
                cs = slice(512 * ch, 512 * (ch + 1))
                for k in range(CT):
                    t.matmul(out=pHa[ch % 2][:], lhsT=w1b[:, pp, 0, k], rhs=xnb[:, k, cs],
                             start=(k == 0), stop=(k == CT - 1))
                for k in range(CT):
                    i = t.matmul(out=pHg[ch % 2][:], lhsT=w1b[:, pp, 1, k],
                                 rhs=xnb[:, k, cs],
                                 start=(k == 0), stop=(k == CT - 1))
                kb.inc(i, "hm")

    def p_w1_s(sc, b_hm=b_hm, b_v=b_v):
        for p in range(22):
            for ch in range(4):
                u = 4 * p + ch
                sc._eng.wait_ge(kb.h("hm"), b_hm + u + 1)
                if u >= 2:
                    sc._eng.wait_ge(kb.h("V"), b_v + u - 1)
                sc.activation(out=gel[:, ch % 2], in_=pHg[ch % 2][:], func=AF.Gelu,
                              bias=b1g[:, p:p + 1])

    def p_w1_v(v, b_sc=b_sc):
        for p in range(22):
            for ch in range(4):
                u = 4 * p + ch
                v._eng.wait_ge(kb.h("SC"), b_sc + u + 1)
                v.scalar_tensor_tensor(out=hbuf[:, p, 512 * ch:512 * (ch + 1)],
                                       in0=pHa[ch % 2][:],
                                       scalar=b1a[:, p:p + 1], in1=gel[:, ch % 2],
                                       op0=ALU.add, op1=ALU.mult)
    kb.phase(tensor=p_w1_t, scalar=p_w1_s, vector=p_w1_v, gpsimd=p_w1_g)

    b_wm, b_v2, b_w2, b_st, b_ld = (kb.n("wm"), kb.n("V"), kb.n("w2l"),
                                    kb.n("st"), kb.n("ld"))

    def p_w2_sy(sy, b_v2=b_v2):
        for ch in range(4):
            for k in range(CT):
                kb.inc(sy.dma_start(out=xres[:, k],
                                    in_=xf[128 * k:128 * (k + 1),
                                           1 + 512 * ch:1 + 512 * (ch + 1)]), "ld", 16)
            for oc in range(CT):
                u = CT * ch + oc
                sy.wait_ge(kb.h("V"), b_v2 + u + 1)
                kb.inc(sy.dma_start(
                    out=yf[128 * oc:128 * (oc + 1), 512 * ch:512 * (ch + 1)],
                    in_=yst[:, u % 2]), "st", 16)
        sy.wait_ge(kb.h("st"), kb.n("st"))

    def p_w2_t(t, b_v2=b_v2, b_ld=b_ld):
        for ch in range(4):
            t.wait_ge(kb.h("ld"), b_ld + 16 * CT * (ch + 1))
            for oc in range(CT):
                u = CT * ch + oc
                pp = u % 2
                if u >= 2:
                    t.wait_ge(kb.h("V"), b_v2 + u - 1)
                for c in range(22):
                    i = t.matmul(out=pW[pp][:], lhsT=w2b[:, oc, c],
                                 rhs=hbuf[:, c, 512 * ch:512 * (ch + 1)],
                                 start=(c == 0), stop=(c == 21))
                kb.inc(i, "wm")

    def p_w2_v(v, b_wm=b_wm, b_st=b_st):
        for ch in range(4):
            for oc in range(CT):
                u = CT * ch + oc
                v._eng.wait_ge(kb.h("wm"), b_wm + u + 1)
                if u >= 2:
                    v._eng.wait_ge(kb.h("st"), b_st + 16 * (u - 1))
                v.scalar_tensor_tensor(out=yst[:, u % 2], in0=pW[u % 2][:],
                                       scalar=b2c[:, oc:oc + 1],
                                       in1=xres[:, oc], op0=ALU.add, op1=ALU.add)
    kb.phase(tensor=p_w2_t, vector=p_w2_v, sync=p_w2_sy)

    def p_fin(sy):
        sy.wait_ge(kb.h("st"), kb.n("st"))
    kb.phase(sync=p_fin)


L1C_XW = 2052


def emit_rms(kb, xt, outb, x0, n0, w, gc, pA):
    done = 0
    while done < w:
        cw = min(512, w - done)
        xs = slice(x0 + done, x0 + done + cw)
        ns = slice(n0 + done, n0 + done + cw)
        done += cw

        def p_cast(sc, xs=xs, cw=cw):
            for k in range(CT):
                sc.activation(out=kb.t_lnxb[:, k, 0:cw], in_=xt[:, k, xs], func=AF.Copy)
        kb.phase(scalar=p_cast)

        def p_sq(v, cw=cw):
            for k in range(CT):
                v.tensor_mul(out=kb.t_lnxb[:, k, 0:cw], in0=kb.t_lnxb[:, k, 0:cw],
                             in1=kb.t_lnxb[:, k, 0:cw])
        kb.phase(vector=p_sq)

        def p_mm(t, cw=cw):
            for k in range(CT):
                i = t.matmul(out=pA[:, 0:cw], lhsT=kb.t_ones[:], rhs=kb.t_lnxb[:, k, 0:cw],
                             start=(k == 0), stop=(k == CT - 1))
            kb.inc(i, "m")
        kb.phase(tensor=p_mm)

        def p_r(sc, cw=cw):
            sc.activation(out=kb.t_lnrow[:, 0, 0:cw], in_=pA[:, 0:cw], func=AF.Sqrt)
        kb.phase(scalar=p_r)

        def p_rv(v, cw=cw):
            v.tensor_scalar_max(kb.t_lnrow[:, 0, 0:cw], kb.t_lnrow[:, 0, 0:cw], 1e-12)
            v.reciprocal(out=kb.t_lnrow[:, 1, 0:cw], in_=kb.t_lnrow[:, 0, 0:cw])
        kb.phase(vector=p_rv)

        def p_out(v, xs=xs, ns=ns, cw=cw):
            for k in range(CT):
                v.scalar_tensor_tensor(out=outb[:, k, ns], in0=xt[:, k, xs],
                                       scalar=gc[:, k:k + 1], in1=kb.t_lnrow[:, 1, 0:cw],
                                       op0=ALU.mult, op1=ALU.mult)
        kb.phase(vector=p_out)


def build_l1a_pre():
    kb = KB()
    io = {"xh": kb.dram_in("xh", [D, L1C_XW], F32),
          "dwc": kb.dram_in("dwc", [128, 32], F32),
          "dwb": kb.dram_in("dwb", [128, 8], F32),
          "pwbias": kb.dram_in("pwbias", [128, 8], F32),
          "gcv": kb.dram_in("gcv", [128, 8], F32),
          "ggr": kb.dram_in("ggr", [128, 8], F32),
          "pwT": kb.dram_in("pwT", [D, D], F32),
          "whg": kb.dram_in("whg", [4, 256, 768], F32),
          "xa": kb.dram_out("xa", [D, T], F32),
          "hl": kb.dram_out("hl", [128, 12, T], F32),
          "al": kb.dram_out("al", [128, 12, T], F32),
          "summ": kb.dram_out("summ", [128, 2, 12], F32)}
    alloc_common(kb)
    kb.sems("ld", "m", "st")
    kb.start()
    kb.push()
    emit_l1a_pre(kb, io)
    kb.pop()
    return kb


def emit_l1a_pre(kb, io, hmask=None):
    xh = io["xh"]
    dwc_d, dwb_d, pwbias_d = io["dwc"], io["dwb"], io["pwbias"]
    gcv_d, ggr_d, pwT_d, whg_d = io["gcv"], io["ggr"], io["pwT"], io["whg"]
    xa_d, hl_d, al_d, sm_d = io["xa"], io["hl"], io["al"], io["summ"]
    xt = kb.sbuf("xt", [128, CT, L1C_XW], F32)
    alloc_ln(kb)
    cols = kb.sbuf("cols", [128, 64], F32)
    summ = kb.sbuf("summ_s", [128, 2, 12], F32)
    zocol = kb.sbuf("zocol", [128, 2], F32)

    ov = ExitStack()

    def osb(name, shape, dtype):
        return ov.enter_context(kb.sbuf_cm(name, shape, dtype))

    xn = osb("xn", [128, CT, L1C_XW], BF16)
    pwb = osb("pwb", [128, CT, CT, 128], BF16)
    acc = osb("acc", [128, CT, 512], F32)
    ycb = osb("ycb", [128, CT, 512], BF16)

    pA = kb.psum("pA", [128, 512])
    pP = [kb.psum(f"pP{i}", [128, 512]) for i in range(2)]

    def p_load_sy(sy):
        for k in range(CT):
            kb.inc(sy.dma_start(out=xt[:, k], in_=xh[128 * k:128 * (k + 1), :]), "ld", 16)
        kb.inc(sy.dma_start(out=cols[:, 0:32], in_=dwc_d), "ld", 16)
        kb.inc(sy.dma_start(out=cols[:, 32:40], in_=dwb_d), "ld", 16)
        kb.inc(sy.dma_start(out=cols[:, 40:48], in_=pwbias_d), "ld", 16)
        kb.inc(sy.dma_start(out=cols[:, 48:56], in_=gcv_d), "ld", 16)
        kb.inc(sy.dma_start(out=cols[:, 56:64], in_=ggr_d), "ld", 16)

    def p_load_g(g):
        load_common(kb, g)
        for dd in range(CT):
            for ee in range(CT):
                kb.inc(g.dma_start(out=pwb[:, dd, ee],
                                   in_=pwT_d[128 * dd:128 * (dd + 1),
                                             128 * ee:128 * (ee + 1)]), "ld", 16)
    kb.phase(sync=p_load_sy, gpsimd=p_load_g)

    def p_gc(v):
        v.tensor_scalar(out=cols[:, 48:64], in0=cols[:, 48:64], scalar1=1.0,
                        scalar2=32.0, op0=ALU.add, op1=ALU.mult)
        v.memset(zocol[:, 0:1], 0.0)
        v.memset(zocol[:, 1:2], 1.0)
        if hmask is not None:
            for k in range(CT):
                v.tensor_scalar_mul(xt[:, k, 0:3], xt[:, k, 0:3], hmask[:, 0:1])
    kb.phase(vector=p_gc)

    emit_rms(kb, xt, xn, 0, 0, L1C_XW, cols[:, 48:56], pA)

    for ch in range(4):
        t0 = 512 * ch

        def p_conv(v, t0=t0):
            for c in range(CT):
                v.tensor_scalar_mul(acc[:, c], xn[:, c, t0:t0 + 512], cols[:, 4 * c:4 * c + 1])
                for kk in range(1, 4):
                    v.scalar_tensor_tensor(out=acc[:, c], in0=xn[:, c, t0 + kk:t0 + kk + 512],
                                           scalar=cols[:, 4 * c + kk:4 * c + kk + 1],
                                           in1=acc[:, c], op0=ALU.mult, op1=ALU.add)
                v.tensor_scalar_add(acc[:, c], acc[:, c], cols[:, 32 + c:33 + c])
        kb.phase(vector=p_conv)

        def p_cvt(sc):
            for c in range(CT):
                sc.activation(out=ycb[:, c], in_=acc[:, c], func=AF.Copy)
        kb.phase(scalar=p_cvt)

        b_m, b_vv = kb.n("m"), kb.n("V")

        def p_pw(t, b_vv=b_vv):
            for ee in range(CT):
                if ee >= 2:
                    t.wait_ge(kb.h("V"), b_vv + (ee - 1))
                for dd in range(CT):
                    i = t.matmul(out=pP[ee % 2][:], lhsT=pwb[:, dd, ee], rhs=ycb[:, dd],
                                 start=(dd == 0), stop=(dd == CT - 1))
                kb.inc(i, "m")

        def p_pwe(v, b_m=b_m, t0=t0):
            for ee in range(CT):
                v._eng.wait_ge(kb.h("m"), b_m + ee + 1)
                v.scalar_tensor_tensor(out=xt[:, ee, 3 + t0:3 + t0 + 512],
                                       in0=pP[ee % 2][:], scalar=cols[:, 40 + ee:41 + ee],
                                       in1=xt[:, ee, 3 + t0:3 + t0 + 512],
                                       op0=ALU.add, op1=ALU.add)
        kb.phase(tensor=p_pw, vector=p_pwe)

    def p_stxa(sy):
        for k in range(CT):
            kb.inc(sy.dma_start(out=xa_d[128 * k:128 * (k + 1), :], in_=xt[:, k, 3:2051]),
                   "st", 16)
    kb.phase(sync=p_stxa)

    ov.close()
    ov2 = ExitStack()

    def osb2(name, shape, dtype):
        return ov2.enter_context(kb.sbuf_cm(name, shape, dtype))

    ngb = osb2("ngb", [128, CT, T], BF16)
    whgb = osb2("whgb", [128, 2, 2, 128], BF16)
    hgf = osb2("hgf", [128, 2, T], F32)
    af = osb2("af", [128, T], F32)
    bf2 = osb2("bf2", [128, T], F32)
    tmp = osb2("tmp", [128, T], F32)
    hsc = osb2("hsc", [128, T], F32)
    asc = osb2("asc", [128, T], F32)
    ones2k = osb2("ones2k", [128, T], F32)

    def p_ones(v):
        v.memset(ones2k[:], 1.0)
    kb.phase(vector=p_ones)

    emit_rms(kb, xt, ngb, 3, 0, T, cols[:, 56:64], pA)

    for h4 in range(4):
        for et in range(3):
            j = 3 * h4 + et

            def p_wl(g, h4=h4, et=et):
                for dd in range(2):
                    kb.inc(g.dma_start(out=whgb[:, 0, dd],
                                       in_=whg_d[h4, 128 * dd:128 * (dd + 1),
                                                 128 * et:128 * (et + 1)]), "ld", 16)
                    kb.inc(g.dma_start(out=whgb[:, 1, dd],
                                       in_=whg_d[h4, 128 * dd:128 * (dd + 1),
                                                 128 * (3 + et):128 * (4 + et)]), "ld", 16)
            kb.phase(gpsimd=p_wl)

            b_m, b_sc = kb.n("m"), kb.n("SC")

            def p_mm(t, h4=h4, b_sc=b_sc):
                for tt in range(4):
                    if tt >= 1:
                        t.wait_ge(kb.h("SC"), b_sc + 2 * tt)
                    for hh in range(2):
                        for dd in range(2):
                            i = t.matmul(out=pP[hh][:],
                                         lhsT=whgb[:, hh, dd],
                                         rhs=ngb[:, 2 * h4 + dd, 512 * tt:512 * (tt + 1)],
                                         start=(dd == 0), stop=(dd == 1))
                        kb.inc(i, "m")

            def p_ev(sc, b_m=b_m):
                for tt in range(4):
                    sc._eng.wait_ge(kb.h("m"), b_m + 2 * tt + 2)
                    for hh in range(2):
                        sc.activation(out=hgf[:, hh, 512 * tt:512 * (tt + 1)],
                                      in_=pP[hh][:], func=AF.Copy)
            kb.phase(tensor=p_mm, scalar=p_ev)

            def p_el_s(sc):
                sc.activation(out=af[:], in_=hgf[:, 1], func=AF.Sigmoid, scale=-1.0)
                sc.activation(out=bf2[:], in_=hgf[:, 1], func=AF.Sigmoid)
                sc.activation(out=tmp[:], in_=hgf[:, 0], func=AF.Sigmoid)
            kb.phase(scalar=p_el_s)

            def p_el_v(v):
                v.tensor_scalar_add(hgf[:, 0], hgf[:, 0], 0.5)
                v.tensor_max(out=hgf[:, 0], in0=hgf[:, 0], in1=tmp[:])
                v.tensor_mul(out=bf2[:], in0=bf2[:], in1=hgf[:, 0])
                v.tensor_tensor_scan(out=hsc[:], data0=af[:], data1=bf2[:],
                                     initial=zocol[:, 0:1], op0=ALU.mult, op1=ALU.add)
                v.tensor_tensor_scan(out=asc[:], data0=af[:], data1=ones2k[:],
                                     initial=zocol[:, 1:2], op0=ALU.mult, op1=ALU.mult)
                v.tensor_copy(out=summ[:, 0, j:j + 1], in_=asc[:, T - 1:T])
                v.tensor_copy(out=summ[:, 1, j:j + 1], in_=hsc[:, T - 1:T])
            kb.phase(vector=p_el_v)

            def p_st(sy, j=j):
                kb.inc(sy.dma_start(out=hl_d[:, j], in_=hsc[:]), "st", 16)
                kb.inc(sy.dma_start(out=al_d[:, j], in_=asc[:]), "st", 16)
            kb.phase(sync=p_st)

    def p_fin(sy):
        if "summP" in io:
            kb.inc(sy.dma_start(out=io["summP"], in_=summ[:, 0]), "st", 16)
            kb.inc(sy.dma_start(out=io["summE"], in_=summ[:, 1]), "st", 16)
        else:
            kb.inc(sy.dma_start(out=sm_d, in_=summ[:]), "st", 16)
        sy.wait_ge(kb.h("st"), kb.n("st"))
    kb.phase(sync=p_fin)
    ov2.close()


def build_l1a_post():
    kb = KB()
    io = {"xa": kb.dram_in("xa", [D, T], F32),
          "hl": kb.dram_in("hl", [128, 12, T], F32),
          "al": kb.dram_in("al", [128, 12, T], F32),
          "smP": kb.dram_in("smP", [128, 8, 12], F32),
          "smE": kb.dram_in("smE", [128, 8, 12], F32),
          "wout": kb.dram_in("wout", [12, 128, 256], F32),
          "gff": kb.dram_in("gff", [128, 8], F32),
          "wg_h": kb.dram_in("wg_h", [22, D, 128], F32),
          "wv_h": kb.dram_in("wv_h", [22, D, 128], F32),
          "wf_h": kb.dram_in("wf_h", [8, 2816, 128], F32),
          "yb": kb.dram_out("yb", [D, T], F32)}
    alloc_common(kb)
    kb.sems("ld", "m", "st", "hx")
    kb.start()
    kb.push()
    emit_l1a_post(kb, io)
    kb.pop()
    return kb


def emit_l1a_post(kb, io, summ_ag=None):
    xa_d, hl_d, al_d = io["xa"], io["hl"], io["al"]
    wout_d, gff_d = io["wout"], io["gff"]
    wg_d, wv_d, wf_d = io["wg_h"], io["wv_h"], io["wf_h"]
    yb_d = io["yb"]
    x2_d = kb.dram_int("x2bounce", [D, T], F32)

    xa = kb.sbuf("xa_s", [128, CT, T], F32)
    alloc_ln(kb)
    gffc = kb.sbuf("gffc", [128, 8], F32)
    Hc = kb.sbuf("Hc", [128, 12], F32)
    sm = kb.sbuf("sm", [128, 2, 8, 12], F32)
    wob = kb.sbuf("wob", [128, 12, 256], BF16)
    nfb = kb.sbuf("nfb", [128, CT, T], BF16)

    ov = ExitStack()

    def osb(name, shape, dtype):
        return ov.enter_context(kb.sbuf_cm(name, shape, dtype))

    mcols = kb.sbuf("mcols", [128, 16], F32)
    hlc = osb("hlc", [128, 12, 512], F32)
    alc = osb("alc", [128, 12, 512], F32)
    htb = osb("htb", [128, 12, 512], BF16)

    pA = kb.psum("pA", [128, 512])
    pP = [kb.psum(f"pP{i}", [128, 512]) for i in range(2)]

    def p_load_sy(sy):
        for k in range(CT):
            kb.inc(sy.dma_start(out=xa[:, k], in_=xa_d[128 * k:128 * (k + 1), :]), "ld", 16)
        if summ_ag is None:
            kb.inc(sy.dma_start(out=sm[:, 0], in_=io["smP"]), "ld", 16)
            kb.inc(sy.dma_start(out=sm[:, 1], in_=io["smE"]), "ld", 16)
        else:
            agP, agE, mP_d, mE_d = summ_ag
            for jj in range(8):
                kb.inc(sy.dma_start(out=sm[:, 0, jj], in_=agP[jj]), "ld", 16)
                kb.inc(sy.dma_start(out=sm[:, 1, jj], in_=agE[jj]), "ld", 16)
            kb.inc(sy.dma_start(out=mcols[:, 0:8], in_=mP_d), "ld", 16)
            kb.inc(sy.dma_start(out=mcols[:, 8:16], in_=mE_d), "ld", 16)
        kb.inc(sy.dma_start(out=gffc[:], in_=gff_d), "ld", 16)

    def p_load_g(g):
        load_common(kb, g)
        for j in range(12):
            kb.inc(g.dma_start(out=wob[:, j], in_=wout_d[j]), "ld", 16)
    kb.phase(sync=p_load_sy, gpsimd=p_load_g)

    if summ_ag is not None:
        def p_meff(v):
            for jj in range(8):
                v.tensor_scalar_add(sm[:, 0, jj], sm[:, 0, jj], -1.0)
                v.tensor_scalar_mul(sm[:, 0, jj], sm[:, 0, jj], mcols[:, jj:jj + 1])
                v.tensor_scalar_add(sm[:, 0, jj], sm[:, 0, jj], 1.0)
                v.tensor_scalar_mul(sm[:, 1, jj], sm[:, 1, jj], mcols[:, 8 + jj:9 + jj])
        kb.phase(vector=p_meff)

    def p_comb(v):
        v.tensor_scalar(out=gffc[:], in0=gffc[:], scalar1=1.0, scalar2=32.0,
                        op0=ALU.add, op1=ALU.mult)
        v.memset(Hc[:], 0.0)
        for jj in range(8):
            v.tensor_mul(out=Hc[:], in0=Hc[:], in1=sm[:, 0, jj])
            v.tensor_add(out=Hc[:], in0=Hc[:], in1=sm[:, 1, jj])
    kb.phase(vector=p_comb)

    for tt in range(4):
        ts = slice(512 * tt, 512 * (tt + 1))

        def p_lh(sy, ts=ts):
            for j in range(12):
                kb.inc(sy.dma_start(out=hlc[:, j], in_=hl_d[:, j, ts]), "ld", 16)
                kb.inc(sy.dma_start(out=alc[:, j], in_=al_d[:, j, ts]), "ld", 16)
        kb.phase(sync=p_lh)

        def p_ht(v):
            for j in range(12):
                v.scalar_tensor_tensor(out=htb[:, j], in0=alc[:, j], scalar=Hc[:, j:j + 1],
                                       in1=hlc[:, j], op0=ALU.mult, op1=ALU.add)
        kb.phase(vector=p_ht)

        b_m, b_vv = kb.n("m"), kb.n("V")

        def p_go(t, b_vv=b_vv):
            for och in range(CT):
                if och >= 2:
                    t.wait_ge(kb.h("V"), b_vv + (och - 1))
                h4, oo = och // 2, och % 2
                for et in range(3):
                    i = t.matmul(out=pP[och % 2][:],
                                 lhsT=wob[:, 3 * h4 + et, 128 * oo:128 * (oo + 1)],
                                 rhs=htb[:, 3 * h4 + et],
                                 start=(et == 0), stop=(et == 2))
                kb.inc(i, "m")

        def p_goe(v, b_m=b_m, ts=ts):
            for och in range(CT):
                v._eng.wait_ge(kb.h("m"), b_m + och + 1)
                v.tensor_add(out=xa[:, och, ts], in0=pP[och % 2][:], in1=xa[:, och, ts])
        kb.phase(tensor=p_go, vector=p_goe)

    def p_stx2(sy):
        for k in range(CT):
            kb.inc(sy.dma_start(out=x2_d[128 * k:128 * (k + 1), :], in_=xa[:, k]), "hx", 16)
        sy.wait_ge(kb.h("hx"), kb.n("hx"))
    kb.phase(sync=p_stx2)

    emit_rms(kb, xa, nfb, 0, 0, T, gffc[:, 0:8], pA)

    ov.close()
    ov2 = ExitStack()

    def osb2(name, shape, dtype):
        return ov2.enter_context(kb.sbuf_cm(name, shape, dtype))

    x2c = osb2("x2c", [128, CT, 512], F32)
    gv = osb2("gv", [128, 22, 512], BF16)
    sgf = osb2("sgf", [128, 512], F32)
    svf = osb2("svf", [128, 512], F32)
    wgv = osb2("wgv", [128, 2, 2, CT, 128], BF16)
    wfb = osb2("wfb", [128, 2, 22, 128], BF16)
    yb = osb2("yb_s", [128, CT, 512], F32)

    pGg = kb.psum("pGg", [128, 512])
    pGv = kb.psum("pGv", [128, 512])
    pF = [kb.psum(f"pF{i}", [128, 512]) for i in range(2)]

    for tt in range(4):
        ts = slice(512 * tt, 512 * (tt + 1))

        def p_lx2(sy, ts=ts):
            for k in range(CT):
                kb.inc(sy.dma_start(out=x2c[:, k], in_=x2_d[128 * k:128 * (k + 1), ts]),
                       "ld", 16)
        kb.phase(sync=p_lx2)

        b_ld, b_m, b_sc, b_vv = kb.n("ld"), kb.n("m"), kb.n("SC"), kb.n("V")

        def p_wstr(g, b_m=b_m):
            for ft in range(22):
                pp = ft % 2
                if ft >= 2:
                    g._eng.wait_ge(kb.h("m"), b_m + 2 * (ft - 1))
                for dd in range(CT):
                    kb.inc(g.dma_start(out=wgv[:, pp, 0, dd],
                                       in_=wg_d[ft, 128 * dd:128 * (dd + 1), :]), "ld", 16)
                    kb.inc(g.dma_start(out=wgv[:, pp, 1, dd],
                                       in_=wv_d[ft, 128 * dd:128 * (dd + 1), :]), "ld", 16)

        def p_gvt(t, b_ld=b_ld, b_sc=b_sc, ts=ts):
            for ft in range(22):
                pp = ft % 2
                t.wait_ge(kb.h("ld"), b_ld + 16 * 16 * (ft + 1))
                if ft >= 1:
                    t.wait_ge(kb.h("SC"), b_sc + 2 * ft)
                for dd in range(CT):
                    i = t.matmul(out=pGg[:], lhsT=wgv[:, pp, 0, dd], rhs=nfb[:, dd, ts],
                                 start=(dd == 0), stop=(dd == CT - 1))
                kb.inc(i, "m")
                for dd in range(CT):
                    i = t.matmul(out=pGv[:], lhsT=wgv[:, pp, 1, dd], rhs=nfb[:, dd, ts],
                                 start=(dd == 0), stop=(dd == CT - 1))
                kb.inc(i, "m")

        def p_gvs(sc, b_m=b_m, b_vv=b_vv):
            for ft in range(22):
                sc._eng.wait_ge(kb.h("m"), b_m + 2 * ft + 1)
                if ft >= 1:
                    sc._eng.wait_ge(kb.h("V"), b_vv + ft)
                sc.activation(out=sgf[:], in_=pGg[:], func=AF.Silu)
                sc._eng.wait_ge(kb.h("m"), b_m + 2 * ft + 2)
                sc.activation(out=svf[:], in_=pGv[:], func=AF.Copy)

        def p_gvv(v, b_sc=b_sc):
            for ft in range(22):
                v._eng.wait_ge(kb.h("SC"), b_sc + 2 * ft + 2)
                v.tensor_mul(out=gv[:, ft], in0=sgf[:], in1=svf[:])
        kb.phase(gpsimd=p_wstr, tensor=p_gvt, scalar=p_gvs, vector=p_gvv)

        b_ld2, b_m2, b_vv2 = kb.n("ld"), kb.n("m"), kb.n("V")

        def p_fstr(g, b_m2=b_m2):
            for ot in range(CT):
                pp = ot % 2
                if ot >= 2:
                    g._eng.wait_ge(kb.h("m"), b_m2 + (ot - 1))
                for ft in range(22):
                    kb.inc(g.dma_start(out=wfb[:, pp, ft],
                                       in_=wf_d[ot, 128 * ft:128 * (ft + 1), :]), "ld", 16)

        def p_ft(t, b_ld2=b_ld2, b_vv2=b_vv2):
            for ot in range(CT):
                pp = ot % 2
                t.wait_ge(kb.h("ld"), b_ld2 + 16 * 22 * (ot + 1))
                if ot >= 2:
                    t.wait_ge(kb.h("V"), b_vv2 + (ot - 1))
                for ft in range(22):
                    i = t.matmul(out=pF[pp][:], lhsT=wfb[:, pp, ft], rhs=gv[:, ft],
                                 start=(ft == 0), stop=(ft == 21))
                kb.inc(i, "m")

        def p_fe(v, b_m2=b_m2):
            for ot in range(CT):
                v._eng.wait_ge(kb.h("m"), b_m2 + ot + 1)
                v.tensor_add(out=yb[:, ot], in0=pF[ot % 2][:], in1=x2c[:, ot])
        kb.phase(gpsimd=p_fstr, tensor=p_ft, vector=p_fe)

        def p_sty(sy, ts=ts):
            for k in range(CT):
                kb.inc(sy.dma_start(out=yb_d[128 * k:128 * (k + 1), ts], in_=yb[:, k]),
                       "st", 16)
        kb.phase(sync=p_sty)

    def p_fin(sy):
        sy.wait_ge(kb.h("st"), kb.n("st"))
    kb.phase(sync=p_fin)
    ov2.close()


def build_addln(with_add):
    kb = KB()
    io = {"a": kb.dram_in("a", [D, T], F32),
          "yln": kb.dram_out("yln", [D, T], F32)}
    if with_add:
        io["b"] = kb.dram_in("b", [D, T], F32)
        io["ysum"] = kb.dram_out("ysum", [D, T], F32)
    alloc_common(kb)
    kb.sems("ld", "m", "st")
    kb.start()
    kb.push()
    emit_addln(kb, io, with_add)
    kb.pop()
    return kb


def emit_addln(kb, io, with_add, W=T):
    a_d, yln = io["a"], io["yln"]
    xt = kb.sbuf("xt", [128, CT, W], F32)
    xnb = kb.sbuf("xnb", [128, CT, W], F32)
    alloc_ln(kb)
    pA = kb.psum("pA", [128, 512])
    pB = kb.psum("pB", [128, 512])

    def p_load_sy(sy):
        for k in range(CT):
            kb.inc(sy.dma_start(out=xt[:, k], in_=a_d[128 * k:128 * (k + 1), :]), "ld", 16)

    def p_load_g(g):
        load_common(kb, g)
        if with_add:
            for k in range(CT):
                kb.inc(g.dma_start(out=xnb[:, k],
                                   in_=io["b"][128 * k:128 * (k + 1), :]), "ld", 16)
    kb.phase(sync=p_load_sy, gpsimd=p_load_g)

    if with_add:
        def p_add(v):
            for k in range(CT):
                v.tensor_add(out=xt[:, k], in0=xt[:, k], in1=xnb[:, k])
        kb.phase(vector=p_add)

        def p_st1(sy):
            for k in range(CT):
                kb.inc(sy.dma_start(out=io["ysum"][128 * k:128 * (k + 1), :], in_=xt[:, k]),
                       "st", 16)
        kb.phase(sync=p_st1)

    emit_ln(kb, xt, xnb, 0, 0, W, pA, pB)

    def p_st2(sy):
        for k in range(CT):
            kb.inc(sy.dma_start(out=yln[128 * k:128 * (k + 1), :], in_=xnb[:, k]),
                   "st", 16)
        sy.wait_ge(kb.h("st"), kb.n("st"))
    kb.phase(sync=p_st2)


def _rotary_tables(pos):
    DH, WIN = 64, 256
    half = np.arange(0, DH, 2, dtype=np.float64)
    inv_freq = 1.0 / 10000.0 ** (half / DH)
    fr = pos[:, None] * inv_freq[None, :]
    freqs = np.concatenate([fr, fr], -1)
    base = (half + 0.4 * DH) / (1.4 * DH)
    power = (pos - S // 2) / (WIN // 2)
    sc = base[None, :] ** power[:, None]
    scale = np.concatenate([sc, sc], -1)
    cos, sin = np.cos(freqs), np.sin(freqs)
    sign = np.where(np.arange(64) < 32, -1.0, 1.0)[None, :]
    perm64 = np.concatenate([np.arange(32, 64), np.arange(0, 32)])
    out = []
    for s_, f in ((scale * 8.0, None), (1.0 / scale, None)):
        cs = (cos * s_).T
        sn = (sin * s_ * sign).T
        snp = sn[perm64]
        out.append(np.concatenate([cs, cs], 0).astype(BF))
        out.append(np.concatenate([snp, snp], 0).astype(BF))
    return out


def _f0_masks(s):
    m = np.zeros((4, 128, 384), dtype=BF)
    for mi in range(4):
        ig = s - 128 + 128 * mi + np.arange(128)[:, None]
        jg = s - 384 + 128 * mi + np.arange(384)[None, :]
        ok = (ig - jg >= 0) & (ig - jg <= 256) & (jg >= 0)
        ok = ok | (ig < 0)
        m[mi] = ok.astype(BF)
    return m


def _prep_f0_layer(inputs, li):
    Wqkv = inputs["L0_ffn_Wqkv"][li]
    wq = np.ascontiguousarray(Wqkv[:, 0:1024])
    wk = np.ascontiguousarray(Wqkv[:, 1024:2048])
    wv = np.ascontiguousarray(Wqkv[:, 2048:3072])
    qs = inputs["L0_ffn_q_scale"][li].astype(np.float32)
    ks = inputs["L0_ffn_k_scale"][li].astype(np.float32)
    qsp = np.concatenate([qs[32:], qs[:32]])
    ksp = np.concatenate([ks[32:], ks[:32]])
    wo = inputs["L0_ffn_Wo"][li]
    W1 = inputs["L0_ffn_W1"][li]
    b1 = inputs["L0_ffn_b1"][li]
    W2 = inputs["L0_ffn_W2"][li]
    b2 = inputs["L0_ffn_b2"][li]
    a, g = W1[:, 0:2730], W1[:, 2730:5460]
    w1p = np.zeros((44, D, 128), np.float32)
    b1a = np.zeros((128, 22), np.float32)
    b1g = np.zeros((128, 22), np.float32)
    for p in range(22):
        c0, c1 = 128 * p, min(128 * (p + 1), 2730)
        w1p[2 * p][:, 0:c1 - c0] = a[:, c0:c1]
        w1p[2 * p + 1][:, 0:c1 - c0] = g[:, c0:c1]
        b1a[0:c1 - c0, p] = b1[c0:c1]
        b1g[0:c1 - c0, p] = b1[2730 + c0:2730 + c1]
    w2pad = np.zeros((2816, D), np.float32)
    w2pad[0:2730] = W2
    w2po = np.ascontiguousarray(
        w2pad.reshape(2816, 8, 128).transpose(1, 0, 2))
    return {
        "wq": wq, "wk": wk, "wv": wv, "wo": wo,
        "qs_col": np.tile(qs, 2).reshape(128, 1),
        "ks_col": np.tile(ks, 2).reshape(128, 1),
        "qsp_col": np.tile(qsp, 2).reshape(128, 1),
        "ksp_col": np.tile(ksp, 2).reshape(128, 1),
        "w1p": w1p, "w2po": w2po,
        "g_col": np.ascontiguousarray(
            inputs["L0_ffn_ln_g"][li].reshape(8, 128).T.astype(np.float32)),
        "b_col": np.ascontiguousarray(
            inputs["L0_ffn_ln_b"][li].reshape(8, 128).T.astype(np.float32)),
        "b1a_col": b1a, "b1g_col": b1g,
        "b2_col": np.ascontiguousarray(b2.reshape(8, 128).T.astype(np.float32)),
    }


def build_moe_probs():
    kb = KB()
    io = {"xh": kb.dram_in("xh", [D, T], F32),
          "gw": kb.dram_in("gw", [D, 8], F32),
          "pr": kb.dram_out("pr", [128, 16, 8], F32)}
    alloc_common(kb)
    kb.sems("ld", "m", "st")
    kb.start()
    kb.push()
    emit_moe_probs(kb, io)
    kb.pop()
    return kb


def emit_moe_probs(kb, io):
    xh, gw_d, pr_d = io["xh"], io["gw"], io["pr"]
    xt = kb.sbuf("xt", [128, CT, T], F32)
    gwb = kb.sbuf("gwb", [128, CT, 8], F32)
    prb = kb.sbuf("prb", [128, 16, 8], F32)
    nmx = kb.sbuf("nmx", [128, 2], F32)
    rs = kb.sbuf("rs", [128, 2], F32)
    pS = [kb.psum(f"pS{i}", [128, 8]) for i in range(2)]

    def p_load(sy):
        for k in range(CT):
            kb.inc(sy.dma_start(out=xt[:, k], in_=xh[128 * k:128 * (k + 1), :]), "ld", 16)
            kb.inc(sy.dma_start(out=gwb[:, k], in_=gw_d[128 * k:128 * (k + 1), :]), "ld", 16)

    def p_load_g(g):
        load_common(kb, g)
    kb.phase(sync=p_load, gpsimd=p_load_g)

    b_m, b_sc = kb.n("m"), kb.n("SC")

    def p_mm(t, b_sc=b_sc):
        for lt in range(16):
            if lt >= 2:
                t.wait_ge(kb.h("SC"), b_sc + (lt - 1))
            for dd in range(CT):
                i = t.matmul(out=pS[lt % 2][:], lhsT=xt[:, dd, 128 * lt:128 * (lt + 1)],
                             rhs=gwb[:, dd], start=(dd == 0), stop=(dd == CT - 1))
            kb.inc(i, "m")

    b_vv = kb.n("V")

    def p_sm_v(v, b_m=b_m, b_sc=b_sc):
        for lt in range(16):
            pp = lt % 2
            v._eng.wait_ge(kb.h("m"), b_m + lt + 1)
            if lt >= 2:
                v._eng.wait_ge(kb.h("SC"), b_sc + (lt - 1))
            v.reduce_max(out=nmx[:, pp:pp + 1], in_=pS[pp][:], axis=AX.X, negate=True)
            v._eng.wait_ge(kb.h("SC"), b_sc + lt + 1)
            v.reciprocal(out=rs[:, pp:pp + 1], in_=rs[:, pp:pp + 1])
            v.tensor_scalar_mul(prb[:, lt], prb[:, lt], rs[:, pp:pp + 1])

    def p_sm_s(sc, b_vv=b_vv):
        for lt in range(16):
            pp = lt % 2
            sc._eng.wait_ge(kb.h("V"), b_vv + 3 * lt + 1)
            sc.activation(out=prb[:, lt], in_=pS[pp][:], func=AF.Exp,
                          bias=nmx[:, pp:pp + 1], accum_out=rs[:, pp:pp + 1])
    kb.phase(tensor=p_mm, vector=p_sm_v, scalar=p_sm_s)

    def p_st(sy):
        kb.inc(sy.dma_start(out=pr_d, in_=prb[:]), "st", 16)
        sy.wait_ge(kb.h("st"), kb.n("st"))
    kb.phase(sync=p_st)


def build_moe_main(niter=45):
    kb = KB()
    io = {"prA": kb.dram_in("prA", [128, 8, 128], F32),
          "xh": kb.dram_in("xh", [D, T], F32),
          "y3": kb.dram_in("y3", [D, T], F32),
          "scol": kb.dram_in("scol", [128, 16], F32),
          "w1": kb.dram_in("w1", [8, D, 4096], F32),
          "w2": kb.dram_in("w2", [8, 4096, D], F32),
          "yo": kb.dram_out("yo", [D, T], F32),
          "gates_o": kb.dram_out("gates_o", [128, 8, 128], F32),
          "grow_o": kb.dram_out("grow_o", [8, T], F32)}
    alloc_common(kb)
    kb.sems("ld", "m", "st")
    kb.start()
    kb.push()
    emit_moe_main(kb, io)
    kb.pop()
    return kb


def emit_moe_main(kb, io, niter=45, prall=None):
    CAP = 4096.0
    xh, y3_d, sc_d = io["xh"], io["y3"], io["scol"]
    w1_d, w2_d, out_d = io["w1"], io["w2"], io["yo"]
    gates_d, grow_d = io.get("gates_o"), io.get("grow_o")

    d_identf = kb.cin("identf")
    d_ltril = kb.cin("ltrilf")
    d_onesf8 = kb.cin("onesf8")
    d_ident8 = kb.cin("ident8f")
    d_onescolf = kb.cin("onescolf")
    d_onesrowf = kb.cin("onesrow_f")
    d_oh8 = kb.cin("oh8f")

    xnb = kb.sbuf("xnb", [128, CT, T], BF16)
    if prall is None:
        prA = kb.sbuf("prA_s", [128, 8, 128], F32)

        def pre(e):
            return prA[:, e]
    else:
        prB = kb.sbuf("prB_s", [128, 128, 8], F32)

        def pre(e):
            return prB[:, :, e]
    gates = kb.sbuf("gates", [128, 8, 128], F32)
    mgt = kb.sbuf("mgt", [128, 8, 128], F32)
    msk = kb.sbuf("msk", [128, 8, 128], F32)
    eqb = kb.sbuf("eqb", [128, 128], F32)
    cnts = kb.sbuf("cnts", [128, 8], F32)
    candm = kb.sbuf("candm", [128, 8], F32)
    rows = kb.sbuf("rows", [1, 64], F32)
    zero1 = kb.sbuf("zero1", [1, 1], F32)
    bc = kb.sbuf("bc", [128, 24], F32)
    vcol = kb.sbuf("vcol", [8, 1], F32)
    d8 = kb.sbuf("d8", [8, 8], F32)
    tident = kb.sbuf("tident", [128, 128], F32)
    tltril = kb.sbuf("tltril", [128, 128], F32)
    tonesf8 = kb.sbuf("tonesf8", [8, 128], F32)
    tident8 = kb.sbuf("tident8", [8, 8], F32)
    tonescolf = kb.sbuf("tonescolf", [128, 1], F32)
    tonesrowf = kb.sbuf("tonesrowf", [1, 128], F32)
    toh8 = kb.sbuf("toh8", [8, 8, 128], F32)
    scol = kb.sbuf("scol_s", [128, 16], F32)
    rank = kb.sbuf("rank", [128, 128], F32)
    gts = kb.sbuf("gts", [128, 128], F32)
    ownb = kb.sbuf("ownb", [128, 16], F32)
    ows = kb.sbuf("ows", [16, 128], F32)
    prow = kb.sbuf("prow", [1, 256], F32)
    grow = kb.sbuf("grow", [8, T], F32)

    ov0 = ExitStack()
    xtf = ov0.enter_context(kb.sbuf_cm("xtf", [128, CT, T], F32))

    pH = [kb.psum(f"pH{i}", [128, 512]) for i in range(2)]
    pW = [kb.psum(f"pW{i}", [128, 512]) for i in range(2)]
    pB = kb.psum("pB", [128, 512])
    pT = kb.psum("pT", [1, 512])
    pX = kb.psum("pX", [128, 128])
    pO = kb.psum("pO", [128, 128])

    def p_load(sy):
        for k in range(CT):
            kb.inc(sy.dma_start(out=xtf[:, k], in_=xh[128 * k:128 * (k + 1), :]), "ld", 16)
        if prall is None:
            kb.inc(sy.dma_start(out=prA[:], in_=io["prA"]), "ld", 16)
        else:
            for c in range(NCORE):
                kb.inc(sy.dma_start(out=prB[:, 16 * c:16 * (c + 1), :], in_=prall[c]),
                       "ld", 16)
        kb.inc(sy.dma_start(out=scol[:], in_=sc_d), "ld", 16)

    def p_load_g(g):
        load_common(kb, g)
        kb.inc(g.dma_start(out=tident[:], in_=d_identf), "ld", 16)
        kb.inc(g.dma_start(out=tltril[:], in_=d_ltril), "ld", 16)
        kb.inc(g.dma_start(out=tonesf8[:], in_=d_onesf8), "ld", 16)
        kb.inc(g.dma_start(out=tident8[:], in_=d_ident8), "ld", 16)
        kb.inc(g.dma_start(out=tonescolf[:], in_=d_onescolf), "ld", 16)
        kb.inc(g.dma_start(out=tonesrowf[:], in_=d_onesrowf), "ld", 16)
        kb.inc(g.dma_start(out=toh8[:], in_=d_oh8), "ld", 16)
    kb.phase(sync=p_load, gpsimd=p_load_g)

    def p_cast(sc):
        for k in range(CT):
            sc.activation(out=xnb[:, k], in_=xtf[:, k], func=AF.Copy)
    kb.phase(scalar=p_cast)

    def p_init(v):
        v.memset(rows[0:1, 0:8], 0.5)
        v.memset(rows[0:1, 8:16], 0.0)
        v.memset(rows[0:1, 16:24], 1.0)
        v.memset(zero1[:], 0.0)
    kb.phase(vector=p_init)

    for it in range(niter):
        delta = 0.25 * (0.5 ** it)

        def p_bc(t):
            i = t.matmul(out=pB[:, 0:8], lhsT=tonesrowf[:], rhs=rows[0:1, 0:8],
                         start=True, stop=True)
            kb.inc(i, "m")
        kb.phase(tensor=p_bc)

        def p_msk(v):
            v.tensor_copy(out=bc[:, 0:8], in_=pB[:, 0:8])
            for e in range(8):
                v.tensor_scalar(out=msk[:, e], in0=pre(e), scalar1=bc[:, e:e + 1],
                                scalar2=0.0, op0=ALU.is_gt, op1=ALU.bypass)
                v.reduce_sum(out=cnts[:, e:e + 1], in_=msk[:, e], axis=AX.X)
        kb.phase(vector=p_msk)

        def p_tot(t):
            i = t.matmul(out=pT[0:1, 0:8], lhsT=tonescolf[:], rhs=cnts[:],
                         start=True, stop=True)
            kb.inc(i, "m")
        kb.phase(tensor=p_tot)

        def p_upd(v, delta=delta):
            v.tensor_scalar(out=rows[0:1, 24:32], in0=pT[0:1, 0:8],
                            scalar1=CAP - 0.5, scalar2=0.0, op0=ALU.is_gt, op1=ALU.bypass)
            v.tensor_mul(out=rows[0:1, 32:40], in0=rows[0:1, 24:32], in1=rows[0:1, 0:8])
            v.tensor_max(out=rows[0:1, 8:16], in0=rows[0:1, 8:16], in1=rows[0:1, 32:40])
            v.tensor_scalar(out=rows[0:1, 40:48], in0=rows[0:1, 24:32], scalar1=2.0,
                            scalar2=0.0, op0=ALU.mult, op1=ALU.bypass)
            v.tensor_add(out=rows[0:1, 40:48], in0=rows[0:1, 40:48], in1=rows[0:1, 0:8])
            v.tensor_tensor(out=rows[0:1, 16:24], in0=rows[0:1, 16:24],
                            in1=rows[0:1, 40:48], op=ALU.min)
            v.tensor_scalar(out=rows[0:1, 32:40], in0=rows[0:1, 24:32], scalar1=2.0,
                            scalar2=-1.0, op0=ALU.mult, op1=ALU.add)
            v.tensor_scalar(out=rows[0:1, 32:40], in0=rows[0:1, 32:40],
                            scalar1=float(delta), scalar2=0.0, op0=ALU.mult, op1=ALU.bypass)
            v.tensor_add(out=rows[0:1, 0:8], in0=rows[0:1, 0:8], in1=rows[0:1, 32:40])
        kb.phase(vector=p_upd)

    def p_bc2(t):
        i = t.matmul(out=pB[:, 0:16], lhsT=tonesrowf[:], rhs=rows[0:1, 8:24],
                     start=True, stop=True)
        kb.inc(i, "m")
    kb.phase(tensor=p_bc2)

    def p_cand(v):
        v.tensor_copy(out=bc[:, 0:16], in_=pB[:, 0:16])
        for e in range(8):
            v.tensor_scalar(out=msk[:, e], in0=pre(e), scalar1=bc[:, e:e + 1],
                            scalar2=0.0, op0=ALU.is_gt, op1=ALU.bypass)
            v.tensor_scalar(out=gates[:, e], in0=pre(e), scalar1=bc[:, 8 + e:9 + e],
                            scalar2=0.0, op0=ALU.is_gt, op1=ALU.bypass)
            v.tensor_sub(out=msk[:, e], in0=msk[:, e], in1=gates[:, e])
            v.tensor_mul(out=msk[:, e], in0=msk[:, e], in1=pre(e))
            v.reduce_max(out=candm[:, e:e + 1], in_=msk[:, e], axis=AX.X)
    kb.phase(vector=p_cand)

    def p_tr1(t):
        i = t.transpose(out=pX[0:8, :], in_=candm[:], identity=tident[:])
        kb.inc(i, "m")
    kb.phase(tensor=p_tr1)

    def p_vst(v):
        v.reduce_max(out=vcol[:], in_=pX[0:8, 0:128], axis=AX.X)
        v.tensor_scalar_mul(d8[:], tident8[:], vcol[0:8, 0:1])
    kb.phase(vector=p_vst)

    def p_bc3(t):
        i = t.matmul(out=pB[:, 0:8], lhsT=tonesf8[:], rhs=d8[:], start=True, stop=True)
        kb.inc(i, "m")
    kb.phase(tensor=p_bc3)

    def p_ngt(v):
        v.tensor_copy(out=bc[:, 8:16], in_=pB[:, 0:8])
        for e in range(8):
            v.tensor_scalar(out=mgt[:, e], in0=pre(e), scalar1=bc[:, 8 + e:9 + e],
                            scalar2=0.0, op0=ALU.is_gt, op1=ALU.bypass)
            v.reduce_sum(out=cnts[:, e:e + 1], in_=mgt[:, e], axis=AX.X)
    kb.phase(vector=p_ngt)

    def p_tot2(t):
        i = t.matmul(out=pT[0:1, 0:8], lhsT=tonescolf[:], rhs=cnts[:],
                     start=True, stop=True)
        kb.inc(i, "m")
    kb.phase(tensor=p_tot2)

    def p_r(v):
        v.tensor_scalar(out=rows[0:1, 48:56], in0=pT[0:1, 0:8], scalar1=-1.0,
                        scalar2=CAP, op0=ALU.mult, op1=ALU.add)
    kb.phase(vector=p_r)

    def p_bc4(t):
        i = t.matmul(out=pB[:, 0:8], lhsT=tonesrowf[:], rhs=rows[0:1, 48:56],
                     start=True, stop=True)
        kb.inc(i, "m")
    kb.phase(tensor=p_bc4)

    def p_rbc(v):
        v.tensor_copy(out=bc[:, 16:24], in_=pB[:, 0:8])
    kb.phase(vector=p_rbc)

    for e in range(8):
        def p_eq(v, e=e):
            v.tensor_scalar(out=eqb[:], in0=pre(e), scalar1=bc[:, 8 + e:9 + e],
                            scalar2=0.0, op0=ALU.is_equal, op1=ALU.bypass)
        kb.phase(vector=p_eq)

        def p_pfx(t):
            i = t.matmul(out=pT[0:1, 0:128], lhsT=tonescolf[:], rhs=eqb[:],
                         start=True, stop=True)
            kb.inc(i, "m")
            i = t.matmul(out=pX[:], lhsT=tltril[:], rhs=eqb[:], start=True, stop=True)
            kb.inc(i, "m")
        kb.phase(tensor=p_pfx)

        def p_scan(v):
            v.tensor_copy(out=prow[0:1, 0:128], in_=pT[0:1, 0:128])
            v.tensor_tensor_scan(out=prow[0:1, 128:256], data0=tonesrowf[0:1, :],
                                 data1=prow[0:1, 0:128], initial=zero1[:],
                                 op0=ALU.mult, op1=ALU.add)
            v.tensor_sub(out=prow[0:1, 0:128], in0=prow[0:1, 128:256],
                         in1=prow[0:1, 0:128])
        kb.phase(vector=p_scan)

        def p_pcb(t):
            i = t.matmul(out=pB[:, 0:128], lhsT=tonesrowf[:], rhs=prow[0:1, 0:128],
                         start=True, stop=True)
            kb.inc(i, "m")
        kb.phase(tensor=p_pcb)

        def p_gate(v, e=e):
            v.tensor_copy(out=rank[:], in_=pX[:])
            v.tensor_add(out=rank[:], in0=rank[:], in1=pB[:, 0:128])
            v.tensor_scalar(out=rank[:], in0=rank[:], scalar1=bc[:, 16 + e:17 + e],
                            scalar2=-1.0, op0=ALU.subtract, op1=ALU.mult)
            v.tensor_scalar(out=rank[:], in0=rank[:], scalar1=0.5, scalar2=0.0, op0=ALU.is_gt, op1=ALU.bypass)
            v.tensor_mul(out=rank[:], in0=rank[:], in1=eqb[:])
            v.tensor_add(out=rank[:], in0=rank[:], in1=mgt[:, e])
            v.tensor_mul(out=gates[:, e], in0=rank[:], in1=pre(e))
        kb.phase(vector=p_gate)

        def p_tr2(t, e=e):
            i = t.transpose(out=pX[:], in_=gates[:, e], identity=tident[:])
            kb.inc(i, "m")
        kb.phase(tensor=p_tr2)

        def p_cp1(v):
            v.tensor_copy(out=gts[:], in_=pX[:])
        kb.phase(vector=p_cp1)

        def p_own(t):
            i = t.matmul(out=pO[:, 0:16], lhsT=gts[:], rhs=scol[:], start=True, stop=True)
            kb.inc(i, "m")
        kb.phase(tensor=p_own)

        def p_cp2(v):
            v.tensor_copy(out=ownb[:], in_=pO[:, 0:16])
        kb.phase(vector=p_cp2)

        def p_tr3(t):
            i = t.transpose(out=pO[0:16, 0:128], in_=ownb[:], identity=tident[:])
            kb.inc(i, "m")
        kb.phase(tensor=p_tr3)

        def p_cp3(v):
            v.tensor_copy(out=ows[:], in_=pO[0:16, 0:128])
        kb.phase(vector=p_cp3)

        def p_gr(sy, e=e):
            for lt in range(16):
                kb.inc(sy.dma_start(out=grow[e:e + 1, 128 * lt:128 * (lt + 1)],
                                    in_=ows[lt:lt + 1, :]), "st", 16)
            sy.wait_ge(kb.h("st"), kb.n("st"))
        kb.phase(sync=p_gr)

    if gates_d is not None:
        def p_dbg(sy):
            kb.inc(sy.dma_start(out=gates_d, in_=gates[:]), "st", 16)
            kb.inc(sy.dma_start(out=grow_d, in_=grow[:]), "st", 16)
        kb.phase(sync=p_dbg)

    ov0.close()
    ov1 = ExitStack()

    def osb(name, shape, dtype):
        return ov1.enter_context(kb.sbuf_cm(name, shape, dtype))

    hb = osb("hb", [128, 32, 1024], BF16)
    w1b = osb("w1b", [128, 2, CT, 128], BF16)
    w2b = osb("w2b", [128, 2, 32, 128], BF16)
    gbc = osb("gbc", [128, 1024], F32)
    yacc = osb("yacc", [128, CT, 1024], F32)
    ytmp = osb("ytmp", [128, 512], F32)
    y3c = osb("y3c", [128, CT, 512], F32)

    for ch in range(2):
        def p_z(v):
            for k in range(CT):
                v.memset(yacc[:, k], 0.0)
        kb.phase(vector=p_z)

        for e in range(8):
            b_m = kb.n("m")
            b_vg = kb.n("V")

            def p_gb(t, e=e, ch=ch, b_vg=b_vg):
                for half in range(2):
                    if half == 1:
                        t.wait_ge(kb.h("V"), b_vg + 1)
                    i = t.matmul(out=pB[:],
                                 lhsT=toh8[:, e], rhs=grow[0:8, 1024 * ch + 512 * half:
                                                           1024 * ch + 512 * (half + 1)],
                                 start=True, stop=True)
                    kb.inc(i, "m")

            def p_gbe(v, b_m=b_m):
                for half in range(2):
                    v._eng.wait_ge(kb.h("m"), b_m + half + 1)
                    v.tensor_copy(out=gbc[:, 512 * half:512 * (half + 1)], in_=pB[:])
            kb.phase(tensor=p_gb, vector=p_gbe)

            b_ld, b_m2, b_sc = kb.n("ld"), kb.n("m"), kb.n("SC")

            def p_w1s(g, e=e, b_m2=b_m2):
                for ht in range(32):
                    if ht >= 2:
                        g._eng.wait_ge(kb.h("m"), b_m2 + 2 * (ht - 1))
                    for dd in range(CT):
                        kb.inc(g.dma_start(out=w1b[:, ht % 2, dd],
                                           in_=w1_d[e, 128 * dd:128 * (dd + 1),
                                                    128 * ht:128 * (ht + 1)]), "ld", 16)

            def p_hm(t, ch=ch, b_ld=b_ld, b_sc=b_sc):
                for ht in range(32):
                    t.wait_ge(kb.h("ld"), b_ld + 16 * 8 * (ht + 1))
                    for half in range(2):
                        u = 2 * ht + half
                        if u >= 2:
                            t.wait_ge(kb.h("SC"), b_sc + u - 1)
                        for dd in range(CT):
                            i = t.matmul(out=pH[u % 2][:], lhsT=w1b[:, ht % 2, dd],
                                         rhs=xnb[:, dd, 1024 * ch + 512 * half:
                                                 1024 * ch + 512 * (half + 1)],
                                         start=(dd == 0), stop=(dd == CT - 1))
                        kb.inc(i, "m")

            def p_he(sc, b_m2=b_m2):
                for ht in range(32):
                    for half in range(2):
                        u = 2 * ht + half
                        sc._eng.wait_ge(kb.h("m"), b_m2 + u + 1)
                        sc.activation(out=hb[:, ht, 512 * half:512 * (half + 1)],
                                      in_=pH[u % 2][:], func=AF.Gelu)
            kb.phase(gpsimd=p_w1s, tensor=p_hm, scalar=p_he)

            b_ld3, b_m3, b_vv3 = kb.n("ld"), kb.n("m"), kb.n("V")

            def p_w2s(g, e=e, b_m3=b_m3):
                for ot in range(CT):
                    if ot >= 2:
                        g._eng.wait_ge(kb.h("m"), b_m3 + 2 * (ot - 1))
                    for ht in range(32):
                        kb.inc(g.dma_start(out=w2b[:, ot % 2, ht],
                                           in_=w2_d[e, 128 * ht:128 * (ht + 1),
                                                    128 * ot:128 * (ot + 1)]), "ld", 16)

            def p_wm(t, b_ld3=b_ld3, b_vv3=b_vv3):
                for ot in range(CT):
                    t.wait_ge(kb.h("ld"), b_ld3 + 16 * 32 * (ot + 1))
                    for half in range(2):
                        u = 2 * ot + half
                        if u >= 2:
                            t.wait_ge(kb.h("V"), b_vv3 + 2 * (u - 1))
                        for ht in range(32):
                            i = t.matmul(out=pW[u % 2][:], lhsT=w2b[:, ot % 2, ht],
                                         rhs=hb[:, ht, 512 * half:512 * (half + 1)],
                                         start=(ht == 0), stop=(ht == 31))
                        kb.inc(i, "m")

            def p_wa(v, b_m3=b_m3):
                for ot in range(CT):
                    for half in range(2):
                        u = 2 * ot + half
                        v._eng.wait_ge(kb.h("m"), b_m3 + u + 1)
                        v.tensor_mul(out=ytmp[:], in0=pW[u % 2][:],
                                     in1=gbc[:, 512 * half:512 * (half + 1)])
                        v.tensor_add(out=yacc[:, ot, 512 * half:512 * (half + 1)],
                                     in0=yacc[:, ot, 512 * half:512 * (half + 1)],
                                     in1=ytmp[:])
            kb.phase(gpsimd=p_w2s, tensor=p_wm, vector=p_wa)

        for half in range(2):
            cs = slice(1024 * ch + 512 * half, 1024 * ch + 512 * (half + 1))

            def p_ly3(sy, cs=cs):
                for k in range(CT):
                    kb.inc(sy.dma_start(out=y3c[:, k], in_=y3_d[128 * k:128 * (k + 1), cs]),
                           "ld", 16)
            kb.phase(sync=p_ly3)

            def p_add(v, half=half):
                for k in range(CT):
                    v.tensor_add(out=yacc[:, k, 512 * half:512 * (half + 1)],
                                 in0=yacc[:, k, 512 * half:512 * (half + 1)],
                                 in1=y3c[:, k])
            kb.phase(vector=p_add)

            def p_sty(sy, cs=cs, half=half):
                for k in range(CT):
                    kb.inc(sy.dma_start(out=out_d[128 * k:128 * (k + 1), cs],
                                        in_=yacc[:, k, 512 * half:512 * (half + 1)]),
                           "st", 16)
            kb.phase(sync=p_sty)

    def p_fin(sy):
        sy.wait_ge(kb.h("st"), kb.n("st"))
    kb.phase(sync=p_fin)
    ov1.close()


def emit_collective(kb, kind, in_ap, out_ap, op=None):
    kb.sems_once("cc")

    def p_cc(g):
        i = g.collective_compute(kind, op if op is not None else ALU.bypass,
                                 replica_groups=[list(range(NCORE))],
                                 ins=[in_ap.opt()], outs=[out_ap.opt()])
        kb.inc(i, "cc", 1)
        g.wait_ge(kb.h("cc"), kb.n("cc"))
    kb.phase(gpsimd=p_cc)


def emit_d2d(kb, dst_ap, src_ap, rows, wdt, dtype=F32):
    kb.push()
    eb = kb.sbuf("ebuf", [128, wdt], dtype)

    def p_cp(sy):
        for k in range(rows // 128):
            kb.inc(sy.dma_start(out=eb[:], in_=src_ap[128 * k:128 * (k + 1), :]), "ld", 16)
            sy.wait_ge(kb.h("ld"), kb.n("ld"))
            kb.inc(sy.dma_start(out=dst_ap[128 * k:128 * (k + 1), :], in_=eb[:]), "st", 16)
            sy.wait_ge(kb.h("st"), kb.n("st"))
    kb.phase(sync=p_cp)
    kb.pop()


def emit_halo_fetch(kb, ag_flat, hidx_s, hm_s, dst_ap, wdt):
    kb.push()
    hb = kb.sbuf("halob", [128, CT, wdt], F32)

    def p_g(g):
        for k in range(CT):
            kb.inc(g.indirect_dma_start(
                out=hb[:, k], out_offset=None, in_=ag_flat,
                in_offset=IndirectOffsetOnAxis(ap=hidx_s[:, k:k + 1], axis=0)), "ld", 16)
    kb.phase(gpsimd=p_g)

    def p_m(v):
        for k in range(CT):
            v.tensor_scalar_mul(hb[:, k], hb[:, k], hm_s[:, 0:1])
    kb.phase(vector=p_m)

    def p_o(sy):
        for k in range(CT):
            kb.inc(sy.dma_start(out=dst_ap[128 * k:128 * (k + 1), :], in_=hb[:, k]),
                   "st", 16)
        sy.wait_ge(kb.h("st"), kb.n("st"))
    kb.phase(sync=p_o)
    kb.pop()


def build_fused():
    kb = KB()
    xh = kb.dram_in("xh", [D, L0A_XW], F32)
    encq = kb.dram_in("encq", [8, D, 128], F32)
    encv = kb.dram_in("encv", [8, D, 128], F32)
    dec = kb.dram_in("decoder", [D, D], F32)
    heb = kb.dram_in("hebbian", [8, 128, 128], F32)
    mw1 = kb.dram_in("m_w1", [D, 512], F32)
    mb1 = kb.dram_in("m_b1", [1, 512], F32)
    mw2 = kb.dram_in("m_w2", [512, D], F32)
    mb2 = kb.dram_in("m_b2", [1, D], F32)
    baseT = kb.dram_in("base_T", [128, 8], F32)
    wq6 = kb.dram_in("wq6", [6, D, 1024], F32)
    wk6 = kb.dram_in("wk6", [6, D, 1024], F32)
    wv6 = kb.dram_in("wv6", [6, D, 1024], F32)
    wo6 = kb.dram_in("wo6", [6, 1024, D], F32)
    qs6 = kb.dram_in("qs6", [6, 128, 1], F32)
    ks6 = kb.dram_in("ks6", [6, 128, 1], F32)
    csq = kb.dram_in("csq", [128, 2176], BF16)
    snq = kb.dram_in("snq", [128, 2176], BF16)
    csk = kb.dram_in("csk", [128, 2432], BF16)
    snk = kb.dram_in("snk", [128, 2432], BF16)
    maskp = kb.dram_in("maskp", [4, 128, 384], BF16)
    w1p6 = kb.dram_in("w1p6", [6, 44, D, 128], F32)
    w2po6 = kb.dram_in("w2po6", [6, 8, 2816, 128], F32)
    g6 = kb.dram_in("g6", [6, 128, 8], F32)
    b6 = kb.dram_in("b6", [6, 128, 8], F32)
    b1a6 = kb.dram_in("b1a6", [6, 128, 22], F32)
    b1g6 = kb.dram_in("b1g6", [6, 128, 22], F32)
    b2_6 = kb.dram_in("b2_6", [6, 128, 8], F32)
    dwc = kb.dram_in("dwc", [128, 32], F32)
    dwb = kb.dram_in("dwb", [128, 8], F32)
    pwbias = kb.dram_in("pwbias", [128, 8], F32)
    gcv = kb.dram_in("gcv", [128, 8], F32)
    ggr = kb.dram_in("ggr", [128, 8], F32)
    pwT = kb.dram_in("pwT", [D, D], F32)
    whg = kb.dram_in("whg", [4, 256, 768], F32)
    wout = kb.dram_in("wout", [12, 128, 256], F32)
    gff = kb.dram_in("gff", [128, 8], F32)
    wg_h = kb.dram_in("wg_h", [22, D, 128], F32)
    wv_h = kb.dram_in("wv_h", [22, D, 128], F32)
    wf_h = kb.dram_in("wf_h", [8, 2816, 128], F32)
    mP_d = kb.dram_in("mP", [128, 8], F32)
    mE_d = kb.dram_in("mE", [128, 8], F32)
    gw = kb.dram_in("gw", [D, 8], F32)
    sc_d = kb.dram_in("scol", [128, 16], F32)
    w1_d = kb.dram_in("w1", [8, D, 4096], F32)
    w2_d = kb.dram_in("w2", [8, 4096, D], F32)
    hidx_d = kb.dram_in("hidx", [128, CT], I32)
    hmask_d = kb.dram_in("hmask", [128, 1], F32)
    yo = kb.dram_out("yo", [D, T], F32)

    qs_in = kb.dram_int("qs_in", [128, 16], F32)
    qs_ag = kb.dram_int("qs_ag", [NCORE, 128, 16], F32, shared=True)
    y1_d = kb.dram_int("y1_d", [D, L0A_W], F32)
    xcur = [kb.dram_int(f"xcur{i}", [D, F0A_XW], F32) for i in range(2)]
    ya_d = kb.dram_int("ya_d", [D, 2180], F32)
    ag_in = kb.dram_int("ag_in", [D, 385], F32)
    ag_out = kb.dram_int("ag_out", [NCORE * D, 385], F32, shared=True)
    x6_d = kb.dram_int("x6_d", [D, T], F32)
    y2_d = kb.dram_int("y2_d", [D, T], F32)
    y2ln_d = kb.dram_int("y2ln_d", [D, L1C_XW], F32)
    e2_in = kb.dram_int("e2_in", [D, 3], F32)
    e2_out = kb.dram_int("e2_out", [NCORE * D, 3], F32, shared=True)
    xa1_d = kb.dram_int("xa1_d", [D, T], F32)
    hl_d = kb.dram_int("hl_d", [128, 12, T], F32)
    al_d = kb.dram_int("al_d", [128, 12, T], F32)
    sp_in = kb.dram_int("sp_in", [128, 12], F32)
    se_in = kb.dram_int("se_in", [128, 12], F32)
    sp_ag = kb.dram_int("sp_ag", [NCORE, 128, 12], F32, shared=True)
    se_ag = kb.dram_int("se_ag", [NCORE, 128, 12], F32, shared=True)
    yb_d = kb.dram_int("yb_d", [D, T], F32)
    y3_d = kb.dram_int("y3_d", [D, T], F32)
    y3ln_d = kb.dram_int("y3ln_d", [D, T], F32)
    pr_in = kb.dram_int("pr_in", [128, 16, 8], F32)
    pr_ag = kb.dram_int("pr_ag", [NCORE, 128, 16, 8], F32, shared=True)

    alloc_common(kb)
    hm_s = kb.sbuf("hm_s", [128, 1], F32)
    hidx_s = kb.sbuf("hidx_s", [128, CT], I32)
    kb.sems("ld", "m", "st", "am", "tm", "qm", "im", "ic", "dn",
            "km", "sm", "wm", "hm", "w1l", "w2l", "hx", "cc")
    kb.start()

    def p_pre(sy):
        kb.inc(sy.dma_start(out=hm_s[:], in_=hmask_d), "ld", 16)
        kb.inc(sy.dma_start(out=hidx_s[:], in_=hidx_d), "ld", 16)
    kb.phase(sync=p_pre)

    with kb.nc.named_scope("l0a_sig"):
        kb.push()
        emit_l0a_sig(kb, {"xh": xh, "encq": encq, "qsums": qs_in, "x0": 543})
        kb.pop()
        emit_collective(kb, "AllGather", qs_in, qs_ag)
    with kb.nc.named_scope("l0a_main"):
        kb.push()
        emit_l0a_main(kb, {"xh": xh, "qsums_all": qs_ag, "encq": encq, "encv": encv,
                           "decoder": dec, "hebbian": heb, "m_w1": mw1, "m_b1": mb1,
                           "m_w2": mw2, "m_b2": mb2, "base_T": baseT, "y1": y1_d})
        kb.pop()
    with kb.nc.named_scope("ln_y1"):
        kb.push()
        emit_addln(kb, {"a": y1_d[:, 127:2560], "yln": xcur[0][:, 0:2433]},
                   with_add=False, W=2433)
        kb.pop()

    for li in range(6):
        src = xcur[li % 2]
        nxt = xcur[1 - li % 2]
        with kb.nc.named_scope(f"f0a_{li}"):
            kb.push()
            emit_f0_attn(kb, {"xh": src, "wq": wq6[li], "wk": wk6[li], "wv": wv6[li],
                              "wo": wo6[li], "qs_col": qs6[li], "ks_col": ks6[li],
                              "csq": csq, "snq": snq, "csk": csk, "snk": snk,
                              "maskp": maskp, "ya": ya_d[:, 0:2176]}, with_t0=True)
            kb.pop()
        with kb.nc.named_scope(f"f0f_{li}"):
            kb.push()
            emit_f0_ffn(kb, {"xf": ya_d[:, 127:2179], "w1p": w1p6[li], "w2po": w2po6[li],
                             "g_col": g6[li], "b_col": b6[li], "b1a_col": b1a6[li],
                             "b1g_col": b1g6[li], "b2_col": b2_6[li],
                             "yf": (nxt[:, 385:2433] if li < 5 else x6_d)},
                        hmask=hm_s)
            kb.pop()
        if li < 5:
            with kb.nc.named_scope(f"halo_{li}"):
                emit_d2d(kb, ag_in, nxt[:, 2048:2433], D, 385)
                emit_collective(kb, "AllGather", ag_in, ag_out)
                emit_halo_fetch(kb, ag_out, hidx_s, hm_s, nxt[:, 0:385], 385)

    with kb.nc.named_scope("y2_addln"):
        kb.push()
        emit_addln(kb, {"a": y1_d[:, 512:2560], "b": x6_d, "ysum": y2_d,
                        "yln": y2ln_d[:, 3:2051]}, with_add=True)
        kb.pop()
        emit_d2d(kb, e2_in, y2ln_d[:, 2048:2051], D, 3)
        emit_collective(kb, "AllGather", e2_in, e2_out)
        emit_halo_fetch(kb, e2_out, hidx_s, hm_s, y2ln_d[:, 0:3], 3)

    with kb.nc.named_scope("l1a_pre"):
        kb.push()
        emit_l1a_pre(kb, {"xh": y2ln_d, "dwc": dwc, "dwb": dwb, "pwbias": pwbias,
                          "gcv": gcv, "ggr": ggr, "pwT": pwT, "whg": whg,
                          "xa": xa1_d, "hl": hl_d, "al": al_d, "summ": None,
                          "summP": sp_in, "summE": se_in})
        kb.pop()
        emit_collective(kb, "AllGather", sp_in, sp_ag)
        emit_collective(kb, "AllGather", se_in, se_ag)
    with kb.nc.named_scope("l1a_post"):
        kb.push()
        emit_l1a_post(kb, {"xa": xa1_d, "hl": hl_d, "al": al_d, "wout": wout,
                           "gff": gff, "wg_h": wg_h, "wv_h": wv_h, "wf_h": wf_h,
                           "yb": yb_d},
                      summ_ag=(sp_ag, se_ag, mP_d, mE_d))
        kb.pop()

    with kb.nc.named_scope("y3_addln"):
        kb.push()
        emit_addln(kb, {"a": y2_d, "b": yb_d, "ysum": y3_d, "yln": y3ln_d},
                   with_add=True)
        kb.pop()

    with kb.nc.named_scope("moe_sel"):
        kb.push()
        emit_moe_probs(kb, {"xh": y3ln_d, "gw": gw, "pr": pr_in})
        kb.pop()
        emit_collective(kb, "AllGather", pr_in, pr_ag)
    with kb.nc.named_scope("moe_main"):
        kb.push()
        emit_moe_main(kb, {"xh": y3ln_d, "y3": y3_d, "scol": sc_d, "w1": w1_d,
                           "w2": w2_d, "yo": yo, "gates_o": None, "grow_o": None},
                      prall=pr_ag)
        kb.pop()
    return kb


def _prep_l1a(inputs):
    dw = inputs["L1_attn_dw_w"].astype(np.float32)
    o = {}
    o["dwc"] = np.ascontiguousarray(
        dw.reshape(8, 128, 4).transpose(1, 0, 2).reshape(128, 32))
    o["dwb"] = np.ascontiguousarray(inputs["L1_attn_dw_b"].reshape(8, 128).T)
    o["pwbias"] = np.ascontiguousarray(inputs["L1_attn_pw_b"].reshape(8, 128).T)
    o["gcv"] = np.ascontiguousarray(inputs["L1_attn_conv_gamma"].reshape(8, 128).T)
    o["ggr"] = np.ascontiguousarray(inputs["L1_attn_gru_gamma"].reshape(8, 128).T)
    o["gff"] = np.ascontiguousarray(inputs["L1_attn_ff_gamma"].reshape(8, 128).T)
    o["pwT"] = np.ascontiguousarray(inputs["L1_attn_pw_w"].T)
    o["whg"] = np.ascontiguousarray(inputs["L1_attn_w_hg"])
    wout = inputs["L1_attn_w_out"]
    o["wout"] = np.ascontiguousarray(
        wout.reshape(4, 3, 128, 256).reshape(12, 128, 256))
    wg = np.pad(inputs["L1_attn_w_gate"], ((0, 0), (0, 86)))
    wv = np.pad(inputs["L1_attn_w_value"], ((0, 0), (0, 86)))
    o["wg_h"] = np.ascontiguousarray(wg.reshape(1024, 22, 128).transpose(1, 0, 2))
    o["wv_h"] = np.ascontiguousarray(wv.reshape(1024, 22, 128).transpose(1, 0, 2))
    wf = np.pad(inputs["L1_attn_w_ff_out"], ((0, 86), (0, 0)))
    o["wf_h"] = np.ascontiguousarray(wf.reshape(2816, 8, 128).transpose(1, 0, 2))
    return o


def _pack_summ(summ_all, c):
    smP = np.ones((128, 8, 12), dtype=np.float32)
    smE = np.zeros((128, 8, 12), dtype=np.float32)
    for jj in range(c):
        smP[:, jj] = summ_all[jj][:, 0, :]
        smE[:, jj] = summ_all[jj][:, 1, :]
    return smP, smE


def _run(kb, maps):
    for m in maps:
        m.update(kb.cin_used)
    return run_bass_kernel_spmd(kb.nc, maps, core_ids=list(range(NCORE)))


def _slices(full, w=T, lpad=0, rpad=0):
    fp = np.pad(full, ((0, 0), (lpad, rpad))) if (lpad or rpad) else full
    return [np.ascontiguousarray(fp[:, c * T:c * T + w]) for c in range(NCORE)]


def kernel_unfused(**inputs):
    x = np.asarray(inputs["L0_attn_x"])
    xT = np.ascontiguousarray(x[0].T.astype(np.float32))

    kbS = build_l0a_sig()
    maps = [{"xh": sl, "encq": inputs["L0_attn_enc_q"]}
            for sl in _slices(xT)]
    r = _run(kbS, maps)
    qsums_all = np.stack([r.results[c]["qsums"] for c in range(NCORE)])

    kbM = build_l0a_main()
    maps = [{"xh": sl, "qsums_all": qsums_all,
             "encq": inputs["L0_attn_enc_q"], "encv": inputs["L0_attn_enc_v"],
             "decoder": inputs["L0_attn_decoder"], "hebbian": inputs["L0_attn_hebbian"],
             "m_w1": inputs["L0_attn_m_w1"],
             "m_b1": np.ascontiguousarray(inputs["L0_attn_m_b1"].reshape(1, 512)),
             "m_w2": inputs["L0_attn_m_w2"],
             "m_b2": np.ascontiguousarray(inputs["L0_attn_m_b2"].reshape(1, 1024)),
             "base_T": np.ascontiguousarray(inputs["L0_attn_base_metric"].T)}
            for sl in _slices(xT, w=L0A_XW, lpad=543, rpad=1)]
    r = _run(kbM, maps)
    y1 = np.concatenate([r.results[c]["y1"][:, 512:] for c in range(NCORE)], axis=1)

    kbL = build_addln(False)
    r = _run(kbL, [{"a": sl} for sl in _slices(y1)])
    cur = np.concatenate([r.results[c]["yln"] for c in range(NCORE)], axis=1)

    kbFA = build_f0_attn()
    kbFF = build_f0_ffn()
    for li in range(6):
        wts = _prep_f0_layer(inputs, li)
        maps = []
        for c, sl in enumerate(_slices(cur, w=F0A_XW, lpad=385, rpad=3)):
            s = c * T
            m = {"xh": sl}
            for key in ["wq", "wk", "wv", "wo", "qs_col", "ks_col", "qsp_col", "ksp_col"]:
                m[key] = wts[key]
            cq, sq_, _, _ = _rotary_tables(np.arange(s - 128, s + T, dtype=np.float64))
            _, _, ck, sk_ = _rotary_tables(np.arange(s - 384, s + T, dtype=np.float64))
            m["csq"], m["snq"], m["csk"], m["snk"] = cq, sq_, ck, sk_
            m["maskp"] = _f0_masks(s)
            maps.append(m)
        r = _run(kbFA, maps)
        xa = np.concatenate([r.results[c]["ya"] for c in range(NCORE)], axis=1)

        maps = []
        for sl in _slices(xa, w=2052, lpad=1, rpad=3):
            m = {"xf": sl}
            for key in ["w1p", "w2po", "g_col", "b_col", "b1a_col", "b1g_col", "b2_col"]:
                m[key] = wts[key]
            maps.append(m)
        r = _run(kbFF, maps)
        cur = np.concatenate([r.results[c]["yf"] for c in range(NCORE)], axis=1)

    kbAL = build_addln(True)
    maps = [{"a": a, "b": b} for a, b in zip(_slices(y1), _slices(cur))]
    r = _run(kbAL, maps)
    y2 = np.concatenate([r.results[c]["ysum"] for c in range(NCORE)], axis=1)
    y2ln = np.concatenate([r.results[c]["yln"] for c in range(NCORE)], axis=1)

    wl1 = _prep_l1a(inputs)
    kbP = build_l1a_pre()
    maps = []
    for sl in _slices(y2ln, w=L1C_XW, lpad=3, rpad=1):
        m = {"xh": sl}
        for key in ["dwc", "dwb", "pwbias", "gcv", "ggr", "pwT", "whg"]:
            m[key] = wl1[key]
        maps.append(m)
    r = _run(kbP, maps)
    xas = [r.results[c]["xa"] for c in range(NCORE)]
    hls = [r.results[c]["hl"] for c in range(NCORE)]
    als = [r.results[c]["al"] for c in range(NCORE)]
    summ_all = [r.results[c]["summ"] for c in range(NCORE)]

    kbQ = build_l1a_post()
    maps = []
    for c in range(NCORE):
        smP, smE = _pack_summ(summ_all, c)
        m = {"xa": xas[c], "hl": hls[c], "al": als[c], "smP": smP, "smE": smE}
        for key in ["wout", "gff", "wg_h", "wv_h", "wf_h"]:
            m[key] = wl1[key]
        maps.append(m)
    r = _run(kbQ, maps)
    yb = np.concatenate([r.results[c]["yb"] for c in range(NCORE)], axis=1)

    maps = [{"a": a, "b": b} for a, b in zip(_slices(y2), _slices(yb))]
    r = _run(kbAL, maps)
    y3 = np.concatenate([r.results[c]["ysum"] for c in range(NCORE)], axis=1)
    y3ln = np.concatenate([r.results[c]["yln"] for c in range(NCORE)], axis=1)

    kb1 = build_moe_probs()
    maps = [{"xh": sl, "gw": inputs["L1_ffn_gate_w"]} for sl in _slices(y3ln)]
    r = _run(kb1, maps)
    prs = [r.results[c]["pr"] for c in range(NCORE)]
    prA = np.zeros((128, 8, 128), dtype=np.float32)
    for c in range(NCORE):
        for lt in range(16):
            prA[:, :, 16 * c + lt] = prs[c][:, lt, :]

    kb2 = build_moe_main()
    maps = []
    for c, (slh, sl3) in enumerate(zip(_slices(y3ln), _slices(y3))):
        scol = np.zeros((128, 16), dtype=np.float32)
        for lt in range(16):
            scol[16 * c + lt, lt] = 1.0
        maps.append({"prA": prA, "xh": slh, "y3": sl3, "scol": scol,
                     "w1": inputs["L1_ffn_w1"], "w2": inputs["L1_ffn_w2"]})
    r = _run(kb2, maps)
    out = np.concatenate([r.results[c]["yo"] for c in range(NCORE)], axis=1)
    return np.ascontiguousarray(out.T).reshape(1, S, D).astype(np.float32)


def kernel(**inputs):
    x = np.asarray(inputs["L0_attn_x"])
    xT = np.ascontiguousarray(x[0].T.astype(np.float32))
    xTp = np.pad(xT, ((0, 0), (543, 1)))
    f0w = [_prep_f0_layer(inputs, li) for li in range(6)]
    wl1 = _prep_l1a(inputs)
    kb = build_fused()

    shared = {"encq": inputs["L0_attn_enc_q"], "encv": inputs["L0_attn_enc_v"],
              "decoder": inputs["L0_attn_decoder"], "hebbian": inputs["L0_attn_hebbian"],
              "m_w1": inputs["L0_attn_m_w1"],
              "m_b1": np.ascontiguousarray(inputs["L0_attn_m_b1"].reshape(1, 512)),
              "m_w2": inputs["L0_attn_m_w2"],
              "m_b2": np.ascontiguousarray(inputs["L0_attn_m_b2"].reshape(1, 1024)),
              "base_T": np.ascontiguousarray(inputs["L0_attn_base_metric"].T),
              "gw": inputs["L1_ffn_gate_w"],
              "w1": inputs["L1_ffn_w1"], "w2": inputs["L1_ffn_w2"]}
    for k6, kk in [("wq6", "wq"), ("wk6", "wk"), ("wv6", "wv"), ("wo6", "wo"),
                   ("w1p6", "w1p"), ("w2po6", "w2po"), ("g6", "g_col"),
                   ("b6", "b_col"), ("b1a6", "b1a_col"), ("b1g6", "b1g_col"),
                   ("b2_6", "b2_col")]:
        shared[k6] = np.ascontiguousarray(np.stack([f0w[li][kk] for li in range(6)]))
    shared["qs6"] = np.ascontiguousarray(
        np.stack([f0w[li]["qs_col"] for li in range(6)]))
    shared["ks6"] = np.ascontiguousarray(
        np.stack([f0w[li]["ks_col"] for li in range(6)]))
    for key in ["dwc", "dwb", "pwbias", "gcv", "ggr", "pwT", "whg", "wout",
                "gff", "wg_h", "wv_h", "wf_h"]:
        shared[key] = wl1[key]

    maps = []
    for c in range(NCORE):
        s = c * T
        m = dict(shared)
        m["xh"] = np.ascontiguousarray(xTp[:, s:s + L0A_XW])
        cq, sq_, _, _ = _rotary_tables(np.arange(s - 128, s + T, dtype=np.float64))
        _, _, ck, sk_ = _rotary_tables(np.arange(s - 384, s + T, dtype=np.float64))
        m["csq"], m["snq"], m["csk"], m["snk"] = cq, sq_, ck, sk_
        m["maskp"] = _f0_masks(s)
        scol = np.zeros((128, 16), dtype=np.float32)
        for lt in range(16):
            scol[16 * c + lt, lt] = 1.0
        m["scol"] = scol
        hidx = np.zeros((128, CT), dtype=np.int32)
        if c > 0:
            for k in range(CT):
                hidx[:, k] = (c - 1) * 1024 + 128 * k + np.arange(128)
        m["hidx"] = hidx
        m["hmask"] = np.full((128, 1), 1.0 if c > 0 else 0.0, dtype=np.float32)
        mm = np.zeros((128, 8), dtype=np.float32)
        mm[:, :c] = 1.0
        m["mP"] = mm
        m["mE"] = mm.copy()
        m.update(kb.cin_used)
        maps.append(m)
    r = run_bass_kernel_spmd(kb.nc, maps, core_ids=list(range(NCORE)))
    out = np.concatenate([r.results[c]["yo"] for c in range(NCORE)], axis=1)
    return np.ascontiguousarray(out.T).reshape(1, S, D).astype(np.float32)


# version 07
# speedup vs baseline: 1.0816x; 1.0320x over previous
import math
import numpy as np
import ml_dtypes
from contextlib import ExitStack

import concourse.bass as bass
import concourse.mybir as mybir
from concourse.bass_utils import run_bass_kernel_spmd
from bass_rust import IndirectOffsetOnAxis

F32 = mybir.dt.float32
BF16 = mybir.dt.bfloat16
I32 = mybir.dt.int32
AF = mybir.ActivationFunctionType
ALU = mybir.AluOpType
AX = mybir.AxisListType

NCORE = 8
S = 16384
T = 2048
D = 1024
CT = 8
BF = ml_dtypes.bfloat16

L0A_W = 512 + T
L0A_NT = L0A_W // 128
L0A_XW = 2592

DEBUG_L0A = False


def host_consts():
    c = {}
    c["identb"] = np.eye(128, dtype=BF)
    c["ones128b"] = np.ones((128, 128), dtype=BF)
    c["onescol_b"] = np.ones((128, 1), dtype=BF)
    c["neghalf_f"] = np.full((128, 1), -0.5, dtype=np.float32)
    c["onesrow_f"] = np.ones((1, 128), dtype=np.float32)
    c["ones1f"] = np.ones((1, 1), dtype=np.float32)
    misc = np.zeros((128, 8), dtype=np.float32)
    misc[:, 0] = 1e-8
    misc[:, 1] = 1e-5
    misc[:, 2] = 1e-6
    misc[:, 3] = 1e-12
    misc[:, 4] = -2.0
    misc[:, 5] = 1.0
    misc[:, 6] = 1e-30
    misc[:, 7] = 0.5
    c["miscc"] = misc
    p = np.arange(128)[:, None]
    q = np.arange(128)[None, :]
    comb = np.zeros((128, 256), dtype=np.float32)
    comb[:, :128] = (q >= p - 31) & (q <= p)
    pb = p[:32]
    comb[:32, 128:] = ((q >= pb + 97) & (q <= pb + 127)) & (pb < 31)
    c["bandAB"] = comb.astype(BF)
    ind2 = np.zeros((128, 2), dtype=BF)
    ind2[0:64, 0] = 1
    ind2[64:128, 1] = 1
    c["ind2b"] = ind2
    sel8 = np.zeros((8, 4, 128), dtype=BF)
    for pt in range(4):
        sel8[2 * pt, pt, 0:64] = 1
        sel8[2 * pt + 1, pt, 64:128] = 1
    c["sel8b"] = sel8
    c["identf"] = np.eye(128, dtype=np.float32)
    c["ltrilf"] = (np.arange(128)[:, None] < np.arange(128)[None, :]).astype(np.float32)
    c["onesf8"] = np.ones((8, 128), dtype=np.float32)
    c["ident8f"] = np.eye(8, dtype=np.float32)
    c["onescolf"] = np.ones((128, 1), dtype=np.float32)
    oh8 = np.zeros((8, 8, 128), dtype=np.float32)
    for e in range(8):
        oh8[e, e, :] = 1.0
    c["oh8f"] = oh8
    return c


_CONSTS = host_consts()

_CHAINED = {
    "tensor_copy", "tensor_tensor", "tensor_add", "tensor_sub", "tensor_mul",
    "tensor_max", "tensor_scalar", "tensor_single_scalar", "tensor_scalar_add",
    "tensor_scalar_sub", "tensor_scalar_mul", "tensor_scalar_max",
    "tensor_scalar_min", "tensor_relu", "tensor_reduce", "reduce_max",
    "reduce_sum", "scalar_tensor_tensor", "tensor_tensor_scan", "memset",
    "memzero", "reciprocal", "select", "copy_predicated", "activation",
    "square", "sqrt", "mul", "add", "copy", "sign", "iota", "affine_select",
    "bn_stats", "bn_aggr", "stream_shuffle", "pool", "pool_avg",
    "pool_max", "max", "max_index",
}


def _ap_span(ap):
    try:
        pat = [list(p) for p in ap.ap]
        plo = ap.base_partition
        pcnt = pat[0][1] if pat else 128
        lo = ap.offset
        span = 1
        for st, ct in pat[1:]:
            span += (ct - 1) * abs(st)
        return (ap.tensor.name, plo, plo + pcnt, lo, lo + span)
    except Exception:
        return (getattr(getattr(ap, "tensor", None), "name", "?"), 0, 128, 0, 1 << 60)


def _sp_olap(a, b):
    return (a[0] == b[0] and a[1] < b[2] and b[1] < a[2]
            and a[3] < b[4] and b[3] < a[4])


def _op_aps(a, k):
    outs, ins = [], []
    if "out" in k:
        for key, v in k.items():
            if not hasattr(v, "tensor"):
                continue
            (outs if key in ("out", "accum_out") else ins).append(v)
        for v in a:
            if hasattr(v, "tensor"):
                ins.append(v)
    else:
        rest = list(a)
        if rest and hasattr(rest[0], "tensor"):
            outs.append(rest[0])
            rest = rest[1:]
        for v in rest:
            if hasattr(v, "tensor"):
                ins.append(v)
        for key, v in k.items():
            if hasattr(v, "tensor"):
                (outs if key in ("out", "accum_out") else ins).append(v)
    return outs, ins


class CEng:

    def __init__(self, kb, eng, sem_name):
        self._kb = kb
        self._eng = eng
        self._sem = sem_name
        self._R = []
        self._W = []

    def __getattr__(self, name):
        fn = getattr(self._eng, name)
        if name not in _CHAINED:
            return fn
        kb = self._kb
        sem = self._sem
        eng = self._eng

        def wrap(*a, **k):
            outs, ins = _op_aps(a, k)
            if not outs:
                hazard = True
            else:
                os_ = [_ap_span(x) for x in outs]
                is_ = [_ap_span(x) for x in ins]
                hazard = (
                    any(_sp_olap(x, w) for w in self._W for x in is_)
                    or any(_sp_olap(x, w) for w in self._W for x in os_)
                    or any(_sp_olap(x, r) for r in self._R for x in os_))
            if hazard:
                cnt = kb.n(sem)
                if cnt > 0:
                    eng.wait_ge(kb.h(sem), cnt)
                self._R = []
                self._W = []
            inst = fn(*a, **k)
            kb.inc(inst, sem, 1)
            if outs:
                self._R.extend(_ap_span(x) for x in ins)
                self._W.extend(_ap_span(x) for x in outs)
            return inst
        return wrap


class KB:
    def __init__(self):
        self.nc = bass.Bass("TRN2", target_bir_lowering=False)
        self.ctx = ExitStack()
        self._sems = {}
        self.block = None
        self.stage = None
        self.cin_used = {}
        self._cins = {}
        self._names = {}

    def dram_in(self, name, shape, dtype):
        return self.nc.dram_tensor(name, list(shape), dtype, kind="ExternalInput").ap()

    def dram_out(self, name, shape, dtype):
        return self.nc.dram_tensor(name, list(shape), dtype, kind="ExternalOutput").ap()

    def dram_int(self, name, shape, dtype, shared=False):
        kw = {"addr_space": "Shared"} if shared else {}
        return self.nc.dram_tensor(name, list(shape), dtype, kind="Internal", **kw).ap()

    def push(self):
        self.stage = ExitStack()
        return self.stage

    def pop(self):
        self.stage.close()
        self.stage = None

    def sems_once(self, *names):
        for n in names:
            if n not in self._sems:
                self.sem(n)

    def cin(self, name):
        if name in self._cins:
            return self._cins[name]
        a = _CONSTS[name]
        self.cin_used[name] = a
        dt = BF16 if a.dtype == BF else (F32 if a.dtype == np.float32 else I32)
        ap = self.dram_in(name, a.shape, dt)
        self._cins[name] = ap
        return ap

    def _uniq(self, name):
        i = self._names.get(name, 0)
        self._names[name] = i + 1
        return name if i == 0 else f"{name}_{i}"

    def sbuf(self, name, shape, dtype):
        ctx = self.stage if self.stage is not None else self.ctx
        return ctx.enter_context(self.nc.sbuf_tensor(self._uniq(name), list(shape), dtype))

    def sbuf_cm(self, name, shape, dtype):
        return self.nc.sbuf_tensor(self._uniq(name), list(shape), dtype)

    def psum(self, name, shape, dtype=F32):
        ctx = self.stage if self.stage is not None else self.ctx
        return ctx.enter_context(self.nc.psum_tensor(self._uniq(name), list(shape), dtype))

    def sem(self, name):
        h = self.ctx.enter_context(self.nc.semaphore(name))
        self._sems[name] = [h, 0]

    def sems(self, *names):
        for n in names:
            self.sem(n)

    def inc(self, inst, name, k=1):
        s = self._sems[name]
        inst.then_inc(s[0], k)
        s[1] += k
        return inst

    def n(self, name):
        return self._sems[name][1]

    def h(self, name):
        return self._sems[name][0]

    def snap(self):
        return {k: v[1] for k, v in self._sems.items()}

    def start(self):
        self.block = self.ctx.enter_context(self.nc.Block())
        self.sems("V", "SC", "G")
        return self.block

    def phase(self, **engine_fns):
        snapshot = self.snap()
        for engname, fn in engine_fns.items():
            @getattr(self.block, engname)
            def _(eng, fn=fn, engname=engname):
                for k, c in snapshot.items():
                    if c > 0:
                        eng.wait_ge(self._sems[k][0], c)
                if engname == "vector":
                    fn(CEng(self, eng, "V"))
                elif engname == "scalar":
                    fn(CEng(self, eng, "SC"))
                elif engname == "gpsimd":
                    fn(CEng(self, eng, "G"))
                else:
                    fn(eng)

    def done(self):
        self.ctx.close()
        return self.nc


def alloc_common(kb):
    kb.t_ones = kb.sbuf("t_ones", [128, 128], BF16)
    kb.t_misc = kb.sbuf("t_misc", [128, 8], F32)
    kb.d_ones = kb.cin("ones128b")
    kb.d_misc = kb.cin("miscc")


def load_common(kb, g):
    kb.inc(g.dma_start(out=kb.t_ones[:], in_=kb.d_ones), "ld", 16)
    kb.inc(g.dma_start(out=kb.t_misc[:], in_=kb.d_misc), "ld", 16)


def alloc_ln(kb, scr4=None):
    kb.t_lnxb = kb.sbuf("lnxb", [128, CT, 512], BF16)
    kb.t_lnrow = scr4 if scr4 is not None else kb.sbuf("lnrow", [128, 4, 512], F32)


def emit_ln(kb, xt, xnb, x0, n0, w, pA, pB):
    done = 0
    while done < w:
        cw = min(512, w - done)
        xs = slice(x0 + done, x0 + done + cw)
        ns = slice(n0 + done, n0 + done + cw)
        done += cw

        def p_cast(sc, xs=xs, cw=cw):
            for k in range(CT):
                sc.activation(out=kb.t_lnxb[:, k, 0:cw], in_=xt[:, k, xs], func=AF.Copy)
        kb.phase(scalar=p_cast)

        def p_mmA(t, cw=cw):
            for k in range(CT):
                i = t.matmul(out=pA[:, 0:cw], lhsT=kb.t_ones[:], rhs=kb.t_lnxb[:, k, 0:cw],
                             start=(k == 0), stop=(k == CT - 1))
            kb.inc(i, "m")
        kb.phase(tensor=p_mmA)

        def p_sq(v, cw=cw):
            for k in range(CT):
                v.tensor_mul(out=kb.t_lnxb[:, k, 0:cw], in0=kb.t_lnxb[:, k, 0:cw],
                             in1=kb.t_lnxb[:, k, 0:cw])
        kb.phase(vector=p_sq)

        def p_mmB(t, cw=cw):
            for k in range(CT):
                i = t.matmul(out=pB[:, 0:cw], lhsT=kb.t_ones[:], rhs=kb.t_lnxb[:, k, 0:cw],
                             start=(k == 0), stop=(k == CT - 1))
            kb.inc(i, "m")
        kb.phase(tensor=p_mmB)

        def p_v1(v, cw=cw):
            v.tensor_scalar_mul(kb.t_lnrow[:, 0, 0:cw], pA[:, 0:cw], 1.0 / D)
            v.tensor_scalar_mul(kb.t_lnrow[:, 1, 0:cw], pB[:, 0:cw], 1.0 / D)
        kb.phase(vector=p_v1)

        def p_s1(sc, cw=cw):
            sc.activation(out=kb.t_lnrow[:, 2, 0:cw], in_=kb.t_lnrow[:, 0, 0:cw],
                          func=AF.Square)
        kb.phase(scalar=p_s1)

        def p_v2(v, cw=cw):
            v.tensor_sub(out=kb.t_lnrow[:, 1, 0:cw], in0=kb.t_lnrow[:, 1, 0:cw],
                         in1=kb.t_lnrow[:, 2, 0:cw])
        kb.phase(vector=p_v2)

        def p_s2(sc, cw=cw):
            sc.activation(out=kb.t_lnrow[:, 1, 0:cw], in_=kb.t_lnrow[:, 1, 0:cw],
                          func=AF.Sqrt, bias=kb.t_misc[:, 1:2])
        kb.phase(scalar=p_s2)

        def p_v3(v, xs=xs, ns=ns, cw=cw):
            v.reciprocal(out=kb.t_lnrow[:, 3, 0:cw], in_=kb.t_lnrow[:, 1, 0:cw])
            for k in range(CT):
                v.tensor_sub(out=kb.t_lnrow[:, 2, 0:cw], in0=xt[:, k, xs],
                             in1=kb.t_lnrow[:, 0, 0:cw])
                v.tensor_mul(out=xnb[:, k, ns], in0=kb.t_lnrow[:, 2, 0:cw],
                             in1=kb.t_lnrow[:, 3, 0:cw])
        kb.phase(vector=p_v3)


def build_l0a_sig():
    kb = KB()
    io = {"xh": kb.dram_in("xh", [D, T], F32),
          "encq": kb.dram_in("encq", [8, D, 128], F32),
          "qsums": kb.dram_out("qsums", [128, 16], F32), "x0": 0}
    alloc_common(kb)
    kb.sems("ld", "m", "st")
    kb.start()
    kb.push()
    emit_l0a_sig(kb, io)
    kb.pop()
    return kb


def emit_l0a_sig(kb, io):
    xh, encq, qsums, x0 = io["xh"], io["encq"], io["qsums"], io["x0"]
    xt = kb.sbuf("xt", [128, CT, T], F32)
    xnb = kb.sbuf("xnb", [128, CT, T], BF16)
    eq = kb.sbuf("eq", [128, 8, CT, 128], BF16)
    alloc_ln(kb)
    qtile = kb.sbuf("qtile", [128, 2, 512], F32)
    acc = kb.sbuf("acc", [128, 16], F32)
    red = kb.sbuf("red", [128, 2], F32)
    pQ = [kb.psum(f"pQ{i}", [128, 512]) for i in range(2)]
    pA = kb.psum("pA", [128, 512])
    pB = kb.psum("pB", [128, 512])

    def p_load_sy(sy):
        for k in range(CT):
            kb.inc(sy.dma_start(out=xt[:, k],
                                in_=xh[128 * k:128 * (k + 1), x0:x0 + T]), "ld", 16)

    def p_load_g(g):
        load_common(kb, g)
        for h in range(8):
            for k in range(CT):
                kb.inc(g.dma_start(out=eq[:, h, k],
                                   in_=encq[h, 128 * k:128 * (k + 1), :]), "ld", 16)
    kb.phase(sync=p_load_sy, gpsimd=p_load_g)

    emit_ln(kb, xt, xnb, 0, 0, T, pA, pB)

    NCH = T // 512
    NIT = NCH * 8
    b_m, b_sc, b_v = kb.n("m"), kb.n("SC"), kb.n("V")

    def p_q_mm(t):
        for idx in range(NIT):
            ch, h = divmod(idx, 8)
            if idx >= 2:
                t.wait_ge(kb.h("SC"), b_sc + idx - 1)
            for k in range(CT):
                i = t.matmul(out=pQ[idx % 2][:], lhsT=eq[:, h, k],
                             rhs=xnb[:, k, 512 * ch:512 * (ch + 1)],
                             start=(k == 0), stop=(k == CT - 1))
            kb.inc(i, "m")

    def p_q_relu(sc):
        for idx in range(NIT):
            sc._eng.wait_ge(kb.h("m"), b_m + idx + 1)
            if idx >= 2:
                sc._eng.wait_ge(kb.h("V"), b_v + 1 + 4 * (idx - 1))
            sc.activation(out=qtile[:, idx % 2], in_=pQ[idx % 2][:], func=AF.Relu)

    def p_q_sum(v):
        v.memset(acc[:], 0.0)
        for idx in range(NIT):
            h = idx % 8
            v._eng.wait_ge(kb.h("SC"), b_sc + idx + 1)
            v.reduce_sum(out=red[:, 0:1], in_=qtile[:, idx % 2], axis=AX.X)
            v.reduce_sum(out=red[:, 1:2], in_=qtile[:, idx % 2, ::8], axis=AX.X)
            v.tensor_add(out=acc[:, h:h + 1], in0=acc[:, h:h + 1], in1=red[:, 0:1])
            v.tensor_add(out=acc[:, 8 + h:9 + h], in0=acc[:, 8 + h:9 + h],
                         in1=red[:, 1:2])
    kb.phase(tensor=p_q_mm, scalar=p_q_relu, vector=p_q_sum)

    def p_store(g):
        kb.inc(g.dma_start(out=qsums, in_=acc[:]), "st", 16)
        g.wait_ge(kb.h("st"), kb.n("st"))
    kb.phase(gpsimd=p_store)


def build_l0a_main(nsplit=10):
    kb = KB()
    io = {"xh": kb.dram_in("xh", [D, L0A_XW], F32),
          "qsums_all": kb.dram_in("qsums_all", [NCORE, 128, 16], F32),
          "encq": kb.dram_in("encq", [8, D, 128], F32),
          "encv": kb.dram_in("encv", [8, D, 128], F32),
          "decoder": kb.dram_in("decoder", [D, D], F32),
          "hebbian": kb.dram_in("hebbian", [8, 128, 128], F32),
          "m_w1": kb.dram_in("m_w1", [D, 512], F32),
          "m_b1": kb.dram_in("m_b1", [1, 512], F32),
          "m_w2": kb.dram_in("m_w2", [512, D], F32),
          "m_b2": kb.dram_in("m_b2", [1, D], F32),
          "base_T": kb.dram_in("base_T", [128, 8], F32),
          "y1": kb.dram_out("y1", [D, L0A_W], F32)}
    alloc_common(kb)
    kb.sems("ld", "m", "st", "am", "tm", "qm", "im", "ic", "dn")
    kb.start()
    kb.push()
    emit_l0a_main(kb, io, nsplit=nsplit)
    kb.pop()
    return kb


def emit_l0a_main(kb, io, nsplit=10):
    xh, qsums_all = io["xh"], io["qsums_all"]
    encq, encv, dec, heb = io["encq"], io["encv"], io["decoder"], io["hebbian"]
    mw1, mb1, mw2, mb2 = io["m_w1"], io["m_b1"], io["m_w2"], io["m_b2"]
    baseT, y1 = io["base_T"], io["y1"]
    d_identb = kb.cin("identb")
    d_onescol_b = kb.cin("onescol_b")
    d_neghalf = kb.cin("neghalf_f")
    d_onesrow = kb.cin("onesrow_f")
    d_ones1f = kb.cin("ones1f")
    d_band = kb.cin("bandAB")
    if DEBUG_L0A:
        dbg_sig = kb.dram_out("dbg_sig", [128, 8], F32)
        dbg_mdiag = kb.dram_out("dbg_mdiag", [128, 24], F32)
        dbg_xnb = kb.dram_out("dbg_xnb", [128, CT, 256], F32)
        dbg_su = kb.dram_out("dbg_su", [128, 8, 256], F32)
        dbg_arow = kb.dram_out("dbg_arow", [1, 8, 416], F32)
        dbg_infl = kb.dram_out("dbg_infl", [128, 2, 256], F32)
        dbg_sinvq = kb.dram_out("dbg_sinvq", [128, 2, 8], F32)
        dbg_vtm = kb.dram_out("dbg_vtm", [128, 8, 128], F32)
        dbg_qtm = kb.dram_out("dbg_qtm", [128, 8, 128], F32)

    xt = kb.sbuf("xt", [128, CT, L0A_XW], F32)
    mlp_ctx = ExitStack()

    def mlp_sbuf(name, shape, dtype):
        return mlp_ctx.enter_context(kb.sbuf_cm(name, shape, dtype))

    t_idb = kb.sbuf("t_idb", [128, 128], BF16)
    t_oneb = kb.sbuf("t_oneb", [128, 1], BF16)
    t_nh = kb.sbuf("t_nh", [128, 1], F32)
    t_onr = kb.sbuf("t_onr", [1, 128], F32)
    t_one1f = kb.sbuf("t_one1f", [1, 1], F32)
    t_band = kb.sbuf("t_band", [128, 256], BF16)
    eq = kb.sbuf("eq", [128, 8, CT, 128], BF16)
    ev = kb.sbuf("ev", [128, 8, CT, 128], BF16)
    sdec = kb.sbuf("sdec", [128, CT, D], BF16)
    shebs = kb.sbuf("shebs", [128, 8, 128], F32)
    sbase = kb.sbuf("sbase", [128, 8], F32)
    modcol = kb.sbuf("modcol", [128, 8], F32)
    uscale = kb.sbuf("uscale", [128, 8], F32)
    recipm = kb.sbuf("recipm", [128, 8], F32)
    scr4 = kb.sbuf("scr4", [128, 4, 512], F32)
    alloc_ln(kb, scr4=scr4)
    srow1 = kb.sbuf("srow1", [1, 512], F32)
    srow2 = kb.sbuf("srow2", [1, D], F32)
    sig = kb.sbuf("sig", [128, 8], F32)
    sigb = kb.sbuf("sigb", [128, 8], BF16)
    h1row = kb.sbuf("h1row", [1, 512], F32)
    h1colb = kb.sbuf("h1colb", [128, 4], BF16)
    modrow = kb.sbuf("modrow", [1, D], F32)
    sheb = shebs
    smw1 = mlp_sbuf("smw1", [128, CT, 512], BF16)
    smw2 = mlp_sbuf("smw2", [128, 4, D], BF16)
    ssum = mlp_sbuf("ssum", [128, NCORE, 16], F32)

    pG = [kb.psum(f"pG{i}", [128, 256]) for i in range(2)]
    pCH = [kb.psum(f"pCH{i}", [128, 512]) for i in range(2)]
    pT = [kb.psum(f"pT{i}", [128, 128], BF16) for i in range(2)]
    pDec = [kb.psum(f"pDec{i}", [128, 512]) for i in range(2)]

    def p_load_sy(sy):
        for k in range(CT):
            kb.inc(sy.dma_start(out=xt[:, k], in_=xh[128 * k:128 * (k + 1), :]), "ld", 16)
        kb.inc(sy.dma_start(out=t_nh[:], in_=d_neghalf), "ld", 16)
        kb.inc(sy.dma_start(out=t_onr[:], in_=d_onesrow), "ld", 16)
        kb.inc(sy.dma_start(out=t_one1f[:], in_=d_ones1f), "ld", 16)
        for h in range(8):
            kb.inc(sy.dma_start(out=sheb[:, h], in_=heb[h]), "ld", 16)
        for r in range(NCORE):
            kb.inc(sy.dma_start(out=ssum[:, r], in_=qsums_all[r]), "ld", 16)
        kb.inc(sy.dma_start(out=srow1[:], in_=mb1), "ld", 16)
        kb.inc(sy.dma_start(out=srow2[:], in_=mb2), "ld", 16)
        kb.inc(sy.dma_start(out=sbase[:], in_=baseT), "ld", 16)

    def p_load_g(g):
        load_common(kb, g)
        kb.inc(g.dma_start(out=t_idb[:], in_=d_identb), "ld", 16)
        kb.inc(g.dma_start(out=t_oneb[:], in_=d_onescol_b), "ld", 16)
        kb.inc(g.dma_start(out=t_band[:], in_=d_band), "ld", 16)
        for h in range(8):
            for k in range(CT):
                kb.inc(g.dma_start(out=eq[:, h, k], in_=encq[h, 128 * k:128 * (k + 1), :]), "ld", 16)
                kb.inc(g.dma_start(out=ev[:, h, k], in_=encv[h, 128 * k:128 * (k + 1), :]), "ld", 16)
        for k in range(CT):
            kb.inc(g.dma_start(out=sdec[:, k], in_=dec[128 * k:128 * (k + 1), :]), "ld", 16)
            kb.inc(g.dma_start(out=smw1[:, k], in_=mw1[128 * k:128 * (k + 1), :]), "ld", 16)
        for j in range(4):
            kb.inc(g.dma_start(out=smw2[:, j], in_=mw2[128 * j:128 * (j + 1), :]), "ld", 16)
    kb.phase(sync=p_load_sy, gpsimd=p_load_g)

    def p_sig(v):
        v.tensor_add(out=sig[:], in0=ssum[:, 0, 0:8], in1=ssum[:, 1, 0:8])
        for r in range(2, NCORE):
            v.tensor_add(out=sig[:], in0=sig[:], in1=ssum[:, r, 0:8])
        v.tensor_add(out=modcol[:], in0=ssum[:, 0, 8:16], in1=ssum[:, 1, 8:16])
        for r in range(2, NCORE):
            v.tensor_add(out=modcol[:], in0=modcol[:], in1=ssum[:, r, 8:16])
        v.tensor_scalar_mul(sig[:], sig[:], 1.0 / S)
        v.tensor_scalar_mul(modcol[:], modcol[:], 0.5 / (S / 8.0))
        v.tensor_add(out=sig[:], in0=sig[:], in1=modcol[:])
        v.tensor_copy(out=sigb[:], in_=sig[:])
    kb.phase(vector=p_sig)

    def p_mlp1(t):
        for h in range(8):
            i = t.matmul(out=pDec[0][0:1, :], lhsT=sigb[:, h:h + 1], rhs=smw1[:, h],
                         start=(h == 0), stop=(h == 7))
        kb.inc(i, "m")
    kb.phase(tensor=p_mlp1)

    def p_mlp2(v):
        v.tensor_add(out=h1row[:], in0=pDec[0][0:1, :], in1=srow1[:])
    kb.phase(vector=p_mlp2)

    def p_mlp3(sc):
        sc.activation(out=h1row[:], in_=h1row[:], func=AF.Gelu)
    kb.phase(scalar=p_mlp3)

    def p_mlp4(t):
        for j in range(4):
            kb.inc(t.matmul(out=pCH[0][:, j:j + 1], lhsT=h1row[0:1, 128 * j:128 * (j + 1)],
                            rhs=t_one1f[:], start=True, stop=True), "m")
    kb.phase(tensor=p_mlp4)

    def p_mlp5(v):
        v.tensor_copy(out=h1colb[:], in_=pCH[0][:, 0:4])
    kb.phase(vector=p_mlp5)

    def p_mlp6(t):
        for hf in range(2):
            for j in range(4):
                i = t.matmul(out=pDec[hf][0:1, :], lhsT=h1colb[:, j:j + 1],
                             rhs=smw2[:, j, 512 * hf:512 * (hf + 1)],
                             start=(j == 0), stop=(j == 3))
            kb.inc(i, "m")
    kb.phase(tensor=p_mlp6)

    def p_mlp7(v):
        v.tensor_add(out=modrow[0:1, 0:512], in0=pDec[0][0:1, :], in1=srow2[0:1, 0:512])
        v.tensor_add(out=modrow[0:1, 512:1024], in0=pDec[1][0:1, :],
                     in1=srow2[0:1, 512:1024])
    kb.phase(vector=p_mlp7)

    def p_mlp8(t):
        for j in range(8):
            kb.inc(t.matmul(out=pCH[0][:, j:j + 1], lhsT=modrow[0:1, 128 * j:128 * (j + 1)],
                            rhs=t_one1f[:], start=True, stop=True), "m")
    kb.phase(tensor=p_mlp8)

    def p_md1(v):
        v.tensor_copy(out=modcol[:], in_=pCH[0][:, 0:8])
        v.tensor_scalar_mul(modcol[:], modcol[:], 0.1)
        v.tensor_add(out=modcol[:], in0=modcol[:], in1=sbase[:])
    kb.phase(vector=p_md1)

    def p_md2(sc):
        sc.activation(out=uscale[:], in_=modcol[:], func=AF.Abs)
        sc.activation(out=uscale[:], in_=uscale[:], func=AF.Exp, scale=-1.0)
        sc.activation(out=uscale[:], in_=uscale[:], func=AF.Ln, bias=kb.t_misc[:, 5:6])
    kb.phase(scalar=p_md2)

    def p_md3(v):
        v.tensor_scalar_max(recipm[:], modcol[:], 0.0)
        v.tensor_add(out=recipm[:], in0=recipm[:], in1=uscale[:])
        v.tensor_scalar_add(recipm[:], recipm[:], 1e-6)
    kb.phase(vector=p_md3)

    def p_md4(sc):
        sc.activation(out=uscale[:], in_=recipm[:], func=AF.Sqrt)
    kb.phase(scalar=p_md4)

    def p_md5(v):
        v.reciprocal(out=recipm[:], in_=recipm[:])
        v.reciprocal(out=modcol[:], in_=uscale[:])
    kb.phase(vector=p_md5)

    def p_hebs(v):
        for h in range(8):
            v.tensor_scalar_mul(shebs[:, h], sheb[:, h], modcol[:, h:h + 1])
    kb.phase(vector=p_hebs)

    mlp_ctx.close()
    NT_H = L0A_NT // nsplit
    WU = NT_H * 128 + 160
    su = kb.sbuf("su", [128, 8, WU], F32)
    arow = kb.sbuf("arow", [1, 8, WU], F32)
    xnb = kb.sbuf("xnb", [128, CT, WU], BF16)
    svtm = kb.sbuf("svtm", [128, NT_H + 1, 8, 128], BF16)
    sqtm = kb.sbuf("sqtm", [128, NT_H, 8, 128], BF16)
    sinvq = kb.sbuf("sinvq", [128, NT_H, 8], F32)
    u2s = scr4
    sinvd = kb.sbuf("sinvd", [128, 2], F32)
    sdq = scr4[:, 2:4, 0:256]
    scf = scr4[:, 0:2, 0:128]
    bfp = kb.t_lnxb
    sinfl = bfp[:, 0, :].rearrange("p (a b) -> p a b", a=2)
    schb = bfp[:, 1, 0:256].rearrange("p (a b) -> p a b", a=2)
    soh = bfp[:, 2, 0:256].rearrange("p (a b) -> p a b", a=2)
    sflat = bfp[:, 3:5, :].rearrange("p a b -> p (a b)").rearrange(
        "p (o h e) -> p o h e", o=1, h=8)

    for half in range(nsplit):
        m0 = half * NT_H
        u0 = m0 * 128

        wln = min(WU, L0A_XW - u0)
        emit_ln(kb, xt, xnb, u0, 0, wln, pDec[0], pDec[1])
        if wln < WU:
            def p_zf(v, wln=wln):
                for k in range(CT):
                    v.memset(xnb[:, k, wln:WU], 0.0)
            kb.phase(vector=p_zf)

        chunks = []
        off = 0
        while off < WU:
            chunks.append((off, min(512, WU - off)))
            off += 512
        b_am, b_sc, b_v = kb.n("am"), kb.n("SC"), kb.n("V")

        def p_u_t(t, chunks=chunks, b_am=b_am, b_sc=b_sc, b_v=b_v):
            for ci, (c0, w) in enumerate(chunks):
                for h in range(8):
                    idx = ci * 8 + h
                    if idx >= 2:
                        t.wait_ge(kb.h("SC"), b_sc + 2 * (idx - 2) + 1)
                    for k in range(CT):
                        i = t.matmul(out=pDec[idx % 2][:, 0:w], lhsT=eq[:, h, k],
                                     rhs=xnb[:, k, c0:c0 + w],
                                     start=(k == 0), stop=(k == CT - 1))
                    kb.inc(i, "am")
                    t.wait_ge(kb.h("SC"), b_sc + 2 * idx + 2)
                    if idx >= 2:
                        t.wait_ge(kb.h("V"), b_v + idx - 1)
                    kb.inc(t.matmul(out=pCH[idx % 2][0:1, 0:w], lhsT=t_nh[:],
                                    rhs=u2s[:, idx % 2, 0:w], start=True, stop=True), "am")

        def p_u_s(sc, chunks=chunks, b_am=b_am, b_v=b_v):
            for ci, (c0, w) in enumerate(chunks):
                for h in range(8):
                    idx = ci * 8 + h
                    sc._eng.wait_ge(kb.h("am"), b_am + 2 * idx + 1)
                    sc.activation(out=su[:, h, c0:c0 + w], in_=pDec[idx % 2][:, 0:w],
                                  func=AF.Relu, scale=uscale[:, h:h + 1])
                    if idx >= 2:
                        sc._eng.wait_ge(kb.h("V"), b_v + idx - 1)
                    sc.activation(out=u2s[:, idx % 2, 0:w], in_=su[:, h, c0:c0 + w],
                                  func=AF.Square)

        def p_u_v(v, chunks=chunks, b_am=b_am):
            for ci, (c0, w) in enumerate(chunks):
                for h in range(8):
                    idx = ci * 8 + h
                    v._eng.wait_ge(kb.h("am"), b_am + 2 * idx + 2)
                    v.tensor_copy(out=arow[0:1, h, c0:c0 + w],
                                  in_=pCH[idx % 2][0:1, 0:w])
        kb.phase(tensor=p_u_t, scalar=p_u_s, vector=p_u_v)

        b_tm, b_sc = kb.n("tm"), kb.n("SC")
        vt_jobs = [(kv, 128 * kv) for kv in range(NT_H + 1) if 128 * kv < WU - 30]

        def p_vt_t(t, vt_jobs=vt_jobs, b_sc=b_sc):
            for ji, (kv, uw) in enumerate(vt_jobs):
                for h in range(8):
                    idx = ji * 8 + h
                    if idx >= 2:
                        t.wait_ge(kb.h("SC"), b_sc + idx - 1)
                    w = min(128, WU - uw)
                    for k in range(CT):
                        i = t.matmul(out=pDec[idx % 2][:, 0:128], lhsT=xnb[:, k, uw:uw + w],
                                     rhs=ev[:, h, k], start=(k == 0), stop=(k == CT - 1))
                    kb.inc(i, "tm")

        def p_vt_s(sc, vt_jobs=vt_jobs, b_tm=b_tm):
            for ji, (kv, uw) in enumerate(vt_jobs):
                for h in range(8):
                    idx = ji * 8 + h
                    sc._eng.wait_ge(kb.h("tm"), b_tm + idx + 1)
                    w = min(128, WU - uw)
                    sc.activation(out=svtm[0:w, kv, h], in_=pDec[idx % 2][0:w, 0:128],
                                  func=AF.Relu)
        kb.phase(tensor=p_vt_t, scalar=p_vt_s)

        b_qm, b_sc, b_v = kb.n("qm"), kb.n("SC"), kb.n("V")

        def p_qt_t(t, b_qm=b_qm, b_sc=b_sc, b_v=b_v):
            for mi in range(NT_H):
                iw = 31 + 128 * mi
                for h in range(8):
                    idx = mi * 8 + h
                    if idx >= 2:
                        t.wait_ge(kb.h("SC"), b_sc + 2 * (idx - 2) + 1)
                    for k in range(CT):
                        i = t.matmul(out=pDec[idx % 2][:, 0:128], lhsT=xnb[:, k, iw:iw + 128],
                                     rhs=eq[:, h, k], start=(k == 0), stop=(k == CT - 1))
                    kb.inc(i, "qm")
                    t.wait_ge(kb.h("SC"), b_sc + 2 * idx + 2)
                    if idx >= 2:
                        t.wait_ge(kb.h("V"), b_v + idx - 1)
                    kb.inc(t.matmul(out=pG[idx % 2][0:128, 128:129],
                                    lhsT=u2s[:, idx % 2, 0:128],
                                    rhs=recipm[:, h:h + 1], start=True, stop=True), "qm")

        def p_qt_s(sc, b_qm=b_qm, b_v=b_v):
            for mi in range(NT_H):
                iw = 31 + 128 * mi
                for h in range(8):
                    idx = mi * 8 + h
                    sc._eng.wait_ge(kb.h("qm"), b_qm + 2 * idx + 1)
                    sc.activation(out=sqtm[:, mi, h], in_=pDec[idx % 2][:, 0:128],
                                  func=AF.Relu)
                    if idx >= 2:
                        sc._eng.wait_ge(kb.h("V"), b_v + idx - 1)
                    sc.activation(out=u2s[:, idx % 2, 0:128], in_=su[:, h, iw:iw + 128],
                                  func=AF.Square)

        def p_qt_v(v, b_qm=b_qm):
            for mi in range(NT_H):
                for h in range(8):
                    idx = mi * 8 + h
                    v._eng.wait_ge(kb.h("qm"), b_qm + 2 * idx + 2)
                    v.tensor_copy(out=sinvq[:, mi, h:h + 1],
                                  in_=pG[idx % 2][0:128, 128:129])
        kb.phase(tensor=p_qt_t, scalar=p_qt_s, vector=p_qt_v)

        def p_qf_s(sc):
            for mi in range(NT_H):
                sc.activation(out=sinvq[:, mi], in_=sinvq[:, mi], func=AF.Sqrt)
        kb.phase(scalar=p_qf_s)

        def p_qf_v(v):
            for mi in range(NT_H):
                v.tensor_scalar_max(sinvq[:, mi], sinvq[:, mi], 1e-12)
                v.reciprocal(out=sinvq[:, mi], in_=sinvq[:, mi])
        kb.phase(vector=p_qf_v)

        b_im, b_ic = kb.n("im"), kb.n("ic")
        b_sc, b_v = kb.n("SC"), kb.n("V")
        VPT = 8 * 7 + 8

        def vcnt(mi, h, j, b_v=b_v):
            return b_v + mi * VPT + h * 7 + j

        def scnt(mi, h, j, b_sc=b_sc):
            return b_sc + (mi * 8 + h) * 3 + j

        def p_at_t(t, b_im=b_im, b_ic=b_ic, b_v=b_v):
            for mi in range(NT_H):
                iw = 31 + 128 * mi
                jA = iw - 31
                jB = iw + 97
                for h in range(8):
                    idx = mi * 8 + h
                    pp = idx % 2
                    pmi, ph = divmod(idx - 2, 8)
                    if idx >= 2:
                        t.wait_ge(kb.h("SC"), scnt(pmi, ph, 2))
                    t.matmul(out=pG[pp][:, 0:128], lhsT=su[:, h, jA:jA + 128],
                             rhs=su[:, h, iw:iw + 128], start=True, stop=False)
                    t.matmul(out=pG[pp][:, 0:128], lhsT=arow[0:1, h, jA:jA + 128],
                             rhs=t_onr[0:1, 0:128], start=False, stop=False)
                    t.matmul(out=pG[pp][:, 0:128], lhsT=t_onr[0:1, 0:128],
                             rhs=arow[0:1, h, iw:iw + 128], start=False, stop=True)
                    t.matmul(out=pG[pp][0:31, 128:256], lhsT=su[:, h, jB:jB + 31],
                             rhs=su[:, h, iw:iw + 128], start=True, stop=False)
                    t.matmul(out=pG[pp][0:31, 128:256], lhsT=arow[0:1, h, jB:jB + 31],
                             rhs=t_onr[0:1, 0:128], start=False, stop=False)
                    kb.inc(t.matmul(out=pG[pp][0:31, 128:256], lhsT=t_onr[0:1, 0:31],
                                    rhs=arow[0:1, h, iw:iw + 128], start=False, stop=True),
                           "im")
                    t.wait_ge(kb.h("V"), vcnt(mi, h, 2))
                    t.matmul(out=pCH[pp][:, 256:257], lhsT=sinfl[:, pp, 0:128],
                             rhs=t_oneb[:], start=True, stop=False)
                    t.matmul(out=pCH[pp][:, 256:257], lhsT=sinfl[0:31, pp, 128:256],
                             rhs=t_oneb[0:31, :], start=False, stop=True)
                    t.matmul(out=pCH[pp][:, 0:128], lhsT=sinfl[:, pp, 0:128],
                             rhs=svtm[:, mi, h], start=True, stop=False)
                    t.matmul(out=pCH[pp][:, 0:128], lhsT=sinfl[0:31, pp, 128:256],
                             rhs=svtm[0:31, mi + 1, h], start=False, stop=True)
                    kb.inc(t.matmul(out=pCH[pp][:, 128:256], lhsT=su[:, h, iw:iw + 128],
                                    rhs=shebs[:, h], start=True, stop=True), "im")
                    t.wait_ge(kb.h("V"), vcnt(mi, h, 7))
                    if idx >= 2:
                        t.wait_ge(kb.h("SC"), scnt(pmi, ph, 3))
                    kb.inc(t.transpose(out=pT[pp][:], in_=soh[:, pp], identity=t_idb[:]),
                           "im")
                t.wait_ge(kb.h("SC"), scnt(mi, 7, 3))
                if mi >= 1:
                    t.wait_ge(kb.h("V"), b_v + mi * VPT)
                for ohalf in range(2):
                    for o in range(4):
                        oc = ohalf * 4 + o
                        for k in range(CT):
                            i = t.matmul(out=pDec[ohalf][:, 128 * o:128 * (o + 1)],
                                         lhsT=sdec[:, k, 128 * oc:128 * (oc + 1)],
                                         rhs=sflat[:, 0, k],
                                         start=(k == 0), stop=(k == CT - 1))
                kb.inc(i, "ic")

        def p_at_s(sc, b_im=b_im, b_ic=b_ic):
            for mi in range(NT_H):
                for h in range(8):
                    idx = mi * 8 + h
                    pp = idx % 2
                    sc._eng.wait_ge(kb.h("V"), vcnt(mi, h, 1))
                    sc.activation(out=sdq[:, pp], in_=sdq[:, pp], func=AF.Sqrt,
                                  scale=kb.t_misc[:, 4:5], bias=kb.t_misc[:, 0:1])
                    sc.activation(out=sinfl[:, pp], in_=sdq[:, pp], func=AF.Exp,
                                  scale=-1.0)
                    sc._eng.wait_ge(kb.h("im"), b_im + 3 * idx + 3)
                    if h == 0 and mi >= 1:
                        sc._eng.wait_ge(kb.h("ic"), b_ic + mi)
                    sc.activation(out=sflat[:, 0, h], in_=pT[pp][:], func=AF.Copy)

        def p_at_v(v, b_im=b_im, b_ic=b_ic, half=half, u0=u0):
            for mi in range(NT_H):
                for h in range(8):
                    idx = mi * 8 + h
                    pp = idx % 2
                    pmi, ph = divmod(idx - 2, 8)
                    v._eng.wait_ge(kb.h("im"), b_im + 3 * idx + 1)
                    if idx >= 2:
                        v._eng.wait_ge(kb.h("SC"), scnt(pmi, ph, 2))
                    v.tensor_scalar_min(sdq[:, pp], pG[pp][:], 0.0)
                    v._eng.wait_ge(kb.h("SC"), scnt(mi, h, 2))
                    v.tensor_mul(out=sinfl[:, pp], in0=sinfl[:, pp], in1=t_band[:])
                    v._eng.wait_ge(kb.h("im"), b_im + 3 * idx + 2)
                    v.tensor_scalar_add(sinvd[:, pp:pp + 1], pCH[pp][:, 256:257], 1e-8)
                    v.reciprocal(out=sinvd[:, pp:pp + 1], in_=sinvd[:, pp:pp + 1])
                    v.tensor_scalar_mul(scf[:, pp], pCH[pp][:, 0:128], sinvd[:, pp:pp + 1])
                    v.scalar_tensor_tensor(out=schb[:, pp], in0=pCH[pp][:, 128:256],
                                           scalar=sinvq[:, mi, h:h + 1], in1=scf[:, pp],
                                           op0=ALU.mult, op1=ALU.add)
                    v.tensor_mul(out=soh[:, pp], in0=sqtm[:, mi, h], in1=schb[:, pp])
                v._eng.wait_ge(kb.h("ic"), b_ic + mi + 1)
                xoff = u0 + 31 + 128 * mi
                for ohalf in range(2):
                    for o in range(4):
                        oc = ohalf * 4 + o
                        v.tensor_add(out=xt[:, oc, xoff:xoff + 128],
                                     in0=xt[:, oc, xoff:xoff + 128],
                                     in1=pDec[ohalf][:, 128 * o:128 * (o + 1)])
        kb.phase(tensor=p_at_t, scalar=p_at_s, vector=p_at_v)

    if DEBUG_L0A:
        def p_dbg(sy):
            kb.inc(sy.dma_start(out=dbg_sig, in_=sig[:]), "st", 16)
            kb.inc(sy.dma_start(out=dbg_mdiag[:, 0:8], in_=recipm[:]), "st", 16)
            kb.inc(sy.dma_start(out=dbg_mdiag[:, 8:16], in_=uscale[:]), "st", 16)
            kb.inc(sy.dma_start(out=dbg_mdiag[:, 16:24], in_=modcol[:]), "st", 16)
            sy.wait_ge(kb.h("st"), kb.n("st"))

        def p_dbg_g(g):
            for k in range(CT):
                kb.inc(g.dma_start(out=dbg_xnb[:, k], in_=xnb[:, k, 0:256]), "st", 16)
            for h in range(8):
                kb.inc(g.dma_start(out=dbg_su[:, h], in_=su[:, h, 0:256]), "st", 16)
                kb.inc(g.dma_start(out=dbg_vtm[:, h], in_=svtm[:, 0, h]), "st", 16)
                kb.inc(g.dma_start(out=dbg_qtm[:, h], in_=sqtm[:, 0, h]), "st", 16)
            kb.inc(g.dma_start(out=dbg_arow, in_=arow[0:1, :, :]), "st", 16)
            kb.inc(g.dma_start(out=dbg_infl, in_=sinfl[:]), "st", 16)
            kb.inc(g.dma_start(out=dbg_sinvq, in_=sinvq[:]), "st", 16)
            g.wait_ge(kb.h("st"), kb.n("st"))
        kb.phase(sync=p_dbg, gpsimd=p_dbg_g)

    def p_store(sy):
        for k in range(CT):
            kb.inc(sy.dma_start(out=y1[128 * k:128 * (k + 1), :],
                                in_=xt[:, k, 31:31 + L0A_W]), "st", 16)
        sy.wait_ge(kb.h("st"), kb.n("st"))
    kb.phase(sync=p_store)


F0A_XW = 2436
F0A_NKV = 19
F0A_NIT = 17


def build_f0_attn(dbg=False):
    kb = KB()
    io = {"xh": kb.dram_in("xh", [D, F0A_XW], F32),
          "wq": kb.dram_in("wq", [D, 1024], F32),
          "wk": kb.dram_in("wk", [D, 1024], F32),
          "wv": kb.dram_in("wv", [D, 1024], F32),
          "wo": kb.dram_in("wo", [1024, D], F32),
          "qs_col": kb.dram_in("qs_col", [128, 1], F32),
          "ks_col": kb.dram_in("ks_col", [128, 1], F32),
          "csq": kb.dram_in("csq", [128, 2176], BF16),
          "snq": kb.dram_in("snq", [128, 2176], BF16),
          "csk": kb.dram_in("csk", [128, 2432], BF16),
          "snk": kb.dram_in("snk", [128, 2432], BF16),
          "maskp": kb.dram_in("maskp", [4, 128, 384], BF16),
          "ya": kb.dram_out("ya", [D, T], F32)}
    kb.dram_in("qsp_col", [128, 1], F32)
    kb.dram_in("ksp_col", [128, 1], F32)
    alloc_common(kb)
    kb.sems("ld", "km", "qm", "sm", "wm", "st")
    kb.start()
    kb.push()
    emit_f0_attn(kb, io, dbg=dbg)
    kb.pop()
    return kb


def emit_f0_attn(kb, io, with_t0=False, dbg=False):
    xh = io["xh"]
    wq_d, wk_d, wv_d, wo_d = io["wq"], io["wk"], io["wv"], io["wo"]
    qs_d, ks_d = io["qs_col"], io["ks_col"]
    csq_d, snq_d, csk_d, snk_d = io["csq"], io["snq"], io["csk"], io["snk"]
    mask_d, ya = io["maskp"], io["ya"]
    M0 = 0 if with_t0 else 1
    d_identb = kb.cin("identb")
    d_ind2 = kb.cin("ind2b")
    d_sel8 = kb.cin("sel8b")
    if dbg:
        dbg_q = kb.dram_out("dbg_q", [128, 4, 128], F32)
        dbg_k = kb.dram_out("dbg_k", [128, 4, 128], F32)
        dbg_v = kb.dram_out("dbg_v", [128, 512], F32)
        dbg_p = kb.dram_out("dbg_p", [128, 2, 384], F32)
        dbg_av = kb.dram_out("dbg_av", [128, 4, 128], F32)

    xt = kb.sbuf("xt", [128, CT, F0A_XW], F32)
    t_idb = kb.sbuf("t_idb", [128, 128], BF16)
    t_ind2 = kb.sbuf("t_ind2", [128, 2], BF16)
    t_sel8 = kb.sbuf("t_sel8", [8, 4, 128], BF16)
    wkb = kb.sbuf("wkb", [128, CT, 512], BF16)
    wqb = wkb
    wvb = kb.sbuf("wvb", [128, CT, 512], BF16)
    wob = kb.sbuf("wob", [128, 4, D], BF16)
    qs_c = kb.sbuf("qs_c", [128, 4], F32)
    csf = kb.sbuf("csf", [128, 2432], BF16)
    snf = kb.sbuf("snf", [128, 2432], BF16)
    csqf = cskf = csf
    snqf = snkf = snf
    kfull = kb.sbuf("kfull", [128, 4, F0A_NKV, 128], BF16)
    vfull = kb.sbuf("vfull", [128, F0A_NKV, 512], BF16)
    maskp = kb.sbuf("maskp_s", [128, 4, 384], BF16)
    xsb = kb.sbuf("xsb", [128, 2, CT, 132], BF16)
    kq = kb.sbuf("kq", [128, 2, 4, 128], BF16)
    kq2 = kb.sbuf("kq2", [128, 2, 4, 128], BF16)
    rcol = kb.sbuf("rcol", [128, 2, 8], F32)
    rcolb = kb.sbuf("rcolb", [128, 2, 8], BF16)
    srowb = kb.sbuf("srowb", [8, 2, 128], BF16)
    kn1 = kb.sbuf("kn1", [128, 4, 128], BF16)
    rta = kb.sbuf("rta", [128, 128], F32)
    rtb = kb.sbuf("rtb", [128, 128], F32)
    qpb = kb.sbuf("qpb", [128, 2, 4, 128], BF16)
    pmb = kb.sbuf("pmb", [128, 2, 384], BF16)
    pmt = kb.sbuf("pmt", [128, 2, 3, 128], BF16)
    avs = kb.sbuf("avs", [128, 2, 4, 128], BF16)
    nrow = kb.sbuf("nrow", [128, 2, 2], F32)
    dlt = kb.sbuf("dlt", [128, CT, (F0A_NIT - M0) * 128], BF16)

    pQ = [kb.psum(f"pQ{i}", [128, 512]) for i in range(2)]
    pNs = kb.psum("pNs", [128, 512])
    pTT = [kb.psum(f"pTT{i}", [128, 384], BF16) for i in range(2)]
    pSA = [kb.psum(f"pSA{i}", [128, 512]) for i in range(2)]
    pWo = kb.psum("pWo", [128, 512])
    pBC = pWo

    def p_load_sy(sy):
        for k in range(CT):
            kb.inc(sy.dma_start(out=xt[:, k], in_=xh[128 * k:128 * (k + 1), :]), "ld", 16)
        kb.inc(sy.dma_start(out=qs_c[:, 0:1], in_=qs_d), "ld", 16)
        kb.inc(sy.dma_start(out=qs_c[:, 1:2], in_=ks_d), "ld", 16)

    def p_load_g(g):
        load_common(kb, g)
        kb.inc(g.dma_start(out=t_idb[:], in_=d_identb), "ld", 16)
        kb.inc(g.dma_start(out=t_ind2[:], in_=d_ind2), "ld", 16)
        kb.inc(g.dma_start(out=t_sel8[:], in_=d_sel8), "ld", 16)
        for m in range(4):
            kb.inc(g.dma_start(out=maskp[:, m], in_=mask_d[m]), "ld", 16)
    kb.phase(sync=p_load_sy, gpsimd=p_load_g)


    def _partner(qtr):
        return 32 * (qtr + 1) if qtr % 2 == 0 else 32 * (qtr - 1)

    for half in range(2):
        def p_wl(g, half=half):
            for k in range(CT):
                kb.inc(g.dma_start(out=wkb[:, k], in_=wk_d[128 * k:128 * (k + 1),
                                                          512 * half:512 * (half + 1)]), "ld", 16)
                kb.inc(g.dma_start(out=wvb[:, k], in_=wv_d[128 * k:128 * (k + 1),
                                                          512 * half:512 * (half + 1)]), "ld", 16)
            for c in range(4):
                r0 = 512 * half + 128 * c
                kb.inc(g.dma_start(out=wob[:, c], in_=wo_d[r0:r0 + 128, :]), "ld", 16)
            kb.inc(g.dma_start(out=cskf[:], in_=csk_d), "ld", 16)
            kb.inc(g.dma_start(out=snkf[:], in_=snk_d), "ld", 16)
        kb.phase(gpsimd=p_wl)

        def p_foldk(v):
            v.tensor_scalar_mul(cskf[:], cskf[:], qs_c[:, 1:2])
            v.tensor_scalar_mul(snkf[:], snkf[:], qs_c[:, 1:2])
        kb.phase(vector=p_foldk)

        b_km, b_sc, b_v = kb.n("km"), kb.n("SC"), kb.n("V")
        SCK = 18
        VCK = 33

        def p_kv_t(t, b_km=b_km, b_sc=b_sc, b_v=b_v):
            for kv in range(F0A_NKV):
                pp = kv % 2
                t.wait_ge(kb.h("SC"), b_sc + SCK * kv + 8)
                if kv >= 2:
                    t.wait_ge(kb.h("SC"), b_sc + SCK * (kv - 2) + 17)
                for pt in range(4):
                    for k in range(CT):
                        t.matmul(out=pQ[pp][:, 128 * pt:128 * (pt + 1)],
                                 lhsT=wkb[:, k, 128 * pt:128 * (pt + 1)],
                                 rhs=xsb[:, pp, k, 1:129] if k < 4 else xsb[:, pp, k, 0:128],
                                 start=(k == 0), stop=(k == CT - 1))
                for k in range(CT):
                    i = t.matmul(out=pSA[pp][:],
                                 lhsT=xsb[:, pp, k, 1:129] if k < 4 else xsb[:, pp, k, 0:128],
                                 rhs=wvb[:, k],
                                 start=(k == 0), stop=(k == CT - 1))
                kb.inc(i, "km")
                t.wait_ge(kb.h("SC"), b_sc + SCK * kv + 16)
                if kv >= 1:
                    t.wait_ge(kb.h("V"), b_v + VCK * (kv - 1) + 5)
                for pt in range(4):
                    i = t.matmul(out=pNs[:, 2 * pt:2 * pt + 2], lhsT=kq2[:, pp, pt],
                                 rhs=t_ind2[:], start=True, stop=True)
                kb.inc(i, "km")
                t.wait_ge(kb.h("V"), b_v + VCK * kv + 4)
                kb.inc(t.matmul(out=pNs[0:8, 128:256], lhsT=rcolb[:, pp], rhs=t_idb[:],
                                start=True, stop=True), "km")
                t.wait_ge(kb.h("V"), b_v + VCK * kv + 5)
                for pt in range(4):
                    i = t.matmul(out=pBC[:, 128 * pt:128 * (pt + 1)],
                                 lhsT=t_sel8[:, pt], rhs=srowb[:, pp],
                                 start=True, stop=True)
                kb.inc(i, "km")

        def p_kv_s(sc, b_km=b_km, b_v=b_v):
            for kv in range(F0A_NKV):
                pp = kv % 2
                x0 = 128 * kv
                if kv >= 2:
                    sc._eng.wait_ge(kb.h("km"), b_km + 4 * (kv - 2) + 1)
                for k in range(CT):
                    sc.activation(out=xsb[:, pp, k, 0:129], in_=xt[:, k, x0:x0 + 129],
                                  func=AF.Copy)
                sc._eng.wait_ge(kb.h("km"), b_km + 4 * kv + 1)
                for pt in range(4):
                    sc.activation(out=kq[:, pp, pt], in_=pQ[pp][:, 128 * pt:128 * (pt + 1)],
                                  func=AF.Copy)
                    sc.activation(out=kq2[:, pp, pt], in_=pQ[pp][:, 128 * pt:128 * (pt + 1)],
                                  func=AF.Square)
                sc.activation(out=vfull[:, kv], in_=pSA[pp][:], func=AF.Copy)
                sc._eng.wait_ge(kb.h("V"), b_v + VCK * kv + 1)
                sc.activation(out=rcol[:, pp], in_=rcol[:, pp], func=AF.Sqrt)

        def p_kv_v(v, b_km=b_km, b_sc=b_sc):
            for kv in range(F0A_NKV):
                pp = kv % 2
                j0 = 128 * kv
                v._eng.wait_ge(kb.h("km"), b_km + 4 * kv + 2)
                v.tensor_copy(out=rcol[:, pp], in_=pNs[:, 0:8])
                v._eng.wait_ge(kb.h("SC"), b_sc + SCK * kv + 18)
                v.tensor_scalar_max(rcol[:, pp], rcol[:, pp], 1e-12)
                v.reciprocal(out=rcol[:, pp], in_=rcol[:, pp])
                v.tensor_copy(out=rcolb[:, pp], in_=rcol[:, pp])
                v._eng.wait_ge(kb.h("km"), b_km + 4 * kv + 3)
                v.tensor_copy(out=srowb[:, pp], in_=pNs[0:8, 128:256])
                v._eng.wait_ge(kb.h("km"), b_km + 4 * kv + 4)
                for pt in range(4):
                    v.tensor_mul(out=kn1[:, pt], in0=kq[:, pp, pt],
                                 in1=pBC[:, 128 * pt:128 * (pt + 1)])
                    v.tensor_mul(out=rta[:], in0=kn1[:, pt], in1=cskf[:, j0:j0 + 128])
                    for qtr in range(4):
                        a = 32 * qtr
                        prt = _partner(qtr)
                        v.tensor_mul(out=rtb[a:a + 32, :], in0=kn1[prt:prt + 32, pt],
                                     in1=snkf[prt:prt + 32, j0:j0 + 128])
                    v.tensor_add(out=kfull[:, pt, kv], in0=rta[:], in1=rtb[:])
        kb.phase(tensor=p_kv_t, scalar=p_kv_s, vector=p_kv_v)

        def p_wq(g, half=half):
            for k in range(CT):
                kb.inc(g.dma_start(out=wqb[:, k], in_=wq_d[128 * k:128 * (k + 1),
                                                          512 * half:512 * (half + 1)]), "ld", 16)
            kb.inc(g.dma_start(out=csqf[:, 0:2176], in_=csq_d), "ld", 16)
            kb.inc(g.dma_start(out=snqf[:, 0:2176], in_=snq_d), "ld", 16)
        kb.phase(gpsimd=p_wq)

        def p_foldq(v):
            v.tensor_scalar_mul(csqf[:, 0:2176], csqf[:, 0:2176], qs_c[:, 0:1])
            v.tensor_scalar_mul(snqf[:, 0:2176], snqf[:, 0:2176], qs_c[:, 0:1])
        kb.phase(vector=p_foldq)

        b_qm, b_sm, b_wm = kb.n("qm"), kb.n("sm"), kb.n("wm")
        b_sc, b_v = kb.n("SC"), kb.n("V")
        SCM = 61
        VR = 8 if half == 0 else 16
        VCM = 73 + VR

        def sco(m, j):
            return b_sc + SCM * m + j

        def vco(m, j):
            return b_v + VCM * m + j

        def p_i_t(t, b_qm=b_qm, b_sm=b_sm, b_wm=b_wm):
            for jt, m in enumerate(range(M0, F0A_NIT)):
                pp = jt % 2
                i0 = 128 * m
                t.wait_ge(kb.h("SC"), sco(jt, 12))
                if jt >= 2:
                    t.wait_ge(kb.h("SC"), sco(jt - 2, 20))
                for pt in range(4):
                    for k in range(CT):
                        i = t.matmul(out=pQ[pp][:, 128 * pt:128 * (pt + 1)],
                                     lhsT=wqb[:, k, 128 * pt:128 * (pt + 1)],
                                     rhs=xsb[:, pp, k, 1:129] if k < 4 else xsb[:, pp, k, 0:128],
                                     start=(k == 0), stop=(k == CT - 1))
                kb.inc(i, "qm")
                t.wait_ge(kb.h("SC"), sco(jt, 20))
                if jt >= 1:
                    t.wait_ge(kb.h("V"), vco(jt - 1, 5))
                for pt in range(4):
                    i = t.matmul(out=pNs[:, 2 * pt:2 * pt + 2], lhsT=kq2[:, pp, pt],
                                 rhs=t_ind2[:], start=True, stop=True)
                kb.inc(i, "qm")
                t.wait_ge(kb.h("V"), vco(jt, 4))
                kb.inc(t.matmul(out=pNs[0:8, 128:256], lhsT=rcolb[:, pp], rhs=t_idb[:],
                                start=True, stop=True), "qm")
                t.wait_ge(kb.h("V"), vco(jt, 5))
                for pt in range(4):
                    i = t.matmul(out=pBC[:, 128 * pt:128 * (pt + 1)],
                                 lhsT=t_sel8[:, pt], rhs=srowb[:, pp],
                                 start=True, stop=True)
                kb.inc(i, "qm")
                for h in range(8):
                    sp = h % 2
                    pt = h // 2
                    es = 64 * (h % 2)
                    t.wait_ge(kb.h("V"), vco(jt, 33))
                    if h >= 2:
                        t.wait_ge(kb.h("SC"), sco(jt, 21 + 5 * (h - 1)))
                    elif jt >= 1:
                        t.wait_ge(kb.h("SC"), sco(jt - 1, 21 + 5 * (6 + h) + 5))
                    for jj in range(3):
                        i = t.matmul(out=pSA[sp][:, 128 * jj:128 * (jj + 1)],
                                     lhsT=qpb[es:es + 64, pp, pt, :],
                                     rhs=kfull[es:es + 64, pt, m + jj, :],
                                     start=True, stop=True)
                    kb.inc(i, "sm")
                    t.wait_ge(kb.h("V"), vco(jt, 33 + 5 * h + 5))
                    if h >= 2:
                        t.wait_ge(kb.h("SC"), sco(jt, 21 + 5 * (h - 2) + 4))
                    for jj in range(3):
                        i = t.transpose(out=pTT[sp][:, 128 * jj:128 * (jj + 1)],
                                        in_=pmb[:, sp, 128 * jj:128 * (jj + 1)],
                                        identity=t_idb[:])
                    kb.inc(i, "sm")
                    t.wait_ge(kb.h("SC"), sco(jt, 21 + 5 * h + 4))
                    for jj in range(3):
                        i = t.matmul(out=pSA[sp][0:64, 384:512],
                                     lhsT=vfull[:, m + jj, 64 * h:64 * h + 64],
                                     rhs=pmt[:, sp, jj], start=(jj == 0), stop=(jj == 2))
                    kb.inc(i, "sm")
                t.wait_ge(kb.h("SC"), sco(jt, 61))
                if jt >= 1:
                    t.wait_ge(kb.h("V"), vco(jt - 1, VCM))
                for grp in range(2):
                    for oc in range(4):
                        och = 4 * grp + oc
                        for c in range(4):
                            i = t.matmul(out=pWo[:, 128 * oc:128 * (oc + 1)],
                                         lhsT=wob[:, c, 128 * och:128 * (och + 1)],
                                         rhs=avs[:, pp, c, :],
                                         start=(c == 0), stop=(c == 3))
                    kb.inc(i, "wm")
                    if grp == 0:
                        t.wait_ge(kb.h("V"), vco(jt, 73 + VR // 2))

        def p_i_s(sc, b_qm=b_qm, b_sm=b_sm):
            for jt, m in enumerate(range(M0, F0A_NIT)):
                pp = jt % 2
                x0 = 256 + 128 * m
                if jt >= 2:
                    sc._eng.wait_ge(kb.h("qm"), b_qm + 4 * (jt - 2) + 1)
                for k in range(CT):
                    sc.activation(out=xsb[:, pp, k, 0:129], in_=xt[:, k, x0:x0 + 129],
                                  func=AF.Copy)
                for k in range(4, CT):
                    sc.activation(out=xsb[:, pp, k, 0:1],
                                  in_=(xt[:, k, x0:x0 + 1] if jt == 0
                                       else xsb[:, 1 - pp, k, 128:129]),
                                  func=AF.Copy)
                sc._eng.wait_ge(kb.h("qm"), b_qm + 4 * jt + 1)
                for pt in range(4):
                    sc.activation(out=kq[:, pp, pt], in_=pQ[pp][:, 128 * pt:128 * (pt + 1)],
                                  func=AF.Copy)
                    sc.activation(out=kq2[:, pp, pt], in_=pQ[pp][:, 128 * pt:128 * (pt + 1)],
                                  func=AF.Square)
                sc._eng.wait_ge(kb.h("V"), vco(jt, 1))
                sc.activation(out=rcol[:, pp], in_=rcol[:, pp], func=AF.Sqrt)
                for h in range(8):
                    sp = h % 2
                    sc._eng.wait_ge(kb.h("V"), vco(jt, 33 + 5 * h + 1))
                    sc.activation(out=pmb[:, sp], in_=pSA[sp][:, 0:384], func=AF.Exp,
                                  bias=nrow[:, sp, 0:1])
                    sc._eng.wait_ge(kb.h("sm"), b_sm + 3 * (8 * jt + h) + 2)
                    for jj in range(3):
                        sc.activation(out=pmt[:, sp, jj],
                                      in_=pTT[sp][:, 128 * jj:128 * (jj + 1)],
                                      func=AF.Copy)
                    sc._eng.wait_ge(kb.h("sm"), b_sm + 3 * (8 * jt + h) + 3)
                    if h == 0 and jt >= 2:
                        sc._eng.wait_ge(kb.h("wm"), b_wm + 2 * (jt - 1))
                    sc.activation(out=avs[64 * (h % 2):64 * (h % 2) + 64, pp, h // 2, :],
                                  in_=pSA[sp][0:64, 384:512], func=AF.Copy)

        def p_i_v(v, b_qm=b_qm, b_sm=b_sm, b_wm=b_wm):
            for jt, m in enumerate(range(M0, F0A_NIT)):
                pp = jt % 2
                i0 = 128 * m
                v._eng.wait_ge(kb.h("qm"), b_qm + 4 * jt + 2)
                v.tensor_copy(out=rcol[:, pp], in_=pNs[:, 0:8])
                v._eng.wait_ge(kb.h("SC"), sco(jt, 21))
                v.tensor_scalar_max(rcol[:, pp], rcol[:, pp], 1e-12)
                v.reciprocal(out=rcol[:, pp], in_=rcol[:, pp])
                v.tensor_copy(out=rcolb[:, pp], in_=rcol[:, pp])
                v._eng.wait_ge(kb.h("qm"), b_qm + 4 * jt + 3)
                v.tensor_copy(out=srowb[:, pp], in_=pNs[0:8, 128:256])
                v._eng.wait_ge(kb.h("qm"), b_qm + 4 * jt + 4)
                for pt in range(4):
                    v.tensor_mul(out=kn1[:, pt], in0=kq[:, pp, pt],
                                 in1=pBC[:, 128 * pt:128 * (pt + 1)])
                    v.tensor_mul(out=rta[:], in0=kn1[:, pt], in1=csqf[:, i0:i0 + 128])
                    for qtr in range(4):
                        a = 32 * qtr
                        prt = _partner(qtr)
                        v.tensor_mul(out=rtb[a:a + 32, :], in0=kn1[prt:prt + 32, pt],
                                     in1=snqf[prt:prt + 32, i0:i0 + 128])
                    v.tensor_add(out=qpb[:, pp, pt], in0=rta[:], in1=rtb[:])
                for h in range(8):
                    sp = h % 2
                    v._eng.wait_ge(kb.h("sm"), b_sm + 3 * (8 * jt + h) + 1)
                    v.reduce_max(out=nrow[:, sp, 0:1], in_=pSA[sp][:, 0:384],
                                 axis=AX.X, negate=True)
                    v._eng.wait_ge(kb.h("SC"), sco(jt, 21 + 5 * h + 1))
                    v.tensor_mul(out=pmb[:, sp], in0=pmb[:, sp], in1=maskp[:, min(m, 3)])
                    v.reduce_sum(out=nrow[:, sp, 1:2], in_=pmb[:, sp], axis=AX.X)
                    v.reciprocal(out=nrow[:, sp, 1:2], in_=nrow[:, sp, 1:2])
                    v.tensor_scalar_mul(pmb[:, sp], pmb[:, sp], nrow[:, sp, 1:2])
                xoff = 257 + 128 * m
                for grp in range(2):
                    v._eng.wait_ge(kb.h("wm"), b_wm + 2 * jt + grp + 1)
                    for oc in range(4):
                        och = 4 * grp + oc
                        if half == 0:
                            v.tensor_copy(out=dlt[:, och, 128 * (m - M0):128 * (m - M0 + 1)],
                                          in_=pWo[:, 128 * oc:128 * (oc + 1)])
                        else:
                            v.tensor_add(out=rta[:], in0=pWo[:, 128 * oc:128 * (oc + 1)],
                                         in1=dlt[:, och, 128 * (m - M0):128 * (m - M0 + 1)])
                            v.tensor_add(out=xt[:, och, xoff:xoff + 128],
                                         in0=xt[:, och, xoff:xoff + 128],
                                         in1=rta[:])
        kb.phase(tensor=p_i_t, scalar=p_i_s, vector=p_i_v)

    if dbg:
        def p_dbg(g):
            for pt in range(4):
                kb.inc(g.dma_start(out=dbg_q[:, pt], in_=qpb[:, 0, pt]), "st", 16)
                kb.inc(g.dma_start(out=dbg_k[:, pt], in_=kfull[:, pt, 18]), "st", 16)
                kb.inc(g.dma_start(out=dbg_av[:, pt], in_=avs[:, 0, pt]), "st", 16)
            kb.inc(g.dma_start(out=dbg_v, in_=vfull[:, 18]), "st", 16)
            kb.inc(g.dma_start(out=dbg_p, in_=pmb[:]), "st", 16)
            g.wait_ge(kb.h("st"), kb.n("st"))
        kb.phase(gpsimd=p_dbg)

    yx0 = 385 - 128 * (1 - M0)

    def p_store(sy):
        for k in range(CT):
            kb.inc(sy.dma_start(out=ya[128 * k:128 * (k + 1), :],
                                in_=xt[:, k, yx0:2433]), "ld", 16)
        sy.wait_ge(kb.h("ld"), kb.n("ld"))
    kb.phase(sync=p_store)


def build_f0_ffn():
    kb = KB()
    io = {"xf": kb.dram_in("xf", [D, 2052], F32),
          "w1p": kb.dram_in("w1p", [44, D, 128], F32),
          "w2po": kb.dram_in("w2po", [8, 2816, 128], F32),
          "g_col": kb.dram_in("g_col", [128, 8], F32),
          "b_col": kb.dram_in("b_col", [128, 8], F32),
          "b1a_col": kb.dram_in("b1a_col", [128, 22], F32),
          "b1g_col": kb.dram_in("b1g_col", [128, 22], F32),
          "b2_col": kb.dram_in("b2_col", [128, 8], F32),
          "yf": kb.dram_out("yf", [D, T], F32)}
    alloc_common(kb)
    kb.sems("ld", "m", "hm", "wm", "w1l", "w2l", "st")
    kb.start()
    kb.push()
    emit_f0_ffn(kb, io)
    kb.pop()
    return kb


def emit_f0_ffn(kb, io, hmask=None):
    xf, w1p_d, w2po_d = io["xf"], io["w1p"], io["w2po"]
    g_d, b_d, b1a_d, b1g_d, b2_d = (io["g_col"], io["b_col"], io["b1a_col"],
                                    io["b1g_col"], io["b2_col"])
    yf = io["yf"]
    gcol = kb.sbuf("gcol", [128, 8], F32)
    bcol = kb.sbuf("bcol", [128, 8], F32)
    b1a = kb.sbuf("b1a", [128, 22], F32)
    b1g = kb.sbuf("b1g", [128, 22], F32)
    b2c = kb.sbuf("b2c", [128, 8], F32)
    xnb = kb.sbuf("xnb", [128, CT, T], BF16)

    pA = kb.psum("pA", [128, 512])
    pB = kb.psum("pB", [128, 512])
    pHa = [kb.psum(f"pHa{i}", [128, 512]) for i in range(2)]
    pHg = [kb.psum(f"pHg{i}", [128, 512]) for i in range(2)]
    pW = [kb.psum(f"pW{i}", [128, 512]) for i in range(2)]

    ov_a = ExitStack()
    xt = ov_a.enter_context(kb.sbuf_cm("xt", [128, CT, 2052], F32))
    kb.t_lnxb = ov_a.enter_context(kb.sbuf_cm("lnxb", [128, CT, 512], BF16))
    kb.t_lnrow = ov_a.enter_context(kb.sbuf_cm("lnrow", [128, 4, 512], F32))
    sc512 = ov_a.enter_context(kb.sbuf_cm("sc512", [128, 512], F32))

    def p_load_sy(sy):
        for k in range(CT):
            kb.inc(sy.dma_start(out=xt[:, k], in_=xf[128 * k:128 * (k + 1), :]), "ld", 16)
        kb.inc(sy.dma_start(out=gcol[:], in_=g_d), "ld", 16)
        kb.inc(sy.dma_start(out=bcol[:], in_=b_d), "ld", 16)
        kb.inc(sy.dma_start(out=b1a[:], in_=b1a_d), "ld", 16)
        kb.inc(sy.dma_start(out=b1g[:], in_=b1g_d), "ld", 16)
        kb.inc(sy.dma_start(out=b2c[:], in_=b2_d), "ld", 16)

    def p_load_g(g):
        load_common(kb, g)
    kb.phase(sync=p_load_sy, gpsimd=p_load_g)

    if hmask is not None:
        def p_hm(v):
            for k in range(CT):
                v.tensor_scalar_mul(xt[:, k, 0:1], xt[:, k, 0:1], hmask[:, 0:1])
        kb.phase(vector=p_hm)

    NCH = T // 512
    for ch in range(NCH):
        xs0 = 1 + 512 * ch

        def p_cast(sc, xs0=xs0):
            for k in range(CT):
                off = xs0 if k < 4 else xs0 - 1
                sc.activation(out=kb.t_lnxb[:, k], in_=xt[:, k, off:off + 512],
                              func=AF.Copy)
        kb.phase(scalar=p_cast)

        def p_mmA(t):
            for k in range(CT):
                i = t.matmul(out=pA[:], lhsT=kb.t_ones[:], rhs=kb.t_lnxb[:, k],
                             start=(k == 0), stop=(k == CT - 1))
            kb.inc(i, "m")
        kb.phase(tensor=p_mmA)

        def p_sq(v):
            for k in range(CT):
                v.tensor_mul(out=kb.t_lnxb[:, k], in0=kb.t_lnxb[:, k],
                             in1=kb.t_lnxb[:, k])
        kb.phase(vector=p_sq)

        def p_mmB(t):
            for k in range(CT):
                i = t.matmul(out=pB[:], lhsT=kb.t_ones[:], rhs=kb.t_lnxb[:, k],
                             start=(k == 0), stop=(k == CT - 1))
            kb.inc(i, "m")
        kb.phase(tensor=p_mmB)

        def p_v1(v):
            v.tensor_scalar_mul(kb.t_lnrow[:, 0], pA[:], 1.0 / D)
            v.tensor_scalar_mul(kb.t_lnrow[:, 1], pB[:], 1.0 / D)
        kb.phase(vector=p_v1)

        def p_s1(sc):
            sc.activation(out=kb.t_lnrow[:, 2], in_=kb.t_lnrow[:, 0], func=AF.Square)
        kb.phase(scalar=p_s1)

        def p_v2(v):
            v.tensor_sub(out=kb.t_lnrow[:, 1], in0=kb.t_lnrow[:, 1],
                         in1=kb.t_lnrow[:, 2])
        kb.phase(vector=p_v2)

        def p_s2(sc):
            sc.activation(out=kb.t_lnrow[:, 1], in_=kb.t_lnrow[:, 1],
                          func=AF.Sqrt, bias=kb.t_misc[:, 1:2])
        kb.phase(scalar=p_s2)

        def p_v3b(v, xs0=xs0, ch=ch):
            v.reciprocal(out=kb.t_lnrow[:, 3], in_=kb.t_lnrow[:, 1])
            for k in range(CT):
                off = xs0 if k < 4 else xs0 - 1
                v.tensor_sub(out=kb.t_lnrow[:, 2], in0=xt[:, k, off:off + 512],
                             in1=kb.t_lnrow[:, 0])
                v.tensor_mul(out=sc512[:], in0=kb.t_lnrow[:, 2], in1=kb.t_lnrow[:, 3])
                v.tensor_scalar(out=xnb[:, k, 512 * ch:512 * (ch + 1)], in0=sc512[:],
                                scalar1=gcol[:, k:k + 1], scalar2=bcol[:, k:k + 1],
                                op0=ALU.mult, op1=ALU.add)
        kb.phase(vector=p_v3b)
    ov_a.close()

    hbuf = kb.sbuf("hbuf", [128, 22, T], BF16)
    w1b = kb.sbuf("w1b", [128, 2, 2, CT, 128], BF16)
    w2b = kb.sbuf("w2b", [128, CT, 22, 128], BF16)
    gel = kb.sbuf("gel", [128, 2, 512], BF16)
    xres = kb.sbuf("xres", [128, CT, 512], F32)
    yst = kb.sbuf("yst", [128, 2, 512], F32)

    b_hm, b_sc, b_v, b_w1 = kb.n("hm"), kb.n("SC"), kb.n("V"), kb.n("w1l")

    def p_w1_g(g, b_hm=b_hm):
        for p in range(22):
            pp = p % 2
            if p >= 2:
                g._eng.wait_ge(kb.h("hm"), b_hm + 4 * (p - 1))
            kb.inc(g.dma_start(out=w1b[:, pp, 0],
                               in_=w1p_d[2 * p].rearrange("(k p) f -> p k f", p=128)),
                   "w1l", 16)
            kb.inc(g.dma_start(out=w1b[:, pp, 1],
                               in_=w1p_d[2 * p + 1].rearrange("(k p) f -> p k f", p=128)),
                   "w1l", 16)
        for oc in range(CT):
            kb.inc(g.dma_start(out=w2b[:, oc],
                               in_=w2po_d[oc].rearrange("(c p) f -> p c f", p=128)),
                   "w2l", 16)

    def p_w1_t(t, b_w1=b_w1, b_sc=b_sc, b_v=b_v):
        for p in range(22):
            pp = p % 2
            t.wait_ge(kb.h("w1l"), b_w1 + 32 * (p + 1))
            for ch in range(4):
                u = 4 * p + ch
                if u >= 2:
                    t.wait_ge(kb.h("SC"), b_sc + u - 1)
                    t.wait_ge(kb.h("V"), b_v + u - 1)
                cs = slice(512 * ch, 512 * (ch + 1))
                for k in range(CT):
                    t.matmul(out=pHa[ch % 2][:], lhsT=w1b[:, pp, 0, k], rhs=xnb[:, k, cs],
                             start=(k == 0), stop=(k == CT - 1))
                for k in range(CT):
                    i = t.matmul(out=pHg[ch % 2][:], lhsT=w1b[:, pp, 1, k],
                                 rhs=xnb[:, k, cs],
                                 start=(k == 0), stop=(k == CT - 1))
                kb.inc(i, "hm")

    def p_w1_s(sc, b_hm=b_hm, b_v=b_v):
        for p in range(22):
            for ch in range(4):
                u = 4 * p + ch
                sc._eng.wait_ge(kb.h("hm"), b_hm + u + 1)
                if u >= 2:
                    sc._eng.wait_ge(kb.h("V"), b_v + u - 1)
                sc.activation(out=gel[:, ch % 2], in_=pHg[ch % 2][:], func=AF.Gelu,
                              bias=b1g[:, p:p + 1])

    def p_w1_v(v, b_sc=b_sc):
        for p in range(22):
            for ch in range(4):
                u = 4 * p + ch
                v._eng.wait_ge(kb.h("SC"), b_sc + u + 1)
                v.scalar_tensor_tensor(out=hbuf[:, p, 512 * ch:512 * (ch + 1)],
                                       in0=pHa[ch % 2][:],
                                       scalar=b1a[:, p:p + 1], in1=gel[:, ch % 2],
                                       op0=ALU.add, op1=ALU.mult)
    kb.phase(tensor=p_w1_t, scalar=p_w1_s, vector=p_w1_v, gpsimd=p_w1_g)

    b_wm, b_v2, b_w2, b_st, b_ld = (kb.n("wm"), kb.n("V"), kb.n("w2l"),
                                    kb.n("st"), kb.n("ld"))

    def p_w2_sy(sy, b_v2=b_v2):
        for ch in range(4):
            for k in range(CT):
                kb.inc(sy.dma_start(out=xres[:, k],
                                    in_=xf[128 * k:128 * (k + 1),
                                           1 + 512 * ch:1 + 512 * (ch + 1)]), "ld", 16)
            for oc in range(CT):
                u = CT * ch + oc
                sy.wait_ge(kb.h("V"), b_v2 + u + 1)
                kb.inc(sy.dma_start(
                    out=yf[128 * oc:128 * (oc + 1), 512 * ch:512 * (ch + 1)],
                    in_=yst[:, u % 2]), "st", 16)
        sy.wait_ge(kb.h("st"), kb.n("st"))

    def p_w2_t(t, b_v2=b_v2, b_ld=b_ld):
        for ch in range(4):
            t.wait_ge(kb.h("ld"), b_ld + 16 * CT * (ch + 1))
            for oc in range(CT):
                u = CT * ch + oc
                pp = u % 2
                if u >= 2:
                    t.wait_ge(kb.h("V"), b_v2 + u - 1)
                for c in range(22):
                    i = t.matmul(out=pW[pp][:], lhsT=w2b[:, oc, c],
                                 rhs=hbuf[:, c, 512 * ch:512 * (ch + 1)],
                                 start=(c == 0), stop=(c == 21))
                kb.inc(i, "wm")

    def p_w2_v(v, b_wm=b_wm, b_st=b_st):
        for ch in range(4):
            for oc in range(CT):
                u = CT * ch + oc
                v._eng.wait_ge(kb.h("wm"), b_wm + u + 1)
                if u >= 2:
                    v._eng.wait_ge(kb.h("st"), b_st + 16 * (u - 1))
                v.scalar_tensor_tensor(out=yst[:, u % 2], in0=pW[u % 2][:],
                                       scalar=b2c[:, oc:oc + 1],
                                       in1=xres[:, oc], op0=ALU.add, op1=ALU.add)
    kb.phase(tensor=p_w2_t, vector=p_w2_v, sync=p_w2_sy)

    def p_fin(sy):
        sy.wait_ge(kb.h("st"), kb.n("st"))
    kb.phase(sync=p_fin)


L1C_XW = 2052


def emit_rms(kb, xt, outb, x0, n0, w, gc, pA):
    done = 0
    while done < w:
        cw = min(512, w - done)
        xs = slice(x0 + done, x0 + done + cw)
        ns = slice(n0 + done, n0 + done + cw)
        done += cw

        def p_cast(sc, xs=xs, cw=cw):
            for k in range(CT):
                sc.activation(out=kb.t_lnxb[:, k, 0:cw], in_=xt[:, k, xs], func=AF.Copy)
        kb.phase(scalar=p_cast)

        def p_sq(v, cw=cw):
            for k in range(CT):
                v.tensor_mul(out=kb.t_lnxb[:, k, 0:cw], in0=kb.t_lnxb[:, k, 0:cw],
                             in1=kb.t_lnxb[:, k, 0:cw])
        kb.phase(vector=p_sq)

        def p_mm(t, cw=cw):
            for k in range(CT):
                i = t.matmul(out=pA[:, 0:cw], lhsT=kb.t_ones[:], rhs=kb.t_lnxb[:, k, 0:cw],
                             start=(k == 0), stop=(k == CT - 1))
            kb.inc(i, "m")
        kb.phase(tensor=p_mm)

        def p_r(sc, cw=cw):
            sc.activation(out=kb.t_lnrow[:, 0, 0:cw], in_=pA[:, 0:cw], func=AF.Sqrt)
        kb.phase(scalar=p_r)

        def p_rv(v, cw=cw):
            v.tensor_scalar_max(kb.t_lnrow[:, 0, 0:cw], kb.t_lnrow[:, 0, 0:cw], 1e-12)
            v.reciprocal(out=kb.t_lnrow[:, 1, 0:cw], in_=kb.t_lnrow[:, 0, 0:cw])
        kb.phase(vector=p_rv)

        def p_out(v, xs=xs, ns=ns, cw=cw):
            for k in range(CT):
                v.scalar_tensor_tensor(out=outb[:, k, ns], in0=xt[:, k, xs],
                                       scalar=gc[:, k:k + 1], in1=kb.t_lnrow[:, 1, 0:cw],
                                       op0=ALU.mult, op1=ALU.mult)
        kb.phase(vector=p_out)


def build_l1a_pre():
    kb = KB()
    io = {"xh": kb.dram_in("xh", [D, L1C_XW], F32),
          "dwc": kb.dram_in("dwc", [128, 32], F32),
          "dwb": kb.dram_in("dwb", [128, 8], F32),
          "pwbias": kb.dram_in("pwbias", [128, 8], F32),
          "gcv": kb.dram_in("gcv", [128, 8], F32),
          "ggr": kb.dram_in("ggr", [128, 8], F32),
          "pwT": kb.dram_in("pwT", [D, D], F32),
          "whg": kb.dram_in("whg", [4, 256, 768], F32),
          "xa": kb.dram_out("xa", [D, T], F32),
          "hl": kb.dram_out("hl", [128, 12, T], F32),
          "al": kb.dram_out("al", [128, 12, T], F32),
          "summ": kb.dram_out("summ", [128, 2, 12], F32)}
    alloc_common(kb)
    kb.sems("ld", "m", "st")
    kb.start()
    kb.push()
    emit_l1a_pre(kb, io)
    kb.pop()
    return kb


def emit_l1a_pre(kb, io, hmask=None):
    xh = io["xh"]
    dwc_d, dwb_d, pwbias_d = io["dwc"], io["dwb"], io["pwbias"]
    gcv_d, ggr_d, pwT_d, whg_d = io["gcv"], io["ggr"], io["pwT"], io["whg"]
    xa_d, hl_d, al_d, sm_d = io["xa"], io["hl"], io["al"], io["summ"]
    xt = kb.sbuf("xt", [128, CT, L1C_XW], F32)
    alloc_ln(kb)
    cols = kb.sbuf("cols", [128, 64], F32)
    summ = kb.sbuf("summ_s", [128, 2, 12], F32)
    zocol = kb.sbuf("zocol", [128, 2], F32)

    ov = ExitStack()

    def osb(name, shape, dtype):
        return ov.enter_context(kb.sbuf_cm(name, shape, dtype))

    xn = osb("xn", [128, CT, L1C_XW], BF16)
    pwb = osb("pwb", [128, CT, CT, 128], BF16)
    acc = osb("acc", [128, CT, 512], F32)
    ycb = osb("ycb", [128, CT, 512], BF16)

    pA = kb.psum("pA", [128, 512])
    pP = [kb.psum(f"pP{i}", [128, 512]) for i in range(2)]

    def p_load_sy(sy):
        for k in range(CT):
            kb.inc(sy.dma_start(out=xt[:, k], in_=xh[128 * k:128 * (k + 1), :]), "ld", 16)
        kb.inc(sy.dma_start(out=cols[:, 0:32], in_=dwc_d), "ld", 16)
        kb.inc(sy.dma_start(out=cols[:, 32:40], in_=dwb_d), "ld", 16)
        kb.inc(sy.dma_start(out=cols[:, 40:48], in_=pwbias_d), "ld", 16)
        kb.inc(sy.dma_start(out=cols[:, 48:56], in_=gcv_d), "ld", 16)
        kb.inc(sy.dma_start(out=cols[:, 56:64], in_=ggr_d), "ld", 16)

    def p_load_g(g):
        load_common(kb, g)
        for dd in range(CT):
            for ee in range(CT):
                kb.inc(g.dma_start(out=pwb[:, dd, ee],
                                   in_=pwT_d[128 * dd:128 * (dd + 1),
                                             128 * ee:128 * (ee + 1)]), "ld", 16)
    kb.phase(sync=p_load_sy, gpsimd=p_load_g)

    def p_gc(v):
        v.tensor_scalar(out=cols[:, 48:64], in0=cols[:, 48:64], scalar1=1.0,
                        scalar2=32.0, op0=ALU.add, op1=ALU.mult)
        v.memset(zocol[:, 0:1], 0.0)
        v.memset(zocol[:, 1:2], 1.0)
        if hmask is not None:
            for k in range(CT):
                v.tensor_scalar_mul(xt[:, k, 0:3], xt[:, k, 0:3], hmask[:, 0:1])
    kb.phase(vector=p_gc)

    emit_rms(kb, xt, xn, 0, 0, L1C_XW, cols[:, 48:56], pA)

    for ch in range(4):
        t0 = 512 * ch

        def p_conv(v, t0=t0):
            for c in range(CT):
                v.tensor_scalar_mul(acc[:, c], xn[:, c, t0:t0 + 512], cols[:, 4 * c:4 * c + 1])
                for kk in range(1, 4):
                    v.scalar_tensor_tensor(out=acc[:, c], in0=xn[:, c, t0 + kk:t0 + kk + 512],
                                           scalar=cols[:, 4 * c + kk:4 * c + kk + 1],
                                           in1=acc[:, c], op0=ALU.mult, op1=ALU.add)
                v.tensor_scalar_add(acc[:, c], acc[:, c], cols[:, 32 + c:33 + c])
        kb.phase(vector=p_conv)

        def p_cvt(sc):
            for c in range(CT):
                sc.activation(out=ycb[:, c], in_=acc[:, c], func=AF.Copy)
        kb.phase(scalar=p_cvt)

        b_m, b_vv = kb.n("m"), kb.n("V")

        def p_pw(t, b_vv=b_vv):
            for ee in range(CT):
                if ee >= 2:
                    t.wait_ge(kb.h("V"), b_vv + (ee - 1))
                for dd in range(CT):
                    i = t.matmul(out=pP[ee % 2][:], lhsT=pwb[:, dd, ee], rhs=ycb[:, dd],
                                 start=(dd == 0), stop=(dd == CT - 1))
                kb.inc(i, "m")

        def p_pwe(v, b_m=b_m, t0=t0):
            for ee in range(CT):
                v._eng.wait_ge(kb.h("m"), b_m + ee + 1)
                v.scalar_tensor_tensor(out=xt[:, ee, 3 + t0:3 + t0 + 512],
                                       in0=pP[ee % 2][:], scalar=cols[:, 40 + ee:41 + ee],
                                       in1=xt[:, ee, 3 + t0:3 + t0 + 512],
                                       op0=ALU.add, op1=ALU.add)
        kb.phase(tensor=p_pw, vector=p_pwe)

    def p_stxa(sy):
        for k in range(CT):
            kb.inc(sy.dma_start(out=xa_d[128 * k:128 * (k + 1), :], in_=xt[:, k, 3:2051]),
                   "st", 16)
    kb.phase(sync=p_stxa)

    ov.close()
    ov2 = ExitStack()

    def osb2(name, shape, dtype):
        return ov2.enter_context(kb.sbuf_cm(name, shape, dtype))

    ngb = osb2("ngb", [128, CT, T], BF16)
    whgb = osb2("whgb", [128, 2, 2, 128], BF16)
    hgf = osb2("hgf", [128, 2, T], F32)
    af = osb2("af", [128, T], F32)
    bf2 = osb2("bf2", [128, T], F32)
    tmp = osb2("tmp", [128, T], F32)
    hsc = osb2("hsc", [128, T], F32)
    asc = osb2("asc", [128, T], F32)
    ones2k = osb2("ones2k", [128, T], F32)

    def p_ones(v):
        v.memset(ones2k[:], 1.0)
    kb.phase(vector=p_ones)

    emit_rms(kb, xt, ngb, 3, 0, T, cols[:, 56:64], pA)

    for h4 in range(4):
        for et in range(3):
            j = 3 * h4 + et

            def p_wl(g, h4=h4, et=et):
                for dd in range(2):
                    kb.inc(g.dma_start(out=whgb[:, 0, dd],
                                       in_=whg_d[h4, 128 * dd:128 * (dd + 1),
                                                 128 * et:128 * (et + 1)]), "ld", 16)
                    kb.inc(g.dma_start(out=whgb[:, 1, dd],
                                       in_=whg_d[h4, 128 * dd:128 * (dd + 1),
                                                 128 * (3 + et):128 * (4 + et)]), "ld", 16)
            kb.phase(gpsimd=p_wl)

            b_m, b_sc = kb.n("m"), kb.n("SC")

            def p_mm(t, h4=h4, b_sc=b_sc):
                for tt in range(4):
                    if tt >= 1:
                        t.wait_ge(kb.h("SC"), b_sc + 2 * tt)
                    for hh in range(2):
                        for dd in range(2):
                            i = t.matmul(out=pP[hh][:],
                                         lhsT=whgb[:, hh, dd],
                                         rhs=ngb[:, 2 * h4 + dd, 512 * tt:512 * (tt + 1)],
                                         start=(dd == 0), stop=(dd == 1))
                        kb.inc(i, "m")

            def p_ev(sc, b_m=b_m):
                for tt in range(4):
                    sc._eng.wait_ge(kb.h("m"), b_m + 2 * tt + 2)
                    for hh in range(2):
                        sc.activation(out=hgf[:, hh, 512 * tt:512 * (tt + 1)],
                                      in_=pP[hh][:], func=AF.Copy)
            kb.phase(tensor=p_mm, scalar=p_ev)

            def p_el_s(sc):
                sc.activation(out=af[:], in_=hgf[:, 1], func=AF.Sigmoid, scale=-1.0)
                sc.activation(out=bf2[:], in_=hgf[:, 1], func=AF.Sigmoid)
                sc.activation(out=tmp[:], in_=hgf[:, 0], func=AF.Sigmoid)
            kb.phase(scalar=p_el_s)

            def p_el_v(v):
                v.tensor_scalar_add(hgf[:, 0], hgf[:, 0], 0.5)
                v.tensor_max(out=hgf[:, 0], in0=hgf[:, 0], in1=tmp[:])
                v.tensor_mul(out=bf2[:], in0=bf2[:], in1=hgf[:, 0])
                v.tensor_tensor_scan(out=hsc[:], data0=af[:], data1=bf2[:],
                                     initial=zocol[:, 0:1], op0=ALU.mult, op1=ALU.add)
                v.tensor_tensor_scan(out=asc[:], data0=af[:], data1=ones2k[:],
                                     initial=zocol[:, 1:2], op0=ALU.mult, op1=ALU.mult)
                v.tensor_copy(out=summ[:, 0, j:j + 1], in_=asc[:, T - 1:T])
                v.tensor_copy(out=summ[:, 1, j:j + 1], in_=hsc[:, T - 1:T])
            kb.phase(vector=p_el_v)

            def p_st(sy, j=j):
                kb.inc(sy.dma_start(out=hl_d[:, j], in_=hsc[:]), "st", 16)
                kb.inc(sy.dma_start(out=al_d[:, j], in_=asc[:]), "st", 16)
            kb.phase(sync=p_st)

    def p_fin(sy):
        if "summP" in io:
            kb.inc(sy.dma_start(out=io["summP"], in_=summ[:, 0]), "st", 16)
            kb.inc(sy.dma_start(out=io["summE"], in_=summ[:, 1]), "st", 16)
        else:
            kb.inc(sy.dma_start(out=sm_d, in_=summ[:]), "st", 16)
        sy.wait_ge(kb.h("st"), kb.n("st"))
    kb.phase(sync=p_fin)
    ov2.close()


def build_l1a_post():
    kb = KB()
    io = {"xa": kb.dram_in("xa", [D, T], F32),
          "hl": kb.dram_in("hl", [128, 12, T], F32),
          "al": kb.dram_in("al", [128, 12, T], F32),
          "smP": kb.dram_in("smP", [128, 8, 12], F32),
          "smE": kb.dram_in("smE", [128, 8, 12], F32),
          "wout": kb.dram_in("wout", [12, 128, 256], F32),
          "gff": kb.dram_in("gff", [128, 8], F32),
          "wg_h": kb.dram_in("wg_h", [22, D, 128], F32),
          "wv_h": kb.dram_in("wv_h", [22, D, 128], F32),
          "wf_h": kb.dram_in("wf_h", [8, 2816, 128], F32),
          "yb": kb.dram_out("yb", [D, T], F32)}
    alloc_common(kb)
    kb.sems("ld", "m", "st", "hx")
    kb.start()
    kb.push()
    emit_l1a_post(kb, io)
    kb.pop()
    return kb


def emit_l1a_post(kb, io, summ_ag=None):
    xa_d, hl_d, al_d = io["xa"], io["hl"], io["al"]
    wout_d, gff_d = io["wout"], io["gff"]
    wg_d, wv_d, wf_d = io["wg_h"], io["wv_h"], io["wf_h"]
    yb_d = io["yb"]
    x2_d = kb.dram_int("x2bounce", [D, T], F32)

    xa = kb.sbuf("xa_s", [128, CT, T], F32)
    alloc_ln(kb)
    gffc = kb.sbuf("gffc", [128, 8], F32)
    Hc = kb.sbuf("Hc", [128, 12], F32)
    sm = kb.sbuf("sm", [128, 2, 8, 12], F32)
    wob = kb.sbuf("wob", [128, 12, 256], BF16)
    nfb = kb.sbuf("nfb", [128, CT, T], BF16)

    ov = ExitStack()

    def osb(name, shape, dtype):
        return ov.enter_context(kb.sbuf_cm(name, shape, dtype))

    mcols = kb.sbuf("mcols", [128, 16], F32)
    hlc = osb("hlc", [128, 12, 512], F32)
    alc = osb("alc", [128, 12, 512], F32)
    htb = osb("htb", [128, 12, 512], BF16)

    pA = kb.psum("pA", [128, 512])
    pP = [kb.psum(f"pP{i}", [128, 512]) for i in range(2)]

    def p_load_sy(sy):
        for k in range(CT):
            kb.inc(sy.dma_start(out=xa[:, k], in_=xa_d[128 * k:128 * (k + 1), :]), "ld", 16)
        if summ_ag is None:
            kb.inc(sy.dma_start(out=sm[:, 0], in_=io["smP"]), "ld", 16)
            kb.inc(sy.dma_start(out=sm[:, 1], in_=io["smE"]), "ld", 16)
        else:
            agP, agE, mP_d, mE_d = summ_ag
            for jj in range(8):
                kb.inc(sy.dma_start(out=sm[:, 0, jj], in_=agP[jj]), "ld", 16)
                kb.inc(sy.dma_start(out=sm[:, 1, jj], in_=agE[jj]), "ld", 16)
            kb.inc(sy.dma_start(out=mcols[:, 0:8], in_=mP_d), "ld", 16)
            kb.inc(sy.dma_start(out=mcols[:, 8:16], in_=mE_d), "ld", 16)
        kb.inc(sy.dma_start(out=gffc[:], in_=gff_d), "ld", 16)

    def p_load_g(g):
        load_common(kb, g)
        for j in range(12):
            kb.inc(g.dma_start(out=wob[:, j], in_=wout_d[j]), "ld", 16)
    kb.phase(sync=p_load_sy, gpsimd=p_load_g)

    if summ_ag is not None:
        def p_meff(v):
            for jj in range(8):
                v.tensor_scalar_add(sm[:, 0, jj], sm[:, 0, jj], -1.0)
                v.tensor_scalar_mul(sm[:, 0, jj], sm[:, 0, jj], mcols[:, jj:jj + 1])
                v.tensor_scalar_add(sm[:, 0, jj], sm[:, 0, jj], 1.0)
                v.tensor_scalar_mul(sm[:, 1, jj], sm[:, 1, jj], mcols[:, 8 + jj:9 + jj])
        kb.phase(vector=p_meff)

    def p_comb(v):
        v.tensor_scalar(out=gffc[:], in0=gffc[:], scalar1=1.0, scalar2=32.0,
                        op0=ALU.add, op1=ALU.mult)
        v.memset(Hc[:], 0.0)
        for jj in range(8):
            v.tensor_mul(out=Hc[:], in0=Hc[:], in1=sm[:, 0, jj])
            v.tensor_add(out=Hc[:], in0=Hc[:], in1=sm[:, 1, jj])
    kb.phase(vector=p_comb)

    for tt in range(4):
        ts = slice(512 * tt, 512 * (tt + 1))

        def p_lh(sy, ts=ts):
            for j in range(12):
                kb.inc(sy.dma_start(out=hlc[:, j], in_=hl_d[:, j, ts]), "ld", 16)
                kb.inc(sy.dma_start(out=alc[:, j], in_=al_d[:, j, ts]), "ld", 16)
        kb.phase(sync=p_lh)

        def p_ht(v):
            for j in range(12):
                v.scalar_tensor_tensor(out=htb[:, j], in0=alc[:, j], scalar=Hc[:, j:j + 1],
                                       in1=hlc[:, j], op0=ALU.mult, op1=ALU.add)
        kb.phase(vector=p_ht)

        b_m, b_vv = kb.n("m"), kb.n("V")

        def p_go(t, b_vv=b_vv):
            for och in range(CT):
                if och >= 2:
                    t.wait_ge(kb.h("V"), b_vv + (och - 1))
                h4, oo = och // 2, och % 2
                for et in range(3):
                    i = t.matmul(out=pP[och % 2][:],
                                 lhsT=wob[:, 3 * h4 + et, 128 * oo:128 * (oo + 1)],
                                 rhs=htb[:, 3 * h4 + et],
                                 start=(et == 0), stop=(et == 2))
                kb.inc(i, "m")

        def p_goe(v, b_m=b_m, ts=ts):
            for och in range(CT):
                v._eng.wait_ge(kb.h("m"), b_m + och + 1)
                v.tensor_add(out=xa[:, och, ts], in0=pP[och % 2][:], in1=xa[:, och, ts])
        kb.phase(tensor=p_go, vector=p_goe)

    def p_stx2(sy):
        for k in range(CT):
            kb.inc(sy.dma_start(out=x2_d[128 * k:128 * (k + 1), :], in_=xa[:, k]), "hx", 16)
        sy.wait_ge(kb.h("hx"), kb.n("hx"))
    kb.phase(sync=p_stx2)

    emit_rms(kb, xa, nfb, 0, 0, T, gffc[:, 0:8], pA)

    ov.close()
    ov2 = ExitStack()

    def osb2(name, shape, dtype):
        return ov2.enter_context(kb.sbuf_cm(name, shape, dtype))

    x2c = osb2("x2c", [128, CT, 512], F32)
    gv = osb2("gv", [128, 22, 512], BF16)
    sgf = osb2("sgf", [128, 512], F32)
    svf = osb2("svf", [128, 512], F32)
    wgv = osb2("wgv", [128, 2, 2, CT, 128], BF16)
    wfb = osb2("wfb", [128, 2, 22, 128], BF16)
    yb = osb2("yb_s", [128, CT, 512], F32)

    pGg = kb.psum("pGg", [128, 512])
    pGv = kb.psum("pGv", [128, 512])
    pF = [kb.psum(f"pF{i}", [128, 512]) for i in range(2)]

    for tt in range(4):
        ts = slice(512 * tt, 512 * (tt + 1))

        def p_lx2(sy, ts=ts):
            for k in range(CT):
                kb.inc(sy.dma_start(out=x2c[:, k], in_=x2_d[128 * k:128 * (k + 1), ts]),
                       "ld", 16)
        kb.phase(sync=p_lx2)

        b_ld, b_m, b_sc, b_vv = kb.n("ld"), kb.n("m"), kb.n("SC"), kb.n("V")

        def p_wstr(g, b_m=b_m):
            for ft in range(22):
                pp = ft % 2
                if ft >= 2:
                    g._eng.wait_ge(kb.h("m"), b_m + 2 * (ft - 1))
                for dd in range(CT):
                    kb.inc(g.dma_start(out=wgv[:, pp, 0, dd],
                                       in_=wg_d[ft, 128 * dd:128 * (dd + 1), :]), "ld", 16)
                    kb.inc(g.dma_start(out=wgv[:, pp, 1, dd],
                                       in_=wv_d[ft, 128 * dd:128 * (dd + 1), :]), "ld", 16)

        def p_gvt(t, b_ld=b_ld, b_sc=b_sc, ts=ts):
            for ft in range(22):
                pp = ft % 2
                t.wait_ge(kb.h("ld"), b_ld + 16 * 16 * (ft + 1))
                if ft >= 1:
                    t.wait_ge(kb.h("SC"), b_sc + 2 * ft)
                for dd in range(CT):
                    i = t.matmul(out=pGg[:], lhsT=wgv[:, pp, 0, dd], rhs=nfb[:, dd, ts],
                                 start=(dd == 0), stop=(dd == CT - 1))
                kb.inc(i, "m")
                for dd in range(CT):
                    i = t.matmul(out=pGv[:], lhsT=wgv[:, pp, 1, dd], rhs=nfb[:, dd, ts],
                                 start=(dd == 0), stop=(dd == CT - 1))
                kb.inc(i, "m")

        def p_gvs(sc, b_m=b_m, b_vv=b_vv):
            for ft in range(22):
                sc._eng.wait_ge(kb.h("m"), b_m + 2 * ft + 1)
                if ft >= 1:
                    sc._eng.wait_ge(kb.h("V"), b_vv + ft)
                sc.activation(out=sgf[:], in_=pGg[:], func=AF.Silu)
                sc._eng.wait_ge(kb.h("m"), b_m + 2 * ft + 2)
                sc.activation(out=svf[:], in_=pGv[:], func=AF.Copy)

        def p_gvv(v, b_sc=b_sc):
            for ft in range(22):
                v._eng.wait_ge(kb.h("SC"), b_sc + 2 * ft + 2)
                v.tensor_mul(out=gv[:, ft], in0=sgf[:], in1=svf[:])
        kb.phase(gpsimd=p_wstr, tensor=p_gvt, scalar=p_gvs, vector=p_gvv)

        b_ld2, b_m2, b_vv2 = kb.n("ld"), kb.n("m"), kb.n("V")

        def p_fstr(g, b_m2=b_m2):
            for ot in range(CT):
                pp = ot % 2
                if ot >= 2:
                    g._eng.wait_ge(kb.h("m"), b_m2 + (ot - 1))
                for ft in range(22):
                    kb.inc(g.dma_start(out=wfb[:, pp, ft],
                                       in_=wf_d[ot, 128 * ft:128 * (ft + 1), :]), "ld", 16)

        def p_ft(t, b_ld2=b_ld2, b_vv2=b_vv2):
            for ot in range(CT):
                pp = ot % 2
                t.wait_ge(kb.h("ld"), b_ld2 + 16 * 22 * (ot + 1))
                if ot >= 2:
                    t.wait_ge(kb.h("V"), b_vv2 + (ot - 1))
                for ft in range(22):
                    i = t.matmul(out=pF[pp][:], lhsT=wfb[:, pp, ft], rhs=gv[:, ft],
                                 start=(ft == 0), stop=(ft == 21))
                kb.inc(i, "m")

        def p_fe(v, b_m2=b_m2):
            for ot in range(CT):
                v._eng.wait_ge(kb.h("m"), b_m2 + ot + 1)
                v.tensor_add(out=yb[:, ot], in0=pF[ot % 2][:], in1=x2c[:, ot])
        kb.phase(gpsimd=p_fstr, tensor=p_ft, vector=p_fe)

        def p_sty(sy, ts=ts):
            for k in range(CT):
                kb.inc(sy.dma_start(out=yb_d[128 * k:128 * (k + 1), ts], in_=yb[:, k]),
                       "st", 16)
        kb.phase(sync=p_sty)

    def p_fin(sy):
        sy.wait_ge(kb.h("st"), kb.n("st"))
    kb.phase(sync=p_fin)
    ov2.close()


def build_addln(with_add):
    kb = KB()
    io = {"a": kb.dram_in("a", [D, T], F32),
          "yln": kb.dram_out("yln", [D, T], F32)}
    if with_add:
        io["b"] = kb.dram_in("b", [D, T], F32)
        io["ysum"] = kb.dram_out("ysum", [D, T], F32)
    alloc_common(kb)
    kb.sems("ld", "m", "st")
    kb.start()
    kb.push()
    emit_addln(kb, io, with_add)
    kb.pop()
    return kb


def emit_addln(kb, io, with_add, W=T):
    a_d, yln = io["a"], io["yln"]
    xt = kb.sbuf("xt", [128, CT, W], F32)
    xnb = kb.sbuf("xnb", [128, CT, W], F32)
    alloc_ln(kb)
    pA = kb.psum("pA", [128, 512])
    pB = kb.psum("pB", [128, 512])

    def p_load_sy(sy):
        for k in range(CT):
            kb.inc(sy.dma_start(out=xt[:, k], in_=a_d[128 * k:128 * (k + 1), :]), "ld", 16)

    def p_load_g(g):
        load_common(kb, g)
        if with_add:
            for k in range(CT):
                kb.inc(g.dma_start(out=xnb[:, k],
                                   in_=io["b"][128 * k:128 * (k + 1), :]), "ld", 16)
    kb.phase(sync=p_load_sy, gpsimd=p_load_g)

    if with_add:
        def p_add(v):
            for k in range(CT):
                v.tensor_add(out=xt[:, k], in0=xt[:, k], in1=xnb[:, k])
        kb.phase(vector=p_add)

        def p_st1(sy):
            for k in range(CT):
                kb.inc(sy.dma_start(out=io["ysum"][128 * k:128 * (k + 1), :], in_=xt[:, k]),
                       "st", 16)
        kb.phase(sync=p_st1)

    emit_ln(kb, xt, xnb, 0, 0, W, pA, pB)

    def p_st2(sy):
        for k in range(CT):
            kb.inc(sy.dma_start(out=yln[128 * k:128 * (k + 1), :], in_=xnb[:, k]),
                   "st", 16)
        sy.wait_ge(kb.h("st"), kb.n("st"))
    kb.phase(sync=p_st2)


def _rotary_tables(pos):
    DH, WIN = 64, 256
    half = np.arange(0, DH, 2, dtype=np.float64)
    inv_freq = 1.0 / 10000.0 ** (half / DH)
    fr = pos[:, None] * inv_freq[None, :]
    freqs = np.concatenate([fr, fr], -1)
    base = (half + 0.4 * DH) / (1.4 * DH)
    power = (pos - S // 2) / (WIN // 2)
    sc = base[None, :] ** power[:, None]
    scale = np.concatenate([sc, sc], -1)
    cos, sin = np.cos(freqs), np.sin(freqs)
    sign = np.where(np.arange(64) < 32, -1.0, 1.0)[None, :]
    perm64 = np.concatenate([np.arange(32, 64), np.arange(0, 32)])
    out = []
    for s_, f in ((scale * 8.0, None), (1.0 / scale, None)):
        cs = (cos * s_).T
        sn = (sin * s_ * sign).T
        snp = sn[perm64]
        out.append(np.concatenate([cs, cs], 0).astype(BF))
        out.append(np.concatenate([snp, snp], 0).astype(BF))
    return out


def _f0_masks(s):
    m = np.zeros((4, 128, 384), dtype=BF)
    for mi in range(4):
        ig = s - 128 + 128 * mi + np.arange(128)[:, None]
        jg = s - 384 + 128 * mi + np.arange(384)[None, :]
        ok = (ig - jg >= 0) & (ig - jg <= 256) & (jg >= 0)
        ok = ok | (ig < 0)
        m[mi] = ok.astype(BF)
    return m


def _prep_f0_layer(inputs, li):
    Wqkv = inputs["L0_ffn_Wqkv"][li]
    wq = np.ascontiguousarray(Wqkv[:, 0:1024])
    wk = np.ascontiguousarray(Wqkv[:, 1024:2048])
    wv = np.ascontiguousarray(Wqkv[:, 2048:3072])
    qs = inputs["L0_ffn_q_scale"][li].astype(np.float32)
    ks = inputs["L0_ffn_k_scale"][li].astype(np.float32)
    qsp = np.concatenate([qs[32:], qs[:32]])
    ksp = np.concatenate([ks[32:], ks[:32]])
    wo = inputs["L0_ffn_Wo"][li]
    W1 = inputs["L0_ffn_W1"][li]
    b1 = inputs["L0_ffn_b1"][li]
    W2 = inputs["L0_ffn_W2"][li]
    b2 = inputs["L0_ffn_b2"][li]
    a, g = W1[:, 0:2730], W1[:, 2730:5460]
    w1p = np.zeros((44, D, 128), np.float32)
    b1a = np.zeros((128, 22), np.float32)
    b1g = np.zeros((128, 22), np.float32)
    for p in range(22):
        c0, c1 = 128 * p, min(128 * (p + 1), 2730)
        w1p[2 * p][:, 0:c1 - c0] = a[:, c0:c1]
        w1p[2 * p + 1][:, 0:c1 - c0] = g[:, c0:c1]
        b1a[0:c1 - c0, p] = b1[c0:c1]
        b1g[0:c1 - c0, p] = b1[2730 + c0:2730 + c1]
    w2pad = np.zeros((2816, D), np.float32)
    w2pad[0:2730] = W2
    w2po = np.ascontiguousarray(
        w2pad.reshape(2816, 8, 128).transpose(1, 0, 2))
    return {
        "wq": wq, "wk": wk, "wv": wv, "wo": wo,
        "qs_col": np.tile(qs, 2).reshape(128, 1),
        "ks_col": np.tile(ks, 2).reshape(128, 1),
        "qsp_col": np.tile(qsp, 2).reshape(128, 1),
        "ksp_col": np.tile(ksp, 2).reshape(128, 1),
        "w1p": w1p, "w2po": w2po,
        "g_col": np.ascontiguousarray(
            inputs["L0_ffn_ln_g"][li].reshape(8, 128).T.astype(np.float32)),
        "b_col": np.ascontiguousarray(
            inputs["L0_ffn_ln_b"][li].reshape(8, 128).T.astype(np.float32)),
        "b1a_col": b1a, "b1g_col": b1g,
        "b2_col": np.ascontiguousarray(b2.reshape(8, 128).T.astype(np.float32)),
    }


def build_moe_probs():
    kb = KB()
    io = {"xh": kb.dram_in("xh", [D, T], F32),
          "gw": kb.dram_in("gw", [D, 8], F32),
          "pr": kb.dram_out("pr", [128, 16, 8], F32)}
    alloc_common(kb)
    kb.sems("ld", "m", "st")
    kb.start()
    kb.push()
    emit_moe_probs(kb, io)
    kb.pop()
    return kb


def emit_moe_probs(kb, io):
    xh, gw_d, pr_d = io["xh"], io["gw"], io["pr"]
    xt = kb.sbuf("xt", [128, CT, T], F32)
    gwb = kb.sbuf("gwb", [128, CT, 8], F32)
    prb = kb.sbuf("prb", [128, 16, 8], F32)
    nmx = kb.sbuf("nmx", [128, 2], F32)
    rs = kb.sbuf("rs", [128, 2], F32)
    pS = [kb.psum(f"pS{i}", [128, 8]) for i in range(2)]

    def p_load(sy):
        for k in range(CT):
            kb.inc(sy.dma_start(out=xt[:, k], in_=xh[128 * k:128 * (k + 1), :]), "ld", 16)
            kb.inc(sy.dma_start(out=gwb[:, k], in_=gw_d[128 * k:128 * (k + 1), :]), "ld", 16)

    def p_load_g(g):
        load_common(kb, g)
    kb.phase(sync=p_load, gpsimd=p_load_g)

    b_m, b_sc = kb.n("m"), kb.n("SC")

    def p_mm(t, b_sc=b_sc):
        for lt in range(16):
            if lt >= 2:
                t.wait_ge(kb.h("SC"), b_sc + (lt - 1))
            for dd in range(CT):
                i = t.matmul(out=pS[lt % 2][:], lhsT=xt[:, dd, 128 * lt:128 * (lt + 1)],
                             rhs=gwb[:, dd], start=(dd == 0), stop=(dd == CT - 1))
            kb.inc(i, "m")

    b_vv = kb.n("V")

    def p_sm_v(v, b_m=b_m, b_sc=b_sc):
        for lt in range(16):
            pp = lt % 2
            v._eng.wait_ge(kb.h("m"), b_m + lt + 1)
            if lt >= 2:
                v._eng.wait_ge(kb.h("SC"), b_sc + (lt - 1))
            v.reduce_max(out=nmx[:, pp:pp + 1], in_=pS[pp][:], axis=AX.X, negate=True)
            v._eng.wait_ge(kb.h("SC"), b_sc + lt + 1)
            v.reciprocal(out=rs[:, pp:pp + 1], in_=rs[:, pp:pp + 1])
            v.tensor_scalar_mul(prb[:, lt], prb[:, lt], rs[:, pp:pp + 1])

    def p_sm_s(sc, b_vv=b_vv):
        for lt in range(16):
            pp = lt % 2
            sc._eng.wait_ge(kb.h("V"), b_vv + 3 * lt + 1)
            sc.activation(out=prb[:, lt], in_=pS[pp][:], func=AF.Exp,
                          bias=nmx[:, pp:pp + 1], accum_out=rs[:, pp:pp + 1])
    kb.phase(tensor=p_mm, vector=p_sm_v, scalar=p_sm_s)

    def p_st(sy):
        kb.inc(sy.dma_start(out=pr_d, in_=prb[:]), "st", 16)
        sy.wait_ge(kb.h("st"), kb.n("st"))
    kb.phase(sync=p_st)


def build_moe_main(niter=45):
    kb = KB()
    io = {"prA": kb.dram_in("prA", [128, 8, 128], F32),
          "xh": kb.dram_in("xh", [D, T], F32),
          "y3": kb.dram_in("y3", [D, T], F32),
          "scol": kb.dram_in("scol", [128, 16], F32),
          "w1": kb.dram_in("w1", [8, D, 4096], F32),
          "w2": kb.dram_in("w2", [8, 4096, D], F32),
          "yo": kb.dram_out("yo", [D, T], F32),
          "gates_o": kb.dram_out("gates_o", [128, 8, 128], F32),
          "grow_o": kb.dram_out("grow_o", [8, T], F32)}
    alloc_common(kb)
    kb.sems("ld", "m", "st")
    kb.start()
    kb.push()
    emit_moe_main(kb, io)
    kb.pop()
    return kb


def emit_moe_main(kb, io, niter=45, prall=None):
    CAP = 4096.0
    xh, y3_d, sc_d = io["xh"], io["y3"], io["scol"]
    w1_d, w2_d, out_d = io["w1"], io["w2"], io["yo"]
    gates_d, grow_d = io.get("gates_o"), io.get("grow_o")

    d_identf = kb.cin("identf")
    d_ltril = kb.cin("ltrilf")
    d_onesf8 = kb.cin("onesf8")
    d_ident8 = kb.cin("ident8f")
    d_onescolf = kb.cin("onescolf")
    d_onesrowf = kb.cin("onesrow_f")
    d_oh8 = kb.cin("oh8f")

    xnb = kb.sbuf("xnb", [128, CT, T], BF16)
    if prall is None:
        prA = kb.sbuf("prA_s", [128, 8, 128], F32)

        def pre(e):
            return prA[:, e]
    else:
        prB = kb.sbuf("prB_s", [128, 128, 8], F32)

        def pre(e):
            return prB[:, :, e]
    gates = kb.sbuf("gates", [128, 8, 128], F32)
    mgt = kb.sbuf("mgt", [128, 8, 128], F32)
    msk = kb.sbuf("msk", [128, 8, 128], F32)
    eqb = kb.sbuf("eqb", [128, 128], F32)
    cnts = kb.sbuf("cnts", [128, 8], F32)
    candm = kb.sbuf("candm", [128, 8], F32)
    rows = kb.sbuf("rows", [1, 64], F32)
    zero1 = kb.sbuf("zero1", [1, 1], F32)
    bc = kb.sbuf("bc", [128, 24], F32)
    vcol = kb.sbuf("vcol", [8, 1], F32)
    d8 = kb.sbuf("d8", [8, 8], F32)
    tident = kb.sbuf("tident", [128, 128], F32)
    tltril = kb.sbuf("tltril", [128, 128], F32)
    tonesf8 = kb.sbuf("tonesf8", [8, 128], F32)
    tident8 = kb.sbuf("tident8", [8, 8], F32)
    tonescolf = kb.sbuf("tonescolf", [128, 1], F32)
    tonesrowf = kb.sbuf("tonesrowf", [1, 128], F32)
    toh8 = kb.sbuf("toh8", [8, 8, 128], F32)
    scol = kb.sbuf("scol_s", [128, 16], F32)
    rank = kb.sbuf("rank", [128, 128], F32)
    gts = kb.sbuf("gts", [128, 128], F32)
    ownb = kb.sbuf("ownb", [128, 16], F32)
    ows = kb.sbuf("ows", [16, 128], F32)
    prow = kb.sbuf("prow", [1, 256], F32)
    grow = kb.sbuf("grow", [8, T], F32)

    ov0 = ExitStack()
    xtf = ov0.enter_context(kb.sbuf_cm("xtf", [128, CT, T], F32))

    pH = [kb.psum(f"pH{i}", [128, 512]) for i in range(2)]
    pW = [kb.psum(f"pW{i}", [128, 512]) for i in range(2)]
    pB = kb.psum("pB", [128, 512])
    pT = kb.psum("pT", [1, 512])
    pX = kb.psum("pX", [128, 128])
    pO = kb.psum("pO", [128, 128])

    def p_load(sy):
        for k in range(CT):
            kb.inc(sy.dma_start(out=xtf[:, k], in_=xh[128 * k:128 * (k + 1), :]), "ld", 16)
        if prall is None:
            kb.inc(sy.dma_start(out=prA[:], in_=io["prA"]), "ld", 16)
        else:
            for c in range(NCORE):
                kb.inc(sy.dma_start(out=prB[:, 16 * c:16 * (c + 1), :], in_=prall[c]),
                       "ld", 16)
        kb.inc(sy.dma_start(out=scol[:], in_=sc_d), "ld", 16)

    def p_load_g(g):
        load_common(kb, g)
        kb.inc(g.dma_start(out=tident[:], in_=d_identf), "ld", 16)
        kb.inc(g.dma_start(out=tltril[:], in_=d_ltril), "ld", 16)
        kb.inc(g.dma_start(out=tonesf8[:], in_=d_onesf8), "ld", 16)
        kb.inc(g.dma_start(out=tident8[:], in_=d_ident8), "ld", 16)
        kb.inc(g.dma_start(out=tonescolf[:], in_=d_onescolf), "ld", 16)
        kb.inc(g.dma_start(out=tonesrowf[:], in_=d_onesrowf), "ld", 16)
        kb.inc(g.dma_start(out=toh8[:], in_=d_oh8), "ld", 16)
    kb.phase(sync=p_load, gpsimd=p_load_g)

    def p_cast(sc):
        for k in range(CT):
            sc.activation(out=xnb[:, k], in_=xtf[:, k], func=AF.Copy)
    kb.phase(scalar=p_cast)

    def p_init(v):
        v.memset(rows[0:1, 0:8], 0.5)
        v.memset(rows[0:1, 8:16], 0.0)
        v.memset(rows[0:1, 16:24], 1.0)
        v.memset(zero1[:], 0.0)
    kb.phase(vector=p_init)

    for it in range(niter):
        delta = 0.25 * (0.5 ** it)

        def p_bc(t):
            i = t.matmul(out=pB[:, 0:8], lhsT=tonesrowf[:], rhs=rows[0:1, 0:8],
                         start=True, stop=True)
            kb.inc(i, "m")
        kb.phase(tensor=p_bc)

        def p_msk(v):
            v.tensor_copy(out=bc[:, 0:8], in_=pB[:, 0:8])
            for e in range(8):
                v.tensor_scalar(out=msk[:, e], in0=pre(e), scalar1=bc[:, e:e + 1],
                                scalar2=0.0, op0=ALU.is_gt, op1=ALU.bypass)
                v.reduce_sum(out=cnts[:, e:e + 1], in_=msk[:, e], axis=AX.X)
        kb.phase(vector=p_msk)

        def p_tot(t):
            i = t.matmul(out=pT[0:1, 0:8], lhsT=tonescolf[:], rhs=cnts[:],
                         start=True, stop=True)
            kb.inc(i, "m")
        kb.phase(tensor=p_tot)

        def p_upd(v, delta=delta):
            v.tensor_scalar(out=rows[0:1, 24:32], in0=pT[0:1, 0:8],
                            scalar1=CAP - 0.5, scalar2=0.0, op0=ALU.is_gt, op1=ALU.bypass)
            v.tensor_mul(out=rows[0:1, 32:40], in0=rows[0:1, 24:32], in1=rows[0:1, 0:8])
            v.tensor_max(out=rows[0:1, 8:16], in0=rows[0:1, 8:16], in1=rows[0:1, 32:40])
            v.tensor_scalar(out=rows[0:1, 40:48], in0=rows[0:1, 24:32], scalar1=2.0,
                            scalar2=0.0, op0=ALU.mult, op1=ALU.bypass)
            v.tensor_add(out=rows[0:1, 40:48], in0=rows[0:1, 40:48], in1=rows[0:1, 0:8])
            v.tensor_tensor(out=rows[0:1, 16:24], in0=rows[0:1, 16:24],
                            in1=rows[0:1, 40:48], op=ALU.min)
            v.tensor_scalar(out=rows[0:1, 32:40], in0=rows[0:1, 24:32], scalar1=2.0,
                            scalar2=-1.0, op0=ALU.mult, op1=ALU.add)
            v.tensor_scalar(out=rows[0:1, 32:40], in0=rows[0:1, 32:40],
                            scalar1=float(delta), scalar2=0.0, op0=ALU.mult, op1=ALU.bypass)
            v.tensor_add(out=rows[0:1, 0:8], in0=rows[0:1, 0:8], in1=rows[0:1, 32:40])
        kb.phase(vector=p_upd)

    def p_bc2(t):
        i = t.matmul(out=pB[:, 0:16], lhsT=tonesrowf[:], rhs=rows[0:1, 8:24],
                     start=True, stop=True)
        kb.inc(i, "m")
    kb.phase(tensor=p_bc2)

    def p_cand(v):
        v.tensor_copy(out=bc[:, 0:16], in_=pB[:, 0:16])
        for e in range(8):
            v.tensor_scalar(out=msk[:, e], in0=pre(e), scalar1=bc[:, e:e + 1],
                            scalar2=0.0, op0=ALU.is_gt, op1=ALU.bypass)
            v.tensor_scalar(out=gates[:, e], in0=pre(e), scalar1=bc[:, 8 + e:9 + e],
                            scalar2=0.0, op0=ALU.is_gt, op1=ALU.bypass)
            v.tensor_sub(out=msk[:, e], in0=msk[:, e], in1=gates[:, e])
            v.tensor_mul(out=msk[:, e], in0=msk[:, e], in1=pre(e))
            v.reduce_max(out=candm[:, e:e + 1], in_=msk[:, e], axis=AX.X)
    kb.phase(vector=p_cand)

    def p_tr1(t):
        i = t.transpose(out=pX[0:8, :], in_=candm[:], identity=tident[:])
        kb.inc(i, "m")
    kb.phase(tensor=p_tr1)

    def p_vst(v):
        v.reduce_max(out=vcol[:], in_=pX[0:8, 0:128], axis=AX.X)
        v.tensor_scalar_mul(d8[:], tident8[:], vcol[0:8, 0:1])
    kb.phase(vector=p_vst)

    def p_bc3(t):
        i = t.matmul(out=pB[:, 0:8], lhsT=tonesf8[:], rhs=d8[:], start=True, stop=True)
        kb.inc(i, "m")
    kb.phase(tensor=p_bc3)

    def p_ngt(v):
        v.tensor_copy(out=bc[:, 8:16], in_=pB[:, 0:8])
        for e in range(8):
            v.tensor_scalar(out=mgt[:, e], in0=pre(e), scalar1=bc[:, 8 + e:9 + e],
                            scalar2=0.0, op0=ALU.is_gt, op1=ALU.bypass)
            v.reduce_sum(out=cnts[:, e:e + 1], in_=mgt[:, e], axis=AX.X)
    kb.phase(vector=p_ngt)

    def p_tot2(t):
        i = t.matmul(out=pT[0:1, 0:8], lhsT=tonescolf[:], rhs=cnts[:],
                     start=True, stop=True)
        kb.inc(i, "m")
    kb.phase(tensor=p_tot2)

    def p_r(v):
        v.tensor_scalar(out=rows[0:1, 48:56], in0=pT[0:1, 0:8], scalar1=-1.0,
                        scalar2=CAP, op0=ALU.mult, op1=ALU.add)
    kb.phase(vector=p_r)

    def p_bc4(t):
        i = t.matmul(out=pB[:, 0:8], lhsT=tonesrowf[:], rhs=rows[0:1, 48:56],
                     start=True, stop=True)
        kb.inc(i, "m")
    kb.phase(tensor=p_bc4)

    def p_rbc(v):
        v.tensor_copy(out=bc[:, 16:24], in_=pB[:, 0:8])
    kb.phase(vector=p_rbc)

    for e in range(8):
        def p_eq(v, e=e):
            v.tensor_scalar(out=eqb[:], in0=pre(e), scalar1=bc[:, 8 + e:9 + e],
                            scalar2=0.0, op0=ALU.is_equal, op1=ALU.bypass)
        kb.phase(vector=p_eq)

        def p_pfx(t):
            i = t.matmul(out=pT[0:1, 0:128], lhsT=tonescolf[:], rhs=eqb[:],
                         start=True, stop=True)
            kb.inc(i, "m")
            i = t.matmul(out=pX[:], lhsT=tltril[:], rhs=eqb[:], start=True, stop=True)
            kb.inc(i, "m")
        kb.phase(tensor=p_pfx)

        def p_scan(v):
            v.tensor_copy(out=prow[0:1, 0:128], in_=pT[0:1, 0:128])
            v.tensor_tensor_scan(out=prow[0:1, 128:256], data0=tonesrowf[0:1, :],
                                 data1=prow[0:1, 0:128], initial=zero1[:],
                                 op0=ALU.mult, op1=ALU.add)
            v.tensor_sub(out=prow[0:1, 0:128], in0=prow[0:1, 128:256],
                         in1=prow[0:1, 0:128])
        kb.phase(vector=p_scan)

        def p_pcb(t):
            i = t.matmul(out=pB[:, 0:128], lhsT=tonesrowf[:], rhs=prow[0:1, 0:128],
                         start=True, stop=True)
            kb.inc(i, "m")
        kb.phase(tensor=p_pcb)

        def p_gate(v, e=e):
            v.tensor_copy(out=rank[:], in_=pX[:])
            v.tensor_add(out=rank[:], in0=rank[:], in1=pB[:, 0:128])
            v.tensor_scalar(out=rank[:], in0=rank[:], scalar1=bc[:, 16 + e:17 + e],
                            scalar2=-1.0, op0=ALU.subtract, op1=ALU.mult)
            v.tensor_scalar(out=rank[:], in0=rank[:], scalar1=0.5, scalar2=0.0, op0=ALU.is_gt, op1=ALU.bypass)
            v.tensor_mul(out=rank[:], in0=rank[:], in1=eqb[:])
            v.tensor_add(out=rank[:], in0=rank[:], in1=mgt[:, e])
            v.tensor_mul(out=gates[:, e], in0=rank[:], in1=pre(e))
        kb.phase(vector=p_gate)

        def p_tr2(t, e=e):
            i = t.transpose(out=pX[:], in_=gates[:, e], identity=tident[:])
            kb.inc(i, "m")
        kb.phase(tensor=p_tr2)

        def p_cp1(v):
            v.tensor_copy(out=gts[:], in_=pX[:])
        kb.phase(vector=p_cp1)

        def p_own(t):
            i = t.matmul(out=pO[:, 0:16], lhsT=gts[:], rhs=scol[:], start=True, stop=True)
            kb.inc(i, "m")
        kb.phase(tensor=p_own)

        def p_cp2(v):
            v.tensor_copy(out=ownb[:], in_=pO[:, 0:16])
        kb.phase(vector=p_cp2)

        def p_tr3(t):
            i = t.transpose(out=pO[0:16, 0:128], in_=ownb[:], identity=tident[:])
            kb.inc(i, "m")
        kb.phase(tensor=p_tr3)

        def p_cp3(v):
            v.tensor_copy(out=ows[:], in_=pO[0:16, 0:128])
        kb.phase(vector=p_cp3)

        def p_gr(sy, e=e):
            for lt in range(16):
                kb.inc(sy.dma_start(out=grow[e:e + 1, 128 * lt:128 * (lt + 1)],
                                    in_=ows[lt:lt + 1, :]), "st", 16)
            sy.wait_ge(kb.h("st"), kb.n("st"))
        kb.phase(sync=p_gr)

    if gates_d is not None:
        def p_dbg(sy):
            kb.inc(sy.dma_start(out=gates_d, in_=gates[:]), "st", 16)
            kb.inc(sy.dma_start(out=grow_d, in_=grow[:]), "st", 16)
        kb.phase(sync=p_dbg)

    ov0.close()
    ov1 = ExitStack()

    def osb(name, shape, dtype):
        return ov1.enter_context(kb.sbuf_cm(name, shape, dtype))

    hb = osb("hb", [128, 32, 1024], BF16)
    w1b = osb("w1b", [128, 2, CT, 128], BF16)
    w2b = osb("w2b", [128, 2, 32, 128], BF16)
    gbc = osb("gbc", [128, 1024], F32)
    yacc = osb("yacc", [128, CT, 1024], F32)
    ytmp = osb("ytmp", [128, 512], F32)
    y3c = osb("y3c", [128, CT, 512], F32)

    for ch in range(2):
        def p_z(v):
            for k in range(CT):
                v.memset(yacc[:, k], 0.0)
        kb.phase(vector=p_z)

        for e in range(8):
            b_m = kb.n("m")
            b_vg = kb.n("V")

            def p_gb(t, e=e, ch=ch, b_vg=b_vg):
                for half in range(2):
                    if half == 1:
                        t.wait_ge(kb.h("V"), b_vg + 1)
                    i = t.matmul(out=pB[:],
                                 lhsT=toh8[:, e], rhs=grow[0:8, 1024 * ch + 512 * half:
                                                           1024 * ch + 512 * (half + 1)],
                                 start=True, stop=True)
                    kb.inc(i, "m")

            def p_gbe(v, b_m=b_m):
                for half in range(2):
                    v._eng.wait_ge(kb.h("m"), b_m + half + 1)
                    v.tensor_copy(out=gbc[:, 512 * half:512 * (half + 1)], in_=pB[:])
            kb.phase(tensor=p_gb, vector=p_gbe)

            b_ld, b_m2, b_sc = kb.n("ld"), kb.n("m"), kb.n("SC")

            def p_w1s(g, e=e, b_m2=b_m2):
                for ht in range(32):
                    if ht >= 2:
                        g._eng.wait_ge(kb.h("m"), b_m2 + 2 * (ht - 1))
                    for dd in range(CT):
                        kb.inc(g.dma_start(out=w1b[:, ht % 2, dd],
                                           in_=w1_d[e, 128 * dd:128 * (dd + 1),
                                                    128 * ht:128 * (ht + 1)]), "ld", 16)

            def p_hm(t, ch=ch, b_ld=b_ld, b_sc=b_sc):
                for ht in range(32):
                    t.wait_ge(kb.h("ld"), b_ld + 16 * 8 * (ht + 1))
                    for half in range(2):
                        u = 2 * ht + half
                        if u >= 2:
                            t.wait_ge(kb.h("SC"), b_sc + u - 1)
                        for dd in range(CT):
                            i = t.matmul(out=pH[u % 2][:], lhsT=w1b[:, ht % 2, dd],
                                         rhs=xnb[:, dd, 1024 * ch + 512 * half:
                                                 1024 * ch + 512 * (half + 1)],
                                         start=(dd == 0), stop=(dd == CT - 1))
                        kb.inc(i, "m")

            def p_he(sc, b_m2=b_m2):
                for ht in range(32):
                    for half in range(2):
                        u = 2 * ht + half
                        sc._eng.wait_ge(kb.h("m"), b_m2 + u + 1)
                        sc.activation(out=hb[:, ht, 512 * half:512 * (half + 1)],
                                      in_=pH[u % 2][:], func=AF.Gelu)
            kb.phase(gpsimd=p_w1s, tensor=p_hm, scalar=p_he)

            b_ld3, b_m3, b_vv3 = kb.n("ld"), kb.n("m"), kb.n("V")

            def p_w2s(g, e=e, b_m3=b_m3):
                for ot in range(CT):
                    if ot >= 2:
                        g._eng.wait_ge(kb.h("m"), b_m3 + 2 * (ot - 1))
                    for ht in range(32):
                        kb.inc(g.dma_start(out=w2b[:, ot % 2, ht],
                                           in_=w2_d[e, 128 * ht:128 * (ht + 1),
                                                    128 * ot:128 * (ot + 1)]), "ld", 16)

            def p_wm(t, b_ld3=b_ld3, b_vv3=b_vv3):
                for ot in range(CT):
                    t.wait_ge(kb.h("ld"), b_ld3 + 16 * 32 * (ot + 1))
                    for half in range(2):
                        u = 2 * ot + half
                        if u >= 2:
                            t.wait_ge(kb.h("V"), b_vv3 + 2 * (u - 1))
                        for ht in range(32):
                            i = t.matmul(out=pW[u % 2][:], lhsT=w2b[:, ot % 2, ht],
                                         rhs=hb[:, ht, 512 * half:512 * (half + 1)],
                                         start=(ht == 0), stop=(ht == 31))
                        kb.inc(i, "m")

            def p_wa(v, b_m3=b_m3):
                for ot in range(CT):
                    for half in range(2):
                        u = 2 * ot + half
                        v._eng.wait_ge(kb.h("m"), b_m3 + u + 1)
                        v.tensor_mul(out=ytmp[:], in0=pW[u % 2][:],
                                     in1=gbc[:, 512 * half:512 * (half + 1)])
                        v.tensor_add(out=yacc[:, ot, 512 * half:512 * (half + 1)],
                                     in0=yacc[:, ot, 512 * half:512 * (half + 1)],
                                     in1=ytmp[:])
            kb.phase(gpsimd=p_w2s, tensor=p_wm, vector=p_wa)

        for half in range(2):
            cs = slice(1024 * ch + 512 * half, 1024 * ch + 512 * (half + 1))

            def p_ly3(sy, cs=cs):
                for k in range(CT):
                    kb.inc(sy.dma_start(out=y3c[:, k], in_=y3_d[128 * k:128 * (k + 1), cs]),
                           "ld", 16)
            kb.phase(sync=p_ly3)

            def p_add(v, half=half):
                for k in range(CT):
                    v.tensor_add(out=yacc[:, k, 512 * half:512 * (half + 1)],
                                 in0=yacc[:, k, 512 * half:512 * (half + 1)],
                                 in1=y3c[:, k])
            kb.phase(vector=p_add)

            def p_sty(sy, cs=cs, half=half):
                for k in range(CT):
                    kb.inc(sy.dma_start(out=out_d[128 * k:128 * (k + 1), cs],
                                        in_=yacc[:, k, 512 * half:512 * (half + 1)]),
                           "st", 16)
            kb.phase(sync=p_sty)

    def p_fin(sy):
        sy.wait_ge(kb.h("st"), kb.n("st"))
    kb.phase(sync=p_fin)
    ov1.close()


def emit_collective(kb, kind, in_ap, out_ap, op=None):
    kb.sems_once("cc")

    def p_cc(g):
        i = g.collective_compute(kind, op if op is not None else ALU.bypass,
                                 replica_groups=[list(range(NCORE))],
                                 ins=[in_ap.opt()], outs=[out_ap.opt()])
        kb.inc(i, "cc", 1)
        g.wait_ge(kb.h("cc"), kb.n("cc"))
    kb.phase(gpsimd=p_cc)


def emit_d2d(kb, dst_ap, src_ap, rows, wdt, dtype=F32):
    kb.push()
    eb = kb.sbuf("ebuf", [128, wdt], dtype)

    def p_cp(sy):
        for k in range(rows // 128):
            kb.inc(sy.dma_start(out=eb[:], in_=src_ap[128 * k:128 * (k + 1), :]), "ld", 16)
            sy.wait_ge(kb.h("ld"), kb.n("ld"))
            kb.inc(sy.dma_start(out=dst_ap[128 * k:128 * (k + 1), :], in_=eb[:]), "st", 16)
            sy.wait_ge(kb.h("st"), kb.n("st"))
    kb.phase(sync=p_cp)
    kb.pop()


def emit_halo_fetch(kb, ag_flat, hidx_s, hm_s, dst_ap, wdt):
    kb.push()
    hb = kb.sbuf("halob", [128, CT, wdt], F32)

    def p_g(g):
        for k in range(CT):
            kb.inc(g.indirect_dma_start(
                out=hb[:, k], out_offset=None, in_=ag_flat,
                in_offset=IndirectOffsetOnAxis(ap=hidx_s[:, k:k + 1], axis=0)), "ld", 16)
    kb.phase(gpsimd=p_g)

    def p_m(v):
        for k in range(CT):
            v.tensor_scalar_mul(hb[:, k], hb[:, k], hm_s[:, 0:1])
    kb.phase(vector=p_m)

    def p_o(sy):
        for k in range(CT):
            kb.inc(sy.dma_start(out=dst_ap[128 * k:128 * (k + 1), :], in_=hb[:, k]),
                   "st", 16)
        sy.wait_ge(kb.h("st"), kb.n("st"))
    kb.phase(sync=p_o)
    kb.pop()


def build_fused():
    kb = KB()
    xh = kb.dram_in("xh", [D, L0A_XW], F32)
    encq = kb.dram_in("encq", [8, D, 128], F32)
    encv = kb.dram_in("encv", [8, D, 128], F32)
    dec = kb.dram_in("decoder", [D, D], F32)
    heb = kb.dram_in("hebbian", [8, 128, 128], F32)
    mw1 = kb.dram_in("m_w1", [D, 512], F32)
    mb1 = kb.dram_in("m_b1", [1, 512], F32)
    mw2 = kb.dram_in("m_w2", [512, D], F32)
    mb2 = kb.dram_in("m_b2", [1, D], F32)
    baseT = kb.dram_in("base_T", [128, 8], F32)
    wq6 = kb.dram_in("wq6", [6, D, 1024], F32)
    wk6 = kb.dram_in("wk6", [6, D, 1024], F32)
    wv6 = kb.dram_in("wv6", [6, D, 1024], F32)
    wo6 = kb.dram_in("wo6", [6, 1024, D], F32)
    qs6 = kb.dram_in("qs6", [6, 128, 1], F32)
    ks6 = kb.dram_in("ks6", [6, 128, 1], F32)
    csq = kb.dram_in("csq", [128, 2176], BF16)
    snq = kb.dram_in("snq", [128, 2176], BF16)
    csk = kb.dram_in("csk", [128, 2432], BF16)
    snk = kb.dram_in("snk", [128, 2432], BF16)
    maskp = kb.dram_in("maskp", [4, 128, 384], BF16)
    w1p6 = kb.dram_in("w1p6", [6, 44, D, 128], F32)
    w2po6 = kb.dram_in("w2po6", [6, 8, 2816, 128], F32)
    g6 = kb.dram_in("g6", [6, 128, 8], F32)
    b6 = kb.dram_in("b6", [6, 128, 8], F32)
    b1a6 = kb.dram_in("b1a6", [6, 128, 22], F32)
    b1g6 = kb.dram_in("b1g6", [6, 128, 22], F32)
    b2_6 = kb.dram_in("b2_6", [6, 128, 8], F32)
    dwc = kb.dram_in("dwc", [128, 32], F32)
    dwb = kb.dram_in("dwb", [128, 8], F32)
    pwbias = kb.dram_in("pwbias", [128, 8], F32)
    gcv = kb.dram_in("gcv", [128, 8], F32)
    ggr = kb.dram_in("ggr", [128, 8], F32)
    pwT = kb.dram_in("pwT", [D, D], F32)
    whg = kb.dram_in("whg", [4, 256, 768], F32)
    wout = kb.dram_in("wout", [12, 128, 256], F32)
    gff = kb.dram_in("gff", [128, 8], F32)
    wg_h = kb.dram_in("wg_h", [22, D, 128], F32)
    wv_h = kb.dram_in("wv_h", [22, D, 128], F32)
    wf_h = kb.dram_in("wf_h", [8, 2816, 128], F32)
    mP_d = kb.dram_in("mP", [128, 8], F32)
    mE_d = kb.dram_in("mE", [128, 8], F32)
    gw = kb.dram_in("gw", [D, 8], F32)
    sc_d = kb.dram_in("scol", [128, 16], F32)
    w1_d = kb.dram_in("w1", [8, D, 4096], F32)
    w2_d = kb.dram_in("w2", [8, 4096, D], F32)
    hidx_d = kb.dram_in("hidx", [128, CT], I32)
    hmask_d = kb.dram_in("hmask", [128, 1], F32)
    yo = kb.dram_out("yo", [D, T], F32)

    qs_in = kb.dram_int("qs_in", [128, 16], F32)
    qs_ag = kb.dram_int("qs_ag", [NCORE, 128, 16], F32, shared=True)
    y1_d = kb.dram_int("y1_d", [D, L0A_W], F32)
    xcur = [kb.dram_int(f"xcur{i}", [D, F0A_XW], F32) for i in range(2)]
    ya_d = kb.dram_int("ya_d", [D, 2180], F32)
    ag_in = kb.dram_int("ag_in", [D, 385], F32)
    ag_out = kb.dram_int("ag_out", [NCORE * D, 385], F32, shared=True)
    x6_d = kb.dram_int("x6_d", [D, T], F32)
    y2_d = kb.dram_int("y2_d", [D, T], F32)
    y2ln_d = kb.dram_int("y2ln_d", [D, L1C_XW], F32)
    e2_in = kb.dram_int("e2_in", [D, 3], F32)
    e2_out = kb.dram_int("e2_out", [NCORE * D, 3], F32, shared=True)
    xa1_d = kb.dram_int("xa1_d", [D, T], F32)
    hl_d = kb.dram_int("hl_d", [128, 12, T], F32)
    al_d = kb.dram_int("al_d", [128, 12, T], F32)
    sp_in = kb.dram_int("sp_in", [128, 12], F32)
    se_in = kb.dram_int("se_in", [128, 12], F32)
    sp_ag = kb.dram_int("sp_ag", [NCORE, 128, 12], F32, shared=True)
    se_ag = kb.dram_int("se_ag", [NCORE, 128, 12], F32, shared=True)
    yb_d = kb.dram_int("yb_d", [D, T], F32)
    y3_d = kb.dram_int("y3_d", [D, T], F32)
    y3ln_d = kb.dram_int("y3ln_d", [D, T], F32)
    pr_in = kb.dram_int("pr_in", [128, 16, 8], F32)
    pr_ag = kb.dram_int("pr_ag", [NCORE, 128, 16, 8], F32, shared=True)

    alloc_common(kb)
    hm_s = kb.sbuf("hm_s", [128, 1], F32)
    hidx_s = kb.sbuf("hidx_s", [128, CT], I32)
    kb.sems("ld", "m", "st", "am", "tm", "qm", "im", "ic", "dn",
            "km", "sm", "wm", "hm", "w1l", "w2l", "hx", "cc")
    kb.start()

    def p_pre(sy):
        kb.inc(sy.dma_start(out=hm_s[:], in_=hmask_d), "ld", 16)
        kb.inc(sy.dma_start(out=hidx_s[:], in_=hidx_d), "ld", 16)
    kb.phase(sync=p_pre)

    with kb.nc.named_scope("l0a_sig"):
        kb.push()
        emit_l0a_sig(kb, {"xh": xh, "encq": encq, "qsums": qs_in, "x0": 543})
        kb.pop()
        emit_collective(kb, "AllGather", qs_in, qs_ag)
    with kb.nc.named_scope("l0a_main"):
        kb.push()
        emit_l0a_main(kb, {"xh": xh, "qsums_all": qs_ag, "encq": encq, "encv": encv,
                           "decoder": dec, "hebbian": heb, "m_w1": mw1, "m_b1": mb1,
                           "m_w2": mw2, "m_b2": mb2, "base_T": baseT, "y1": y1_d})
        kb.pop()
    with kb.nc.named_scope("ln_y1"):
        kb.push()
        emit_addln(kb, {"a": y1_d[:, 127:2560], "yln": xcur[0][:, 0:2433]},
                   with_add=False, W=2433)
        kb.pop()

    for li in range(6):
        src = xcur[li % 2]
        nxt = xcur[1 - li % 2]
        with kb.nc.named_scope(f"f0a_{li}"):
            kb.push()
            emit_f0_attn(kb, {"xh": src, "wq": wq6[li], "wk": wk6[li], "wv": wv6[li],
                              "wo": wo6[li], "qs_col": qs6[li], "ks_col": ks6[li],
                              "csq": csq, "snq": snq, "csk": csk, "snk": snk,
                              "maskp": maskp, "ya": ya_d[:, 0:2176]}, with_t0=True)
            kb.pop()
        with kb.nc.named_scope(f"f0f_{li}"):
            kb.push()
            emit_f0_ffn(kb, {"xf": ya_d[:, 127:2179], "w1p": w1p6[li], "w2po": w2po6[li],
                             "g_col": g6[li], "b_col": b6[li], "b1a_col": b1a6[li],
                             "b1g_col": b1g6[li], "b2_col": b2_6[li],
                             "yf": (nxt[:, 385:2433] if li < 5 else x6_d)},
                        hmask=hm_s)
            kb.pop()
        if li < 5:
            with kb.nc.named_scope(f"halo_{li}"):
                emit_d2d(kb, ag_in, nxt[:, 2048:2433], D, 385)
                emit_collective(kb, "AllGather", ag_in, ag_out)
                emit_halo_fetch(kb, ag_out, hidx_s, hm_s, nxt[:, 0:385], 385)

    with kb.nc.named_scope("y2_addln"):
        kb.push()
        emit_addln(kb, {"a": y1_d[:, 512:2560], "b": x6_d, "ysum": y2_d,
                        "yln": y2ln_d[:, 3:2051]}, with_add=True)
        kb.pop()
        emit_d2d(kb, e2_in, y2ln_d[:, 2048:2051], D, 3)
        emit_collective(kb, "AllGather", e2_in, e2_out)
        emit_halo_fetch(kb, e2_out, hidx_s, hm_s, y2ln_d[:, 0:3], 3)

    with kb.nc.named_scope("l1a_pre"):
        kb.push()
        emit_l1a_pre(kb, {"xh": y2ln_d, "dwc": dwc, "dwb": dwb, "pwbias": pwbias,
                          "gcv": gcv, "ggr": ggr, "pwT": pwT, "whg": whg,
                          "xa": xa1_d, "hl": hl_d, "al": al_d, "summ": None,
                          "summP": sp_in, "summE": se_in})
        kb.pop()
        emit_collective(kb, "AllGather", sp_in, sp_ag)
        emit_collective(kb, "AllGather", se_in, se_ag)
    with kb.nc.named_scope("l1a_post"):
        kb.push()
        emit_l1a_post(kb, {"xa": xa1_d, "hl": hl_d, "al": al_d, "wout": wout,
                           "gff": gff, "wg_h": wg_h, "wv_h": wv_h, "wf_h": wf_h,
                           "yb": yb_d},
                      summ_ag=(sp_ag, se_ag, mP_d, mE_d))
        kb.pop()

    with kb.nc.named_scope("y3_addln"):
        kb.push()
        emit_addln(kb, {"a": y2_d, "b": yb_d, "ysum": y3_d, "yln": y3ln_d},
                   with_add=True)
        kb.pop()

    with kb.nc.named_scope("moe_sel"):
        kb.push()
        emit_moe_probs(kb, {"xh": y3ln_d, "gw": gw, "pr": pr_in})
        kb.pop()
        emit_collective(kb, "AllGather", pr_in, pr_ag)
    with kb.nc.named_scope("moe_main"):
        kb.push()
        emit_moe_main(kb, {"xh": y3ln_d, "y3": y3_d, "scol": sc_d, "w1": w1_d,
                           "w2": w2_d, "yo": yo, "gates_o": None, "grow_o": None},
                      prall=pr_ag)
        kb.pop()
    return kb


def _prep_l1a(inputs):
    dw = inputs["L1_attn_dw_w"].astype(np.float32)
    o = {}
    o["dwc"] = np.ascontiguousarray(
        dw.reshape(8, 128, 4).transpose(1, 0, 2).reshape(128, 32))
    o["dwb"] = np.ascontiguousarray(inputs["L1_attn_dw_b"].reshape(8, 128).T)
    o["pwbias"] = np.ascontiguousarray(inputs["L1_attn_pw_b"].reshape(8, 128).T)
    o["gcv"] = np.ascontiguousarray(inputs["L1_attn_conv_gamma"].reshape(8, 128).T)
    o["ggr"] = np.ascontiguousarray(inputs["L1_attn_gru_gamma"].reshape(8, 128).T)
    o["gff"] = np.ascontiguousarray(inputs["L1_attn_ff_gamma"].reshape(8, 128).T)
    o["pwT"] = np.ascontiguousarray(inputs["L1_attn_pw_w"].T)
    o["whg"] = np.ascontiguousarray(inputs["L1_attn_w_hg"])
    wout = inputs["L1_attn_w_out"]
    o["wout"] = np.ascontiguousarray(
        wout.reshape(4, 3, 128, 256).reshape(12, 128, 256))
    wg = np.pad(inputs["L1_attn_w_gate"], ((0, 0), (0, 86)))
    wv = np.pad(inputs["L1_attn_w_value"], ((0, 0), (0, 86)))
    o["wg_h"] = np.ascontiguousarray(wg.reshape(1024, 22, 128).transpose(1, 0, 2))
    o["wv_h"] = np.ascontiguousarray(wv.reshape(1024, 22, 128).transpose(1, 0, 2))
    wf = np.pad(inputs["L1_attn_w_ff_out"], ((0, 86), (0, 0)))
    o["wf_h"] = np.ascontiguousarray(wf.reshape(2816, 8, 128).transpose(1, 0, 2))
    return o


def _pack_summ(summ_all, c):
    smP = np.ones((128, 8, 12), dtype=np.float32)
    smE = np.zeros((128, 8, 12), dtype=np.float32)
    for jj in range(c):
        smP[:, jj] = summ_all[jj][:, 0, :]
        smE[:, jj] = summ_all[jj][:, 1, :]
    return smP, smE


def _run(kb, maps):
    for m in maps:
        m.update(kb.cin_used)
    return run_bass_kernel_spmd(kb.nc, maps, core_ids=list(range(NCORE)))


def _slices(full, w=T, lpad=0, rpad=0):
    fp = np.pad(full, ((0, 0), (lpad, rpad))) if (lpad or rpad) else full
    return [np.ascontiguousarray(fp[:, c * T:c * T + w]) for c in range(NCORE)]


def kernel_unfused(**inputs):
    x = np.asarray(inputs["L0_attn_x"])
    xT = np.ascontiguousarray(x[0].T.astype(np.float32))

    kbS = build_l0a_sig()
    maps = [{"xh": sl, "encq": inputs["L0_attn_enc_q"]}
            for sl in _slices(xT)]
    r = _run(kbS, maps)
    qsums_all = np.stack([r.results[c]["qsums"] for c in range(NCORE)])

    kbM = build_l0a_main()
    maps = [{"xh": sl, "qsums_all": qsums_all,
             "encq": inputs["L0_attn_enc_q"], "encv": inputs["L0_attn_enc_v"],
             "decoder": inputs["L0_attn_decoder"], "hebbian": inputs["L0_attn_hebbian"],
             "m_w1": inputs["L0_attn_m_w1"],
             "m_b1": np.ascontiguousarray(inputs["L0_attn_m_b1"].reshape(1, 512)),
             "m_w2": inputs["L0_attn_m_w2"],
             "m_b2": np.ascontiguousarray(inputs["L0_attn_m_b2"].reshape(1, 1024)),
             "base_T": np.ascontiguousarray(inputs["L0_attn_base_metric"].T)}
            for sl in _slices(xT, w=L0A_XW, lpad=543, rpad=1)]
    r = _run(kbM, maps)
    y1 = np.concatenate([r.results[c]["y1"][:, 512:] for c in range(NCORE)], axis=1)

    kbL = build_addln(False)
    r = _run(kbL, [{"a": sl} for sl in _slices(y1)])
    cur = np.concatenate([r.results[c]["yln"] for c in range(NCORE)], axis=1)

    kbFA = build_f0_attn()
    kbFF = build_f0_ffn()
    for li in range(6):
        wts = _prep_f0_layer(inputs, li)
        maps = []
        for c, sl in enumerate(_slices(cur, w=F0A_XW, lpad=385, rpad=3)):
            s = c * T
            m = {"xh": sl}
            for key in ["wq", "wk", "wv", "wo", "qs_col", "ks_col", "qsp_col", "ksp_col"]:
                m[key] = wts[key]
            cq, sq_, _, _ = _rotary_tables(np.arange(s - 128, s + T, dtype=np.float64))
            _, _, ck, sk_ = _rotary_tables(np.arange(s - 384, s + T, dtype=np.float64))
            m["csq"], m["snq"], m["csk"], m["snk"] = cq, sq_, ck, sk_
            m["maskp"] = _f0_masks(s)
            maps.append(m)
        r = _run(kbFA, maps)
        xa = np.concatenate([r.results[c]["ya"] for c in range(NCORE)], axis=1)

        maps = []
        for sl in _slices(xa, w=2052, lpad=1, rpad=3):
            m = {"xf": sl}
            for key in ["w1p", "w2po", "g_col", "b_col", "b1a_col", "b1g_col", "b2_col"]:
                m[key] = wts[key]
            maps.append(m)
        r = _run(kbFF, maps)
        cur = np.concatenate([r.results[c]["yf"] for c in range(NCORE)], axis=1)

    kbAL = build_addln(True)
    maps = [{"a": a, "b": b} for a, b in zip(_slices(y1), _slices(cur))]
    r = _run(kbAL, maps)
    y2 = np.concatenate([r.results[c]["ysum"] for c in range(NCORE)], axis=1)
    y2ln = np.concatenate([r.results[c]["yln"] for c in range(NCORE)], axis=1)

    wl1 = _prep_l1a(inputs)
    kbP = build_l1a_pre()
    maps = []
    for sl in _slices(y2ln, w=L1C_XW, lpad=3, rpad=1):
        m = {"xh": sl}
        for key in ["dwc", "dwb", "pwbias", "gcv", "ggr", "pwT", "whg"]:
            m[key] = wl1[key]
        maps.append(m)
    r = _run(kbP, maps)
    xas = [r.results[c]["xa"] for c in range(NCORE)]
    hls = [r.results[c]["hl"] for c in range(NCORE)]
    als = [r.results[c]["al"] for c in range(NCORE)]
    summ_all = [r.results[c]["summ"] for c in range(NCORE)]

    kbQ = build_l1a_post()
    maps = []
    for c in range(NCORE):
        smP, smE = _pack_summ(summ_all, c)
        m = {"xa": xas[c], "hl": hls[c], "al": als[c], "smP": smP, "smE": smE}
        for key in ["wout", "gff", "wg_h", "wv_h", "wf_h"]:
            m[key] = wl1[key]
        maps.append(m)
    r = _run(kbQ, maps)
    yb = np.concatenate([r.results[c]["yb"] for c in range(NCORE)], axis=1)

    maps = [{"a": a, "b": b} for a, b in zip(_slices(y2), _slices(yb))]
    r = _run(kbAL, maps)
    y3 = np.concatenate([r.results[c]["ysum"] for c in range(NCORE)], axis=1)
    y3ln = np.concatenate([r.results[c]["yln"] for c in range(NCORE)], axis=1)

    kb1 = build_moe_probs()
    maps = [{"xh": sl, "gw": inputs["L1_ffn_gate_w"]} for sl in _slices(y3ln)]
    r = _run(kb1, maps)
    prs = [r.results[c]["pr"] for c in range(NCORE)]
    prA = np.zeros((128, 8, 128), dtype=np.float32)
    for c in range(NCORE):
        for lt in range(16):
            prA[:, :, 16 * c + lt] = prs[c][:, lt, :]

    kb2 = build_moe_main()
    maps = []
    for c, (slh, sl3) in enumerate(zip(_slices(y3ln), _slices(y3))):
        scol = np.zeros((128, 16), dtype=np.float32)
        for lt in range(16):
            scol[16 * c + lt, lt] = 1.0
        maps.append({"prA": prA, "xh": slh, "y3": sl3, "scol": scol,
                     "w1": inputs["L1_ffn_w1"], "w2": inputs["L1_ffn_w2"]})
    r = _run(kb2, maps)
    out = np.concatenate([r.results[c]["yo"] for c in range(NCORE)], axis=1)
    return np.ascontiguousarray(out.T).reshape(1, S, D).astype(np.float32)


def kernel(**inputs):
    x = np.asarray(inputs["L0_attn_x"])
    xT = np.ascontiguousarray(x[0].T.astype(np.float32))
    xTp = np.pad(xT, ((0, 0), (543, 1)))
    f0w = [_prep_f0_layer(inputs, li) for li in range(6)]
    wl1 = _prep_l1a(inputs)
    kb = build_fused()

    shared = {"encq": inputs["L0_attn_enc_q"], "encv": inputs["L0_attn_enc_v"],
              "decoder": inputs["L0_attn_decoder"], "hebbian": inputs["L0_attn_hebbian"],
              "m_w1": inputs["L0_attn_m_w1"],
              "m_b1": np.ascontiguousarray(inputs["L0_attn_m_b1"].reshape(1, 512)),
              "m_w2": inputs["L0_attn_m_w2"],
              "m_b2": np.ascontiguousarray(inputs["L0_attn_m_b2"].reshape(1, 1024)),
              "base_T": np.ascontiguousarray(inputs["L0_attn_base_metric"].T),
              "gw": inputs["L1_ffn_gate_w"],
              "w1": inputs["L1_ffn_w1"], "w2": inputs["L1_ffn_w2"]}
    for k6, kk in [("wq6", "wq"), ("wk6", "wk"), ("wv6", "wv"), ("wo6", "wo"),
                   ("w1p6", "w1p"), ("w2po6", "w2po"), ("g6", "g_col"),
                   ("b6", "b_col"), ("b1a6", "b1a_col"), ("b1g6", "b1g_col"),
                   ("b2_6", "b2_col")]:
        shared[k6] = np.ascontiguousarray(np.stack([f0w[li][kk] for li in range(6)]))
    shared["qs6"] = np.ascontiguousarray(
        np.stack([f0w[li]["qs_col"] for li in range(6)]))
    shared["ks6"] = np.ascontiguousarray(
        np.stack([f0w[li]["ks_col"] for li in range(6)]))
    for key in ["dwc", "dwb", "pwbias", "gcv", "ggr", "pwT", "whg", "wout",
                "gff", "wg_h", "wv_h", "wf_h"]:
        shared[key] = wl1[key]

    maps = []
    for c in range(NCORE):
        s = c * T
        m = dict(shared)
        m["xh"] = np.ascontiguousarray(xTp[:, s:s + L0A_XW])
        cq, sq_, _, _ = _rotary_tables(np.arange(s - 128, s + T, dtype=np.float64))
        _, _, ck, sk_ = _rotary_tables(np.arange(s - 384, s + T, dtype=np.float64))
        m["csq"], m["snq"], m["csk"], m["snk"] = cq, sq_, ck, sk_
        m["maskp"] = _f0_masks(s)
        scol = np.zeros((128, 16), dtype=np.float32)
        for lt in range(16):
            scol[16 * c + lt, lt] = 1.0
        m["scol"] = scol
        hidx = np.zeros((128, CT), dtype=np.int32)
        if c > 0:
            for k in range(CT):
                hidx[:, k] = (c - 1) * 1024 + 128 * k + np.arange(128)
        m["hidx"] = hidx
        m["hmask"] = np.full((128, 1), 1.0 if c > 0 else 0.0, dtype=np.float32)
        mm = np.zeros((128, 8), dtype=np.float32)
        mm[:, :c] = 1.0
        m["mP"] = mm
        m["mE"] = mm.copy()
        m.update(kb.cin_used)
        maps.append(m)
    r = run_bass_kernel_spmd(kb.nc, maps, core_ids=list(range(NCORE)))
    out = np.concatenate([r.results[c]["yo"] for c in range(NCORE)], axis=1)
    return np.ascontiguousarray(out.T).reshape(1, S, D).astype(np.float32)
